# version 07
# baseline (speedup 1.0000x reference)
import os
import numpy as np
from contextlib import ExitStack

import concourse.bass as bass
import concourse.mybir as mybir
from concourse.bass_utils import run_bass_kernel_spmd

F32 = mybir.dt.float32
F32R = mybir.dt.float32r
BF16 = mybir.dt.bfloat16
AF = mybir.ActivationFunctionType
ALU = mybir.AluOpType
AX = mybir.AxisListType
ET = mybir.EngineType

S, D, NH = 8192, 4096, 32
QR, KVR, RD, ND, VD = 1536, 512, 64, 128, 128
QK = ND + RD
NCORES, HPC, SQC = 8, 4, 1024
EPS_RMS, EPS_LN = 1e-6, 1e-5
INV_SQRT_QK = float(QK ** (-0.5))

QC_OFF, QROPE_OFF = 0, QR
KC_OFF = QR + NH * RD
KR_OFF = KC_OFF + KVR
AG1R = KR_OFF + RD

DEBUG = False
_KS = os.environ.get("KSTOP", "4")
KSTOP = {"0": 10, "1a": 11, "1b": 12, "1": 13, "1c": 13, "2": 20, "3": 30, "4": 40}[_KS]


class _SkipBlk:
    def __getattr__(self, name):
        return lambda f: None


def _host_layout(inputs):
    x = np.ascontiguousarray(inputs["L0_attn_hidden_states"][0])
    xT = np.ascontiguousarray(x.T)
    w_qa = np.ascontiguousarray(inputs["L0_attn_w_qa"])
    g_qa = np.ascontiguousarray(inputs["L0_attn_g_qa"])
    w_qb = np.ascontiguousarray(inputs["L0_attn_w_qb"])
    w_kva = np.ascontiguousarray(inputs["L0_attn_w_kva"])
    g_kva = np.ascontiguousarray(inputs["L0_attn_g_kva"])
    w_kvb = np.ascontiguousarray(inputs["L0_attn_w_kvb"])
    w_o = np.ascontiguousarray(inputs["L0_attn_w_o"])
    A = np.ascontiguousarray(inputs["L0_ffn_A"])
    B = np.ascontiguousarray(inputs["L0_ffn_B"])
    Wr = np.ascontiguousarray(inputs["L0_ffn_Wr"])

    wq3 = w_qb.reshape(QR, NH, QK)
    wqb_rope = np.ascontiguousarray(wq3[:, :, ND:].reshape(QR, NH * RD))
    wk3 = w_kvb.reshape(KVR, NH, ND + VD)

    A_t = np.ascontiguousarray(A.transpose(2, 0, 1).reshape(D, 1024))
    B_t = np.ascontiguousarray(B.transpose(0, 2, 1).reshape(1024, D))
    WrT = np.ascontiguousarray(Wr.T)
    gqa_t = np.ascontiguousarray(g_qa.reshape(QR // 128, 128).T)
    gkva_t = np.ascontiguousarray(g_kva.reshape(KVR // 128, 128).T)

    inv_freq = 1.0 / (10000.0 ** (np.arange(0, RD, 2, dtype=np.float64) / RD))
    t = np.arange(S, dtype=np.float64)
    freqs = np.outer(t, inv_freq)
    emb = np.concatenate([freqs, freqs], axis=-1)
    cosT = np.ascontiguousarray(np.cos(emb).T.astype(np.float32))
    sinT = np.sin(emb).T.astype(np.float32)
    sinTs = sinT.copy()
    sinTs[: RD // 2] *= -1.0
    sinTs = np.ascontiguousarray(sinTs)

    emat = np.zeros((64, 1024), dtype=np.float32)
    for e_ in range(64):
        emat[e_, 16 * e_:16 * e_ + 16] = 1.0
    ident = np.eye(128, dtype=np.float32)

    per_core = []
    for c in range(NCORES):
        sl = slice(SQC * c, SQC * (c + 1))
        heads = list(range(HPC * c, HPC * (c + 1)))
        per_core.append(dict(
            xt=np.ascontiguousarray(xT[:, sl]),
            wqa=w_qa,
            wqbr=wqb_rope,
            wqbn=np.ascontiguousarray(np.concatenate([wq3[:, h, :ND] for h in heads], 1)),
            wkva=w_kva,
            wkn=np.ascontiguousarray(np.concatenate([wk3[:, h, :ND] for h in heads], 1)),
            wkv=np.ascontiguousarray(np.concatenate([wk3[:, h, ND:] for h in heads], 1)),
            wo=w_o, at=A_t, bt=B_t, wrt=WrT,
            gqa=gqa_t, gkva=gkva_t,
            coss=np.ascontiguousarray(cosT[:, sl]),
            sins=np.ascontiguousarray(sinTs[:, sl]),
            emat=emat, ident=ident,
        ))
    return per_core


class Ctr:
    def __init__(self, handle):
        self.h = handle
        self.v = 0

    def tick(self, instr, n=1):
        self.v += n
        instr.then_inc(self.h, n)
        return self.v


def build_nc():
    nc = bass.Bass("TRN2", target_bir_lowering=False)

    def din(name, shape):
        return nc.dram_tensor(name, shape, F32, kind="ExternalInput").ap()

    xt = din("xt", [D, SQC])
    wqa = din("wqa", [D, QR])
    wqbr = din("wqbr", [QR, NH * RD])
    wqbn = din("wqbn", [QR, HPC * ND])
    wkva = din("wkva", [D, KVR + RD])
    wkn = din("wkn", [KVR, HPC * ND])
    wkv = din("wkv", [KVR, HPC * VD])
    wo = din("wo", [NH * VD, D])
    at = din("at", [D, 1024])
    bt = din("bt", [1024, D])
    wrt = din("wrt", [D, 64])
    gqa = din("gqa", [128, QR // 128])
    gkva = din("gkva", [128, KVR // 128])
    coss = din("coss", [RD, SQC])
    sins = din("sins", [RD, SQC])
    emat = din("emat", [64, 1024])
    ident = din("ident", [128, 128])
    out = nc.dram_tensor("out", [SQC, D], F32, kind="ExternalOutput").ap()

    ag1i = nc.dram_tensor("ag1_in", [AG1R, SQC], BF16, kind="Internal").ap()
    ag1o = nc.dram_tensor("ag1_out", [NCORES * AG1R, SQC], BF16, kind="Internal").ap()
    ag2i = nc.dram_tensor("ag2_in", [HPC * VD, S], BF16, kind="Internal").ap()
    qrl = nc.dram_tensor("qrl", [NCORES * HPC * RD, SQC], BF16, kind="Internal").ap()
    ag2o = nc.dram_tensor("ag2_out", [NH * VD, S], BF16, kind="Internal").ap()

    dbg = {}
    if DEBUG:
        def dout(name, shape, dt=F32):
            return nc.dram_tensor(name, shape, dt, kind="ExternalOutput").ap()
        dbg["qc"] = dout("dbg_qc", [12 * 128, SQC])
        dbg["kc"] = dout("dbg_kc", [4 * 128, SQC])
        dbg["kr"] = dout("dbg_kr", [64, SQC])
        dbg["qrope"] = dout("dbg_qrope", [NH * RD, SQC])
        dbg["kn"] = dout("dbg_kn", [128, HPC * S], BF16)
        dbg["v4"] = dout("dbg_v4", [128, 64 * 512], BF16)
        dbg["av"] = dout("dbg_av", [HPC * VD, S])
        dbg["z"] = dout("dbg_z", [128, 32 * 512])
        dbg["gg"] = dout("dbg_gg", [128, 4 * 64])
        dbg["hg"] = dout("dbg_hg", [128, 8 * 512])

    with ExitStack() as ctx:
        e = ctx.enter_context

        ARENA = 196608
        arena = nc.alloc_sbuf_tensor("arena", [128, ARENA], mybir.dt.uint8)
        arena_ap = arena.ap()

        def sb(name, shape, dtype, off):
            assert off % 32 == 0, name
            nb = shape[-1] * mybir.dt.size(dtype)
            assert off + nb <= ARENA, (name, off + nb)
            return arena_ap[0:shape[0], off:off + nb].bitcast(dtype)

        OA, OB, OC = 0, 65536, 131072
        xtb = sb("xtb", [128, 32 * 1024], BF16, OA)
        kn4 = sb("kn4", [128, HPC * S], BF16, OA)
        zh = sb("zh", [128, 32 * 512], F32, OA)
        qcpre = sb("qcpre", [128, 12 * 1024], F32, OB)
        kctb = sb("kctb", [128, 4 * 1024], BF16, OB)
        krb = sb("krb", [64, 1024], BF16, OB + 8192)
        qrb = [sb(f"qrb{i}", [64, 1024], BF16, OB + 10240 + 2048 * i) for i in range(2)]
        ropescr = sb("ropescr", [64, 1024], F32, OB + 14336)
        ropescr2 = sb("ropescr2", [64, 1024], F32, OB + 18432)
        p1wb = [sb(f"p1wb{i}", [128, 32 * 128], BF16, OB + 49152 + 8192 * i) for i in range(2)]
        v4 = sb("v4", [128, 64 * 512], BF16, OB)
        avsl = sb("avsl", [128, 32 * 512], BF16, OB)
        zbf = sb("zbf", [128, 32 * 512], BF16, OB + 32768)
        hbf = sb("hbf", [128, 8 * 512], BF16, OB)
        hgbf = sb("hgbf", [128, 8 * 512], BF16, OB + 8192)
        atb = [sb(f"atb{i}", [128, 32 * 128], BF16, OB + 16384 + 8192 * i) for i in range(2)]
        btb = [sb(f"btb{i}", [128, 8 * 128], BF16, OB + 16384 + 2048 * i) for i in range(2)]
        vTq = [sb(f"vTq{i}", [128, 4096], F32, OB + 16384 * i) for i in range(2)]
        vTo = [sb(f"vTo{i}", [128, 4096], F32, OB + 32768 + 16384 * i) for i in range(2)]
        row_q = sb("row_q", [1, 1024], F32, OC)
        row_k = sb("row_k", [1, 1024], F32, OC + 4096)
        row_t = sb("row_t", [1, 1024], F32, OC + 8192)
        row_r = None
        row_r2 = None
        q2row = sb("q2row", [1, 512], F32, OC + 18432)
        CC = OC + 20480
        rbsb = sb("rbsb", [128, 512], F32, OC + 2048)
        avob = [sb(f"avob{i}", [128, 512], BF16, OC + 4096 + 1024 * i) for i in range(2)]
        qnb = sb("qnb", [128, 512], BF16, OC + 6144)
        qra = [sb(f"qra{i}", [65, 512], BF16, OC + 7168 + 7168 * i) for i in range(1)] + \
              [sb("qra1", [65, 512], BF16, OC + 14336)]
        accd = sb("accd", [128, 512], F32, OC + 10240)
        sscr = sb("sscr", [128, 512], BF16, OC + 16384)
        sscr2 = sb("sscr2", [64, 512], BF16, OC + 15360)
        hscr = sb("hscr", [128, 512], F32, OC + 10240)
        p1sq = sb("p1sq", [128, 512], BF16, OC)
        p1sq2 = sb("p1sq2", [128, 512], BF16, OC + 2048)
        qsb = sb("qsb", [128, 1024], F32, OC + 4096)
        kvpre = sb("kvpre", [128, 5 * 1024], F32, CC)
        qctb = sb("qctb", [128, 12 * 1024], BF16, CC + 20480)
        roped = sb("roped", [65, S], BF16, CC)
        qcb = sb("qcb", [128, 12 * 512], BF16, CC + 16384)
        wqbnb = sb("wqbnb", [128, 12 * 512], BF16, CC + 28672)
        pbuf = [sb(f"pbuf{i}", [128, 1024], BF16, CC + 40960 + 2048 * i) for i in range(2)]
        kcw = [sb(f"kcw{i}", [128, 4 * 512], BF16, CC + 16384 + 4096 * i) for i in range(2)]
        wknb = sb("wknb", [128, 4 * 512], BF16, CC + 24576)
        wkvb_ = sb("wkvb_", [128, 4 * 512], BF16, CC + 28672)
        wob = [sb(f"wob{i}", [128, 32 * 128], BF16, CC + 8192 * i) for i in range(2)]
        xres = [sb(f"xres{i}", [128, 512], F32, CC + 16384 + 2048 * i) for i in range(2)]
        wrtb = sb("wrtb", [128, 32 * 64], F32, CC + 20480)
        rrsb = sb("rrsb", [64, 512], F32, CC + 28672)
        rsrow = sb("rsrow", [128, 4 * 64], F32, CC + 30720)
        zap = sb("zap", [128, 4 * 64], F32, CC + 31744)
        gvals = sb("gvals", [128, 4 * 64], F32, CC + 32768)
        mx16 = sb("mx16", [128, 16], F32, CC + 33792)
        colr = sb("colr", [128, 8], F32, CC + 33856)
        gtb = sb("gtb", [64, 512], BF16, CC + 34048)
        casb = sb("casb", [128, 8], F32, CC + 35072)
        cwsb = sb("cwsb", [64, 1], F32, CC + 35104)
        rstdb = sb("rstdb", [128, 512], F32, CC + 35840)
        scl2b = sb("scl2b", [128, 512], F32, CC + 37888)
        negmub = sb("negmub", [128, 512], F32, CC + 39936)
        sqzb = sb("sqzb", [128, 512], BF16, CC + 41984)
        sqzb2 = sb("sqzb2", [128, 512], BF16, CC + 43008)
        bnst = sb("bnst", [128, 64], F32, CC + 44032)

        row_r = e(nc.sbuf_tensor("row_r", [1, 1024], F32R)).ap()
        row_r2 = e(nc.sbuf_tensor("row_r2", [1, 512], F32R)).ap()
        c_ident = e(nc.sbuf_tensor("c_ident", [128, 128], F32)).ap()
        c_emat = e(nc.sbuf_tensor("c_emat", [64, 1024], BF16)).ap()
        c_gqa = e(nc.sbuf_tensor("c_gqa", [128, 12], F32)).ap()
        c_gkva = e(nc.sbuf_tensor("c_gkva", [128, 4], F32)).ap()
        c_ones_bf = e(nc.sbuf_tensor("c_ones_bf", [128, 1], BF16)).ap()
        c_ones_r = e(nc.sbuf_tensor("c_ones_r", [128, 1], F32R)).ap()
        c_ones1r = e(nc.sbuf_tensor("c_ones1r", [1, 128], F32R)).ap()
        c_ones_f = e(nc.sbuf_tensor("c_ones_f", [128, 1], F32)).ap()
        c_cos = sb("c_cos", [64, 1024], F32, OB + 22528)
        c_sin = sb("c_sin", [64, 1024], F32, OB + 26624)
        c_knmax = e(nc.sbuf_tensor("c_knmax", [1, 8], F32)).ap()
        c_mrow = e(nc.sbuf_tensor("c_mrow", [1, 16], F32)).ap()

        pg0 = e(nc.psum_tensor("pg0", [128, 1024], F32)).ap()
        pg1 = e(nc.psum_tensor("pg1", [128, 1024], F32)).ap()
        pav = [e(nc.psum_tensor(f"pav{i}", [128, 512], F32)).ap() for i in range(2)]
        pms = e(nc.psum_tensor("pms", [128, 512], F32)).ap()
        psp = e(nc.psum_tensor("psp", [128, 512], F32)).ap()
        pg = [pg0, pg1]

        cPE = Ctr(e(nc.semaphore("cPE")))
        cACT = Ctr(e(nc.semaphore("cACT")))
        cDVE = Ctr(e(nc.semaphore("cDVE")))
        cCC = Ctr(e(nc.semaphore("cCC")))
        dS = Ctr(e(nc.semaphore("dS")))
        dP = Ctr(e(nc.semaphore("dP")))
        dPs = [Ctr(e(nc.semaphore("dPa"))), Ctr(e(nc.semaphore("dPb")))]
        dSk = [Ctr(e(nc.semaphore("dSka"))), Ctr(e(nc.semaphore("dSkb")))]
        dSq = Ctr(e(nc.semaphore("dSq")))
        dSr = [Ctr(e(nc.semaphore("dSra"))), Ctr(e(nc.semaphore("dSrb")))]
        dSv = [Ctr(e(nc.semaphore("dSva"))), Ctr(e(nc.semaphore("dSvb")))]
        dSx = [Ctr(e(nc.semaphore("dSxa"))), Ctr(e(nc.semaphore("dSxb")))]

        blk = e(nc.Block())
        blk1a = blk if KSTOP >= 11 else _SkipBlk()
        blk1b = blk if KSTOP >= 12 else _SkipBlk()
        blk1c = blk if KSTOP >= 13 else _SkipBlk()
        blk2 = blk if KSTOP >= 20 else _SkipBlk()
        blk3 = blk if KSTOP >= 30 else _SkipBlk()
        blk4 = blk if KSTOP >= 40 else _SkipBlk()
        RG = [list(range(NCORES))]
        M = {}

        def lp():
            return nc.allow_low_precision(reason="f32r rounding for broadcast matmuls is intentional")

        @blk.sync
        def _(sy):
            for ap_, src in [(c_ident, ident), (c_gqa, gqa), (c_gkva, gkva)]:
                sy.dma_start(out=ap_[:, :], in_=src).then_inc(dS.h, 16)
                dS.v += 16
        M["d_const"] = dS.v

        @blk.gpsimd
        def _(g):
            g.dma_start(out=c_emat[:, :], in_=emat).then_inc(dP.h, 16)
            dP.v += 16
            for j in range(32):
                g.dma_start(out=xtb[:, 1024 * j:1024 * (j + 1)],
                            in_=xt[128 * j:128 * (j + 1), :]).then_inc(dP.h, 16)
                dP.v += 16
            for m in range(2):
                g.dma_start(out=p1wb[m][:, :],
                            in_=wqa[:, 128 * m:128 * (m + 1)].rearrange("(t p) m -> p t m", p=128)
                            ).then_inc(dP.h, 16)
                dP.v += 16
        M["d_x"] = dP.v

        @blk.vector
        def _(v):
            v.memset(c_ones_bf[:, :], 1.0)
            cDVE.tick(v.memset(c_ones_f[:, :], 1.0))
            v.wait_ge(cDVE.h, cDVE.v)
            v.memset(row_t[0:1, 0:128], 1.0)
            with lp():
                cDVE.tick(v.tensor_copy(c_ones_r[:, :], c_ones_f[:, :]))
            v.wait_ge(cDVE.h, cDVE.v)
            with lp():
                cDVE.tick(v.tensor_copy(c_ones1r[0:1, :], row_t[0:1, 0:128]))
        M["ones"] = cDVE.v

        pe0, act0, dve0 = cPE.v, cACT.v, cDVE.v
        dpa0 = [dPs[0].v, dPs[1].v]

        @blk1a.tensor
        def _(t):
            t.wait_ge(dP.h, M["d_x"])
            t.wait_ge(dS.h, M["d_const"])
            t.wait_ge(cDVE.h, M["ones"])
            for m in range(12):
                if m >= 2:
                    t.wait_ge(dPs[m % 2].h, dpa0[m % 2] + 16 * ((m - 2) // 2 + 1))
                for sq in range(2):
                    if m >= 2:
                        t.wait_ge(cDVE.h, dve0 + 2 * (m - 2) + sq + 1)
                        t.wait_ge(cACT.h, act0 + 2 * (m - 2) + sq + 1)
                    pt = pg[m % 2][:, 512 * sq:512 * sq + 512]
                    for c_ in range(32):
                        ins = t.matmul(pt, p1wb[m % 2][:, 128 * c_:128 * (c_ + 1)],
                                       xtb[:, 1024 * c_ + 512 * sq:1024 * c_ + 512 * sq + 512],
                                       start=(c_ == 0), stop=(c_ == 31))
                    cPE.tick(ins)
                    M[("1a", m, sq)] = cPE.v
                    if m >= 1:
                        j = m - 1
                        t.wait_ge(cACT.h, act0 + 2 * j + sq + 1)
                        dst = pms[0:1, 0:512] if sq == 0 else psp[0:1, 0:512]
                        ins = t.matmul(dst, c_ones_bf, p1sq if sq == 0 else p1sq2,
                                       start=(j == 0), stop=False)
                        cPE.tick(ins)
                        M[("1a_ss", j, sq)] = cPE.v
            for sq in range(2):
                j = 11
                t.wait_ge(cACT.h, act0 + 2 * j + sq + 1)
                dst = pms[0:1, 0:512] if sq == 0 else psp[0:1, 0:512]
                ins = t.matmul(dst, c_ones_bf, p1sq if sq == 0 else p1sq2,
                               start=False, stop=True)
                cPE.tick(ins)
                M[("1a_ss", j, sq)] = cPE.v
        M["1a_pe"] = cPE.v

        @blk1a.gpsimd
        def _(g):
            for m in range(2, 12):
                g.wait_ge(cPE.h, M[("1a", m - 2, 1)])
                g.dma_start(out=p1wb[m % 2][:, :],
                            in_=wqa[:, 128 * m:128 * (m + 1)].rearrange("(t p) m -> p t m", p=128)
                            ).then_inc(dPs[m % 2].h, 16)
                dPs[m % 2].v += 16
                assert dPs[m % 2].v == dpa0[m % 2] + 16 * ((m - 2) // 2 + 1)

        @blk1a.scalar
        def _(s):
            for m in range(12):
                for sq in range(2):
                    s.wait_ge(cPE.h, M[("1a", m, sq)])
                    s.wait_ge(cDVE.h, dve0 + 2 * m + sq + 1)
                    if m >= 1:
                        s.wait_ge(cPE.h, M[("1a_ss", m - 1, sq)])
                    pt = pg[m % 2][:, 512 * sq:512 * sq + 512]
                    ins = s.activation(p1sq if sq == 0 else p1sq2, pt, AF.Square)
                    cACT.tick(ins)
        assert KSTOP < 11 or (cACT.v == act0 + 24)

        @blk1a.vector
        def _(v):
            for m in range(12):
                for sq in range(2):
                    v.wait_ge(cPE.h, M[("1a", m, sq)])
                    ins = v.tensor_copy(
                        qcpre[:, 1024 * m + 512 * sq:1024 * m + 512 * sq + 512],
                        pg[m % 2][:, 512 * sq:512 * sq + 512])
                    cDVE.tick(ins)
        assert KSTOP < 11 or (cDVE.v == dve0 + 24)

        @blk1a.vector
        def _(v):
            v.wait_ge(cPE.h, M["1a_pe"])
            v.tensor_copy(row_q[0:1, 0:512], pms[0:1, 0:512])
            cDVE.tick(v.tensor_copy(row_q[0:1, 512:1024], psp[0:1, 0:512]))
        M["1a_s2"] = cDVE.v

        @blk1a.scalar
        def _(s):
            s.wait_ge(cDVE.h, M["1a_s2"])
            cACT.tick(s.activation(row_t[0:1, :], row_q[0:1, :], AF.Sqrt, scale=1.0 / QR))
        M["1a_sqrt"] = cACT.v

        @blk1a.vector
        def _(v):
            v.wait_ge(cACT.h, M["1a_sqrt"])
            cDVE.tick(v.tensor_scalar(row_t[0:1, :], row_t[0:1, :], EPS_RMS, None, ALU.add))
            v.wait_ge(cDVE.h, cDVE.v)
            cDVE.tick(v.reciprocal(row_q[0:1, :], row_t[0:1, :]))
            v.wait_ge(cDVE.h, cDVE.v)
            cDVE.tick(v.tensor_scalar(row_t[0:1, :], row_q[0:1, :], INV_SQRT_QK, None, ALU.mult))
            v.wait_ge(cDVE.h, cDVE.v)
            with lp():
                cDVE.tick(v.tensor_copy(row_r[0:1, :], row_t[0:1, :]))
        M["1a_scale"] = cDVE.v

        @blk1a.tensor
        def _(t):
            t.wait_ge(cDVE.h, M["1a_scale"])
            t.matmul(pg0[:, 0:512], c_ones1r, row_r[0:1, 0:512], start=True, stop=True)
            cPE.tick(t.matmul(pg0[:, 512:1024], c_ones1r, row_r[0:1, 512:1024], start=True, stop=True))
        M["1a_bc"] = cPE.v

        @blk1a.vector
        def _(v):
            v.wait_ge(cPE.h, M["1a_bc"])
            cDVE.tick(v.tensor_copy(qsb[:, :], pg0[:, :]))
            v.wait_ge(cDVE.h, cDVE.v)
            for m in range(12):
                for sq in range(2):
                    ins = v.scalar_tensor_tensor(
                        qctb[:, 1024 * m + 512 * sq:1024 * m + 512 * sq + 512],
                        qcpre[:, 1024 * m + 512 * sq:1024 * m + 512 * sq + 512],
                        c_gqa[:, m:m + 1],
                        qsb[:, 512 * sq:512 * sq + 512],
                        ALU.mult, ALU.mult)
            cDVE.tick(ins)
        M["1a_qctb"] = cDVE.v

        @blk1a.sync
        def _(sy):
            sy.wait_ge(cDVE.h, M["1a_qctb"])
            for m in range(12):
                sy.dma_start(out=ag1i[QC_OFF + 128 * m:QC_OFF + 128 * (m + 1), :],
                             in_=qctb[:, 1024 * m:1024 * (m + 1)]).then_inc(dS.h, 16)
                dS.v += 16
            if DEBUG:
                for m in range(12):
                    sy.dma_start(out=dbg["qc"][128 * m:128 * (m + 1), :],
                                 in_=qctb[:, 1024 * m:1024 * (m + 1)]).then_inc(dS.h, 16)
                    dS.v += 16

        @blk1b.sync
        def _(sy):
            sy.wait_ge(cDVE.h, M["1a_qctb"])
            sy.dma_start(out=c_cos[:, :], in_=coss).then_inc(dS.h, 16)
            sy.dma_start(out=c_sin[:, :], in_=sins).then_inc(dS.h, 16)
            dS.v += 32
        M["d_cossin"] = dS.v

        pe0, act0, dve0 = cPE.v, cACT.v, cDVE.v
        dpa0 = [dPs[0].v, dPs[1].v]

        @blk1b.gpsimd
        def _(g):
            for m in range(5):
                w = 128 if m < 4 else 64
                if m >= 2:
                    g.wait_ge(cPE.h, pe0 + (2 if m - 2 == 0 else 4 * (m - 2) + 1))
                else:
                    g.wait_ge(cPE.h, M["1a_pe"])
                g.dma_start(out=p1wb[m % 2][:, 0:32 * w],
                            in_=wkva[:, 128 * m:128 * m + w].rearrange("(t p) m -> p t m", p=128)
                            ).then_inc(dPs[m % 2].h, 16)
                dPs[m % 2].v += 16
                assert dPs[m % 2].v == dpa0[m % 2] + 16 * (m // 2 + 1)

        @blk1b.tensor
        def _(t):
            t.wait_ge(cDVE.h, M["1a_qctb"])
            for m in range(5):
                w = 128 if m < 4 else 64
                t.wait_ge(dPs[m % 2].h, dpa0[m % 2] + 16 * (m // 2 + 1))
                for sq in range(2):
                    if m >= 2:
                        t.wait_ge(cDVE.h, dve0 + 2 * (m - 2) + sq + 1)
                        if m - 2 < 4:
                            t.wait_ge(cACT.h, act0 + 2 * (m - 2) + sq + 1)
                    pt = pg[m % 2][0:w, 512 * sq:512 * sq + 512]
                    for c_ in range(32):
                        ins = t.matmul(pt, p1wb[m % 2][:, w * c_:w * (c_ + 1)],
                                       xtb[:, 1024 * c_ + 512 * sq:1024 * c_ + 512 * sq + 512],
                                       start=(c_ == 0), stop=(c_ == 31))
                    cPE.tick(ins)
                    M[("1b", m, sq)] = cPE.v
                    if 1 <= m <= 4:
                        j = m - 1
                        t.wait_ge(cACT.h, act0 + 2 * j + sq + 1)
                        dst = pms[0:1, 0:512] if sq == 0 else psp[0:1, 0:512]
                        ins = t.matmul(dst, c_ones_bf, p1sq if sq == 0 else p1sq2,
                                       start=(j == 0), stop=(j == 3))
                        cPE.tick(ins)
                        M[("1b_ss", j, sq)] = cPE.v
                assert M[("1b", m, 1)] == pe0 + (2 if m == 0 else 4 * m + 1), m
        M["1b_pe"] = cPE.v

        @blk1b.scalar
        def _(s):
            for m in range(4):
                for sq in range(2):
                    s.wait_ge(cPE.h, M[("1b", m, sq)])
                    s.wait_ge(cDVE.h, dve0 + 2 * m + sq + 1)
                    if m >= 1:
                        s.wait_ge(cPE.h, M[("1b_ss", m - 1, sq)])
                    pt = pg[m % 2][:, 512 * sq:512 * sq + 512]
                    ins = s.activation(p1sq if sq == 0 else p1sq2, pt, AF.Square)
                    cACT.tick(ins)
        assert KSTOP < 12 or (cACT.v == act0 + 8)

        @blk1b.vector
        def _(v):
            for m in range(5):
                w = 128 if m < 4 else 64
                for sq in range(2):
                    v.wait_ge(cPE.h, M[("1b", m, sq)])
                    ins = v.tensor_copy(
                        kvpre[0:w, 1024 * m + 512 * sq:1024 * m + 512 * sq + 512],
                        pg[m % 2][0:w, 512 * sq:512 * sq + 512])
                    cDVE.tick(ins)
        assert KSTOP < 12 or (cDVE.v == dve0 + 10)
        M["1b_kvpre"] = cDVE.v

        @blk1b.vector
        def _(v):
            v.wait_ge(cPE.h, M["1b_pe"])
            v.tensor_copy(row_k[0:1, 0:512], pms[0:1, 0:512])
            cDVE.tick(v.tensor_copy(row_k[0:1, 512:1024], psp[0:1, 0:512]))
        M["1b_s2"] = cDVE.v

        @blk1b.scalar
        def _(s):
            s.wait_ge(cDVE.h, M["1b_s2"])
            cACT.tick(s.activation(row_t[0:1, :], row_k[0:1, :], AF.Sqrt, scale=1.0 / KVR))
        M["1b_sqrt"] = cACT.v

        @blk1b.vector
        def _(v):
            v.wait_ge(cACT.h, M["1b_sqrt"])
            cDVE.tick(v.tensor_scalar(row_t[0:1, :], row_t[0:1, :], EPS_RMS, None, ALU.add))
            v.wait_ge(cDVE.h, cDVE.v)
            cDVE.tick(v.reciprocal(row_k[0:1, :], row_t[0:1, :]))
            v.wait_ge(cDVE.h, cDVE.v)
            with lp():
                cDVE.tick(v.tensor_copy(row_r[0:1, :], row_k[0:1, :]))
        M["1b_scale"] = cDVE.v

        @blk1b.tensor
        def _(t):
            t.wait_ge(cDVE.h, M["1b_scale"])
            t.matmul(pg0[:, 0:512], c_ones1r, row_r[0:1, 0:512], start=True, stop=True)
            cPE.tick(t.matmul(pg0[:, 512:1024], c_ones1r, row_r[0:1, 512:1024], start=True, stop=True))
        M["1b_bc"] = cPE.v

        @blk1b.vector
        def _(v):
            v.wait_ge(cPE.h, M["1b_bc"])
            v.wait_ge(cDVE.h, M["1a_qctb"])
            cDVE.tick(v.tensor_copy(qsb[:, :], pg0[:, :]))
            v.wait_ge(cDVE.h, cDVE.v)
            for m in range(4):
                for sq in range(2):
                    ins = v.scalar_tensor_tensor(
                        kctb[:, 1024 * m + 512 * sq:1024 * m + 512 * sq + 512],
                        kvpre[:, 1024 * m + 512 * sq:1024 * m + 512 * sq + 512],
                        c_gkva[:, m:m + 1],
                        qsb[:, 512 * sq:512 * sq + 512],
                        ALU.mult, ALU.mult)
            cDVE.tick(ins)
            v.wait_ge(cDVE.h, cDVE.v)
            v.wait_ge(dS.h, M["d_cossin"])
            kr_ = kvpre[0:64, 4096:5120]
            v.tensor_copy(ropescr[0:32, :], kr_[32:64, :])
            cDVE.tick(v.tensor_copy(ropescr[32:64, :], kr_[0:32, :]))
            v.wait_ge(cDVE.h, cDVE.v)
            v.tensor_mul(ropescr[0:64, :], ropescr[0:64, :], c_sin[:, :])
            cDVE.tick(v.tensor_mul(ropescr2[0:64, :], kr_, c_cos[:, :]))
            v.wait_ge(cDVE.h, cDVE.v)
            cDVE.tick(v.tensor_add(krb[0:64, :], ropescr[0:64, :], ropescr2[0:64, :]))
        M["1b_done"] = cDVE.v

        @blk1b.sync
        def _(sy):
            sy.wait_ge(cDVE.h, M["1b_done"])
            for m in range(4):
                sy.dma_start(out=ag1i[KC_OFF + 128 * m:KC_OFF + 128 * (m + 1), :],
                             in_=kctb[:, 1024 * m:1024 * (m + 1)]).then_inc(dS.h, 16)
                dS.v += 16
            sy.dma_start(out=ag1i[KR_OFF:KR_OFF + 64, :], in_=krb[0:64, :]).then_inc(dS.h, 16)
            dS.v += 16
            if DEBUG:
                for m in range(4):
                    sy.dma_start(out=dbg["kc"][128 * m:128 * (m + 1), :],
                                 in_=kctb[:, 1024 * m:1024 * (m + 1)]).then_inc(dS.h, 16)
                    dS.v += 16
                sy.dma_start(out=dbg["kr"][:, :], in_=krb[0:64, :]).then_inc(dS.h, 16)
                dS.v += 16

        pe0, dve0 = cPE.v, cDVE.v
        dpa0 = [dPs[0].v, dPs[1].v]
        dsr0 = [dSr[0].v, dSr[1].v]

        @blk1c.gpsimd
        def _(g):
            for h in range(NH):
                if h >= 2:
                    g.wait_ge(cPE.h, pe0 + 2 * (h - 1))
                else:
                    g.wait_ge(cPE.h, M["1b_pe"])
                g.dma_start(out=p1wb[h % 2][:, 0:12 * 64],
                            in_=wqbr[:, 64 * h:64 * (h + 1)].rearrange("(t p) m -> p t m", p=128)
                            ).then_inc(dPs[h % 2].h, 16)
                dPs[h % 2].v += 16

        @blk1c.tensor
        def _(t):
            t.wait_ge(cDVE.h, M["1b_done"])
            for h in range(NH):
                t.wait_ge(dPs[h % 2].h, dpa0[h % 2] + 16 * (h // 2 + 1))
                if h >= 2:
                    t.wait_ge(cDVE.h, dve0 + 4 * (h - 2) + 1)
                for sq in range(2):
                    pt = pg[h % 2][0:64, 512 * sq:512 * sq + 512]
                    for c_ in range(12):
                        ins = t.matmul(pt, p1wb[h % 2][:, 64 * c_:64 * (c_ + 1)],
                                       qctb[:, 1024 * c_ + 512 * sq:1024 * c_ + 512 * sq + 512],
                                       start=(c_ == 0), stop=(c_ == 11))
                    cPE.tick(ins)
                M[("1c", h)] = cPE.v
                assert M[("1c", h)] == pe0 + 2 * (h + 1)
        M["1c_pe"] = cPE.v

        @blk1c.vector
        def _(v):
            for h in range(NH):
                v.wait_ge(cPE.h, M[("1c", h)])
                v.wait_ge(cDVE.h, cDVE.v)
                if h >= 2:
                    v.wait_ge(dSr[h % 2].h, dsr0[h % 2] + 16 * ((h - 2) // 2 + 1))
                qr_ = ropescr[0:64, :]
                sh_ = ropescr2[0:64, :]
                v.tensor_copy(qr_[0:64, 0:512], pg[h % 2][0:64, 0:512])
                cDVE.tick(v.tensor_copy(qr_[0:64, 512:1024], pg[h % 2][0:64, 512:1024]))
                v.wait_ge(cDVE.h, cDVE.v)
                v.tensor_copy(sh_[0:32, :], qr_[32:64, :])
                cDVE.tick(v.tensor_copy(sh_[32:64, :], qr_[0:32, :]))
                v.wait_ge(cDVE.h, cDVE.v)
                v.tensor_mul(sh_, sh_, c_sin[:, :])
                cDVE.tick(v.tensor_mul(qr_, qr_, c_cos[:, :]))
                v.wait_ge(cDVE.h, cDVE.v)
                cDVE.tick(v.tensor_add(qrb[h % 2][0:64, :], qr_, sh_))
                M[("1c_r", h)] = cDVE.v
        assert KSTOP < 13 or (cDVE.v == dve0 + 4 * NH)

        @blk1c.sync
        def _(sy):
            for h in range(NH):
                sy.wait_ge(cDVE.h, M[("1c_r", h)])
                sy.dma_start(out=ag1i[QROPE_OFF + 64 * h:QROPE_OFF + 64 * (h + 1), :],
                             in_=qrb[h % 2][0:64, :]).then_inc(dSr[h % 2].h, 16)
                dSr[h % 2].v += 16
                if DEBUG:
                    sy.dma_start(out=dbg["qrope"][64 * h:64 * (h + 1), :],
                                 in_=qrb[h % 2][0:64, :]).then_inc(dSr[h % 2].h, 16)
                    dSr[h % 2].v += 16
        M["ag1_ready"] = (dS.v, dSr[0].v, dSr[1].v)

        @blk2.gpsimd
        def _(g):
            g.wait_ge(dS.h, M["ag1_ready"][0])
            g.wait_ge(dSr[0].h, M["ag1_ready"][1])
            g.wait_ge(dSr[1].h, M["ag1_ready"][2])
            ins = g.collective_compute("AllGather", ALU.bypass, replica_groups=RG,
                                       ins=[ag1i.opt()], outs=[ag1o.opt()])
            cCC.tick(ins)
            g.dma_start(out=wknb[:, :],
                        in_=wkn.rearrange("(t p) m -> p t m", p=128)).then_inc(dP.h, 16)
            g.dma_start(out=wkvb_[:, :],
                        in_=wkv.rearrange("(t p) m -> p t m", p=128)).then_inc(dP.h, 16)
            dP.v += 32
        M["ag1"] = cCC.v
        M["d_wk"] = dP.v

        pid_holder = {}
        pe0, act0, dve0, ds0 = cPE.v, cACT.v, cDVE.v, dS.v

        @blk2.vector
        def _(v):
            cDVE.tick(v.memset(roped[64:65, :], -1.0))
        M["negrow"] = cDVE.v

        @blk2.sync
        def _(sy):
            sy.wait_ge(cCC.h, M["ag1"])
            pid_holder["sp"] = nc.partition_id([ET.SP])
            v3 = ag1o.rearrange("(r x) n -> r x n", x=AG1R)
            sy.dma_start(out=qrl.rearrange("(r x) n -> r x n", x=HPC * RD),
                         in_=v3[:, bass.ts(pid_holder["sp"] + (QROPE_OFF // (HPC * RD)), HPC * RD), :]
                         ).then_inc(dS.h, 16)
            dS.v += 16
            for tt in range(16):
                rr, co = tt // 2, (tt % 2) * 512
                if tt >= 2:
                    sy.wait_ge(cPE.h, pe0 + tt - 1)
                for r_ in range(4):
                    sy.dma_start(
                        out=kcw[tt % 2][:, 512 * r_:512 * (r_ + 1)],
                        in_=ag1o[AG1R * rr + KC_OFF + 128 * r_:AG1R * rr + KC_OFF + 128 * (r_ + 1),
                                 co:co + 512]).then_inc(dSk[tt % 2].h, 16)
                    dSk[tt % 2].v += 16
                M[("kcw", tt)] = (dSk[tt % 2], dSk[tt % 2].v)
            for rr in range(NCORES):
                sy.dma_start(out=roped[0:64, SQC * rr:SQC * (rr + 1)],
                             in_=ag1o[AG1R * rr + KR_OFF:AG1R * rr + KR_OFF + 64, :]
                             ).then_inc(dS.h, 16)
                dS.v += 16
        M["d_roped"] = dS.v

        @blk2.tensor
        def _(t):
            t.wait_ge(dP.h, M["d_wk"])
            t.wait_ge(cDVE.h, dve0)
            for tt in range(16):
                t.wait_ge(M[("kcw", tt)][0].h, M[("kcw", tt)][1])
                if tt >= 1:
                    t.wait_ge(cDVE.h, dve0 + 1 + tt)
                kpt = [pg0[:, 0:512], pg0[:, 512:1024], pg1[:, 0:512], pg1[:, 512:1024]]
                vpt = [pav[0], pav[1], pms, psp]
                for m in range(4):
                    for r_ in range(4):
                        t.matmul(kpt[m],
                                 wknb[:, 512 * r_ + 128 * m:512 * r_ + 128 * (m + 1)],
                                 kcw[tt % 2][:, 512 * r_:512 * (r_ + 1)],
                                 start=(r_ == 0), stop=(r_ == 3))
                for sk in range(4):
                    for r_ in range(4):
                        ins = t.matmul(vpt[sk],
                                       kcw[tt % 2][:, 512 * r_ + 128 * sk:512 * r_ + 128 * (sk + 1)],
                                       wkvb_[:, 512 * r_:512 * (r_ + 1)],
                                       start=(r_ == 0), stop=(r_ == 3))
                cPE.tick(ins)
                M[("2", tt)] = cPE.v
        assert KSTOP < 20 or (cPE.v == pe0 + 16)
        M["2_pe"] = cPE.v

        @blk2.vector
        def _(v):
            for tt in range(16):
                v.wait_ge(cPE.h, M[("2", tt)])
                v.wait_ge(cDVE.h, cDVE.v)
                kpt = [pg0[:, 0:512], pg0[:, 512:1024], pg1[:, 0:512], pg1[:, 512:1024]]
                vpt = [pav[0], pav[1], pms, psp]
                for m in range(4):
                    v.tensor_copy(kn4[:, S * m + 512 * tt:S * m + 512 * (tt + 1)], kpt[m])
                for sk in range(4):
                    ins = v.tensor_copy(v4[:, 512 * (4 * tt + sk):512 * (4 * tt + sk + 1)], vpt[sk])
                cDVE.tick(ins)
        assert KSTOP < 20 or (cDVE.v == dve0 + 1 + 16)
        M["2_dve"] = cDVE.v

        pe0, act0, dve0 = cPE.v, cACT.v, cDVE.v

        @blk2.scalar
        def _(s):
            s.wait_ge(cDVE.h, M["2_dve"])
            s.wait_ge(dS.h, M["d_roped"])
            for h in range(HPC):
                for tt in range(16):
                    u = 16 * h + tt
                    if u >= 1:
                        s.wait_ge(cPE.h, pe0 + u)
                    s.activation(sscr, kn4[:, S * h + 512 * tt:S * h + 512 * (tt + 1)], AF.Square)
                    ins = s.activation(sscr2[0:64, :], roped[0:64, 512 * tt:512 * (tt + 1)], AF.Square)
                    cACT.tick(ins)
        assert KSTOP < 20 or (cACT.v == act0 + 64)

        @blk2.tensor
        def _(t):
            for h in range(HPC):
                for tt in range(16):
                    u = 16 * h + tt
                    t.wait_ge(cACT.h, act0 + u + 1)
                    if u >= 1:
                        t.wait_ge(cDVE.h, dve0 + 17 * (u // 16) + (u % 16))
                    t.matmul(pms[0:1, 0:512], c_ones_bf, sscr, start=True, stop=False)
                    ins = t.matmul(pms[0:1, 0:512], c_ones_bf[0:64, :], sscr2[0:64, :],
                                   start=False, stop=True)
                    cPE.tick(ins)
        assert KSTOP < 20 or (cPE.v == pe0 + 64)

        @blk2.vector
        def _(v):
            for h in range(HPC):
                for tt in range(16):
                    u = 16 * h + tt
                    v.wait_ge(cPE.h, pe0 + u + 1)
                    cDVE.tick(v.reduce_max(c_mrow[0:1, tt:tt + 1], pms[0:1, 0:512], axis=AX.X))
                v.wait_ge(cDVE.h, cDVE.v)
                cDVE.tick(v.reduce_max(c_knmax[0:1, h:h + 1], c_mrow[0:1, 0:16], axis=AX.X))
                v.wait_ge(cDVE.h, cDVE.v)
        assert KSTOP < 20 or (cDVE.v == dve0 + 68)
        M["2_knm"] = cDVE.v

        @blk2.scalar
        def _(s):
            s.wait_ge(cDVE.h, M["2_knm"])
            cACT.tick(s.activation(c_knmax[0:1, 0:HPC], c_knmax[0:1, 0:HPC], AF.Sqrt))
        M["knmax"] = cACT.v

        @blk2.gpsimd
        def _(g):
            g.wait_ge(cPE.h, M["2_pe"])
            g.dma_start(out=wqbnb[:, :],
                        in_=wqbn.rearrange("(t p) m -> p t m", p=128)).then_inc(dP.h, 16)
            dP.v += 16
        M["d_wqbn"] = dP.v

        if DEBUG:
            @blk2.sync
            def _(sy):
                sy.wait_ge(cDVE.h, M["2_dve"])
                sy.dma_start(out=dbg["kn"], in_=kn4).then_inc(dS.h, 16)
                sy.dma_start(out=dbg["v4"], in_=v4).then_inc(dS.h, 16)
                dS.v += 32

        NG = 32
        pe0, act0, dve0 = cPE.v, cACT.v, cDVE.v
        dsq0 = dSq.v
        dsr0 = [dSr[0].v, dSr[1].v]
        dsv0 = [dSv[0].v, dSv[1].v]
        ACT_U, DVE_U = 35, 71

        def act_m(u, i):
            return act0 + ACT_U * u + i

        def dve_m(u, i):
            return dve0 + DVE_U * u + i

        def ds_qcb(b):
            return dsq0 + 16 * 12 * (b + 1)

        def ds_qra_w(eng, u):
            eng.wait_ge(dSr[u % 2].h, dsr0[u % 2] + 16 * (u // 2 + 1))

        def ds_av_w(eng, u):
            eng.wait_ge(dSv[u % 2].h, dsv0[u % 2] + 16 * (u // 2 + 1))

        @blk3.tensor
        def _(t):
            t.wait_ge(cACT.h, M["knmax"])
            t.wait_ge(dP.h, M["d_wqbn"])
            t.wait_ge(cDVE.h, M["negrow"])
            for b in range(16):
                for hh in range(HPC):
                    u = 4 * b + hh
                    t.wait_ge(dSq.h, ds_qcb(b))
                    if u >= 1:
                        t.wait_ge(cDVE.h, dve_m(u - 1, 70))
                    for c_ in range(12):
                        ins = t.matmul(psp, wqbnb[:, 512 * c_ + 128 * hh:512 * c_ + 128 * (hh + 1)],
                                       qcb[:, 512 * c_:512 * (c_ + 1)],
                                       start=(c_ == 0), stop=(c_ == 11))
                    cPE.tick(ins)
                    M[("3qb", u)] = cPE.v
                    t.wait_ge(cACT.h, act_m(u, 2))
                    if u >= 1:
                        t.wait_ge(cDVE.h, dve_m(u - 1, 67))
                    t.matmul(pms[0:1, 0:512], c_ones_bf, sscr, start=True, stop=False)
                    ins = t.matmul(pms[0:1, 0:512], c_ones_bf[0:64, :], sscr2[0:64, :],
                                   start=False, stop=True)
                    cPE.tick(ins)
                    M[("3bm", u)] = cPE.v
                    t.wait_ge(cDVE.h, dve_m(u, 2))
                    if u >= 2:
                        t.wait_ge(cDVE.h, dve_m(u - 2, 71))
                    for g_ in range(NG):
                        par = g_ % 2
                        if g_ >= 2:
                            t.wait_ge(cACT.h, act_m(u, g_ + 2))
                            t.wait_ge(cDVE.h, dve_m(u, 2 * g_))
                        for cc in (2 * g_, 2 * g_ + 1):
                            col = (cc % 2) * 512
                            pt = pg[par][:, col:col + 512]
                            t.matmul(pt, kn4[:, S * hh + 128 * cc:S * hh + 128 * (cc + 1)],
                                     qnb, start=True, stop=False)
                            ins = t.matmul(pt, roped[0:65, 128 * cc:128 * (cc + 1)],
                                           qra[u % 2][0:65, :], start=False, stop=True)
                        cPE.tick(ins)
                        M[("3sc", u, g_)] = cPE.v
                        if g_ >= 1:
                            gg = g_ - 1
                            t.wait_ge(cACT.h, act_m(u, gg + 4))
                            for cc in (2 * gg, 2 * gg + 1):
                                ins = t.matmul(pav[u % 2],
                                               v4[:, 512 * cc + 128 * hh:512 * cc + 128 * (hh + 1)],
                                               pbuf[gg % 2][:, (cc % 2) * 512:(cc % 2) * 512 + 512],
                                               start=(cc == 0), stop=(cc == 63))
                            cPE.tick(ins)
                            M[("3av", u, gg)] = cPE.v
                    gg = NG - 1
                    t.wait_ge(cACT.h, act_m(u, gg + 4))
                    for cc in (2 * gg, 2 * gg + 1):
                        ins = t.matmul(pav[u % 2],
                                       v4[:, 512 * cc + 128 * hh:512 * cc + 128 * (hh + 1)],
                                       pbuf[gg % 2][:, (cc % 2) * 512:(cc % 2) * 512 + 512],
                                       start=(cc == 0), stop=(cc == 63))
                    cPE.tick(ins)
                    M[("3av", u, gg)] = cPE.v
                    t.wait_ge(cDVE.h, dve_m(u, 66))
                    cPE.tick(t.matmul(pms[0:1, 0:512], c_ones_f, accd[:, :], start=True, stop=True))
                    M[("3dn", u)] = cPE.v
                    t.wait_ge(cDVE.h, dve_m(u, 69))
                    cPE.tick(t.matmul(psp, c_ones1r, row_r[0:1, 0:512], start=True, stop=True))
                    M[("3rb", u)] = cPE.v
        M["3_pe"] = cPE.v

        @blk3.scalar
        def _(s):
            for b in range(16):
                for hh in range(HPC):
                    u = 4 * b + hh
                    s.wait_ge(cPE.h, M[("3qb", u)])
                    s.wait_ge(cDVE.h, dve_m(u, 1))
                    ds_qra_w(s, u)
                    if u >= 1:
                        s.wait_ge(cPE.h, M[("3bm", u - 1)])
                    cACT.tick(s.activation(sscr, psp, AF.Square))
                    cACT.tick(s.activation(sscr2[0:64, :], qra[u % 2][0:64, :], AF.Square))
                    s.wait_ge(cPE.h, M[("3bm", u)])
                    cACT.tick(s.activation(q2row[0:1, :], pms[0:1, 0:512], AF.Sqrt))
                    for g_ in range(NG):
                        s.wait_ge(cPE.h, M[("3sc", u, g_)])
                        if g_ >= 2:
                            s.wait_ge(cPE.h, M[("3av", u, g_ - 2)])
                            s.wait_ge(cDVE.h, dve_m(u, 2 * g_))
                        cACT.tick(s.activation(pbuf[g_ % 2][:, :], pg[g_ % 2][:, :], AF.Exp))
        assert KSTOP < 30 or (cACT.v == act0 + ACT_U * 64)

        @blk3.vector
        def _(v):
            for b in range(16):
                for hh in range(HPC):
                    u = 4 * b + hh
                    v.wait_ge(cPE.h, M[("3qb", u)])
                    v.wait_ge(cDVE.h, cDVE.v)
                    if u >= 1:
                        v.wait_ge(cPE.h, M[("3sc", u - 1, NG - 1)])
                    cDVE.tick(v.tensor_copy(qnb, psp))
                    v.wait_ge(cACT.h, act_m(u, 3))
                    if u >= 2:
                        v.wait_ge(cPE.h, M[("3sc", u - 2, NG - 1)])
                    ins = v.tensor_scalar(qra[u % 2][64:65, :], q2row[0:1, 0:512],
                                          c_knmax[0:1, hh:hh + 1], None, ALU.mult)
                    cDVE.tick(ins)
                    for g_ in range(NG):
                        v.wait_ge(cACT.h, act_m(u, g_ + 4))
                        if g_ == 0:
                            cDVE.tick(v.tensor_copy(accd[:, :], pbuf[0][:, 0:512]))
                            v.wait_ge(cDVE.h, cDVE.v)
                            cDVE.tick(v.tensor_tensor(accd[:, :], accd[:, :],
                                                      pbuf[0][:, 512:1024], ALU.add))
                        else:
                            cDVE.tick(v.tensor_tensor(accd[:, :], accd[:, :],
                                                      pbuf[g_ % 2][:, 0:512], ALU.add))
                            v.wait_ge(cDVE.h, cDVE.v)
                            cDVE.tick(v.tensor_tensor(accd[:, :], accd[:, :],
                                                      pbuf[g_ % 2][:, 512:1024], ALU.add))
                        v.wait_ge(cDVE.h, cDVE.v)
                    v.wait_ge(cPE.h, M[("3dn", u)])
                    cDVE.tick(v.tensor_scalar(row_t[0:1, 0:512], pms[0:1, 0:512], 1e-30, None, ALU.add))
                    v.wait_ge(cDVE.h, cDVE.v)
                    cDVE.tick(v.reciprocal(row_q[0:1, 0:512], row_t[0:1, 0:512]))
                    v.wait_ge(cDVE.h, cDVE.v)
                    with lp():
                        cDVE.tick(v.tensor_copy(row_r[0:1, 0:512], row_q[0:1, 0:512]))
                    v.wait_ge(cPE.h, M[("3rb", u)])
                    cDVE.tick(v.tensor_copy(rbsb, psp))
                    v.wait_ge(cDVE.h, cDVE.v)
                    if u >= 2:
                        ds_av_w(v, u - 2)
                    cDVE.tick(v.tensor_tensor(avob[u % 2][:, :], pav[u % 2], rbsb, ALU.mult))
        assert KSTOP < 30 or (cDVE.v == dve0 + DVE_U * 64)

        @blk3.sync
        def _(sy):
            sy.wait_ge(cCC.h, M["ag1"])
            sy.wait_ge(dS.h, M["d_roped"])
            sy.wait_ge(cPE.h, M["2_pe"])
            for b in range(16):
                rr, co = b // 2, (b % 2) * 512
                if b >= 1:
                    sy.wait_ge(cPE.h, M[("3qb", 4 * (b - 1) + 3)])
                for m in range(12):
                    sy.dma_start(
                        out=qcb[:, 512 * m:512 * (m + 1)],
                        in_=ag1o[AG1R * rr + QC_OFF + 128 * m:AG1R * rr + QC_OFF + 128 * (m + 1),
                                 co:co + 512]).then_inc(dSq.h, 16)
                    dSq.v += 16
                for hh in range(HPC):
                    u = 4 * b + hh
                    if u >= 2:
                        sy.wait_ge(cACT.h, act_m(u - 2, 2))
                        sy.wait_ge(cPE.h, M[("3sc", u - 2, NG - 1)])
                    sy.dma_start(out=qra[u % 2][0:64, :],
                                 in_=qrl[HPC * RD * rr + RD * hh:HPC * RD * rr + RD * (hh + 1),
                                         co:co + 512]).then_inc(dSr[u % 2].h, 16)
                    dSr[u % 2].v += 16
                    assert dSr[u % 2].v == dsr0[u % 2] + 16 * (u // 2 + 1)
                for hh in range(HPC):
                    u = 4 * b + hh
                    sy.wait_ge(cDVE.h, dve_m(u, 71))
                    sy.dma_start(out=ag2i[128 * hh:128 * (hh + 1), 512 * b:512 * (b + 1)],
                                 in_=avob[u % 2][:, :]).then_inc(dSv[u % 2].h, 16)
                    dSv[u % 2].v += 16
                    assert dSv[u % 2].v == dsv0[u % 2] + 16 * (u // 2 + 1)
        M["3_out"] = (dSq.v, dSv[0].v, dSv[1].v)

        @blk4.gpsimd
        def _(g):
            g.wait_ge(dSq.h, M["3_out"][0])
            g.wait_ge(dSv[0].h, M["3_out"][1])
            g.wait_ge(dSv[1].h, M["3_out"][2])
            ins = g.collective_compute("AllGather", ALU.bypass, replica_groups=RG,
                                       ins=[ag2i.opt()], outs=[ag2o.opt()])
            cCC.tick(ins)
        M["ag2"] = cCC.v

        if DEBUG:
            @blk4.sync
            def _(sy):
                sy.wait_ge(cCC.h, M["ag2"])
                for hh in range(HPC):
                    sy.dma_start(out=dbg["av"][128 * hh:128 * (hh + 1), :],
                                 in_=ag2i[128 * hh:128 * (hh + 1), :]).then_inc(dS.h, 16)
                    dS.v += 16

        for s_ in range(2):
            pe0, act0, dve0, ds0, dp0 = cPE.v, cACT.v, cDVE.v, dS.v, dP.v
            dpa0 = [dPs[0].v, dPs[1].v]
            dsx0 = [dSx[0].v, dSx[1].v]
            DVE1 = dve0

            @blk4.sync
            def _(sy, s_=s_, ds0=ds0, dve0=dve0, dSx=dSx, dsx0=dsx0):
                sy.wait_ge(cCC.h, M["ag2"])
                if s_ == 1:
                    sy.wait_ge(cPE.h, M[("4wo_last", 0)])
                    sy.wait_ge(cDVE.h, M[("4ln_n", 0, 3)])
                    sy.wait_ge(dSv[0].h, dSv[0].v)
                    sy.wait_ge(dSv[1].h, dSv[1].v)
                pid_sp = pid_holder["sp"]
                v3 = ag2o.rearrange("(a p) n -> p a n", p=128)
                sy.dma_start(out=avsl[:, :].rearrange("p (t n) -> p t n", n=512),
                             in_=v3[:, :, bass.ts(pid_sp * 2 + s_, 512)]).then_inc(dS.h, 16)
                dS.v += 16
                for m in range(32):
                    if m >= 2:
                        sy.wait_ge(cDVE.h, dve0 + 2 * (m - 2) + 1)
                    sy.dma_start(out=xres[m % 2][:, :],
                                 in_=xt[128 * m:128 * (m + 1), 512 * s_:512 * s_ + 512]
                                 ).then_inc(dSx[m % 2].h, 16)
                    dSx[m % 2].v += 16
            assert KSTOP < 40 or (dS.v == ds0 + 16)

            @blk4.tensor
            def _(t, s_=s_, act0=act0, dve0=dve0, ds0=ds0, dp0=dp0, dpa0=dpa0):
                t.wait_ge(dS.h, ds0 + 16)
                t.wait_ge(cDVE.h, dve0)
                t.wait_ge(cACT.h, act0)
                for m in range(32):
                    t.wait_ge(dPs[m % 2].h, dpa0[m % 2] + 16 * (m // 2 + 1))
                    if m >= 2:
                        t.wait_ge(cDVE.h, dve0 + 2 * (m - 2) + 2)
                    pt = pg0[:, 0:512] if m % 2 == 0 else pg0[:, 512:1024]
                    for c_ in range(32):
                        ins = t.matmul(pt, wob[m % 2][:, 128 * c_:128 * (c_ + 1)],
                                       avsl[:, 512 * c_:512 * (c_ + 1)],
                                       start=(c_ == 0), stop=(c_ == 31))
                    cPE.tick(ins)
                    M[("4wo", s_, m)] = cPE.v
                    if m >= 1:
                        mm_ = m - 1
                        t.wait_ge(cDVE.h, dve0 + 2 * mm_ + 2)
                        t.wait_ge(cACT.h, act0 + mm_ + 1)
                        sq_ = sqzb if mm_ % 2 == 0 else sqzb2
                        t.matmul(pav[0][0:1, 0:512], c_ones_bf, zbf[:, 512 * mm_:512 * (mm_ + 1)],
                                 start=(mm_ == 0), stop=False)
                        ins = t.matmul(pav[1][0:1, 0:512], c_ones_bf, sq_,
                                       start=(mm_ == 0), stop=False)
                        cPE.tick(ins)
                        M[("4sq", s_, mm_)] = cPE.v
                M[("4wo_last", s_)] = cPE.v
                mm_ = 31
                t.wait_ge(cDVE.h, dve0 + 2 * mm_ + 2)
                t.wait_ge(cACT.h, act0 + mm_ + 1)
                sq_ = sqzb if mm_ % 2 == 0 else sqzb2
                t.matmul(pav[0][0:1, 0:512], c_ones_bf, zbf[:, 512 * mm_:512 * (mm_ + 1)],
                         start=False, stop=True)
                ins = t.matmul(pav[1][0:1, 0:512], c_ones_bf, sqzb2, start=False, stop=True)
                cPE.tick(ins)
                M[("4sq", s_, 31)] = cPE.v
                M[("4st", s_)] = cPE.v
                t.wait_ge(cDVE.h, dve0 + 75)
                t.matmul(pms, c_ones1r, row_r[0:1, 0:512], start=True, stop=True)
                t.matmul(psp, c_ones1r, row_r[0:1, 512:1024], start=True, stop=True)
                ins = t.matmul(pg1[:, 512:1024], c_ones1r, row_r2[0:1, 0:512], start=True, stop=True)
                cPE.tick(ins)
                M[("4bc", s_)] = cPE.v
                t.wait_ge(dP.h, dp0 + 16)
                t.wait_ge(cDVE.h, dve0 + 78)
                for c_ in range(32):
                    t.matmul(pg1[0:64, 0:512], wrtb[:, 64 * c_:64 * (c_ + 1)],
                             zh[:, 512 * c_:512 * (c_ + 1)],
                             start=(c_ == 0), stop=(c_ == 31))
                    ins = t.matmul(pav[0][0:64, 0:1], wrtb[:, 64 * c_:64 * (c_ + 1)], c_ones_f,
                                   start=(c_ == 0), stop=(c_ == 31))
                cPE.tick(ins)
                M[("4rt", s_)] = cPE.v
                for m in range(8):
                    t.wait_ge(dPs[m % 2].h, dpa0[m % 2] + 16 * (16 + m // 2 + 1))
                    if m >= 2:
                        t.wait_ge(cDVE.h, dve0 + 84 + 3 * (m - 2))
                    if m >= 1:
                        t.wait_ge(cDVE.h, dve0 + 82 + 3 * (m - 1))
                    pt = pg0[:, 0:512] if m % 2 == 0 else pg0[:, 512:1024]
                    for c_ in range(32):
                        t.matmul(pt, atb[m % 2][:, 128 * c_:128 * (c_ + 1)],
                                 zbf[:, 512 * c_:512 * (c_ + 1)],
                                 start=(c_ == 0), stop=(c_ == 31))
                        ins = t.matmul(pav[1][:, m:m + 1],
                                       atb[m % 2][:, 128 * c_:128 * (c_ + 1)], c_ones_bf,
                                       start=(c_ == 0), stop=(c_ == 31))
                    cPE.tick(ins)
                    M[("4h", s_, m)] = cPE.v

            @blk4.gpsimd
            def _(g, s_=s_, dpa0=dpa0):
                for m in range(32):
                    if m >= 2:
                        g.wait_ge(cPE.h, M[("4wo", s_, m - 2)])
                    elif s_ == 1:
                        g.wait_ge(cPE.h, M[("4B", 0, 31)])
                    g.dma_start(out=wob[m % 2][:, :],
                                in_=wo[:, 128 * m:128 * (m + 1)].rearrange("(t p) m -> p t m", p=128)
                                ).then_inc(dPs[m % 2].h, 16)
                    dPs[m % 2].v += 16
                    assert dPs[m % 2].v == dpa0[m % 2] + 16 * (m // 2 + 1)
                g.dma_start(out=wrtb[:, :],
                            in_=wrt.rearrange("(t p) m -> p t m", p=128)).then_inc(dP.h, 16)
                dP.v += 16
                for m in range(8):
                    if m >= 2:
                        g.wait_ge(cPE.h, M[("4h", s_, m - 2)])
                    else:
                        g.wait_ge(cPE.h, M[("4wo", s_, 31)])
                    g.dma_start(out=atb[m % 2][:, :],
                                in_=at[:, 128 * m:128 * (m + 1)].rearrange("(t p) m -> p t m", p=128)
                                ).then_inc(dPs[m % 2].h, 16)
                    dPs[m % 2].v += 16
                    assert dPs[m % 2].v == dpa0[m % 2] + 16 * (16 + m // 2 + 1)
            @blk4.scalar
            def _(s, s_=s_, dve0=dve0):
                for m in range(32):
                    s.wait_ge(cDVE.h, dve0 + 2 * m + 2)
                    if m >= 2:
                        s.wait_ge(cPE.h, M[("4sq", s_, m - 2)])
                    sq_ = sqzb if m % 2 == 0 else sqzb2
                    cACT.tick(s.activation(sq_, zbf[:, 512 * m:512 * (m + 1)], AF.Square))
                s.wait_ge(cDVE.h, dve0 + 71)
                cACT.tick(s.activation(row_t[0:1, 0:512], row_k[0:1, 0:512], AF.Sqrt))
            M[("4act1", s_)] = cACT.v

            @blk4.vector
            def _(v, s_=s_, act0=act0, dve0=dve0, dsx0=dsx0,
                  dsv_tot=(dSv[0].v, dSv[1].v)):
                if s_ == 1:
                    v.wait_ge(dSv[0].h, dsv_tot[0])
                    v.wait_ge(dSv[1].h, dsv_tot[1])
                for m in range(32):
                    v.wait_ge(cPE.h, M[("4wo", s_, m)])
                    v.wait_ge(cDVE.h, cDVE.v)
                    v.wait_ge(dSx[m % 2].h, dsx0[m % 2] + 16 * (m // 2 + 1))
                    pt = pg0[:, 0:512] if m % 2 == 0 else pg0[:, 512:1024]
                    cDVE.tick(v.tensor_tensor(zh[:, 512 * m:512 * (m + 1)], pt,
                                              xres[m % 2][:, :], ALU.add))
                    v.wait_ge(cDVE.h, cDVE.v)
                    cDVE.tick(v.tensor_copy(zbf[:, 512 * m:512 * (m + 1)],
                                            zh[:, 512 * m:512 * (m + 1)]))
                v.wait_ge(cPE.h, M[("4st", s_)])
                cDVE.tick(v.tensor_scalar(row_q[0:1, 512:1024], pav[0][0:1, 0:512],
                                          -1.0 / D, None, ALU.mult))
                v.wait_ge(cDVE.h, cDVE.v)
                with lp():
                    cDVE.tick(v.tensor_copy(row_r[0:1, 512:1024], row_q[0:1, 512:1024]))
                cDVE.tick(v.tensor_scalar(row_t[0:1, 0:512], pav[0][0:1, 0:512],
                                          1.0 / D, None, ALU.mult))
                cDVE.tick(v.tensor_scalar(row_k[0:1, 0:512], pav[1][0:1, 0:512],
                                          1.0 / D, None, ALU.mult))
                v.wait_ge(cDVE.h, cDVE.v)
                cDVE.tick(v.tensor_mul(row_t[0:1, 512:1024], row_t[0:1, 0:512],
                                       row_t[0:1, 0:512]))
                v.wait_ge(cDVE.h, cDVE.v)
                cDVE.tick(v.tensor_sub(row_k[0:1, 0:512], row_k[0:1, 0:512],
                                       row_t[0:1, 512:1024]))
                v.wait_ge(cDVE.h, cDVE.v)
                cDVE.tick(v.tensor_scalar(row_k[0:1, 0:512], row_k[0:1, 0:512],
                                          EPS_LN, None, ALU.add))
                v.wait_ge(cACT.h, M[("4act1", s_)])
                cDVE.tick(v.reciprocal(row_q[0:1, 0:512], row_t[0:1, 0:512]))
                v.wait_ge(cDVE.h, cDVE.v)
                with lp():
                    cDVE.tick(v.tensor_copy(row_r[0:1, 0:512], row_q[0:1, 0:512]))
                cDVE.tick(v.tensor_scalar(row_k[0:1, 512:1024], row_t[0:1, 0:512],
                                          2.0, None, ALU.mult))
                v.wait_ge(cDVE.h, cDVE.v)
                with lp():
                    cDVE.tick(v.tensor_copy(row_r2[0:1, 0:512], row_k[0:1, 512:1024]))
                v.wait_ge(cPE.h, M[("4bc", s_)])
                cDVE.tick(v.tensor_copy(rstdb[:, :], pms))
                cDVE.tick(v.tensor_copy(negmub[:, :], psp))
                cDVE.tick(v.tensor_copy(scl2b[:, :], pg1[:, 512:1024]))
                v.wait_ge(cPE.h, M[("4rt", s_)])
                cDVE.tick(v.tensor_copy(cwsb[0:64, 0:1], pav[0][0:64, 0:1]))
                v.wait_ge(cDVE.h, cDVE.v)
                cDVE.tick(v.scalar_tensor_tensor(hscr[0:64, :], negmub[0:64, :],
                                                 cwsb[0:64, 0:1], pg1[0:64, 0:512],
                                                 ALU.mult, ALU.add))
                v.wait_ge(cDVE.h, cDVE.v)
                cDVE.tick(v.tensor_mul(rrsb[0:64, :], hscr[0:64, :], rstdb[0:64, :]))
                for m in range(8):
                    v.wait_ge(cPE.h, M[("4h", s_, m)])
                    v.wait_ge(cDVE.h, cDVE.v)
                    cDVE.tick(v.tensor_copy(casb[:, m:m + 1], pav[1][:, m:m + 1]))
                    v.wait_ge(cDVE.h, cDVE.v)
                    pt = pg0[:, 0:512] if m % 2 == 0 else pg0[:, 512:1024]
                    cDVE.tick(v.scalar_tensor_tensor(hscr[:, :], negmub[:, :],
                                                     casb[:, m:m + 1], pt,
                                                     ALU.mult, ALU.add))
                    v.wait_ge(cDVE.h, cDVE.v)
                    cDVE.tick(v.tensor_mul(hbf[:, 512 * m:512 * (m + 1)], hscr[:, :],
                                           rstdb[:, :]))
            assert KSTOP < 40 or (cDVE.v == dve0 + 105)
            M[("4dve1", s_)] = cDVE.v

            @blk4.tensor
            def _(t, s_=s_):
                t.wait_ge(cDVE.h, M[("4dve1", s_)] - 24)
                for j in range(4):
                    ins = t.transpose(pms[:, 64 * j:64 * (j + 1)],
                                      rrsb[0:64, 128 * j:128 * (j + 1)], c_ident[0:64, 0:64])
                    cPE.tick(ins)
                M[("4tr", s_)] = cPE.v

            @blk4.vector
            def _(v, s_=s_):
                v.wait_ge(cPE.h, M[("4tr", s_)])
                v.wait_ge(cDVE.h, cDVE.v)
                for j in range(4):
                    cDVE.tick(v.tensor_copy(rsrow[:, 64 * j:64 * (j + 1)], pms[:, 64 * j:64 * (j + 1)]))
                for j in range(4):
                    sl = slice(64 * j, 64 * j + 64)
                    v.wait_ge(cDVE.h, cDVE.v)
                    cDVE.tick(v.max(out=mx16[:, 0:8], in_=rsrow[:, sl]))
                    v.wait_ge(cDVE.h, cDVE.v)
                    cDVE.tick(v.match_replace(out=zap[:, sl], in_to_replace=mx16[:, 0:8],
                                              in_values=rsrow[:, sl], imm_value=-3e38))
                    v.wait_ge(cDVE.h, cDVE.v)
                    cDVE.tick(v.max(out=mx16[:, 8:16], in_=zap[:, sl]))
                    v.wait_ge(cDVE.h, cDVE.v)
                    cDVE.tick(v.match_replace(out=zap[:, sl], in_to_replace=mx16[:, 8:16],
                                              in_values=zap[:, sl], imm_value=-3e38))
                    v.wait_ge(cDVE.h, cDVE.v)
                    cDVE.tick(v.tensor_scalar(colr[:, j:j + 1], mx16[:, 0:1], -1.0, None, ALU.mult))
                    v.wait_ge(cDVE.h, cDVE.v)
                    M[("4tk", s_, j)] = cDVE.v
            M[("4tk_all", s_)] = cDVE.v

            @blk4.scalar
            def _(s, s_=s_):
                for j in range(4):
                    sl = slice(64 * j, 64 * j + 64)
                    s.wait_ge(cDVE.h, M[("4tk", s_, j)])
                    cACT.tick(s.activation(gvals[:, sl], rsrow[:, sl], AF.Exp,
                                           bias=colr[:, j:j + 1]))
                    M[("4ex", s_, j)] = cACT.v

            @blk4.vector
            def _(v, s_=s_):
                for j in range(4):
                    sl = slice(64 * j, 64 * j + 64)
                    v.wait_ge(cACT.h, M[("4ex", s_, j)])
                    v.wait_ge(cDVE.h, cDVE.v)
                    cDVE.tick(v.tensor_scalar(rsrow[:, sl], zap[:, sl], -1e37, None, ALU.is_le))
                    v.wait_ge(cDVE.h, cDVE.v)
                    cDVE.tick(v.tensor_mul(gvals[:, sl], gvals[:, sl], rsrow[:, sl]))
                    v.wait_ge(cDVE.h, cDVE.v)
                    cDVE.tick(v.reduce_sum(colr[:, 4:5], gvals[:, sl], axis=AX.X))
                    v.wait_ge(cDVE.h, cDVE.v)
                    cDVE.tick(v.reciprocal(colr[:, 5:6], colr[:, 4:5]))
                    v.wait_ge(cDVE.h, cDVE.v)
                    cDVE.tick(v.tensor_scalar(gvals[:, sl], gvals[:, sl], colr[:, 5:6],
                                              None, ALU.mult))
                    v.wait_ge(cDVE.h, cDVE.v)
            M[("4g", s_)] = cDVE.v

            @blk4.tensor
            def _(t, s_=s_):
                t.wait_ge(cDVE.h, M[("4g", s_)])
                for j in range(4):
                    ins = t.transpose(psp[0:64, 128 * j:128 * (j + 1)],
                                      gvals[:, 64 * j:64 * (j + 1)], c_ident[:, :])
                    cPE.tick(ins)
                M[("4gt", s_)] = cPE.v

            @blk4.vector
            def _(v, s_=s_):
                v.wait_ge(cPE.h, M[("4gt", s_)])
                v.wait_ge(cDVE.h, cDVE.v)
                for j in range(4):
                    ins = v.tensor_mul(gtb[0:64, 128 * j:128 * (j + 1)],
                                       psp[0:64, 128 * j:128 * (j + 1)],
                                       scl2b[0:64, 128 * j:128 * (j + 1)])
                    cDVE.tick(ins)
            M[("4gtb", s_)] = cDVE.v

            @blk4.tensor
            def _(t, s_=s_):
                t.wait_ge(cDVE.h, M[("4gtb", s_)])
                for j8 in range(8):
                    if j8 >= 2:
                        t.wait_ge(cDVE.h, M[("4gtb", s_)] + (j8 - 2) + 1)
                    pt = pg1[:, 0:512] if j8 % 2 == 0 else pg1[:, 512:1024]
                    ins = t.matmul(pt, c_emat[0:64, 128 * j8:128 * (j8 + 1)],
                                   gtb[0:64, 0:512], start=True, stop=True)
                    cPE.tick(ins)
                    M[("4ge", s_, j8)] = cPE.v

            @blk4.vector
            def _(v, s_=s_):
                for j8 in range(8):
                    v.wait_ge(cPE.h, M[("4ge", s_, j8)])
                    v.wait_ge(cDVE.h, cDVE.v)
                    pt = pg1[:, 0:512] if j8 % 2 == 0 else pg1[:, 512:1024]
                    ins = v.tensor_tensor(hgbf[:, 512 * j8:512 * (j8 + 1)],
                                          hbf[:, 512 * j8:512 * (j8 + 1)], pt, ALU.mult)
                    cDVE.tick(ins)
                    M[("4hg", s_, j8)] = cDVE.v
                    assert M[("4hg", s_, j8)] == M[("4gtb", s_)] + j8 + 1
            M[("4hg_all", s_)] = cDVE.v

            dve_B0 = cDVE.v

            @blk4.tensor
            def _(t, s_=s_, dpa0=dpa0, dve0=dve0, dve_B0=dve_B0):
                t.wait_ge(cDVE.h, M[("4hg_all", s_)])
                for m in range(32):
                    t.wait_ge(dPs[m % 2].h, dpa0[m % 2] + 16 * (20 + m // 2 + 1))
                    if m >= 2:
                        t.wait_ge(cDVE.h, dve_B0 + (m - 2) + 1)
                    pt = pg0[:, 0:512] if m % 2 == 0 else pg0[:, 512:1024]
                    for kr in range(8):
                        ins = t.matmul(pt, btb[m % 2][:, 128 * kr:128 * (kr + 1)],
                                       hgbf[:, 512 * kr:512 * (kr + 1)],
                                       start=(kr == 0), stop=(kr == 7))
                    cPE.tick(ins)
                    M[("4B", s_, m)] = cPE.v

            @blk4.gpsimd
            def _(g, s_=s_, dpa0=dpa0):
                for m in range(32):
                    if m >= 2:
                        g.wait_ge(cPE.h, M[("4B", s_, m - 2)])
                    else:
                        g.wait_ge(cPE.h, M[("4h", s_, 7)])
                    g.dma_start(out=btb[m % 2][:, :],
                                in_=bt[:, 128 * m:128 * (m + 1)].rearrange("(t p) m -> p t m", p=128)
                                ).then_inc(dPs[m % 2].h, 16)
                    dPs[m % 2].v += 16
                    assert dPs[m % 2].v == dpa0[m % 2] + 16 * (20 + m // 2 + 1)

            @blk4.vector
            def _(v, s_=s_, dve_B0=dve_B0):
                for m in range(32):
                    v.wait_ge(cPE.h, M[("4B", s_, m)])
                    v.wait_ge(cDVE.h, cDVE.v)
                    pt = pg0[:, 0:512] if m % 2 == 0 else pg0[:, 512:1024]
                    ins = v.tensor_tensor(zh[:, 512 * m:512 * (m + 1)], pt,
                                          zh[:, 512 * m:512 * (m + 1)], ALU.add)
                    cDVE.tick(ins)
                    M[("4v", s_, m)] = cDVE.v
                    assert cDVE.v == dve_B0 + m + 1
            M[("4v_all", s_)] = cDVE.v

            if DEBUG and s_ == 0:
                @blk4.sync
                def _(sy):
                    sy.wait_ge(cDVE.h, M[("4v_all", 0)])
                    sy.dma_start(out=dbg["z"], in_=zh).then_inc(dS.h, 16)
                    sy.dma_start(out=dbg["gg"], in_=gvals).then_inc(dS.h, 16)
                    sy.dma_start(out=dbg["hg"], in_=hgbf).then_inc(dS.h, 16)
                    dS.v += 48

            for q in range(4):
                dve_q0 = cDVE.v

                @blk4.tensor
                def _(t, s_=s_, q=q, dve_q0=dve_q0):
                    t.wait_ge(cDVE.h, M[("4v_all", s_)])
                    if q >= 1:
                        t.wait_ge(cDVE.h, M[("4ln_e", s_, q - 1, 7)])
                    if q >= 2:
                        t.wait_ge(cDVE.h, M[("4ln_n", s_, q - 2)])
                    for c_ in range(32):
                        grp = c_ // 4
                        pt = pms if grp % 2 == 0 else psp
                        if c_ % 4 == 0 and c_ >= 8:
                            t.wait_ge(cDVE.h, dve_q0 + (grp - 2) + 1)
                        ins = t.transpose(pt[:, 128 * (c_ % 4):128 * (c_ % 4 + 1)],
                                          zh[:, 512 * c_ + 128 * q:512 * c_ + 128 * q + 128],
                                          c_ident[:, :])
                        cPE.tick(ins)
                        M[("4ln_t", s_, q, c_)] = cPE.v

                @blk4.vector
                def _(v, s_=s_, q=q, dve_q0=dve_q0):
                    for grp in range(8):
                        v.wait_ge(cPE.h, M[("4ln_t", s_, q, 4 * grp + 3)])
                        v.wait_ge(cDVE.h, cDVE.v)
                        pt = pms if grp % 2 == 0 else psp
                        ins = v.tensor_copy(vTq[q % 2][:, 512 * grp:512 * (grp + 1)], pt[:, 0:512])
                        cDVE.tick(ins)
                        M[("4ln_e", s_, q, grp)] = cDVE.v
                        assert cDVE.v == dve_q0 + grp + 1
                    v.wait_ge(cDVE.h, cDVE.v)
                    for jj in range(8):
                        ins = v.bn_stats(bnst[:, 6 * jj:6 * jj + 6],
                                         vTq[q % 2][:, 512 * jj:512 * (jj + 1)])
                    cDVE.tick(ins)
                    v.wait_ge(cDVE.h, cDVE.v)
                    cDVE.tick(v.bn_aggr(colr[:, 4:6], bnst[:, 0:48]))
                    v.wait_ge(cDVE.h, cDVE.v)
                    cDVE.tick(v.tensor_scalar(colr[:, 6:7], colr[:, 5:6], EPS_LN, None, ALU.add))
                    v.wait_ge(cDVE.h, cDVE.v)
                    cDVE.tick(v.tensor_scalar(colr[:, 4:5], colr[:, 4:5], -1.0, None, ALU.mult))
                M[("4bn", s_, q)] = cDVE.v

                @blk4.scalar
                def _(s, s_=s_, q=q):
                    s.wait_ge(cDVE.h, M[("4bn", s_, q)])
                    cACT.tick(s.activation(colr[:, 7:8], colr[:, 6:7], AF.Sqrt))
                M[("4sd2", s_, q)] = cACT.v

                @blk4.vector
                def _(v, s_=s_, q=q, ds0=ds0):
                    v.wait_ge(cACT.h, M[("4sd2", s_, q)])
                    cDVE.tick(v.reciprocal(colr[:, 6:7], colr[:, 7:8]))
                    v.wait_ge(cDVE.h, cDVE.v)
                    if q >= 2:
                        v.wait_ge(M[("4out", s_, q - 2)][0].h, M[("4out", s_, q - 2)][1])
                    ins = v.tensor_scalar(vTo[q % 2][:, :], vTq[q % 2][:, :],
                                          colr[:, 4:5], colr[:, 6:7], ALU.add, ALU.mult)
                    cDVE.tick(ins)
                    M[("4ln_n", s_, q)] = cDVE.v

                @blk4.sync
                def _(sy, s_=s_, q=q):
                    sy.wait_ge(cDVE.h, M[("4ln_n", s_, q)])
                    sy.dma_start(out=out[512 * s_ + 128 * q:512 * s_ + 128 * (q + 1), :],
                                 in_=vTo[q % 2][:, :]).then_inc(dSv[q % 2].h, 16)
                    dSv[q % 2].v += 16
                    M[("4out", s_, q)] = (dSv[q % 2], dSv[q % 2].v)

        @blk.sync
        def _(sy):
            for c in [dS, dP, dSq] + dPs + dSk + dSr + dSv + dSx:
                sy.wait_ge(c.h, c.v)

        return nc
    return nc


def kernel(**inputs):
    per_core = _host_layout(inputs)
    nc = build_nc()
    res = run_bass_kernel_spmd(nc, per_core, core_ids=list(range(NCORES)))
    outs = [res.results[c]["out"] for c in range(NCORES)]
    return np.concatenate(outs, axis=0).reshape(1, S, D)


# version 09
# speedup vs baseline: 1.1384x; 1.1384x over previous
import os
import numpy as np
from contextlib import ExitStack

import concourse.bass as bass
import concourse.mybir as mybir
from concourse.bass_utils import run_bass_kernel_spmd

F32 = mybir.dt.float32
F32R = mybir.dt.float32r
BF16 = mybir.dt.bfloat16
AF = mybir.ActivationFunctionType
ALU = mybir.AluOpType
AX = mybir.AxisListType
ET = mybir.EngineType

S, D, NH = 8192, 4096, 32
QR, KVR, RD, ND, VD = 1536, 512, 64, 128, 128
QK = ND + RD
NCORES, HPC, SQC = 8, 4, 1024
EPS_RMS, EPS_LN = 1e-6, 1e-5
INV_SQRT_QK = float(QK ** (-0.5))

QC_OFF, QROPE_OFF = 0, QR
KC_OFF = QR + NH * RD
KR_OFF = KC_OFF + KVR
AG1R = KR_OFF + RD

DEBUG = False
_KS = os.environ.get("KSTOP", "4")
KSTOP = {"0": 10, "1a": 11, "1b": 12, "1": 13, "1c": 13, "2": 20, "3": 30, "4": 40}[_KS]


class _SkipBlk:
    def __getattr__(self, name):
        return lambda f: None


def _host_layout(inputs):
    x = np.ascontiguousarray(inputs["L0_attn_hidden_states"][0])
    xT = np.ascontiguousarray(x.T)
    w_qa = np.ascontiguousarray(inputs["L0_attn_w_qa"])
    g_qa = np.ascontiguousarray(inputs["L0_attn_g_qa"])
    w_qb = np.ascontiguousarray(inputs["L0_attn_w_qb"])
    w_kva = np.ascontiguousarray(inputs["L0_attn_w_kva"])
    g_kva = np.ascontiguousarray(inputs["L0_attn_g_kva"])
    w_kvb = np.ascontiguousarray(inputs["L0_attn_w_kvb"])
    w_o = np.ascontiguousarray(inputs["L0_attn_w_o"])
    A = np.ascontiguousarray(inputs["L0_ffn_A"])
    B = np.ascontiguousarray(inputs["L0_ffn_B"])
    Wr = np.ascontiguousarray(inputs["L0_ffn_Wr"])

    wq3 = w_qb.reshape(QR, NH, QK)
    wqb_rope = np.ascontiguousarray(wq3[:, :, ND:].reshape(QR, NH * RD))
    wk3 = w_kvb.reshape(KVR, NH, ND + VD)

    A_t = np.ascontiguousarray(A.transpose(2, 0, 1).reshape(D, 1024))
    B_t = np.ascontiguousarray(B.transpose(0, 2, 1).reshape(1024, D))
    WrT = np.ascontiguousarray(Wr.T)
    gqa_t = np.ascontiguousarray(g_qa.reshape(QR // 128, 128).T)
    gkva_t = np.ascontiguousarray(g_kva.reshape(KVR // 128, 128).T)

    inv_freq = 1.0 / (10000.0 ** (np.arange(0, RD, 2, dtype=np.float64) / RD))
    t = np.arange(S, dtype=np.float64)
    freqs = np.outer(t, inv_freq)
    emb = np.concatenate([freqs, freqs], axis=-1)
    cosT = np.ascontiguousarray(np.cos(emb).T.astype(np.float32))
    sinT = np.sin(emb).T.astype(np.float32)
    sinTs = sinT.copy()
    sinTs[: RD // 2] *= -1.0
    sinTs = np.ascontiguousarray(sinTs)

    emat = np.zeros((64, 1024), dtype=np.float32)
    for e_ in range(64):
        emat[e_, 16 * e_:16 * e_ + 16] = 1.0
    ident = np.eye(128, dtype=np.float32)

    per_core = []
    for c in range(NCORES):
        sl = slice(SQC * c, SQC * (c + 1))
        heads = list(range(HPC * c, HPC * (c + 1)))
        per_core.append(dict(
            xt=np.ascontiguousarray(xT[:, sl]),
            wqa=w_qa,
            wqbr=wqb_rope,
            wqbn=np.ascontiguousarray(np.concatenate([wq3[:, h, :ND] for h in heads], 1)),
            wkva=w_kva,
            wkn=np.ascontiguousarray(np.concatenate([wk3[:, h, :ND] for h in heads], 1)),
            wkv=np.ascontiguousarray(np.concatenate([wk3[:, h, ND:] for h in heads], 1)),
            wo=w_o, at=A_t, bt=B_t, wrt=WrT,
            gqa=gqa_t, gkva=gkva_t,
            coss=np.ascontiguousarray(cosT[:, sl]),
            sins=np.ascontiguousarray(sinTs[:, sl]),
            emat=emat, ident=ident,
        ))
    return per_core


class Ctr:
    def __init__(self, handle):
        self.h = handle
        self.v = 0

    def tick(self, instr, n=1):
        self.v += n
        instr.then_inc(self.h, n)
        return self.v


def build_nc():
    nc = bass.Bass("TRN2", target_bir_lowering=False)

    def din(name, shape):
        return nc.dram_tensor(name, shape, F32, kind="ExternalInput").ap()

    xt = din("xt", [D, SQC])
    wqa = din("wqa", [D, QR])
    wqbr = din("wqbr", [QR, NH * RD])
    wqbn = din("wqbn", [QR, HPC * ND])
    wkva = din("wkva", [D, KVR + RD])
    wkn = din("wkn", [KVR, HPC * ND])
    wkv = din("wkv", [KVR, HPC * VD])
    wo = din("wo", [NH * VD, D])
    at = din("at", [D, 1024])
    bt = din("bt", [1024, D])
    wrt = din("wrt", [D, 64])
    gqa = din("gqa", [128, QR // 128])
    gkva = din("gkva", [128, KVR // 128])
    coss = din("coss", [RD, SQC])
    sins = din("sins", [RD, SQC])
    emat = din("emat", [64, 1024])
    ident = din("ident", [128, 128])
    out = nc.dram_tensor("out", [SQC, D], F32, kind="ExternalOutput").ap()

    ag1i = nc.dram_tensor("ag1_in", [AG1R, SQC], BF16, kind="Internal").ap()
    ag1o = nc.dram_tensor("ag1_out", [NCORES * AG1R, SQC], BF16, kind="Internal", addr_space="Shared").ap()
    ag2i = nc.dram_tensor("ag2_in", [HPC * VD, S], BF16, kind="Internal").ap()
    qrl = nc.dram_tensor("qrl", [NCORES * HPC * RD, SQC], BF16, kind="Internal").ap()
    ag2o = nc.dram_tensor("ag2_out", [NH * VD, S], BF16, kind="Internal", addr_space="Shared").ap()

    dbg = {}
    if DEBUG:
        def dout(name, shape, dt=F32):
            return nc.dram_tensor(name, shape, dt, kind="ExternalOutput").ap()
        dbg["qc"] = dout("dbg_qc", [12 * 128, SQC])
        dbg["kc"] = dout("dbg_kc", [4 * 128, SQC])
        dbg["kr"] = dout("dbg_kr", [64, SQC])
        dbg["qrope"] = dout("dbg_qrope", [NH * RD, SQC])
        dbg["kn"] = dout("dbg_kn", [128, HPC * S], BF16)
        dbg["v4"] = dout("dbg_v4", [128, 64 * 512], BF16)
        dbg["av"] = dout("dbg_av", [HPC * VD, S])
        dbg["z"] = dout("dbg_z", [128, 32 * 512])
        dbg["gg"] = dout("dbg_gg", [128, 4 * 64])
        dbg["hg"] = dout("dbg_hg", [128, 8 * 512])

    with ExitStack() as ctx:
        e = ctx.enter_context

        ARENA = 196608
        arena = nc.alloc_sbuf_tensor("arena", [128, ARENA], mybir.dt.uint8)
        arena_ap = arena.ap()

        def sb(name, shape, dtype, off):
            assert off % 32 == 0, name
            nb = shape[-1] * mybir.dt.size(dtype)
            assert off + nb <= ARENA, (name, off + nb)
            return arena_ap[0:shape[0], off:off + nb].bitcast(dtype)

        OA, OB, OC = 0, 65536, 131072
        xtb = sb("xtb", [128, 32 * 1024], BF16, OA)
        kn4 = sb("kn4", [128, HPC * S], BF16, OA)
        zh = sb("zh", [128, 32 * 512], F32, OA)
        qcpre = sb("qcpre", [128, 12 * 1024], F32, OB)
        kctb = sb("kctb", [128, 4 * 1024], BF16, OB)
        krb = sb("krb", [64, 1024], BF16, OB + 8192)
        qrb = [sb(f"qrb{i}", [64, 1024], BF16, OB + 10240 + 2048 * i) for i in range(2)]
        ropescr = sb("ropescr", [64, 1024], F32, OB + 14336)
        ropescr2 = sb("ropescr2", [64, 1024], F32, OB + 18432)
        p1wb = [sb(f"p1wb{i}", [128, 32 * 128], BF16, OB + 49152 + 8192 * i) for i in range(2)]
        v4 = sb("v4", [128, 64 * 512], BF16, OB)
        avsl = sb("avsl", [128, 32 * 512], BF16, OB)
        zbf = sb("zbf", [128, 32 * 512], BF16, OB + 32768)
        hbf = sb("hbf", [128, 8 * 512], BF16, OB)
        hgbf = sb("hgbf", [128, 8 * 512], BF16, OB + 8192)
        atb = [sb(f"atb{i}", [128, 32 * 128], BF16, OB + 16384 + 8192 * i) for i in range(2)]
        btb = [sb(f"btb{i}", [128, 8 * 128], BF16, OB + 16384 + 2048 * i) for i in range(2)]
        vTq = [sb(f"vTq{i}", [128, 4096], F32, OB + 16384 * i) for i in range(2)]
        vTo = [sb(f"vTo{i}", [128, 4096], F32, OB + 32768 + 16384 * i) for i in range(2)]
        row_q = sb("row_q", [1, 1024], F32, OC)
        row_k = sb("row_k", [1, 1024], F32, OC + 4096)
        row_t = sb("row_t", [1, 1024], F32, OC + 8192)
        row_r = None
        row_r2 = None
        q2row = sb("q2row", [1, 512], F32, OC + 18432)
        CC = OC + 20480
        rbsb = sb("rbsb", [128, 512], F32, OC + 2048)
        avob = [sb(f"avob{i}", [128, 512], BF16, OC + 4096 + 1024 * i) for i in range(2)]
        qnb = sb("qnb", [128, 512], BF16, OC + 6144)
        qra = [sb(f"qra{i}", [65, 512], BF16, OC + 7168 + 7168 * i) for i in range(1)] + \
              [sb("qra1", [65, 512], BF16, OC + 14336)]
        accd = sb("accd", [128, 512], F32, OC + 10240)
        sscr = sb("sscr", [128, 512], BF16, OC + 16384)
        sscr2 = sb("sscr2", [64, 512], BF16, OC + 15360)
        hscr = sb("hscr", [128, 512], F32, OC + 10240)
        p1sq = sb("p1sq", [128, 512], BF16, OC)
        p1sq2 = sb("p1sq2", [128, 512], BF16, OC + 2048)
        qsb = sb("qsb", [128, 1024], F32, OC + 4096)
        kvpre = sb("kvpre", [128, 5 * 1024], F32, CC)
        qctb = sb("qctb", [128, 12 * 1024], BF16, CC + 20480)
        roped = sb("roped", [65, S], BF16, CC)
        qcb = sb("qcb", [128, 12 * 512], BF16, CC + 16384)
        wqbnb = sb("wqbnb", [128, 12 * 512], BF16, CC + 28672)
        pbuf = [sb(f"pbuf{i}", [128, 1024], BF16, CC + 40960 + 2048 * i) for i in range(2)]
        kcw = [sb(f"kcw{i}", [128, 4 * 512], BF16, CC + 16384 + 4096 * i) for i in range(2)]
        wknb = sb("wknb", [128, 4 * 512], BF16, CC + 24576)
        wkvb_ = sb("wkvb_", [128, 4 * 512], BF16, CC + 28672)
        wob = [sb(f"wob{i}", [128, 32 * 128], BF16, CC + 8192 * i) for i in range(2)]
        xres = [sb(f"xres{i}", [128, 512], F32, CC + 16384 + 2048 * i) for i in range(2)]
        wrtb = sb("wrtb", [128, 32 * 64], F32, CC + 20480)
        rrsb = sb("rrsb", [64, 512], F32, CC + 28672)
        rsrow = sb("rsrow", [128, 4 * 64], F32, CC + 30720)
        zap = sb("zap", [128, 4 * 64], F32, CC + 31744)
        gvals = sb("gvals", [128, 4 * 64], F32, CC + 32768)
        mx16 = sb("mx16", [128, 16], F32, CC + 33792)
        colr = sb("colr", [128, 8], F32, CC + 33856)
        gtb = sb("gtb", [64, 512], BF16, CC + 34048)
        casb = sb("casb", [128, 8], F32, CC + 35072)
        cwsb = sb("cwsb", [64, 1], F32, CC + 35104)
        rstdb = sb("rstdb", [128, 512], F32, CC + 35840)
        scl2b = sb("scl2b", [128, 512], F32, CC + 37888)
        negmub = sb("negmub", [128, 512], F32, CC + 39936)
        sqzb = sb("sqzb", [128, 512], BF16, CC + 41984)
        sqzb2 = sb("sqzb2", [128, 512], BF16, CC + 43008)
        bnst = sb("bnst", [128, 64], F32, CC + 44032)

        row_r = e(nc.sbuf_tensor("row_r", [1, 1024], F32R)).ap()
        row_r2 = e(nc.sbuf_tensor("row_r2", [1, 512], F32R)).ap()
        c_ident = e(nc.sbuf_tensor("c_ident", [128, 128], F32)).ap()
        c_emat = e(nc.sbuf_tensor("c_emat", [64, 1024], BF16)).ap()
        c_gqa = e(nc.sbuf_tensor("c_gqa", [128, 12], F32)).ap()
        c_gkva = e(nc.sbuf_tensor("c_gkva", [128, 4], F32)).ap()
        c_ones_bf = e(nc.sbuf_tensor("c_ones_bf", [128, 1], BF16)).ap()
        c_ones_r = e(nc.sbuf_tensor("c_ones_r", [128, 1], F32R)).ap()
        c_ones1r = e(nc.sbuf_tensor("c_ones1r", [1, 128], F32R)).ap()
        c_ones_f = e(nc.sbuf_tensor("c_ones_f", [128, 1], F32)).ap()
        c_cos = sb("c_cos", [64, 1024], F32, OB + 22528)
        c_sin = sb("c_sin", [64, 1024], F32, OB + 26624)
        c_knmax = e(nc.sbuf_tensor("c_knmax", [1, 8], F32)).ap()
        c_mrow = e(nc.sbuf_tensor("c_mrow", [1, 16], F32)).ap()

        pg0 = e(nc.psum_tensor("pg0", [128, 1024], F32)).ap()
        pg1 = e(nc.psum_tensor("pg1", [128, 1024], F32)).ap()
        pav = [e(nc.psum_tensor(f"pav{i}", [128, 512], F32)).ap() for i in range(2)]
        pms = e(nc.psum_tensor("pms", [128, 512], F32)).ap()
        psp = e(nc.psum_tensor("psp", [128, 512], F32)).ap()
        pg = [pg0, pg1]

        cPE = Ctr(e(nc.semaphore("cPE")))
        cACT = Ctr(e(nc.semaphore("cACT")))
        cDVE = Ctr(e(nc.semaphore("cDVE")))
        cCC = Ctr(e(nc.semaphore("cCC")))
        dS = Ctr(e(nc.semaphore("dS")))
        dP = Ctr(e(nc.semaphore("dP")))
        dPs = [Ctr(e(nc.semaphore("dPa"))), Ctr(e(nc.semaphore("dPb")))]
        dSk = [Ctr(e(nc.semaphore("dSka"))), Ctr(e(nc.semaphore("dSkb")))]
        dSq = Ctr(e(nc.semaphore("dSq")))
        dSr = [Ctr(e(nc.semaphore("dSra"))), Ctr(e(nc.semaphore("dSrb")))]
        dSv = [Ctr(e(nc.semaphore("dSva"))), Ctr(e(nc.semaphore("dSvb")))]
        dSx = [Ctr(e(nc.semaphore("dSxa"))), Ctr(e(nc.semaphore("dSxb")))]

        blk = e(nc.Block())
        blk1a = blk if KSTOP >= 11 else _SkipBlk()
        blk1b = blk if KSTOP >= 12 else _SkipBlk()
        blk1c = blk if KSTOP >= 13 else _SkipBlk()
        blk2 = blk if KSTOP >= 20 else _SkipBlk()
        blk3 = blk if KSTOP >= 30 else _SkipBlk()
        blk4 = blk if KSTOP >= 40 else _SkipBlk()
        RG = [list(range(NCORES))]
        M = {}

        def lp():
            return nc.allow_low_precision(reason="f32r rounding for broadcast matmuls is intentional")

        @blk.sync
        def _(sy):
            for ap_, src in [(c_ident, ident), (c_gqa, gqa), (c_gkva, gkva)]:
                sy.dma_start(out=ap_[:, :], in_=src).then_inc(dS.h, 16)
                dS.v += 16
        M["d_const"] = dS.v

        @blk.gpsimd
        def _(g):
            g.dma_start(out=c_emat[:, :], in_=emat).then_inc(dP.h, 16)
            dP.v += 16
            for j in range(32):
                g.dma_start(out=xtb[:, 1024 * j:1024 * (j + 1)],
                            in_=xt[128 * j:128 * (j + 1), :]).then_inc(dP.h, 16)
                dP.v += 16
            for m in range(2):
                g.dma_start(out=p1wb[m][:, :],
                            in_=wqa[:, 128 * m:128 * (m + 1)].rearrange("(t p) m -> p t m", p=128)
                            ).then_inc(dP.h, 16)
                dP.v += 16
        M["d_x"] = dP.v

        @blk.vector
        def _(v):
            v.memset(c_ones_bf[:, :], 1.0)
            cDVE.tick(v.memset(c_ones_f[:, :], 1.0))
            v.wait_ge(cDVE.h, cDVE.v)
            v.memset(row_t[0:1, 0:128], 1.0)
            with lp():
                cDVE.tick(v.tensor_copy(c_ones_r[:, :], c_ones_f[:, :]))
            v.wait_ge(cDVE.h, cDVE.v)
            with lp():
                cDVE.tick(v.tensor_copy(c_ones1r[0:1, :], row_t[0:1, 0:128]))
        M["ones"] = cDVE.v

        pe0, act0, dve0 = cPE.v, cACT.v, cDVE.v
        dpa0 = [dPs[0].v, dPs[1].v]

        @blk1a.tensor
        def _(t):
            t.wait_ge(dP.h, M["d_x"])
            t.wait_ge(dS.h, M["d_const"])
            t.wait_ge(cDVE.h, M["ones"])
            for m in range(12):
                if m >= 2:
                    t.wait_ge(dPs[m % 2].h, dpa0[m % 2] + 16 * ((m - 2) // 2 + 1))
                for sq in range(2):
                    if m >= 2:
                        t.wait_ge(cDVE.h, dve0 + 2 * (m - 2) + sq + 1)
                        t.wait_ge(cACT.h, act0 + 2 * (m - 2) + sq + 1)
                    pt = pg[m % 2][:, 512 * sq:512 * sq + 512]
                    for c_ in range(32):
                        ins = t.matmul(pt, p1wb[m % 2][:, 128 * c_:128 * (c_ + 1)],
                                       xtb[:, 1024 * c_ + 512 * sq:1024 * c_ + 512 * sq + 512],
                                       start=(c_ == 0), stop=(c_ == 31))
                    cPE.tick(ins)
                    M[("1a", m, sq)] = cPE.v
                    if m >= 1:
                        j = m - 1
                        t.wait_ge(cACT.h, act0 + 2 * j + sq + 1)
                        dst = pms[0:1, 0:512] if sq == 0 else psp[0:1, 0:512]
                        ins = t.matmul(dst, c_ones_bf, p1sq if sq == 0 else p1sq2,
                                       start=(j == 0), stop=False)
                        cPE.tick(ins)
                        M[("1a_ss", j, sq)] = cPE.v
            for sq in range(2):
                j = 11
                t.wait_ge(cACT.h, act0 + 2 * j + sq + 1)
                dst = pms[0:1, 0:512] if sq == 0 else psp[0:1, 0:512]
                ins = t.matmul(dst, c_ones_bf, p1sq if sq == 0 else p1sq2,
                               start=False, stop=True)
                cPE.tick(ins)
                M[("1a_ss", j, sq)] = cPE.v
        M["1a_pe"] = cPE.v

        @blk1a.gpsimd
        def _(g):
            for m in range(2, 12):
                g.wait_ge(cPE.h, M[("1a", m - 2, 1)])
                g.dma_start(out=p1wb[m % 2][:, :],
                            in_=wqa[:, 128 * m:128 * (m + 1)].rearrange("(t p) m -> p t m", p=128)
                            ).then_inc(dPs[m % 2].h, 16)
                dPs[m % 2].v += 16
                assert dPs[m % 2].v == dpa0[m % 2] + 16 * ((m - 2) // 2 + 1)

        @blk1a.scalar
        def _(s):
            for m in range(12):
                for sq in range(2):
                    s.wait_ge(cPE.h, M[("1a", m, sq)])
                    s.wait_ge(cDVE.h, dve0 + 2 * m + sq + 1)
                    if m >= 1:
                        s.wait_ge(cPE.h, M[("1a_ss", m - 1, sq)])
                    pt = pg[m % 2][:, 512 * sq:512 * sq + 512]
                    ins = s.activation(p1sq if sq == 0 else p1sq2, pt, AF.Square)
                    cACT.tick(ins)
        assert KSTOP < 11 or (cACT.v == act0 + 24)

        @blk1a.vector
        def _(v):
            for m in range(12):
                for sq in range(2):
                    v.wait_ge(cPE.h, M[("1a", m, sq)])
                    ins = v.tensor_copy(
                        qcpre[:, 1024 * m + 512 * sq:1024 * m + 512 * sq + 512],
                        pg[m % 2][:, 512 * sq:512 * sq + 512])
                    cDVE.tick(ins)
        assert KSTOP < 11 or (cDVE.v == dve0 + 24)

        @blk1a.vector
        def _(v):
            v.wait_ge(cPE.h, M["1a_pe"])
            v.tensor_copy(row_q[0:1, 0:512], pms[0:1, 0:512])
            cDVE.tick(v.tensor_copy(row_q[0:1, 512:1024], psp[0:1, 0:512]))
        M["1a_s2"] = cDVE.v

        @blk1a.scalar
        def _(s):
            s.wait_ge(cDVE.h, M["1a_s2"])
            cACT.tick(s.activation(row_t[0:1, :], row_q[0:1, :], AF.Sqrt, scale=1.0 / QR))
        M["1a_sqrt"] = cACT.v

        @blk1a.vector
        def _(v):
            v.wait_ge(cACT.h, M["1a_sqrt"])
            cDVE.tick(v.tensor_scalar(row_t[0:1, :], row_t[0:1, :], EPS_RMS, None, ALU.add))
            v.wait_ge(cDVE.h, cDVE.v)
            cDVE.tick(v.reciprocal(row_q[0:1, :], row_t[0:1, :]))
            v.wait_ge(cDVE.h, cDVE.v)
            cDVE.tick(v.tensor_scalar(row_t[0:1, :], row_q[0:1, :], INV_SQRT_QK, None, ALU.mult))
            v.wait_ge(cDVE.h, cDVE.v)
            with lp():
                cDVE.tick(v.tensor_copy(row_r[0:1, :], row_t[0:1, :]))
        M["1a_scale"] = cDVE.v

        @blk1a.tensor
        def _(t):
            t.wait_ge(cDVE.h, M["1a_scale"])
            t.matmul(pg0[:, 0:512], c_ones1r, row_r[0:1, 0:512], start=True, stop=True)
            cPE.tick(t.matmul(pg0[:, 512:1024], c_ones1r, row_r[0:1, 512:1024], start=True, stop=True))
        M["1a_bc"] = cPE.v

        @blk1a.vector
        def _(v):
            v.wait_ge(cPE.h, M["1a_bc"])
            cDVE.tick(v.tensor_copy(qsb[:, :], pg0[:, :]))
            v.wait_ge(cDVE.h, cDVE.v)
            for m in range(12):
                for sq in range(2):
                    ins = v.scalar_tensor_tensor(
                        qctb[:, 1024 * m + 512 * sq:1024 * m + 512 * sq + 512],
                        qcpre[:, 1024 * m + 512 * sq:1024 * m + 512 * sq + 512],
                        c_gqa[:, m:m + 1],
                        qsb[:, 512 * sq:512 * sq + 512],
                        ALU.mult, ALU.mult)
            cDVE.tick(ins)
        M["1a_qctb"] = cDVE.v

        @blk1a.sync
        def _(sy):
            sy.wait_ge(cDVE.h, M["1a_qctb"])
            for m in range(12):
                sy.dma_start(out=ag1i[QC_OFF + 128 * m:QC_OFF + 128 * (m + 1), :],
                             in_=qctb[:, 1024 * m:1024 * (m + 1)]).then_inc(dS.h, 16)
                dS.v += 16
            if DEBUG:
                for m in range(12):
                    sy.dma_start(out=dbg["qc"][128 * m:128 * (m + 1), :],
                                 in_=qctb[:, 1024 * m:1024 * (m + 1)]).then_inc(dS.h, 16)
                    dS.v += 16

        @blk1b.sync
        def _(sy):
            sy.wait_ge(cDVE.h, M["1a_qctb"])
            sy.dma_start(out=c_cos[:, :], in_=coss).then_inc(dS.h, 16)
            sy.dma_start(out=c_sin[:, :], in_=sins).then_inc(dS.h, 16)
            dS.v += 32
        M["d_cossin"] = dS.v

        pe0, act0, dve0 = cPE.v, cACT.v, cDVE.v
        dpa0 = [dPs[0].v, dPs[1].v]

        @blk1b.gpsimd
        def _(g):
            for m in range(5):
                w = 128 if m < 4 else 64
                if m >= 2:
                    g.wait_ge(cPE.h, pe0 + (2 if m - 2 == 0 else 4 * (m - 2) + 1))
                else:
                    g.wait_ge(cPE.h, M["1a_pe"])
                g.dma_start(out=p1wb[m % 2][:, 0:32 * w],
                            in_=wkva[:, 128 * m:128 * m + w].rearrange("(t p) m -> p t m", p=128)
                            ).then_inc(dPs[m % 2].h, 16)
                dPs[m % 2].v += 16
                assert dPs[m % 2].v == dpa0[m % 2] + 16 * (m // 2 + 1)

        @blk1b.tensor
        def _(t):
            t.wait_ge(cDVE.h, M["1a_qctb"])
            for m in range(5):
                w = 128 if m < 4 else 64
                t.wait_ge(dPs[m % 2].h, dpa0[m % 2] + 16 * (m // 2 + 1))
                for sq in range(2):
                    if m >= 2:
                        t.wait_ge(cDVE.h, dve0 + 2 * (m - 2) + sq + 1)
                        if m - 2 < 4:
                            t.wait_ge(cACT.h, act0 + 2 * (m - 2) + sq + 1)
                    pt = pg[m % 2][0:w, 512 * sq:512 * sq + 512]
                    for c_ in range(32):
                        ins = t.matmul(pt, p1wb[m % 2][:, w * c_:w * (c_ + 1)],
                                       xtb[:, 1024 * c_ + 512 * sq:1024 * c_ + 512 * sq + 512],
                                       start=(c_ == 0), stop=(c_ == 31))
                    cPE.tick(ins)
                    M[("1b", m, sq)] = cPE.v
                    if 1 <= m <= 4:
                        j = m - 1
                        t.wait_ge(cACT.h, act0 + 2 * j + sq + 1)
                        dst = pms[0:1, 0:512] if sq == 0 else psp[0:1, 0:512]
                        ins = t.matmul(dst, c_ones_bf, p1sq if sq == 0 else p1sq2,
                                       start=(j == 0), stop=(j == 3))
                        cPE.tick(ins)
                        M[("1b_ss", j, sq)] = cPE.v
                assert M[("1b", m, 1)] == pe0 + (2 if m == 0 else 4 * m + 1), m
        M["1b_pe"] = cPE.v

        @blk1b.scalar
        def _(s):
            for m in range(4):
                for sq in range(2):
                    s.wait_ge(cPE.h, M[("1b", m, sq)])
                    s.wait_ge(cDVE.h, dve0 + 2 * m + sq + 1)
                    if m >= 1:
                        s.wait_ge(cPE.h, M[("1b_ss", m - 1, sq)])
                    pt = pg[m % 2][:, 512 * sq:512 * sq + 512]
                    ins = s.activation(p1sq if sq == 0 else p1sq2, pt, AF.Square)
                    cACT.tick(ins)
        assert KSTOP < 12 or (cACT.v == act0 + 8)

        @blk1b.vector
        def _(v):
            for m in range(5):
                w = 128 if m < 4 else 64
                for sq in range(2):
                    v.wait_ge(cPE.h, M[("1b", m, sq)])
                    ins = v.tensor_copy(
                        kvpre[0:w, 1024 * m + 512 * sq:1024 * m + 512 * sq + 512],
                        pg[m % 2][0:w, 512 * sq:512 * sq + 512])
                    cDVE.tick(ins)
        assert KSTOP < 12 or (cDVE.v == dve0 + 10)
        M["1b_kvpre"] = cDVE.v

        @blk1b.vector
        def _(v):
            v.wait_ge(cPE.h, M["1b_pe"])
            v.tensor_copy(row_k[0:1, 0:512], pms[0:1, 0:512])
            cDVE.tick(v.tensor_copy(row_k[0:1, 512:1024], psp[0:1, 0:512]))
        M["1b_s2"] = cDVE.v

        @blk1b.scalar
        def _(s):
            s.wait_ge(cDVE.h, M["1b_s2"])
            cACT.tick(s.activation(row_t[0:1, :], row_k[0:1, :], AF.Sqrt, scale=1.0 / KVR))
        M["1b_sqrt"] = cACT.v

        @blk1b.vector
        def _(v):
            v.wait_ge(cACT.h, M["1b_sqrt"])
            cDVE.tick(v.tensor_scalar(row_t[0:1, :], row_t[0:1, :], EPS_RMS, None, ALU.add))
            v.wait_ge(cDVE.h, cDVE.v)
            cDVE.tick(v.reciprocal(row_k[0:1, :], row_t[0:1, :]))
            v.wait_ge(cDVE.h, cDVE.v)
            with lp():
                cDVE.tick(v.tensor_copy(row_r[0:1, :], row_k[0:1, :]))
        M["1b_scale"] = cDVE.v

        @blk1b.tensor
        def _(t):
            t.wait_ge(cDVE.h, M["1b_scale"])
            t.matmul(pg0[:, 0:512], c_ones1r, row_r[0:1, 0:512], start=True, stop=True)
            cPE.tick(t.matmul(pg0[:, 512:1024], c_ones1r, row_r[0:1, 512:1024], start=True, stop=True))
        M["1b_bc"] = cPE.v

        @blk1b.vector
        def _(v):
            v.wait_ge(cPE.h, M["1b_bc"])
            v.wait_ge(cDVE.h, M["1a_qctb"])
            cDVE.tick(v.tensor_copy(qsb[:, :], pg0[:, :]))
            v.wait_ge(cDVE.h, cDVE.v)
            for m in range(4):
                for sq in range(2):
                    ins = v.scalar_tensor_tensor(
                        kctb[:, 1024 * m + 512 * sq:1024 * m + 512 * sq + 512],
                        kvpre[:, 1024 * m + 512 * sq:1024 * m + 512 * sq + 512],
                        c_gkva[:, m:m + 1],
                        qsb[:, 512 * sq:512 * sq + 512],
                        ALU.mult, ALU.mult)
            cDVE.tick(ins)
            v.wait_ge(cDVE.h, cDVE.v)
            v.wait_ge(dS.h, M["d_cossin"])
            kr_ = kvpre[0:64, 4096:5120]
            v.tensor_copy(ropescr[0:32, :], kr_[32:64, :])
            cDVE.tick(v.tensor_copy(ropescr[32:64, :], kr_[0:32, :]))
            v.wait_ge(cDVE.h, cDVE.v)
            v.tensor_mul(ropescr[0:64, :], ropescr[0:64, :], c_sin[:, :])
            cDVE.tick(v.tensor_mul(ropescr2[0:64, :], kr_, c_cos[:, :]))
            v.wait_ge(cDVE.h, cDVE.v)
            cDVE.tick(v.tensor_add(krb[0:64, :], ropescr[0:64, :], ropescr2[0:64, :]))
        M["1b_done"] = cDVE.v

        @blk1b.sync
        def _(sy):
            sy.wait_ge(cDVE.h, M["1b_done"])
            for m in range(4):
                sy.dma_start(out=ag1i[KC_OFF + 128 * m:KC_OFF + 128 * (m + 1), :],
                             in_=kctb[:, 1024 * m:1024 * (m + 1)]).then_inc(dS.h, 16)
                dS.v += 16
            sy.dma_start(out=ag1i[KR_OFF:KR_OFF + 64, :], in_=krb[0:64, :]).then_inc(dS.h, 16)
            dS.v += 16
            if DEBUG:
                for m in range(4):
                    sy.dma_start(out=dbg["kc"][128 * m:128 * (m + 1), :],
                                 in_=kctb[:, 1024 * m:1024 * (m + 1)]).then_inc(dS.h, 16)
                    dS.v += 16
                sy.dma_start(out=dbg["kr"][:, :], in_=krb[0:64, :]).then_inc(dS.h, 16)
                dS.v += 16

        pe0, dve0 = cPE.v, cDVE.v
        dpa0 = [dPs[0].v, dPs[1].v]
        dsr0 = [dSr[0].v, dSr[1].v]

        @blk1c.gpsimd
        def _(g):
            for h in range(NH):
                if h >= 2:
                    g.wait_ge(cPE.h, pe0 + 2 * (h - 1))
                else:
                    g.wait_ge(cPE.h, M["1b_pe"])
                g.dma_start(out=p1wb[h % 2][:, 0:12 * 64],
                            in_=wqbr[:, 64 * h:64 * (h + 1)].rearrange("(t p) m -> p t m", p=128)
                            ).then_inc(dPs[h % 2].h, 16)
                dPs[h % 2].v += 16

        @blk1c.tensor
        def _(t):
            t.wait_ge(cDVE.h, M["1b_done"])
            for h in range(NH):
                t.wait_ge(dPs[h % 2].h, dpa0[h % 2] + 16 * (h // 2 + 1))
                if h >= 2:
                    t.wait_ge(cDVE.h, dve0 + 4 * (h - 2) + 1)
                for sq in range(2):
                    pt = pg[h % 2][0:64, 512 * sq:512 * sq + 512]
                    for c_ in range(12):
                        ins = t.matmul(pt, p1wb[h % 2][:, 64 * c_:64 * (c_ + 1)],
                                       qctb[:, 1024 * c_ + 512 * sq:1024 * c_ + 512 * sq + 512],
                                       start=(c_ == 0), stop=(c_ == 11))
                    cPE.tick(ins)
                M[("1c", h)] = cPE.v
                assert M[("1c", h)] == pe0 + 2 * (h + 1)
        M["1c_pe"] = cPE.v

        @blk1c.vector
        def _(v):
            for h in range(NH):
                v.wait_ge(cPE.h, M[("1c", h)])
                v.wait_ge(cDVE.h, cDVE.v)
                if h >= 2:
                    v.wait_ge(dSr[h % 2].h, dsr0[h % 2] + 16 * ((h - 2) // 2 + 1))
                qr_ = ropescr[0:64, :]
                sh_ = ropescr2[0:64, :]
                v.tensor_copy(qr_[0:64, 0:512], pg[h % 2][0:64, 0:512])
                cDVE.tick(v.tensor_copy(qr_[0:64, 512:1024], pg[h % 2][0:64, 512:1024]))
                v.wait_ge(cDVE.h, cDVE.v)
                v.tensor_copy(sh_[0:32, :], qr_[32:64, :])
                cDVE.tick(v.tensor_copy(sh_[32:64, :], qr_[0:32, :]))
                v.wait_ge(cDVE.h, cDVE.v)
                v.tensor_mul(sh_, sh_, c_sin[:, :])
                cDVE.tick(v.tensor_mul(qr_, qr_, c_cos[:, :]))
                v.wait_ge(cDVE.h, cDVE.v)
                cDVE.tick(v.tensor_add(qrb[h % 2][0:64, :], qr_, sh_))
                M[("1c_r", h)] = cDVE.v
        assert KSTOP < 13 or (cDVE.v == dve0 + 4 * NH)

        @blk1c.sync
        def _(sy):
            for h in range(NH):
                sy.wait_ge(cDVE.h, M[("1c_r", h)])
                sy.dma_start(out=ag1i[QROPE_OFF + 64 * h:QROPE_OFF + 64 * (h + 1), :],
                             in_=qrb[h % 2][0:64, :]).then_inc(dSr[h % 2].h, 16)
                dSr[h % 2].v += 16
                if DEBUG:
                    sy.dma_start(out=dbg["qrope"][64 * h:64 * (h + 1), :],
                                 in_=qrb[h % 2][0:64, :]).then_inc(dSr[h % 2].h, 16)
                    dSr[h % 2].v += 16
        M["ag1_ready"] = (dS.v, dSr[0].v, dSr[1].v)

        @blk2.gpsimd
        def _(g):
            g.wait_ge(dS.h, M["ag1_ready"][0])
            g.wait_ge(dSr[0].h, M["ag1_ready"][1])
            g.wait_ge(dSr[1].h, M["ag1_ready"][2])
            ins = g.collective_compute("AllGather", ALU.bypass, replica_groups=RG,
                                       ins=[ag1i.opt()], outs=[ag1o.opt()])
            cCC.tick(ins)
            g.dma_start(out=wknb[:, :],
                        in_=wkn.rearrange("(t p) m -> p t m", p=128)).then_inc(dP.h, 16)
            g.dma_start(out=wkvb_[:, :],
                        in_=wkv.rearrange("(t p) m -> p t m", p=128)).then_inc(dP.h, 16)
            dP.v += 32
        M["ag1"] = cCC.v
        M["d_wk"] = dP.v

        pid_holder = {}
        pe0, act0, dve0, ds0 = cPE.v, cACT.v, cDVE.v, dS.v

        @blk2.vector
        def _(v):
            cDVE.tick(v.memset(roped[64:65, :], -1.0))
        M["negrow"] = cDVE.v

        @blk2.sync
        def _(sy):
            sy.wait_ge(cCC.h, M["ag1"])
            pid_holder["sp"] = nc.partition_id([ET.SP])
            v3 = ag1o.rearrange("(r x) n -> r x n", x=AG1R)
            sy.dma_start(out=qrl.rearrange("(r x) n -> r x n", x=HPC * RD),
                         in_=v3[:, bass.ts(pid_holder["sp"] + (QROPE_OFF // (HPC * RD)), HPC * RD), :]
                         ).then_inc(dS.h, 16)
            dS.v += 16
            for tt in range(16):
                rr, co = tt // 2, (tt % 2) * 512
                if tt >= 2:
                    sy.wait_ge(cPE.h, pe0 + tt - 1)
                for r_ in range(4):
                    sy.dma_start(
                        out=kcw[tt % 2][:, 512 * r_:512 * (r_ + 1)],
                        in_=ag1o[AG1R * rr + KC_OFF + 128 * r_:AG1R * rr + KC_OFF + 128 * (r_ + 1),
                                 co:co + 512]).then_inc(dSk[tt % 2].h, 16)
                    dSk[tt % 2].v += 16
                M[("kcw", tt)] = (dSk[tt % 2], dSk[tt % 2].v)
            for rr in range(NCORES):
                sy.dma_start(out=roped[0:64, SQC * rr:SQC * (rr + 1)],
                             in_=ag1o[AG1R * rr + KR_OFF:AG1R * rr + KR_OFF + 64, :]
                             ).then_inc(dS.h, 16)
                dS.v += 16
        M["d_roped"] = dS.v

        @blk2.tensor
        def _(t):
            t.wait_ge(dP.h, M["d_wk"])
            t.wait_ge(cDVE.h, dve0)
            for tt in range(16):
                t.wait_ge(M[("kcw", tt)][0].h, M[("kcw", tt)][1])
                if tt >= 1:
                    t.wait_ge(cDVE.h, dve0 + 1 + tt)
                kpt = [pg0[:, 0:512], pg0[:, 512:1024], pg1[:, 0:512], pg1[:, 512:1024]]
                vpt = [pav[0], pav[1], pms, psp]
                for m in range(4):
                    for r_ in range(4):
                        t.matmul(kpt[m],
                                 wknb[:, 512 * r_ + 128 * m:512 * r_ + 128 * (m + 1)],
                                 kcw[tt % 2][:, 512 * r_:512 * (r_ + 1)],
                                 start=(r_ == 0), stop=(r_ == 3))
                for sk in range(4):
                    for r_ in range(4):
                        ins = t.matmul(vpt[sk],
                                       kcw[tt % 2][:, 512 * r_ + 128 * sk:512 * r_ + 128 * (sk + 1)],
                                       wkvb_[:, 512 * r_:512 * (r_ + 1)],
                                       start=(r_ == 0), stop=(r_ == 3))
                cPE.tick(ins)
                M[("2", tt)] = cPE.v
        assert KSTOP < 20 or (cPE.v == pe0 + 16)
        M["2_pe"] = cPE.v

        @blk2.vector
        def _(v):
            for tt in range(16):
                v.wait_ge(cPE.h, M[("2", tt)])
                v.wait_ge(cDVE.h, cDVE.v)
                kpt = [pg0[:, 0:512], pg0[:, 512:1024], pg1[:, 0:512], pg1[:, 512:1024]]
                vpt = [pav[0], pav[1], pms, psp]
                for m in range(4):
                    v.tensor_copy(kn4[:, S * m + 512 * tt:S * m + 512 * (tt + 1)], kpt[m])
                for sk in range(4):
                    ins = v.tensor_copy(v4[:, 512 * (4 * tt + sk):512 * (4 * tt + sk + 1)], vpt[sk])
                cDVE.tick(ins)
        assert KSTOP < 20 or (cDVE.v == dve0 + 1 + 16)
        M["2_dve"] = cDVE.v

        pe0, act0, dve0 = cPE.v, cACT.v, cDVE.v

        @blk2.scalar
        def _(s):
            s.wait_ge(cDVE.h, M["2_dve"])
            s.wait_ge(dS.h, M["d_roped"])
            for h in range(HPC):
                for tt in range(16):
                    u = 16 * h + tt
                    if u >= 1:
                        s.wait_ge(cPE.h, pe0 + u)
                    s.activation(sscr, kn4[:, S * h + 512 * tt:S * h + 512 * (tt + 1)], AF.Square)
                    ins = s.activation(sscr2[0:64, :], roped[0:64, 512 * tt:512 * (tt + 1)], AF.Square)
                    cACT.tick(ins)
        assert KSTOP < 20 or (cACT.v == act0 + 64)

        @blk2.tensor
        def _(t):
            for h in range(HPC):
                for tt in range(16):
                    u = 16 * h + tt
                    t.wait_ge(cACT.h, act0 + u + 1)
                    if u >= 1:
                        t.wait_ge(cDVE.h, dve0 + 17 * (u // 16) + (u % 16))
                    t.matmul(pms[0:1, 0:512], c_ones_bf, sscr, start=True, stop=False)
                    ins = t.matmul(pms[0:1, 0:512], c_ones_bf[0:64, :], sscr2[0:64, :],
                                   start=False, stop=True)
                    cPE.tick(ins)
        assert KSTOP < 20 or (cPE.v == pe0 + 64)

        @blk2.vector
        def _(v):
            for h in range(HPC):
                for tt in range(16):
                    u = 16 * h + tt
                    v.wait_ge(cPE.h, pe0 + u + 1)
                    cDVE.tick(v.reduce_max(c_mrow[0:1, tt:tt + 1], pms[0:1, 0:512], axis=AX.X))
                v.wait_ge(cDVE.h, cDVE.v)
                cDVE.tick(v.reduce_max(c_knmax[0:1, h:h + 1], c_mrow[0:1, 0:16], axis=AX.X))
                v.wait_ge(cDVE.h, cDVE.v)
        assert KSTOP < 20 or (cDVE.v == dve0 + 68)
        M["2_knm"] = cDVE.v

        @blk2.scalar
        def _(s):
            s.wait_ge(cDVE.h, M["2_knm"])
            cACT.tick(s.activation(c_knmax[0:1, 0:HPC], c_knmax[0:1, 0:HPC], AF.Ln))
        M["knln"] = cACT.v

        @blk2.vector
        def _(v):
            v.wait_ge(cACT.h, M["knln"])
            v.wait_ge(cDVE.h, cDVE.v)
            cDVE.tick(v.tensor_scalar(c_knmax[0:1, 0:HPC], c_knmax[0:1, 0:HPC],
                                      0.5, None, ALU.mult))
        M["knm2"] = cDVE.v

        @blk2.gpsimd
        def _(g):
            g.wait_ge(cPE.h, M["2_pe"])
            g.dma_start(out=wqbnb[:, :],
                        in_=wqbn.rearrange("(t p) m -> p t m", p=128)).then_inc(dP.h, 16)
            dP.v += 16
        M["d_wqbn"] = dP.v

        if DEBUG:
            @blk2.sync
            def _(sy):
                sy.wait_ge(cDVE.h, M["2_dve"])
                sy.dma_start(out=dbg["kn"], in_=kn4).then_inc(dS.h, 16)
                sy.dma_start(out=dbg["v4"], in_=v4).then_inc(dS.h, 16)
                dS.v += 32

        NG = 32
        pe0, act0, dve0 = cPE.v, cACT.v, cDVE.v
        dsq0 = dSq.v
        dsr0 = [dSr[0].v, dSr[1].v]
        dsv0 = [dSv[0].v, dSv[1].v]
        ACT_U, DVE_U = 36, 70

        def act_m(u, i):
            return act0 + ACT_U * u + i

        def dve_m(u, i):
            return dve0 + DVE_U * u + i

        def ds_qcb(b):
            return dsq0 + 16 * 12 * (b + 1)

        def ds_qra_w(eng, u):
            eng.wait_ge(dSr[u % 2].h, dsr0[u % 2] + 16 * (u // 2 + 1))

        def ds_av_w(eng, u):
            eng.wait_ge(dSv[u % 2].h, dsv0[u % 2] + 16 * (u // 2 + 1))

        @blk3.tensor
        def _(t):
            t.wait_ge(cDVE.h, M["knm2"])
            t.wait_ge(dP.h, M["d_wqbn"])
            t.wait_ge(cDVE.h, M["negrow"])
            for b in range(16):
                for hh in range(HPC):
                    u = 4 * b + hh
                    t.wait_ge(dSq.h, ds_qcb(b))
                    if u >= 1:
                        t.wait_ge(cDVE.h, dve_m(u - 1, 69))
                    for c_ in range(12):
                        ins = t.matmul(psp, wqbnb[:, 512 * c_ + 128 * hh:512 * c_ + 128 * (hh + 1)],
                                       qcb[:, 512 * c_:512 * (c_ + 1)],
                                       start=(c_ == 0), stop=(c_ == 11))
                    cPE.tick(ins)
                    M[("3qb", u)] = cPE.v
                    t.wait_ge(cACT.h, act_m(u, 2))
                    if u >= 1:
                        t.wait_ge(cDVE.h, dve_m(u - 1, 66))
                    t.matmul(pms[0:1, 0:512], c_ones_bf, sscr, start=True, stop=False)
                    ins = t.matmul(pms[0:1, 0:512], c_ones_bf[0:64, :], sscr2[0:64, :],
                                   start=False, stop=True)
                    cPE.tick(ins)
                    M[("3bm", u)] = cPE.v
                    t.wait_ge(cDVE.h, dve_m(u, 1))
                    t.wait_ge(cACT.h, act_m(u, 4))
                    if u >= 2:
                        t.wait_ge(cDVE.h, dve_m(u - 2, 70))
                    for g_ in range(NG):
                        par = g_ % 2
                        if g_ >= 2:
                            t.wait_ge(cACT.h, act_m(u, g_ + 3))
                        for cc in (2 * g_, 2 * g_ + 1):
                            col = (cc % 2) * 512
                            pt = pg[par][:, col:col + 512]
                            t.matmul(pt, kn4[:, S * hh + 128 * cc:S * hh + 128 * (cc + 1)],
                                     qnb, start=True, stop=False)
                            ins = t.matmul(pt, roped[0:65, 128 * cc:128 * (cc + 1)],
                                           qra[u % 2][0:65, :], start=False, stop=True)
                        cPE.tick(ins)
                        M[("3sc", u, g_)] = cPE.v
                        if g_ >= 1:
                            gg = g_ - 1
                            t.wait_ge(cACT.h, act_m(u, gg + 5))
                            for cc in (2 * gg, 2 * gg + 1):
                                ins = t.matmul(pav[u % 2],
                                               v4[:, 512 * cc + 128 * hh:512 * cc + 128 * (hh + 1)],
                                               pbuf[gg % 2][:, (cc % 2) * 512:(cc % 2) * 512 + 512],
                                               start=(cc == 0), stop=(cc == 63))
                            cPE.tick(ins)
                            M[("3av", u, gg)] = cPE.v
                    gg = NG - 1
                    t.wait_ge(cACT.h, act_m(u, gg + 5))
                    for cc in (2 * gg, 2 * gg + 1):
                        ins = t.matmul(pav[u % 2],
                                       v4[:, 512 * cc + 128 * hh:512 * cc + 128 * (hh + 1)],
                                       pbuf[gg % 2][:, (cc % 2) * 512:(cc % 2) * 512 + 512],
                                       start=(cc == 0), stop=(cc == 63))
                    cPE.tick(ins)
                    M[("3av", u, gg)] = cPE.v
                    t.wait_ge(cDVE.h, dve_m(u, 65))
                    cPE.tick(t.matmul(pms[0:1, 0:512], c_ones_f, accd[:, :], start=True, stop=True))
                    M[("3dn", u)] = cPE.v
                    t.wait_ge(cDVE.h, dve_m(u, 68))
                    cPE.tick(t.matmul(psp, c_ones1r, row_r[0:1, 0:512], start=True, stop=True))
                    M[("3rb", u)] = cPE.v
        M["3_pe"] = cPE.v

        @blk3.scalar
        def _(s):
            for b in range(16):
                for hh in range(HPC):
                    u = 4 * b + hh
                    s.wait_ge(cPE.h, M[("3qb", u)])
                    s.wait_ge(cDVE.h, dve_m(u, 1))
                    ds_qra_w(s, u)
                    if u >= 1:
                        s.wait_ge(cPE.h, M[("3bm", u - 1)])
                    cACT.tick(s.activation(sscr, psp, AF.Square))
                    cACT.tick(s.activation(sscr2[0:64, :], qra[u % 2][0:64, :], AF.Square))
                    s.wait_ge(cPE.h, M[("3bm", u)])
                    cACT.tick(s.activation(q2row[0:1, :], pms[0:1, 0:512], AF.Ln))
                    s.wait_ge(cACT.h, cACT.v)
                    if u >= 2:
                        s.wait_ge(cPE.h, M[("3sc", u - 2, NG - 1)])
                    cACT.tick(s.activation(qra[u % 2][64:65, :], q2row[0:1, :], AF.Exp,
                                           scale=0.5, bias=c_knmax[0:1, hh:hh + 1]))
                    for g_ in range(NG):
                        s.wait_ge(cPE.h, M[("3sc", u, g_)])
                        if g_ >= 2:
                            s.wait_ge(cPE.h, M[("3av", u, g_ - 2)])
                            s.wait_ge(cDVE.h, dve_m(u, 2 * g_ - 1))
                        cACT.tick(s.activation(pbuf[g_ % 2][:, :], pg[g_ % 2][:, :], AF.Exp))
        assert KSTOP < 30 or (cACT.v == act0 + ACT_U * 64)

        @blk3.vector
        def _(v):
            for b in range(16):
                for hh in range(HPC):
                    u = 4 * b + hh
                    v.wait_ge(cPE.h, M[("3qb", u)])
                    v.wait_ge(cDVE.h, cDVE.v)
                    if u >= 1:
                        v.wait_ge(cPE.h, M[("3sc", u - 1, NG - 1)])
                    cDVE.tick(v.tensor_copy(qnb, psp))
                    for g_ in range(NG):
                        v.wait_ge(cACT.h, act_m(u, g_ + 5))
                        if g_ == 0:
                            cDVE.tick(v.tensor_copy(accd[:, :], pbuf[0][:, 0:512]))
                            v.wait_ge(cDVE.h, cDVE.v)
                            cDVE.tick(v.tensor_tensor(accd[:, :], accd[:, :],
                                                      pbuf[0][:, 512:1024], ALU.add))
                        else:
                            cDVE.tick(v.tensor_tensor(accd[:, :], accd[:, :],
                                                      pbuf[g_ % 2][:, 0:512], ALU.add))
                            v.wait_ge(cDVE.h, cDVE.v)
                            cDVE.tick(v.tensor_tensor(accd[:, :], accd[:, :],
                                                      pbuf[g_ % 2][:, 512:1024], ALU.add))
                        v.wait_ge(cDVE.h, cDVE.v)
                    v.wait_ge(cPE.h, M[("3dn", u)])
                    cDVE.tick(v.tensor_scalar(row_t[0:1, 0:512], pms[0:1, 0:512], 1e-30, None, ALU.add))
                    v.wait_ge(cDVE.h, cDVE.v)
                    cDVE.tick(v.reciprocal(row_q[0:1, 0:512], row_t[0:1, 0:512]))
                    v.wait_ge(cDVE.h, cDVE.v)
                    with lp():
                        cDVE.tick(v.tensor_copy(row_r[0:1, 0:512], row_q[0:1, 0:512]))
                    v.wait_ge(cPE.h, M[("3rb", u)])
                    cDVE.tick(v.tensor_copy(rbsb, psp))
                    v.wait_ge(cDVE.h, cDVE.v)
                    if u >= 2:
                        ds_av_w(v, u - 2)
                    cDVE.tick(v.tensor_tensor(avob[u % 2][:, :], pav[u % 2], rbsb, ALU.mult))
        assert KSTOP < 30 or (cDVE.v == dve0 + DVE_U * 64)

        @blk3.sync
        def _(sy):
            sy.wait_ge(cCC.h, M["ag1"])
            sy.wait_ge(dS.h, M["d_roped"])
            sy.wait_ge(cPE.h, M["2_pe"])
            for b in range(16):
                rr, co = b // 2, (b % 2) * 512
                if b >= 1:
                    sy.wait_ge(cPE.h, M[("3qb", 4 * (b - 1) + 3)])
                for m in range(12):
                    sy.dma_start(
                        out=qcb[:, 512 * m:512 * (m + 1)],
                        in_=ag1o[AG1R * rr + QC_OFF + 128 * m:AG1R * rr + QC_OFF + 128 * (m + 1),
                                 co:co + 512]).then_inc(dSq.h, 16)
                    dSq.v += 16
                for hh in range(HPC):
                    u = 4 * b + hh
                    if u >= 2:
                        sy.wait_ge(cACT.h, act_m(u - 2, 2))
                        sy.wait_ge(cPE.h, M[("3sc", u - 2, NG - 1)])
                    sy.dma_start(out=qra[u % 2][0:64, :],
                                 in_=qrl[HPC * RD * rr + RD * hh:HPC * RD * rr + RD * (hh + 1),
                                         co:co + 512]).then_inc(dSr[u % 2].h, 16)
                    dSr[u % 2].v += 16
                    assert dSr[u % 2].v == dsr0[u % 2] + 16 * (u // 2 + 1)
                for hh in range(HPC):
                    u = 4 * b + hh
                    sy.wait_ge(cDVE.h, dve_m(u, 70))
                    sy.dma_start(out=ag2i[128 * hh:128 * (hh + 1), 512 * b:512 * (b + 1)],
                                 in_=avob[u % 2][:, :]).then_inc(dSv[u % 2].h, 16)
                    dSv[u % 2].v += 16
                    assert dSv[u % 2].v == dsv0[u % 2] + 16 * (u // 2 + 1)
        M["3_out"] = (dSq.v, dSv[0].v, dSv[1].v)

        @blk4.gpsimd
        def _(g):
            g.wait_ge(dSq.h, M["3_out"][0])
            g.wait_ge(dSv[0].h, M["3_out"][1])
            g.wait_ge(dSv[1].h, M["3_out"][2])
            ins = g.collective_compute("AllGather", ALU.bypass, replica_groups=RG,
                                       ins=[ag2i.opt()], outs=[ag2o.opt()])
            cCC.tick(ins)
        M["ag2"] = cCC.v

        if DEBUG:
            @blk4.sync
            def _(sy):
                sy.wait_ge(cCC.h, M["ag2"])
                for hh in range(HPC):
                    sy.dma_start(out=dbg["av"][128 * hh:128 * (hh + 1), :],
                                 in_=ag2i[128 * hh:128 * (hh + 1), :]).then_inc(dS.h, 16)
                    dS.v += 16

        for s_ in range(2):
            pe0, act0, dve0, ds0, dp0 = cPE.v, cACT.v, cDVE.v, dS.v, dP.v
            dpa0 = [dPs[0].v, dPs[1].v]
            dsx0 = [dSx[0].v, dSx[1].v]
            DVE1 = dve0

            @blk4.sync
            def _(sy, s_=s_, ds0=ds0, dve0=dve0, dSx=dSx, dsx0=dsx0):
                sy.wait_ge(cCC.h, M["ag2"])
                if s_ == 1:
                    sy.wait_ge(cPE.h, M[("4wo_last", 0)])
                    sy.wait_ge(cDVE.h, M[("4ln_n", 0, 3)])
                    sy.wait_ge(dSv[0].h, dSv[0].v)
                    sy.wait_ge(dSv[1].h, dSv[1].v)
                pid_sp = pid_holder["sp"]
                v3 = ag2o.rearrange("(a p) n -> p a n", p=128)
                sy.dma_start(out=avsl[:, :].rearrange("p (t n) -> p t n", n=512),
                             in_=v3[:, :, bass.ts(pid_sp * 2 + s_, 512)]).then_inc(dS.h, 16)
                dS.v += 16
                for m in range(32):
                    if m >= 2:
                        sy.wait_ge(cDVE.h, dve0 + 2 * (m - 2) + 1)
                    sy.dma_start(out=xres[m % 2][:, :],
                                 in_=xt[128 * m:128 * (m + 1), 512 * s_:512 * s_ + 512]
                                 ).then_inc(dSx[m % 2].h, 16)
                    dSx[m % 2].v += 16
            assert KSTOP < 40 or (dS.v == ds0 + 16)

            @blk4.tensor
            def _(t, s_=s_, act0=act0, dve0=dve0, ds0=ds0, dp0=dp0, dpa0=dpa0):
                t.wait_ge(dS.h, ds0 + 16)
                t.wait_ge(cDVE.h, dve0)
                t.wait_ge(cACT.h, act0)
                for m in range(32):
                    t.wait_ge(dPs[m % 2].h, dpa0[m % 2] + 16 * (m // 2 + 1))
                    if m >= 2:
                        t.wait_ge(cDVE.h, dve0 + 2 * (m - 2) + 2)
                    pt = pg0[:, 0:512] if m % 2 == 0 else pg0[:, 512:1024]
                    for c_ in range(32):
                        ins = t.matmul(pt, wob[m % 2][:, 128 * c_:128 * (c_ + 1)],
                                       avsl[:, 512 * c_:512 * (c_ + 1)],
                                       start=(c_ == 0), stop=(c_ == 31))
                    cPE.tick(ins)
                    M[("4wo", s_, m)] = cPE.v
                    if m >= 1:
                        mm_ = m - 1
                        t.wait_ge(cDVE.h, dve0 + 2 * mm_ + 2)
                        t.wait_ge(cACT.h, act0 + mm_ + 1)
                        sq_ = sqzb if mm_ % 2 == 0 else sqzb2
                        t.matmul(pav[0][0:1, 0:512], c_ones_bf, zbf[:, 512 * mm_:512 * (mm_ + 1)],
                                 start=(mm_ == 0), stop=False)
                        ins = t.matmul(pav[1][0:1, 0:512], c_ones_bf, sq_,
                                       start=(mm_ == 0), stop=False)
                        cPE.tick(ins)
                        M[("4sq", s_, mm_)] = cPE.v
                M[("4wo_last", s_)] = cPE.v
                mm_ = 31
                t.wait_ge(cDVE.h, dve0 + 2 * mm_ + 2)
                t.wait_ge(cACT.h, act0 + mm_ + 1)
                sq_ = sqzb if mm_ % 2 == 0 else sqzb2
                t.matmul(pav[0][0:1, 0:512], c_ones_bf, zbf[:, 512 * mm_:512 * (mm_ + 1)],
                         start=False, stop=True)
                ins = t.matmul(pav[1][0:1, 0:512], c_ones_bf, sqzb2, start=False, stop=True)
                cPE.tick(ins)
                M[("4sq", s_, 31)] = cPE.v
                M[("4st", s_)] = cPE.v
                t.wait_ge(cDVE.h, dve0 + 75)
                t.matmul(pms, c_ones1r, row_r[0:1, 0:512], start=True, stop=True)
                t.matmul(psp, c_ones1r, row_r[0:1, 512:1024], start=True, stop=True)
                ins = t.matmul(pg1[:, 512:1024], c_ones1r, row_r2[0:1, 0:512], start=True, stop=True)
                cPE.tick(ins)
                M[("4bc", s_)] = cPE.v
                t.wait_ge(dP.h, dp0 + 16)
                t.wait_ge(cDVE.h, dve0 + 78)
                for c_ in range(32):
                    t.matmul(pg1[0:64, 0:512], wrtb[:, 64 * c_:64 * (c_ + 1)],
                             zh[:, 512 * c_:512 * (c_ + 1)],
                             start=(c_ == 0), stop=(c_ == 31))
                    ins = t.matmul(pav[0][0:64, 0:1], wrtb[:, 64 * c_:64 * (c_ + 1)], c_ones_f,
                                   start=(c_ == 0), stop=(c_ == 31))
                cPE.tick(ins)
                M[("4rt", s_)] = cPE.v
                for m in range(8):
                    t.wait_ge(dPs[m % 2].h, dpa0[m % 2] + 16 * (16 + m // 2 + 1))
                    if m >= 2:
                        t.wait_ge(cDVE.h, dve0 + 84 + 3 * (m - 2))
                    if m >= 1:
                        t.wait_ge(cDVE.h, dve0 + 82 + 3 * (m - 1))
                    pt = pg0[:, 0:512] if m % 2 == 0 else pg0[:, 512:1024]
                    for c_ in range(32):
                        t.matmul(pt, atb[m % 2][:, 128 * c_:128 * (c_ + 1)],
                                 zbf[:, 512 * c_:512 * (c_ + 1)],
                                 start=(c_ == 0), stop=(c_ == 31))
                        ins = t.matmul(pav[1][:, m:m + 1],
                                       atb[m % 2][:, 128 * c_:128 * (c_ + 1)], c_ones_bf,
                                       start=(c_ == 0), stop=(c_ == 31))
                    cPE.tick(ins)
                    M[("4h", s_, m)] = cPE.v

            @blk4.gpsimd
            def _(g, s_=s_, dpa0=dpa0):
                for m in range(32):
                    if m >= 2:
                        g.wait_ge(cPE.h, M[("4wo", s_, m - 2)])
                    elif s_ == 1:
                        g.wait_ge(cPE.h, M[("4B", 0, 31)])
                    g.dma_start(out=wob[m % 2][:, :],
                                in_=wo[:, 128 * m:128 * (m + 1)].rearrange("(t p) m -> p t m", p=128)
                                ).then_inc(dPs[m % 2].h, 16)
                    dPs[m % 2].v += 16
                    assert dPs[m % 2].v == dpa0[m % 2] + 16 * (m // 2 + 1)
                g.dma_start(out=wrtb[:, :],
                            in_=wrt.rearrange("(t p) m -> p t m", p=128)).then_inc(dP.h, 16)
                dP.v += 16
                for m in range(8):
                    if m >= 2:
                        g.wait_ge(cPE.h, M[("4h", s_, m - 2)])
                    else:
                        g.wait_ge(cPE.h, M[("4wo", s_, 31)])
                    g.dma_start(out=atb[m % 2][:, :],
                                in_=at[:, 128 * m:128 * (m + 1)].rearrange("(t p) m -> p t m", p=128)
                                ).then_inc(dPs[m % 2].h, 16)
                    dPs[m % 2].v += 16
                    assert dPs[m % 2].v == dpa0[m % 2] + 16 * (16 + m // 2 + 1)
            @blk4.scalar
            def _(s, s_=s_, dve0=dve0):
                for m in range(32):
                    s.wait_ge(cDVE.h, dve0 + 2 * m + 2)
                    if m >= 2:
                        s.wait_ge(cPE.h, M[("4sq", s_, m - 2)])
                    sq_ = sqzb if m % 2 == 0 else sqzb2
                    cACT.tick(s.activation(sq_, zbf[:, 512 * m:512 * (m + 1)], AF.Square))
                s.wait_ge(cDVE.h, dve0 + 71)
                cACT.tick(s.activation(row_t[0:1, 0:512], row_k[0:1, 0:512], AF.Sqrt))
            M[("4act1", s_)] = cACT.v

            @blk4.vector
            def _(v, s_=s_, act0=act0, dve0=dve0, dsx0=dsx0,
                  dsv_tot=(dSv[0].v, dSv[1].v)):
                if s_ == 1:
                    v.wait_ge(dSv[0].h, dsv_tot[0])
                    v.wait_ge(dSv[1].h, dsv_tot[1])
                for m in range(32):
                    v.wait_ge(cPE.h, M[("4wo", s_, m)])
                    v.wait_ge(cDVE.h, cDVE.v)
                    v.wait_ge(dSx[m % 2].h, dsx0[m % 2] + 16 * (m // 2 + 1))
                    pt = pg0[:, 0:512] if m % 2 == 0 else pg0[:, 512:1024]
                    cDVE.tick(v.tensor_tensor(zh[:, 512 * m:512 * (m + 1)], pt,
                                              xres[m % 2][:, :], ALU.add))
                    v.wait_ge(cDVE.h, cDVE.v)
                    cDVE.tick(v.tensor_copy(zbf[:, 512 * m:512 * (m + 1)],
                                            zh[:, 512 * m:512 * (m + 1)]))
                v.wait_ge(cPE.h, M[("4st", s_)])
                cDVE.tick(v.tensor_scalar(row_q[0:1, 512:1024], pav[0][0:1, 0:512],
                                          -1.0 / D, None, ALU.mult))
                v.wait_ge(cDVE.h, cDVE.v)
                with lp():
                    cDVE.tick(v.tensor_copy(row_r[0:1, 512:1024], row_q[0:1, 512:1024]))
                cDVE.tick(v.tensor_scalar(row_t[0:1, 0:512], pav[0][0:1, 0:512],
                                          1.0 / D, None, ALU.mult))
                cDVE.tick(v.tensor_scalar(row_k[0:1, 0:512], pav[1][0:1, 0:512],
                                          1.0 / D, None, ALU.mult))
                v.wait_ge(cDVE.h, cDVE.v)
                cDVE.tick(v.tensor_mul(row_t[0:1, 512:1024], row_t[0:1, 0:512],
                                       row_t[0:1, 0:512]))
                v.wait_ge(cDVE.h, cDVE.v)
                cDVE.tick(v.tensor_sub(row_k[0:1, 0:512], row_k[0:1, 0:512],
                                       row_t[0:1, 512:1024]))
                v.wait_ge(cDVE.h, cDVE.v)
                cDVE.tick(v.tensor_scalar(row_k[0:1, 0:512], row_k[0:1, 0:512],
                                          EPS_LN, None, ALU.add))
                v.wait_ge(cACT.h, M[("4act1", s_)])
                cDVE.tick(v.reciprocal(row_q[0:1, 0:512], row_t[0:1, 0:512]))
                v.wait_ge(cDVE.h, cDVE.v)
                with lp():
                    cDVE.tick(v.tensor_copy(row_r[0:1, 0:512], row_q[0:1, 0:512]))
                cDVE.tick(v.tensor_scalar(row_k[0:1, 512:1024], row_t[0:1, 0:512],
                                          2.0, None, ALU.mult))
                v.wait_ge(cDVE.h, cDVE.v)
                with lp():
                    cDVE.tick(v.tensor_copy(row_r2[0:1, 0:512], row_k[0:1, 512:1024]))
                v.wait_ge(cPE.h, M[("4bc", s_)])
                cDVE.tick(v.tensor_copy(rstdb[:, :], pms))
                cDVE.tick(v.tensor_copy(negmub[:, :], psp))
                cDVE.tick(v.tensor_copy(scl2b[:, :], pg1[:, 512:1024]))
                v.wait_ge(cPE.h, M[("4rt", s_)])
                cDVE.tick(v.tensor_copy(cwsb[0:64, 0:1], pav[0][0:64, 0:1]))
                v.wait_ge(cDVE.h, cDVE.v)
                cDVE.tick(v.scalar_tensor_tensor(hscr[0:64, :], negmub[0:64, :],
                                                 cwsb[0:64, 0:1], pg1[0:64, 0:512],
                                                 ALU.mult, ALU.add))
                v.wait_ge(cDVE.h, cDVE.v)
                cDVE.tick(v.tensor_mul(rrsb[0:64, :], hscr[0:64, :], rstdb[0:64, :]))
                for m in range(8):
                    v.wait_ge(cPE.h, M[("4h", s_, m)])
                    v.wait_ge(cDVE.h, cDVE.v)
                    cDVE.tick(v.tensor_copy(casb[:, m:m + 1], pav[1][:, m:m + 1]))
                    v.wait_ge(cDVE.h, cDVE.v)
                    pt = pg0[:, 0:512] if m % 2 == 0 else pg0[:, 512:1024]
                    cDVE.tick(v.scalar_tensor_tensor(hscr[:, :], negmub[:, :],
                                                     casb[:, m:m + 1], pt,
                                                     ALU.mult, ALU.add))
                    v.wait_ge(cDVE.h, cDVE.v)
                    cDVE.tick(v.tensor_mul(hbf[:, 512 * m:512 * (m + 1)], hscr[:, :],
                                           rstdb[:, :]))
            assert KSTOP < 40 or (cDVE.v == dve0 + 105)
            M[("4dve1", s_)] = cDVE.v

            @blk4.tensor
            def _(t, s_=s_):
                t.wait_ge(cDVE.h, M[("4dve1", s_)] - 24)
                for j in range(4):
                    ins = t.transpose(pms[:, 64 * j:64 * (j + 1)],
                                      rrsb[0:64, 128 * j:128 * (j + 1)], c_ident[0:64, 0:64])
                    cPE.tick(ins)
                M[("4tr", s_)] = cPE.v

            @blk4.vector
            def _(v, s_=s_):
                v.wait_ge(cPE.h, M[("4tr", s_)])
                v.wait_ge(cDVE.h, cDVE.v)
                for j in range(4):
                    cDVE.tick(v.tensor_copy(rsrow[:, 64 * j:64 * (j + 1)], pms[:, 64 * j:64 * (j + 1)]))
                for j in range(4):
                    sl = slice(64 * j, 64 * j + 64)
                    v.wait_ge(cDVE.h, cDVE.v)
                    cDVE.tick(v.max(out=mx16[:, 0:8], in_=rsrow[:, sl]))
                    v.wait_ge(cDVE.h, cDVE.v)
                    cDVE.tick(v.match_replace(out=zap[:, sl], in_to_replace=mx16[:, 0:8],
                                              in_values=rsrow[:, sl], imm_value=-3e38))
                    v.wait_ge(cDVE.h, cDVE.v)
                    cDVE.tick(v.max(out=mx16[:, 8:16], in_=zap[:, sl]))
                    v.wait_ge(cDVE.h, cDVE.v)
                    cDVE.tick(v.match_replace(out=zap[:, sl], in_to_replace=mx16[:, 8:16],
                                              in_values=zap[:, sl], imm_value=-3e38))
                    v.wait_ge(cDVE.h, cDVE.v)
                    cDVE.tick(v.tensor_scalar(colr[:, j:j + 1], mx16[:, 0:1], -1.0, None, ALU.mult))
                    v.wait_ge(cDVE.h, cDVE.v)
                    M[("4tk", s_, j)] = cDVE.v
            M[("4tk_all", s_)] = cDVE.v

            @blk4.scalar
            def _(s, s_=s_):
                for j in range(4):
                    sl = slice(64 * j, 64 * j + 64)
                    s.wait_ge(cDVE.h, M[("4tk", s_, j)])
                    cACT.tick(s.activation(gvals[:, sl], rsrow[:, sl], AF.Exp,
                                           bias=colr[:, j:j + 1]))
                    M[("4ex", s_, j)] = cACT.v

            @blk4.vector
            def _(v, s_=s_):
                for j in range(4):
                    sl = slice(64 * j, 64 * j + 64)
                    v.wait_ge(cACT.h, M[("4ex", s_, j)])
                    v.wait_ge(cDVE.h, cDVE.v)
                    cDVE.tick(v.tensor_scalar(rsrow[:, sl], zap[:, sl], -1e37, None, ALU.is_le))
                    v.wait_ge(cDVE.h, cDVE.v)
                    cDVE.tick(v.tensor_mul(gvals[:, sl], gvals[:, sl], rsrow[:, sl]))
                    v.wait_ge(cDVE.h, cDVE.v)
                    cDVE.tick(v.reduce_sum(colr[:, 4:5], gvals[:, sl], axis=AX.X))
                    v.wait_ge(cDVE.h, cDVE.v)
                    cDVE.tick(v.reciprocal(colr[:, 5:6], colr[:, 4:5]))
                    v.wait_ge(cDVE.h, cDVE.v)
                    cDVE.tick(v.tensor_scalar(gvals[:, sl], gvals[:, sl], colr[:, 5:6],
                                              None, ALU.mult))
                    v.wait_ge(cDVE.h, cDVE.v)
            M[("4g", s_)] = cDVE.v

            @blk4.tensor
            def _(t, s_=s_):
                t.wait_ge(cDVE.h, M[("4g", s_)])
                for j in range(4):
                    ins = t.transpose(psp[0:64, 128 * j:128 * (j + 1)],
                                      gvals[:, 64 * j:64 * (j + 1)], c_ident[:, :])
                    cPE.tick(ins)
                M[("4gt", s_)] = cPE.v

            @blk4.vector
            def _(v, s_=s_):
                v.wait_ge(cPE.h, M[("4gt", s_)])
                v.wait_ge(cDVE.h, cDVE.v)
                for j in range(4):
                    ins = v.tensor_mul(gtb[0:64, 128 * j:128 * (j + 1)],
                                       psp[0:64, 128 * j:128 * (j + 1)],
                                       scl2b[0:64, 128 * j:128 * (j + 1)])
                    cDVE.tick(ins)
            M[("4gtb", s_)] = cDVE.v

            @blk4.tensor
            def _(t, s_=s_):
                t.wait_ge(cDVE.h, M[("4gtb", s_)])
                for j8 in range(8):
                    if j8 >= 2:
                        t.wait_ge(cDVE.h, M[("4gtb", s_)] + (j8 - 2) + 1)
                    pt = pg1[:, 0:512] if j8 % 2 == 0 else pg1[:, 512:1024]
                    ins = t.matmul(pt, c_emat[0:64, 128 * j8:128 * (j8 + 1)],
                                   gtb[0:64, 0:512], start=True, stop=True)
                    cPE.tick(ins)
                    M[("4ge", s_, j8)] = cPE.v

            @blk4.vector
            def _(v, s_=s_):
                for j8 in range(8):
                    v.wait_ge(cPE.h, M[("4ge", s_, j8)])
                    v.wait_ge(cDVE.h, cDVE.v)
                    pt = pg1[:, 0:512] if j8 % 2 == 0 else pg1[:, 512:1024]
                    ins = v.tensor_tensor(hgbf[:, 512 * j8:512 * (j8 + 1)],
                                          hbf[:, 512 * j8:512 * (j8 + 1)], pt, ALU.mult)
                    cDVE.tick(ins)
                    M[("4hg", s_, j8)] = cDVE.v
                    assert M[("4hg", s_, j8)] == M[("4gtb", s_)] + j8 + 1
            M[("4hg_all", s_)] = cDVE.v

            dve_B0 = cDVE.v

            @blk4.tensor
            def _(t, s_=s_, dpa0=dpa0, dve0=dve0, dve_B0=dve_B0):
                t.wait_ge(cDVE.h, M[("4hg_all", s_)])
                for m in range(32):
                    t.wait_ge(dPs[m % 2].h, dpa0[m % 2] + 16 * (20 + m // 2 + 1))
                    if m >= 2:
                        t.wait_ge(cDVE.h, dve_B0 + (m - 2) + 1)
                    pt = pg0[:, 0:512] if m % 2 == 0 else pg0[:, 512:1024]
                    for kr in range(8):
                        ins = t.matmul(pt, btb[m % 2][:, 128 * kr:128 * (kr + 1)],
                                       hgbf[:, 512 * kr:512 * (kr + 1)],
                                       start=(kr == 0), stop=(kr == 7))
                    cPE.tick(ins)
                    M[("4B", s_, m)] = cPE.v

            @blk4.gpsimd
            def _(g, s_=s_, dpa0=dpa0):
                for m in range(32):
                    if m >= 2:
                        g.wait_ge(cPE.h, M[("4B", s_, m - 2)])
                    else:
                        g.wait_ge(cPE.h, M[("4h", s_, 7)])
                    g.dma_start(out=btb[m % 2][:, :],
                                in_=bt[:, 128 * m:128 * (m + 1)].rearrange("(t p) m -> p t m", p=128)
                                ).then_inc(dPs[m % 2].h, 16)
                    dPs[m % 2].v += 16
                    assert dPs[m % 2].v == dpa0[m % 2] + 16 * (20 + m // 2 + 1)

            @blk4.vector
            def _(v, s_=s_, dve_B0=dve_B0):
                for m in range(32):
                    v.wait_ge(cPE.h, M[("4B", s_, m)])
                    v.wait_ge(cDVE.h, cDVE.v)
                    pt = pg0[:, 0:512] if m % 2 == 0 else pg0[:, 512:1024]
                    ins = v.tensor_tensor(zh[:, 512 * m:512 * (m + 1)], pt,
                                          zh[:, 512 * m:512 * (m + 1)], ALU.add)
                    cDVE.tick(ins)
                    M[("4v", s_, m)] = cDVE.v
                    assert cDVE.v == dve_B0 + m + 1
            M[("4v_all", s_)] = cDVE.v

            if DEBUG and s_ == 0:
                @blk4.sync
                def _(sy):
                    sy.wait_ge(cDVE.h, M[("4v_all", 0)])
                    sy.dma_start(out=dbg["z"], in_=zh).then_inc(dS.h, 16)
                    sy.dma_start(out=dbg["gg"], in_=gvals).then_inc(dS.h, 16)
                    sy.dma_start(out=dbg["hg"], in_=hgbf).then_inc(dS.h, 16)
                    dS.v += 48

            for q in range(4):
                dve_q0 = cDVE.v

                @blk4.tensor
                def _(t, s_=s_, q=q, dve_q0=dve_q0):
                    t.wait_ge(cDVE.h, M[("4v_all", s_)])
                    if q >= 1:
                        t.wait_ge(cDVE.h, M[("4ln_e", s_, q - 1, 7)])
                    if q >= 2:
                        t.wait_ge(cDVE.h, M[("4ln_n", s_, q - 2)])
                    for c_ in range(32):
                        grp = c_ // 4
                        pt = pms if grp % 2 == 0 else psp
                        if c_ % 4 == 0 and c_ >= 8:
                            t.wait_ge(cDVE.h, dve_q0 + (grp - 2) + 1)
                        ins = t.transpose(pt[:, 128 * (c_ % 4):128 * (c_ % 4 + 1)],
                                          zh[:, 512 * c_ + 128 * q:512 * c_ + 128 * q + 128],
                                          c_ident[:, :])
                        cPE.tick(ins)
                        M[("4ln_t", s_, q, c_)] = cPE.v

                @blk4.vector
                def _(v, s_=s_, q=q, dve_q0=dve_q0):
                    for grp in range(8):
                        v.wait_ge(cPE.h, M[("4ln_t", s_, q, 4 * grp + 3)])
                        v.wait_ge(cDVE.h, cDVE.v)
                        pt = pms if grp % 2 == 0 else psp
                        ins = v.tensor_copy(vTq[q % 2][:, 512 * grp:512 * (grp + 1)], pt[:, 0:512])
                        cDVE.tick(ins)
                        M[("4ln_e", s_, q, grp)] = cDVE.v
                        assert cDVE.v == dve_q0 + grp + 1
                    v.wait_ge(cDVE.h, cDVE.v)
                    for jj in range(8):
                        ins = v.bn_stats(bnst[:, 6 * jj:6 * jj + 6],
                                         vTq[q % 2][:, 512 * jj:512 * (jj + 1)])
                    cDVE.tick(ins)
                    v.wait_ge(cDVE.h, cDVE.v)
                    cDVE.tick(v.bn_aggr(colr[:, 4:6], bnst[:, 0:48]))
                    v.wait_ge(cDVE.h, cDVE.v)
                    cDVE.tick(v.tensor_scalar(colr[:, 6:7], colr[:, 5:6], EPS_LN, None, ALU.add))
                    v.wait_ge(cDVE.h, cDVE.v)
                    cDVE.tick(v.tensor_scalar(colr[:, 4:5], colr[:, 4:5], -1.0, None, ALU.mult))
                M[("4bn", s_, q)] = cDVE.v

                @blk4.scalar
                def _(s, s_=s_, q=q):
                    s.wait_ge(cDVE.h, M[("4bn", s_, q)])
                    cACT.tick(s.activation(colr[:, 7:8], colr[:, 6:7], AF.Sqrt))
                M[("4sd2", s_, q)] = cACT.v

                @blk4.vector
                def _(v, s_=s_, q=q, ds0=ds0):
                    v.wait_ge(cACT.h, M[("4sd2", s_, q)])
                    cDVE.tick(v.reciprocal(colr[:, 6:7], colr[:, 7:8]))
                    v.wait_ge(cDVE.h, cDVE.v)
                    if q >= 2:
                        v.wait_ge(M[("4out", s_, q - 2)][0].h, M[("4out", s_, q - 2)][1])
                    ins = v.tensor_scalar(vTo[q % 2][:, :], vTq[q % 2][:, :],
                                          colr[:, 4:5], colr[:, 6:7], ALU.add, ALU.mult)
                    cDVE.tick(ins)
                    M[("4ln_n", s_, q)] = cDVE.v

                @blk4.sync
                def _(sy, s_=s_, q=q):
                    sy.wait_ge(cDVE.h, M[("4ln_n", s_, q)])
                    sy.dma_start(out=out[512 * s_ + 128 * q:512 * s_ + 128 * (q + 1), :],
                                 in_=vTo[q % 2][:, :]).then_inc(dSv[q % 2].h, 16)
                    dSv[q % 2].v += 16
                    M[("4out", s_, q)] = (dSv[q % 2], dSv[q % 2].v)

        @blk.sync
        def _(sy):
            for c in [dS, dP, dSq] + dPs + dSk + dSr + dSv + dSx:
                sy.wait_ge(c.h, c.v)

        return nc
    return nc


def kernel(**inputs):
    per_core = _host_layout(inputs)
    nc = build_nc()
    res = run_bass_kernel_spmd(nc, per_core, core_ids=list(range(NCORES)))
    outs = [res.results[c]["out"] for c in range(NCORES)]
    return np.concatenate(outs, axis=0).reshape(1, S, D)


# version 12
# speedup vs baseline: 1.2305x; 1.0809x over previous
import os
import numpy as np
from contextlib import ExitStack

import concourse.bass as bass
import concourse.mybir as mybir
from concourse.bass_utils import run_bass_kernel_spmd

F32 = mybir.dt.float32
F32R = mybir.dt.float32r
BF16 = mybir.dt.bfloat16
AF = mybir.ActivationFunctionType
ALU = mybir.AluOpType
AX = mybir.AxisListType
ET = mybir.EngineType

S, D, NH = 8192, 4096, 32
QR, KVR, RD, ND, VD = 1536, 512, 64, 128, 128
QK = ND + RD
NCORES, HPC, SQC = 8, 4, 1024
EPS_RMS, EPS_LN = 1e-6, 1e-5
INV_SQRT_QK = float(QK ** (-0.5))

QC_OFF, QROPE_OFF = 0, QR
KC_OFF = QR + NH * RD
KR_OFF = KC_OFF + KVR
AG1R = KR_OFF + RD

DEBUG = False
_KS = os.environ.get("KSTOP", "4")
KSTOP = {"0": 10, "1a": 11, "1b": 12, "1": 13, "1c": 13, "2": 20, "3": 30, "4": 40}[_KS]


class _SkipBlk:
    def __getattr__(self, name):
        return lambda f: None


def _host_layout(inputs):
    x = np.ascontiguousarray(inputs["L0_attn_hidden_states"][0])
    xT = np.ascontiguousarray(x.T)
    w_qa = np.ascontiguousarray(inputs["L0_attn_w_qa"])
    g_qa = np.ascontiguousarray(inputs["L0_attn_g_qa"])
    w_qb = np.ascontiguousarray(inputs["L0_attn_w_qb"])
    w_kva = np.ascontiguousarray(inputs["L0_attn_w_kva"])
    g_kva = np.ascontiguousarray(inputs["L0_attn_g_kva"])
    w_kvb = np.ascontiguousarray(inputs["L0_attn_w_kvb"])
    w_o = np.ascontiguousarray(inputs["L0_attn_w_o"])
    A = np.ascontiguousarray(inputs["L0_ffn_A"])
    B = np.ascontiguousarray(inputs["L0_ffn_B"])
    Wr = np.ascontiguousarray(inputs["L0_ffn_Wr"])

    wq3 = w_qb.reshape(QR, NH, QK)
    wqb_rope = np.ascontiguousarray(wq3[:, :, ND:].reshape(QR, NH * RD))
    wk3 = w_kvb.reshape(KVR, NH, ND + VD)

    A_t = np.ascontiguousarray(A.transpose(2, 0, 1).reshape(D, 1024))
    B_t = np.ascontiguousarray(B.transpose(0, 2, 1).reshape(1024, D))
    WrT = np.ascontiguousarray(Wr.T)
    gqa_t = np.ascontiguousarray(g_qa.reshape(QR // 128, 128).T)
    gkva_t = np.ascontiguousarray(g_kva.reshape(KVR // 128, 128).T)

    inv_freq = 1.0 / (10000.0 ** (np.arange(0, RD, 2, dtype=np.float64) / RD))
    t = np.arange(S, dtype=np.float64)
    freqs = np.outer(t, inv_freq)
    emb = np.concatenate([freqs, freqs], axis=-1)
    cosT = np.ascontiguousarray(np.cos(emb).T.astype(np.float32))
    sinT = np.sin(emb).T.astype(np.float32)
    sinTs = sinT.copy()
    sinTs[: RD // 2] *= -1.0
    sinTs = np.ascontiguousarray(sinTs)

    emat = np.zeros((64, 1024), dtype=np.float32)
    for e_ in range(64):
        emat[e_, 16 * e_:16 * e_ + 16] = 1.0
    ident = np.eye(128, dtype=np.float32)

    per_core = []
    for c in range(NCORES):
        sl = slice(SQC * c, SQC * (c + 1))
        heads = list(range(HPC * c, HPC * (c + 1)))
        per_core.append(dict(
            xt=np.ascontiguousarray(xT[:, sl]),
            wqa=w_qa,
            wqbr=wqb_rope,
            wqbn=np.ascontiguousarray(np.concatenate([wq3[:, h, :ND] for h in heads], 1)),
            wkva=w_kva,
            wkn=np.ascontiguousarray(np.concatenate([wk3[:, h, :ND] for h in heads], 1)),
            wkv=np.ascontiguousarray(np.concatenate([wk3[:, h, ND:] for h in heads], 1)),
            wo=w_o, at=A_t, bt=B_t, wrt=WrT,
            gqa=gqa_t, gkva=gkva_t,
            coss=np.ascontiguousarray(cosT[:, sl]),
            sins=np.ascontiguousarray(sinTs[:, sl]),
            emat=emat, ident=ident,
        ))
    return per_core


class Ctr:
    def __init__(self, handle):
        self.h = handle
        self.v = 0

    def tick(self, instr, n=1):
        self.v += n
        instr.then_inc(self.h, n)
        return self.v


def build_nc():
    nc = bass.Bass("TRN2", target_bir_lowering=False)

    def din(name, shape):
        return nc.dram_tensor(name, shape, F32, kind="ExternalInput").ap()

    xt = din("xt", [D, SQC])
    wqa = din("wqa", [D, QR])
    wqbr = din("wqbr", [QR, NH * RD])
    wqbn = din("wqbn", [QR, HPC * ND])
    wkva = din("wkva", [D, KVR + RD])
    wkn = din("wkn", [KVR, HPC * ND])
    wkv = din("wkv", [KVR, HPC * VD])
    wo = din("wo", [NH * VD, D])
    at = din("at", [D, 1024])
    bt = din("bt", [1024, D])
    wrt = din("wrt", [D, 64])
    gqa = din("gqa", [128, QR // 128])
    gkva = din("gkva", [128, KVR // 128])
    coss = din("coss", [RD, SQC])
    sins = din("sins", [RD, SQC])
    emat = din("emat", [64, 1024])
    ident = din("ident", [128, 128])
    out = nc.dram_tensor("out", [SQC, D], F32, kind="ExternalOutput").ap()

    ag1i = nc.dram_tensor("ag1_in", [AG1R, SQC], BF16, kind="Internal").ap()
    ag1o = nc.dram_tensor("ag1_out", [NCORES * AG1R, SQC], BF16, kind="Internal", addr_space="Shared").ap()
    ag2i = nc.dram_tensor("ag2_in", [HPC * VD, S], BF16, kind="Internal").ap()
    qrl = nc.dram_tensor("qrl", [NCORES * HPC * RD, SQC], BF16, kind="Internal").ap()
    ag2o = nc.dram_tensor("ag2_out", [NH * VD, S], BF16, kind="Internal", addr_space="Shared").ap()

    dbg = {}
    if DEBUG:
        def dout(name, shape, dt=F32):
            return nc.dram_tensor(name, shape, dt, kind="ExternalOutput").ap()
        dbg["qc"] = dout("dbg_qc", [12 * 128, SQC])
        dbg["kc"] = dout("dbg_kc", [4 * 128, SQC])
        dbg["kr"] = dout("dbg_kr", [64, SQC])
        dbg["qrope"] = dout("dbg_qrope", [NH * RD, SQC])
        dbg["kn"] = dout("dbg_kn", [128, HPC * S], BF16)
        dbg["v4"] = dout("dbg_v4", [128, 64 * 512], BF16)
        dbg["av"] = dout("dbg_av", [HPC * VD, S])
        dbg["z"] = dout("dbg_z", [128, 32 * 512])
        dbg["gg"] = dout("dbg_gg", [128, 4 * 64])
        dbg["hg"] = dout("dbg_hg", [128, 8 * 512])

    with ExitStack() as ctx:
        e = ctx.enter_context

        ARENA = 196608
        arena = nc.alloc_sbuf_tensor("arena", [128, ARENA], mybir.dt.uint8)
        arena_ap = arena.ap()

        def sb(name, shape, dtype, off):
            assert off % 32 == 0, name
            nb = shape[-1] * mybir.dt.size(dtype)
            assert off + nb <= ARENA, (name, off + nb)
            return arena_ap[0:shape[0], off:off + nb].bitcast(dtype)

        OA, OB, OC = 0, 65536, 131072
        xtb = sb("xtb", [128, 32 * 1024], BF16, OA)
        kn4 = sb("kn4", [128, HPC * S], BF16, OA)
        zh = sb("zh", [128, 32 * 512], F32, OA)
        qcpre = sb("qcpre", [128, 12 * 1024], F32, OB)
        kctb = sb("kctb", [128, 4 * 1024], BF16, OB)
        krb = sb("krb", [64, 1024], BF16, OB + 8192)
        qrb = [sb(f"qrb{i}", [64, 1024], BF16, OB + 10240 + 2048 * i) for i in range(2)]
        ropescr = sb("ropescr", [64, 1024], F32, OB + 14336)
        ropescr2 = sb("ropescr2", [64, 1024], F32, OB + 18432)
        p1wb = [sb(f"p1wb{i}", [128, 32 * 128], BF16, OB + 49152 + 8192 * i) for i in range(2)]
        v4 = sb("v4", [128, 64 * 512], BF16, OB)
        avsl = sb("avsl", [128, 32 * 512], BF16, OB)
        zbf = sb("zbf", [128, 32 * 512], BF16, OB + 32768)
        hbf = sb("hbf", [128, 8 * 512], BF16, OB)
        hgbf = sb("hgbf", [128, 8 * 512], BF16, OB + 8192)
        atb = [sb(f"atb{i}", [128, 32 * 128], BF16, OB + 16384 + 8192 * i) for i in range(2)]
        btb = [sb(f"btb{i}", [128, 8 * 128], BF16, OB + 16384 + 2048 * i) for i in range(2)]
        vTq = [sb(f"vTq{i}", [128, 4096], F32, OB + 16384 * i) for i in range(2)]
        vTo = [sb(f"vTo{i}", [128, 4096], F32, OB + 32768 + 16384 * i) for i in range(2)]
        row_q = sb("row_q", [1, 1024], F32, OC)
        row_k = sb("row_k", [1, 1024], F32, OC + 4096)
        row_t = sb("row_t", [1, 1024], F32, OC + 8192)
        row_r = None
        row_r2 = None
        q2row = sb("q2row", [1, 512], F32, OC + 18432)
        CC = OC + 20480
        rbsb = sb("rbsb", [128, 512], F32, OC + 2048)
        avob = [sb(f"avob{i}", [128, 512], BF16, OC + 4096 + 1024 * i) for i in range(2)]
        qnb = [sb("qnb", [128, 512], BF16, OC + 6144),
               sb("qnb2", [128, 512], BF16, OC + 17408)]
        qra = [sb(f"qra{i}", [65, 512], BF16, OC + 7168 + 7168 * i) for i in range(1)] + \
              [sb("qra1", [65, 512], BF16, OC + 14336)]
        accd = sb("accd", [128, 512], F32, OC + 10240)
        sscr = sb("sscr", [128, 512], BF16, OC + 16384)
        sscr2 = sb("sscr2", [64, 512], BF16, OC + 15360)
        hscr = sb("hscr", [128, 512], F32, OC + 10240)
        p1sq = sb("p1sq", [128, 512], BF16, OC)
        p1sq2 = sb("p1sq2", [128, 512], BF16, OC + 2048)
        qsb = sb("qsb", [128, 1024], F32, OC + 4096)
        kvpre = sb("kvpre", [128, 5 * 1024], F32, CC)
        qctb = sb("qctb", [128, 12 * 1024], BF16, CC + 20480)
        roped = sb("roped", [65, S], BF16, CC)
        qcb = sb("qcb", [128, 12 * 512], BF16, CC + 16384)
        wqbnb = sb("wqbnb", [128, 12 * 512], BF16, CC + 28672)
        pbuf = [sb(f"pbuf{i}", [128, 1024], BF16, CC + 40960 + 2048 * i) for i in range(2)]
        kcw = [sb(f"kcw{i}", [128, 4 * 512], BF16, CC + 16384 + 4096 * i) for i in range(2)]
        wknb = sb("wknb", [128, 4 * 512], BF16, CC + 24576)
        wkvb_ = sb("wkvb_", [128, 4 * 512], BF16, CC + 28672)
        wob = [sb(f"wob{i}", [128, 32 * 128], BF16, CC + 8192 * i) for i in range(2)]
        xres = [sb(f"xres{i}", [128, 512], F32, CC + 16384 + 2048 * i) for i in range(2)]
        wrtb = sb("wrtb", [128, 32 * 64], F32, CC + 20480)
        rrsb = sb("rrsb", [64, 512], F32, CC + 28672)
        rsrow = sb("rsrow", [128, 4 * 64], F32, CC + 30720)
        zap = sb("zap", [128, 4 * 64], F32, CC + 31744)
        gvals = sb("gvals", [128, 4 * 64], F32, CC + 32768)
        mx16 = sb("mx16", [128, 16], F32, CC + 33792)
        colr = sb("colr", [128, 8], F32, CC + 33856)
        gtb = sb("gtb", [64, 512], BF16, CC + 34048)
        casb = sb("casb", [128, 8], F32, CC + 35072)
        cwsb = sb("cwsb", [64, 1], F32, CC + 35104)
        rstdb = sb("rstdb", [128, 512], F32, CC + 35840)
        scl2b = sb("scl2b", [128, 512], F32, CC + 37888)
        negmub = sb("negmub", [128, 512], F32, CC + 39936)
        sqzb = sb("sqzb", [128, 512], BF16, CC + 41984)
        sqzb2 = sb("sqzb2", [128, 512], BF16, CC + 43008)
        bnst = sb("bnst", [128, 64], F32, CC + 44032)

        row_r = e(nc.sbuf_tensor("row_r", [1, 1024], F32R)).ap()
        row_r2 = e(nc.sbuf_tensor("row_r2", [1, 512], F32R)).ap()
        c_ident = e(nc.sbuf_tensor("c_ident", [128, 128], F32)).ap()
        c_emat = e(nc.sbuf_tensor("c_emat", [64, 1024], BF16)).ap()
        c_gqa = e(nc.sbuf_tensor("c_gqa", [128, 12], F32)).ap()
        c_gkva = e(nc.sbuf_tensor("c_gkva", [128, 4], F32)).ap()
        c_ones_bf = e(nc.sbuf_tensor("c_ones_bf", [128, 1], BF16)).ap()
        c_ones_r = e(nc.sbuf_tensor("c_ones_r", [128, 1], F32R)).ap()
        c_ones1r = e(nc.sbuf_tensor("c_ones1r", [1, 128], F32R)).ap()
        c_ones_f = e(nc.sbuf_tensor("c_ones_f", [128, 1], F32)).ap()
        c_cos = sb("c_cos", [64, 1024], F32, OB + 22528)
        c_sin = sb("c_sin", [64, 1024], F32, OB + 26624)
        c_knmax = e(nc.sbuf_tensor("c_knmax", [1, 8], F32)).ap()
        c_mrow = e(nc.sbuf_tensor("c_mrow", [1, 16], F32)).ap()

        pg0 = e(nc.psum_tensor("pg0", [128, 1024], F32)).ap()
        pg1 = e(nc.psum_tensor("pg1", [128, 1024], F32)).ap()
        pav = [e(nc.psum_tensor(f"pav{i}", [128, 512], F32)).ap() for i in range(2)]
        pms = e(nc.psum_tensor("pms", [128, 512], F32)).ap()
        psp = e(nc.psum_tensor("psp", [128, 512], F32)).ap()
        pg = [pg0, pg1]

        cPE = Ctr(e(nc.semaphore("cPE")))
        cACT = Ctr(e(nc.semaphore("cACT")))
        cDVE = Ctr(e(nc.semaphore("cDVE")))
        cCC = Ctr(e(nc.semaphore("cCC")))
        dS = Ctr(e(nc.semaphore("dS")))
        dP = Ctr(e(nc.semaphore("dP")))
        dPs = [Ctr(e(nc.semaphore("dPa"))), Ctr(e(nc.semaphore("dPb")))]
        dSk = [Ctr(e(nc.semaphore("dSka"))), Ctr(e(nc.semaphore("dSkb")))]
        dSq = Ctr(e(nc.semaphore("dSq")))
        dSr = [Ctr(e(nc.semaphore("dSra"))), Ctr(e(nc.semaphore("dSrb")))]
        dSv = [Ctr(e(nc.semaphore("dSva"))), Ctr(e(nc.semaphore("dSvb")))]
        dSx = [Ctr(e(nc.semaphore("dSxa"))), Ctr(e(nc.semaphore("dSxb")))]

        blk = e(nc.Block())
        blk1a = blk if KSTOP >= 11 else _SkipBlk()
        blk1b = blk if KSTOP >= 12 else _SkipBlk()
        blk1c = blk if KSTOP >= 13 else _SkipBlk()
        blk2 = blk if KSTOP >= 20 else _SkipBlk()
        blk3 = blk if KSTOP >= 30 else _SkipBlk()
        blk4 = blk if KSTOP >= 40 else _SkipBlk()
        RG = [list(range(NCORES))]
        M = {}

        def lp():
            return nc.allow_low_precision(reason="f32r rounding for broadcast matmuls is intentional")

        @blk.sync
        def _(sy):
            for ap_, src in [(c_ident, ident), (c_gqa, gqa), (c_gkva, gkva)]:
                sy.dma_start(out=ap_[:, :], in_=src).then_inc(dS.h, 16)
                dS.v += 16
        M["d_const"] = dS.v

        @blk.gpsimd
        def _(g):
            g.dma_start(out=c_emat[:, :], in_=emat).then_inc(dP.h, 16)
            dP.v += 16
            for j in range(32):
                g.dma_start(out=xtb[:, 1024 * j:1024 * (j + 1)],
                            in_=xt[128 * j:128 * (j + 1), :]).then_inc(dP.h, 16)
                dP.v += 16
            for m in range(2):
                g.dma_start(out=p1wb[m][:, :],
                            in_=wqa[:, 128 * m:128 * (m + 1)].rearrange("(t p) m -> p t m", p=128)
                            ).then_inc(dP.h, 16)
                dP.v += 16
        M["d_x"] = dP.v

        @blk.vector
        def _(v):
            v.memset(c_ones_bf[:, :], 1.0)
            cDVE.tick(v.memset(c_ones_f[:, :], 1.0))
            v.wait_ge(cDVE.h, cDVE.v)
            v.memset(row_t[0:1, 0:128], 1.0)
            with lp():
                cDVE.tick(v.tensor_copy(c_ones_r[:, :], c_ones_f[:, :]))
            v.wait_ge(cDVE.h, cDVE.v)
            with lp():
                cDVE.tick(v.tensor_copy(c_ones1r[0:1, :], row_t[0:1, 0:128]))
        M["ones"] = cDVE.v

        pe0, act0, dve0 = cPE.v, cACT.v, cDVE.v
        dpa0 = [dPs[0].v, dPs[1].v]

        @blk1a.tensor
        def _(t):
            t.wait_ge(dP.h, M["d_x"])
            t.wait_ge(dS.h, M["d_const"])
            t.wait_ge(cDVE.h, M["ones"])
            for m in range(12):
                if m >= 2:
                    t.wait_ge(dPs[m % 2].h, dpa0[m % 2] + 16 * ((m - 2) // 2 + 1))
                for sq in range(2):
                    if m >= 2:
                        t.wait_ge(cDVE.h, dve0 + 2 * (m - 2) + sq + 1)
                        t.wait_ge(cACT.h, act0 + 2 * (m - 2) + sq + 1)
                    pt = pg[m % 2][:, 512 * sq:512 * sq + 512]
                    for c_ in range(32):
                        ins = t.matmul(pt, p1wb[m % 2][:, 128 * c_:128 * (c_ + 1)],
                                       xtb[:, 1024 * c_ + 512 * sq:1024 * c_ + 512 * sq + 512],
                                       start=(c_ == 0), stop=(c_ == 31))
                    cPE.tick(ins)
                    M[("1a", m, sq)] = cPE.v
                    if m >= 1:
                        j = m - 1
                        t.wait_ge(cACT.h, act0 + 2 * j + sq + 1)
                        dst = pms[0:1, 0:512] if sq == 0 else psp[0:1, 0:512]
                        ins = t.matmul(dst, c_ones_bf, p1sq if sq == 0 else p1sq2,
                                       start=(j == 0), stop=False)
                        cPE.tick(ins)
                        M[("1a_ss", j, sq)] = cPE.v
            for sq in range(2):
                j = 11
                t.wait_ge(cACT.h, act0 + 2 * j + sq + 1)
                dst = pms[0:1, 0:512] if sq == 0 else psp[0:1, 0:512]
                ins = t.matmul(dst, c_ones_bf, p1sq if sq == 0 else p1sq2,
                               start=False, stop=True)
                cPE.tick(ins)
                M[("1a_ss", j, sq)] = cPE.v
        M["1a_pe"] = cPE.v

        @blk1a.gpsimd
        def _(g):
            for m in range(2, 12):
                g.wait_ge(cPE.h, M[("1a", m - 2, 1)])
                g.dma_start(out=p1wb[m % 2][:, :],
                            in_=wqa[:, 128 * m:128 * (m + 1)].rearrange("(t p) m -> p t m", p=128)
                            ).then_inc(dPs[m % 2].h, 16)
                dPs[m % 2].v += 16
                assert dPs[m % 2].v == dpa0[m % 2] + 16 * ((m - 2) // 2 + 1)

        @blk1a.scalar
        def _(s):
            for m in range(12):
                for sq in range(2):
                    s.wait_ge(cPE.h, M[("1a", m, sq)])
                    s.wait_ge(cDVE.h, dve0 + 2 * m + sq + 1)
                    if m >= 1:
                        s.wait_ge(cPE.h, M[("1a_ss", m - 1, sq)])
                    pt = pg[m % 2][:, 512 * sq:512 * sq + 512]
                    ins = s.activation(p1sq if sq == 0 else p1sq2, pt, AF.Square)
                    cACT.tick(ins)
        assert KSTOP < 11 or (cACT.v == act0 + 24)

        @blk1a.vector
        def _(v):
            for m in range(12):
                for sq in range(2):
                    v.wait_ge(cPE.h, M[("1a", m, sq)])
                    ins = v.tensor_copy(
                        qcpre[:, 1024 * m + 512 * sq:1024 * m + 512 * sq + 512],
                        pg[m % 2][:, 512 * sq:512 * sq + 512])
                    cDVE.tick(ins)
        assert KSTOP < 11 or (cDVE.v == dve0 + 24)

        @blk1a.vector
        def _(v):
            v.wait_ge(cPE.h, M["1a_pe"])
            v.tensor_copy(row_q[0:1, 0:512], pms[0:1, 0:512])
            cDVE.tick(v.tensor_copy(row_q[0:1, 512:1024], psp[0:1, 0:512]))
        M["1a_s2"] = cDVE.v

        @blk1a.scalar
        def _(s):
            s.wait_ge(cDVE.h, M["1a_s2"])
            cACT.tick(s.activation(row_t[0:1, :], row_q[0:1, :], AF.Sqrt, scale=1.0 / QR))
        M["1a_sqrt"] = cACT.v

        @blk1a.vector
        def _(v):
            v.wait_ge(cACT.h, M["1a_sqrt"])
            cDVE.tick(v.tensor_scalar(row_t[0:1, :], row_t[0:1, :], EPS_RMS, None, ALU.add))
            v.wait_ge(cDVE.h, cDVE.v)
            cDVE.tick(v.reciprocal(row_q[0:1, :], row_t[0:1, :]))
            v.wait_ge(cDVE.h, cDVE.v)
            cDVE.tick(v.tensor_scalar(row_t[0:1, :], row_q[0:1, :], INV_SQRT_QK, None, ALU.mult))
            v.wait_ge(cDVE.h, cDVE.v)
            with lp():
                cDVE.tick(v.tensor_copy(row_r[0:1, :], row_t[0:1, :]))
        M["1a_scale"] = cDVE.v

        @blk1a.tensor
        def _(t):
            t.wait_ge(cDVE.h, M["1a_scale"])
            t.matmul(pg0[:, 0:512], c_ones1r, row_r[0:1, 0:512], start=True, stop=True)
            cPE.tick(t.matmul(pg0[:, 512:1024], c_ones1r, row_r[0:1, 512:1024], start=True, stop=True))
        M["1a_bc"] = cPE.v

        @blk1a.vector
        def _(v):
            v.wait_ge(cPE.h, M["1a_bc"])
            cDVE.tick(v.tensor_copy(qsb[:, :], pg0[:, :]))
            v.wait_ge(cDVE.h, cDVE.v)
            for m in range(12):
                for sq in range(2):
                    ins = v.scalar_tensor_tensor(
                        qctb[:, 1024 * m + 512 * sq:1024 * m + 512 * sq + 512],
                        qcpre[:, 1024 * m + 512 * sq:1024 * m + 512 * sq + 512],
                        c_gqa[:, m:m + 1],
                        qsb[:, 512 * sq:512 * sq + 512],
                        ALU.mult, ALU.mult)
            cDVE.tick(ins)
        M["1a_qctb"] = cDVE.v

        @blk1a.sync
        def _(sy):
            sy.wait_ge(cDVE.h, M["1a_qctb"])
            for m in range(12):
                sy.dma_start(out=ag1i[QC_OFF + 128 * m:QC_OFF + 128 * (m + 1), :],
                             in_=qctb[:, 1024 * m:1024 * (m + 1)]).then_inc(dS.h, 16)
                dS.v += 16
            if DEBUG:
                for m in range(12):
                    sy.dma_start(out=dbg["qc"][128 * m:128 * (m + 1), :],
                                 in_=qctb[:, 1024 * m:1024 * (m + 1)]).then_inc(dS.h, 16)
                    dS.v += 16

        @blk1b.sync
        def _(sy):
            sy.wait_ge(cDVE.h, M["1a_qctb"])
            sy.dma_start(out=c_cos[:, :], in_=coss).then_inc(dS.h, 16)
            sy.dma_start(out=c_sin[:, :], in_=sins).then_inc(dS.h, 16)
            dS.v += 32
        M["d_cossin"] = dS.v

        pe0, act0, dve0 = cPE.v, cACT.v, cDVE.v
        dpa0 = [dPs[0].v, dPs[1].v]

        @blk1b.gpsimd
        def _(g):
            for m in range(5):
                w = 128 if m < 4 else 64
                if m >= 2:
                    g.wait_ge(cPE.h, pe0 + (2 if m - 2 == 0 else 4 * (m - 2) + 1))
                else:
                    g.wait_ge(cPE.h, M["1a_pe"])
                g.dma_start(out=p1wb[m % 2][:, 0:32 * w],
                            in_=wkva[:, 128 * m:128 * m + w].rearrange("(t p) m -> p t m", p=128)
                            ).then_inc(dPs[m % 2].h, 16)
                dPs[m % 2].v += 16
                assert dPs[m % 2].v == dpa0[m % 2] + 16 * (m // 2 + 1)

        @blk1b.tensor
        def _(t):
            t.wait_ge(cDVE.h, M["1a_qctb"])
            for m in range(5):
                w = 128 if m < 4 else 64
                t.wait_ge(dPs[m % 2].h, dpa0[m % 2] + 16 * (m // 2 + 1))
                for sq in range(2):
                    if m >= 2:
                        t.wait_ge(cDVE.h, dve0 + 2 * (m - 2) + sq + 1)
                        if m - 2 < 4:
                            t.wait_ge(cACT.h, act0 + 2 * (m - 2) + sq + 1)
                    pt = pg[m % 2][0:w, 512 * sq:512 * sq + 512]
                    for c_ in range(32):
                        ins = t.matmul(pt, p1wb[m % 2][:, w * c_:w * (c_ + 1)],
                                       xtb[:, 1024 * c_ + 512 * sq:1024 * c_ + 512 * sq + 512],
                                       start=(c_ == 0), stop=(c_ == 31))
                    cPE.tick(ins)
                    M[("1b", m, sq)] = cPE.v
                    if 1 <= m <= 4:
                        j = m - 1
                        t.wait_ge(cACT.h, act0 + 2 * j + sq + 1)
                        dst = pms[0:1, 0:512] if sq == 0 else psp[0:1, 0:512]
                        ins = t.matmul(dst, c_ones_bf, p1sq if sq == 0 else p1sq2,
                                       start=(j == 0), stop=(j == 3))
                        cPE.tick(ins)
                        M[("1b_ss", j, sq)] = cPE.v
                assert M[("1b", m, 1)] == pe0 + (2 if m == 0 else 4 * m + 1), m
        M["1b_pe"] = cPE.v

        @blk1b.scalar
        def _(s):
            for m in range(4):
                for sq in range(2):
                    s.wait_ge(cPE.h, M[("1b", m, sq)])
                    s.wait_ge(cDVE.h, dve0 + 2 * m + sq + 1)
                    if m >= 1:
                        s.wait_ge(cPE.h, M[("1b_ss", m - 1, sq)])
                    pt = pg[m % 2][:, 512 * sq:512 * sq + 512]
                    ins = s.activation(p1sq if sq == 0 else p1sq2, pt, AF.Square)
                    cACT.tick(ins)
        assert KSTOP < 12 or (cACT.v == act0 + 8)

        @blk1b.vector
        def _(v):
            for m in range(5):
                w = 128 if m < 4 else 64
                for sq in range(2):
                    v.wait_ge(cPE.h, M[("1b", m, sq)])
                    ins = v.tensor_copy(
                        kvpre[0:w, 1024 * m + 512 * sq:1024 * m + 512 * sq + 512],
                        pg[m % 2][0:w, 512 * sq:512 * sq + 512])
                    cDVE.tick(ins)
        assert KSTOP < 12 or (cDVE.v == dve0 + 10)
        M["1b_kvpre"] = cDVE.v

        @blk1b.vector
        def _(v):
            v.wait_ge(cPE.h, M["1b_pe"])
            v.tensor_copy(row_k[0:1, 0:512], pms[0:1, 0:512])
            cDVE.tick(v.tensor_copy(row_k[0:1, 512:1024], psp[0:1, 0:512]))
        M["1b_s2"] = cDVE.v

        @blk1b.scalar
        def _(s):
            s.wait_ge(cDVE.h, M["1b_s2"])
            cACT.tick(s.activation(row_t[0:1, :], row_k[0:1, :], AF.Sqrt, scale=1.0 / KVR))
        M["1b_sqrt"] = cACT.v

        @blk1b.vector
        def _(v):
            v.wait_ge(cACT.h, M["1b_sqrt"])
            cDVE.tick(v.tensor_scalar(row_t[0:1, :], row_t[0:1, :], EPS_RMS, None, ALU.add))
            v.wait_ge(cDVE.h, cDVE.v)
            cDVE.tick(v.reciprocal(row_k[0:1, :], row_t[0:1, :]))
            v.wait_ge(cDVE.h, cDVE.v)
            with lp():
                cDVE.tick(v.tensor_copy(row_r[0:1, :], row_k[0:1, :]))
        M["1b_scale"] = cDVE.v

        @blk1b.tensor
        def _(t):
            t.wait_ge(cDVE.h, M["1b_scale"])
            t.matmul(pg0[:, 0:512], c_ones1r, row_r[0:1, 0:512], start=True, stop=True)
            cPE.tick(t.matmul(pg0[:, 512:1024], c_ones1r, row_r[0:1, 512:1024], start=True, stop=True))
        M["1b_bc"] = cPE.v

        @blk1b.vector
        def _(v):
            v.wait_ge(cPE.h, M["1b_bc"])
            v.wait_ge(cDVE.h, M["1a_qctb"])
            cDVE.tick(v.tensor_copy(qsb[:, :], pg0[:, :]))
            v.wait_ge(cDVE.h, cDVE.v)
            for m in range(4):
                for sq in range(2):
                    ins = v.scalar_tensor_tensor(
                        kctb[:, 1024 * m + 512 * sq:1024 * m + 512 * sq + 512],
                        kvpre[:, 1024 * m + 512 * sq:1024 * m + 512 * sq + 512],
                        c_gkva[:, m:m + 1],
                        qsb[:, 512 * sq:512 * sq + 512],
                        ALU.mult, ALU.mult)
            cDVE.tick(ins)
            v.wait_ge(cDVE.h, cDVE.v)
            v.wait_ge(dS.h, M["d_cossin"])
            kr_ = kvpre[0:64, 4096:5120]
            v.tensor_copy(ropescr[0:32, :], kr_[32:64, :])
            cDVE.tick(v.tensor_copy(ropescr[32:64, :], kr_[0:32, :]))
            v.wait_ge(cDVE.h, cDVE.v)
            v.tensor_mul(ropescr[0:64, :], ropescr[0:64, :], c_sin[:, :])
            cDVE.tick(v.tensor_mul(ropescr2[0:64, :], kr_, c_cos[:, :]))
            v.wait_ge(cDVE.h, cDVE.v)
            cDVE.tick(v.tensor_add(krb[0:64, :], ropescr[0:64, :], ropescr2[0:64, :]))
        M["1b_done"] = cDVE.v

        @blk1b.sync
        def _(sy):
            sy.wait_ge(cDVE.h, M["1b_done"])
            for m in range(4):
                sy.dma_start(out=ag1i[KC_OFF + 128 * m:KC_OFF + 128 * (m + 1), :],
                             in_=kctb[:, 1024 * m:1024 * (m + 1)]).then_inc(dS.h, 16)
                dS.v += 16
            sy.dma_start(out=ag1i[KR_OFF:KR_OFF + 64, :], in_=krb[0:64, :]).then_inc(dS.h, 16)
            dS.v += 16
            if DEBUG:
                for m in range(4):
                    sy.dma_start(out=dbg["kc"][128 * m:128 * (m + 1), :],
                                 in_=kctb[:, 1024 * m:1024 * (m + 1)]).then_inc(dS.h, 16)
                    dS.v += 16
                sy.dma_start(out=dbg["kr"][:, :], in_=krb[0:64, :]).then_inc(dS.h, 16)
                dS.v += 16

        pe0, dve0 = cPE.v, cDVE.v
        dpa0 = [dPs[0].v, dPs[1].v]
        dsr0 = [dSr[0].v, dSr[1].v]

        @blk1c.gpsimd
        def _(g):
            for h in range(NH):
                if h >= 2:
                    g.wait_ge(cPE.h, pe0 + 2 * (h - 1))
                else:
                    g.wait_ge(cPE.h, M["1b_pe"])
                g.dma_start(out=p1wb[h % 2][:, 0:12 * 64],
                            in_=wqbr[:, 64 * h:64 * (h + 1)].rearrange("(t p) m -> p t m", p=128)
                            ).then_inc(dPs[h % 2].h, 16)
                dPs[h % 2].v += 16

        @blk1c.tensor
        def _(t):
            t.wait_ge(cDVE.h, M["1b_done"])
            for h in range(NH):
                t.wait_ge(dPs[h % 2].h, dpa0[h % 2] + 16 * (h // 2 + 1))
                if h >= 2:
                    t.wait_ge(cDVE.h, dve0 + 4 * (h - 2) + 1)
                for sq in range(2):
                    pt = pg[h % 2][0:64, 512 * sq:512 * sq + 512]
                    for c_ in range(12):
                        ins = t.matmul(pt, p1wb[h % 2][:, 64 * c_:64 * (c_ + 1)],
                                       qctb[:, 1024 * c_ + 512 * sq:1024 * c_ + 512 * sq + 512],
                                       start=(c_ == 0), stop=(c_ == 11))
                    cPE.tick(ins)
                M[("1c", h)] = cPE.v
                assert M[("1c", h)] == pe0 + 2 * (h + 1)
        M["1c_pe"] = cPE.v

        @blk1c.vector
        def _(v):
            for h in range(NH):
                v.wait_ge(cPE.h, M[("1c", h)])
                v.wait_ge(cDVE.h, cDVE.v)
                if h >= 2:
                    v.wait_ge(dSr[h % 2].h, dsr0[h % 2] + 16 * ((h - 2) // 2 + 1))
                qr_ = ropescr[0:64, :]
                sh_ = ropescr2[0:64, :]
                v.tensor_copy(qr_[0:64, 0:512], pg[h % 2][0:64, 0:512])
                cDVE.tick(v.tensor_copy(qr_[0:64, 512:1024], pg[h % 2][0:64, 512:1024]))
                v.wait_ge(cDVE.h, cDVE.v)
                v.tensor_copy(sh_[0:32, :], qr_[32:64, :])
                cDVE.tick(v.tensor_copy(sh_[32:64, :], qr_[0:32, :]))
                v.wait_ge(cDVE.h, cDVE.v)
                v.tensor_mul(sh_, sh_, c_sin[:, :])
                cDVE.tick(v.tensor_mul(qr_, qr_, c_cos[:, :]))
                v.wait_ge(cDVE.h, cDVE.v)
                cDVE.tick(v.tensor_add(qrb[h % 2][0:64, :], qr_, sh_))
                M[("1c_r", h)] = cDVE.v
        assert KSTOP < 13 or (cDVE.v == dve0 + 4 * NH)

        @blk1c.sync
        def _(sy):
            for h in range(NH):
                sy.wait_ge(cDVE.h, M[("1c_r", h)])
                sy.dma_start(out=ag1i[QROPE_OFF + 64 * h:QROPE_OFF + 64 * (h + 1), :],
                             in_=qrb[h % 2][0:64, :]).then_inc(dSr[h % 2].h, 16)
                dSr[h % 2].v += 16
                if DEBUG:
                    sy.dma_start(out=dbg["qrope"][64 * h:64 * (h + 1), :],
                                 in_=qrb[h % 2][0:64, :]).then_inc(dSr[h % 2].h, 16)
                    dSr[h % 2].v += 16
        M["ag1_ready"] = (dS.v, dSr[0].v, dSr[1].v)

        @blk2.gpsimd
        def _(g):
            g.wait_ge(dS.h, M["ag1_ready"][0])
            g.wait_ge(dSr[0].h, M["ag1_ready"][1])
            g.wait_ge(dSr[1].h, M["ag1_ready"][2])
            ins = g.collective_compute("AllGather", ALU.bypass, replica_groups=RG,
                                       ins=[ag1i.opt()], outs=[ag1o.opt()])
            cCC.tick(ins)
            g.dma_start(out=wknb[:, :],
                        in_=wkn.rearrange("(t p) m -> p t m", p=128)).then_inc(dP.h, 16)
            g.dma_start(out=wkvb_[:, :],
                        in_=wkv.rearrange("(t p) m -> p t m", p=128)).then_inc(dP.h, 16)
            dP.v += 32
        M["ag1"] = cCC.v
        M["d_wk"] = dP.v

        pid_holder = {}
        pe0, act0, dve0, ds0 = cPE.v, cACT.v, cDVE.v, dS.v

        @blk2.vector
        def _(v):
            cDVE.tick(v.memset(roped[64:65, :], -1.0))
        M["negrow"] = cDVE.v

        @blk2.sync
        def _(sy):
            sy.wait_ge(cCC.h, M["ag1"])
            pid_holder["sp"] = nc.partition_id([ET.SP])
            v3 = ag1o.rearrange("(r x) n -> r x n", x=AG1R)
            sy.dma_start(out=qrl.rearrange("(r x) n -> r x n", x=HPC * RD),
                         in_=v3[:, bass.ts(pid_holder["sp"] + (QROPE_OFF // (HPC * RD)), HPC * RD), :]
                         ).then_inc(dS.h, 16)
            dS.v += 16
            for tt in range(16):
                rr, co = tt // 2, (tt % 2) * 512
                if tt >= 2:
                    sy.wait_ge(cPE.h, pe0 + tt - 1)
                for r_ in range(4):
                    sy.dma_start(
                        out=kcw[tt % 2][:, 512 * r_:512 * (r_ + 1)],
                        in_=ag1o[AG1R * rr + KC_OFF + 128 * r_:AG1R * rr + KC_OFF + 128 * (r_ + 1),
                                 co:co + 512]).then_inc(dSk[tt % 2].h, 16)
                    dSk[tt % 2].v += 16
                M[("kcw", tt)] = (dSk[tt % 2], dSk[tt % 2].v)
            for rr in range(NCORES):
                sy.dma_start(out=roped[0:64, SQC * rr:SQC * (rr + 1)],
                             in_=ag1o[AG1R * rr + KR_OFF:AG1R * rr + KR_OFF + 64, :]
                             ).then_inc(dS.h, 16)
                dS.v += 16
        M["d_roped"] = dS.v

        @blk2.tensor
        def _(t):
            t.wait_ge(dP.h, M["d_wk"])
            t.wait_ge(cDVE.h, dve0)
            for tt in range(16):
                t.wait_ge(M[("kcw", tt)][0].h, M[("kcw", tt)][1])
                if tt >= 1:
                    t.wait_ge(cDVE.h, dve0 + 1 + tt)
                kpt = [pg0[:, 0:512], pg0[:, 512:1024], pg1[:, 0:512], pg1[:, 512:1024]]
                vpt = [pav[0], pav[1], pms, psp]
                for m in range(4):
                    for r_ in range(4):
                        t.matmul(kpt[m],
                                 wknb[:, 512 * r_ + 128 * m:512 * r_ + 128 * (m + 1)],
                                 kcw[tt % 2][:, 512 * r_:512 * (r_ + 1)],
                                 start=(r_ == 0), stop=(r_ == 3))
                for sk in range(4):
                    for r_ in range(4):
                        ins = t.matmul(vpt[sk],
                                       kcw[tt % 2][:, 512 * r_ + 128 * sk:512 * r_ + 128 * (sk + 1)],
                                       wkvb_[:, 512 * r_:512 * (r_ + 1)],
                                       start=(r_ == 0), stop=(r_ == 3))
                cPE.tick(ins)
                M[("2", tt)] = cPE.v
        assert KSTOP < 20 or (cPE.v == pe0 + 16)
        M["2_pe"] = cPE.v

        @blk2.vector
        def _(v):
            for tt in range(16):
                v.wait_ge(cPE.h, M[("2", tt)])
                v.wait_ge(cDVE.h, cDVE.v)
                kpt = [pg0[:, 0:512], pg0[:, 512:1024], pg1[:, 0:512], pg1[:, 512:1024]]
                vpt = [pav[0], pav[1], pms, psp]
                for m in range(4):
                    v.tensor_copy(kn4[:, S * m + 512 * tt:S * m + 512 * (tt + 1)], kpt[m])
                for sk in range(4):
                    ins = v.tensor_copy(v4[:, 512 * (4 * tt + sk):512 * (4 * tt + sk + 1)], vpt[sk])
                cDVE.tick(ins)
        assert KSTOP < 20 or (cDVE.v == dve0 + 1 + 16)
        M["2_dve"] = cDVE.v

        pe0, act0, dve0 = cPE.v, cACT.v, cDVE.v

        @blk2.scalar
        def _(s):
            s.wait_ge(cDVE.h, M["2_dve"])
            s.wait_ge(dS.h, M["d_roped"])
            for h in range(HPC):
                for tt in range(16):
                    u = 16 * h + tt
                    if u >= 1:
                        s.wait_ge(cPE.h, pe0 + u)
                    s.activation(sscr, kn4[:, S * h + 512 * tt:S * h + 512 * (tt + 1)], AF.Square)
                    ins = s.activation(sscr2[0:64, :], roped[0:64, 512 * tt:512 * (tt + 1)], AF.Square)
                    cACT.tick(ins)
        assert KSTOP < 20 or (cACT.v == act0 + 64)

        @blk2.tensor
        def _(t):
            for h in range(HPC):
                for tt in range(16):
                    u = 16 * h + tt
                    t.wait_ge(cACT.h, act0 + u + 1)
                    if u >= 1:
                        t.wait_ge(cDVE.h, dve0 + 17 * (u // 16) + (u % 16))
                    t.matmul(pms[0:1, 0:512], c_ones_bf, sscr, start=True, stop=False)
                    ins = t.matmul(pms[0:1, 0:512], c_ones_bf[0:64, :], sscr2[0:64, :],
                                   start=False, stop=True)
                    cPE.tick(ins)
        assert KSTOP < 20 or (cPE.v == pe0 + 64)

        @blk2.vector
        def _(v):
            for h in range(HPC):
                for tt in range(16):
                    u = 16 * h + tt
                    v.wait_ge(cPE.h, pe0 + u + 1)
                    cDVE.tick(v.reduce_max(c_mrow[0:1, tt:tt + 1], pms[0:1, 0:512], axis=AX.X))
                v.wait_ge(cDVE.h, cDVE.v)
                cDVE.tick(v.reduce_max(c_knmax[0:1, h:h + 1], c_mrow[0:1, 0:16], axis=AX.X))
                v.wait_ge(cDVE.h, cDVE.v)
        assert KSTOP < 20 or (cDVE.v == dve0 + 68)
        M["2_knm"] = cDVE.v

        @blk2.scalar
        def _(s):
            s.wait_ge(cDVE.h, M["2_knm"])
            cACT.tick(s.activation(c_knmax[0:1, 0:HPC], c_knmax[0:1, 0:HPC], AF.Ln))
        M["knln"] = cACT.v

        @blk2.vector
        def _(v):
            v.wait_ge(cACT.h, M["knln"])
            v.wait_ge(cDVE.h, cDVE.v)
            cDVE.tick(v.tensor_scalar(c_knmax[0:1, 0:HPC], c_knmax[0:1, 0:HPC],
                                      0.5, None, ALU.mult))
        M["knm2"] = cDVE.v

        @blk2.gpsimd
        def _(g):
            g.wait_ge(cPE.h, M["2_pe"])
            g.dma_start(out=wqbnb[:, :],
                        in_=wqbn.rearrange("(t p) m -> p t m", p=128)).then_inc(dP.h, 16)
            dP.v += 16
        M["d_wqbn"] = dP.v

        if DEBUG:
            @blk2.sync
            def _(sy):
                sy.wait_ge(cDVE.h, M["2_dve"])
                sy.dma_start(out=dbg["kn"], in_=kn4).then_inc(dS.h, 16)
                sy.dma_start(out=dbg["v4"], in_=v4).then_inc(dS.h, 16)
                dS.v += 32

        NG = 32
        pe0, act0, dve0 = cPE.v, cACT.v, cDVE.v
        dsq0 = dSq.v
        dsr0 = [dSr[0].v, dSr[1].v]
        dsv0 = [dSv[0].v, dSv[1].v]
        ACT_U, DVE_U = 36, 70
        import builtins; builtins.DBG3 = (pe0, act0, dve0, dsq0)

        def act_m(u, i):
            return act0 + ACT_U * u + i

        def dve_m(u, i):
            return dve0 + DVE_U * u + i

        def ds_qcb(b):
            return dsq0 + 16 * 12 * (b + 1)

        def ds_qra_w(eng, u):
            eng.wait_ge(dSr[u % 2].h, dsr0[u % 2] + 16 * (u // 2 + 1))

        def ds_av_w(eng, u):
            eng.wait_ge(dSv[u % 2].h, dsv0[u % 2] + 16 * (u // 2 + 1))

        def a_exp(u, g):
            base = act0 + 4 + 36 * u
            off = 1 + g if (g < 16 or u == 63) else 5 + g
            return base + off

        def a_sqn(u):
            return act0 + 1 if u == 0 else act0 + 4 + 36 * (u - 1) + 17

        def a_sqr(u):
            return act0 + 2 if u == 0 else act0 + 4 + 36 * (u - 1) + 18

        def a_ln(u):
            return act0 + 3 if u == 0 else act0 + 4 + 36 * (u - 1) + 19

        def a_bexp(u):
            return act0 + 4 if u == 0 else act0 + 4 + 36 * (u - 1) + 20

        def d_base(u):
            return dve0 + 1 + 70 * u

        def d_accd(u, g):
            off = [3, 7, 9, 11, 13][g] if g < 5 else (2 * g + 7 if g < 10 else 2 * g + 8)
            return d_base(u) + off

        def d_qne(u):
            return dve0 + 1 if u == 0 else d_base(u - 1) + 26

        def d_cast(u):
            return d_base(u + 1) + 5 if u < 63 else dve0 + 4484

        def d_rbsb(u):
            return d_base(u + 1) + 14 if u < 63 else dve0 + 4485

        def d_avob(u):
            return d_base(u + 1) + 15 if u < 63 else dve0 + 4486

        @blk3.tensor
        def _(t):
            t.wait_ge(cDVE.h, M["knm2"])
            t.wait_ge(dP.h, M["d_wqbn"])
            t.wait_ge(cDVE.h, M["negrow"])

            def emit_qb(u):
                if u % 4 == 0:
                    t.wait_ge(dSq.h, ds_qcb(u // 4))
                if u >= 1:
                    t.wait_ge(cDVE.h, d_qne(u - 1))
                    t.wait_ge(cACT.h, a_sqn(u - 1))
                hh = u % 4
                for c_ in range(12):
                    ins = t.matmul(psp, wqbnb[:, 512 * c_ + 128 * hh:512 * c_ + 128 * (hh + 1)],
                                   qcb[:, 512 * c_:512 * (c_ + 1)],
                                   start=(c_ == 0), stop=(c_ == 11))
                cPE.tick(ins)
                M[("3qb", u)] = cPE.v

            def emit_bm(u):
                t.wait_ge(cACT.h, a_sqr(u))
                if u >= 2:
                    t.wait_ge(cDVE.h, d_rbsb(u - 2))
                t.matmul(pms[0:1, 0:512], c_ones_bf, sscr, start=True, stop=False)
                ins = t.matmul(pms[0:1, 0:512], c_ones_bf[0:64, :], sscr2[0:64, :],
                               start=False, stop=True)
                cPE.tick(ins)
                M[("3bm", u)] = cPE.v

            emit_qb(0)
            emit_bm(0)
            for u in range(64):
                hh = u % 4
                def emit_av(gg):
                    t.wait_ge(cACT.h, a_exp(u, gg))
                    if gg == 0 and u >= 2:
                        t.wait_ge(cDVE.h, d_avob(u - 2))
                    for cc in (2 * gg, 2 * gg + 1):
                        ins = t.matmul(pav[u % 2],
                                       v4[:, 512 * cc + 128 * hh:512 * cc + 128 * (hh + 1)],
                                       pbuf[gg % 2][:, (cc % 2) * 512:(cc % 2) * 512 + 512],
                                       start=(cc == 0), stop=(cc == 63))
                    cPE.tick(ins)
                    M[("3av", u, gg)] = cPE.v

                for g_ in range(NG):
                    par = g_ % 2
                    if g_ == 0:
                        t.wait_ge(cDVE.h, d_qne(u))
                        t.wait_ge(cACT.h, a_bexp(u) if u == 0 else a_exp(u - 1, 30))
                    elif g_ == 1:
                        if u >= 1:
                            t.wait_ge(cACT.h, a_exp(u - 1, 31))
                    else:
                        t.wait_ge(cACT.h, a_exp(u, g_ - 2))
                    for cc in (2 * g_, 2 * g_ + 1):
                        col = (cc % 2) * 512
                        pt = pg[par][:, col:col + 512]
                        t.matmul(pt, kn4[:, S * hh + 128 * cc:S * hh + 128 * (cc + 1)],
                                 qnb[u % 2], start=True, stop=False)
                        ins = t.matmul(pt, roped[0:65, 128 * cc:128 * (cc + 1)],
                                       qra[u % 2][0:65, :], start=False, stop=True)
                    cPE.tick(ins)
                    M[("3sc", u, g_)] = cPE.v
                    if g_ == 2 and u >= 1:
                        t.wait_ge(cDVE.h, d_base(u - 1) + 70)
                        cPE.tick(t.matmul(pms[0:1, 0:512], c_ones_f, accd[:, :],
                                          start=True, stop=True))
                        M[("3dn", u - 1)] = cPE.v
                    if g_ >= 2:
                        emit_av(g_ - 2)
                    if g_ == 6 and u >= 1:
                        t.wait_ge(cDVE.h, d_cast(u - 1))
                        cPE.tick(t.matmul(pms, c_ones1r, row_r[0:1, 0:512],
                                          start=True, stop=True))
                        M[("3rb", u - 1)] = cPE.v
                    if g_ == 8 and u <= 62:
                        emit_qb(u + 1)
                    if g_ == 16 and u <= 62:
                        emit_bm(u + 1)
                emit_av(NG - 2)
                emit_av(NG - 1)
            t.wait_ge(cDVE.h, d_base(63) + 70)
            cPE.tick(t.matmul(pms[0:1, 0:512], c_ones_f, accd[:, :], start=True, stop=True))
            M[("3dn", 63)] = cPE.v
            t.wait_ge(cDVE.h, d_cast(63))
            cPE.tick(t.matmul(pms, c_ones1r, row_r[0:1, 0:512], start=True, stop=True))
            M[("3rb", 63)] = cPE.v
        M["3_pe"] = cPE.v

        @blk3.scalar
        def _(s):
            def emit_head(u):
                s.wait_ge(cPE.h, M[("3qb", u)])
                s.wait_ge(cDVE.h, d_qne(u))
                cACT.tick(s.activation(sscr, psp, AF.Square))
                ds_qra_w(s, u)
                cACT.tick(s.activation(sscr2[0:64, :], qra[u % 2][0:64, :], AF.Square))
                s.wait_ge(cPE.h, M[("3bm", u)])
                cACT.tick(s.activation(q2row[0:1, :], pms[0:1, 0:512], AF.Ln))
                s.wait_ge(cACT.h, cACT.v)
                if u >= 2:
                    s.wait_ge(cPE.h, M[("3sc", u - 2, NG - 1)])
                cACT.tick(s.activation(qra[u % 2][64:65, :], q2row[0:1, :], AF.Exp,
                                       scale=0.5, bias=c_knmax[0:1, (u % 4):(u % 4) + 1]))

            emit_head(0)
            for u in range(64):
                for g_ in range(NG):
                    s.wait_ge(cPE.h, M[("3sc", u, g_)])
                    if g_ >= 2:
                        s.wait_ge(cPE.h, M[("3av", u, g_ - 2)])
                        s.wait_ge(cDVE.h, d_accd(u, g_ - 2))
                    elif u >= 1:
                        s.wait_ge(cDVE.h, d_accd(u - 1, 30 + g_))
                    cACT.tick(s.activation(pbuf[g_ % 2][:, :], pg[g_ % 2][:, :], AF.Exp))
                    if g_ == 15 and u <= 62:
                        emit_head(u + 1)
        assert KSTOP < 30 or (cACT.v == act0 + 4 + 36 * 63 + 32)

        @blk3.vector
        def _(v):
            v.wait_ge(cPE.h, M[("3qb", 0)])
            v.wait_ge(cDVE.h, cDVE.v)
            cDVE.tick(v.tensor_copy(qnb[0], psp))

            def acc_pair(u, g_):
                v.wait_ge(cACT.h, a_exp(u, g_))
                if g_ == 0:
                    cDVE.tick(v.tensor_copy(accd[:, :], pbuf[0][:, 0:512]))
                    v.wait_ge(cDVE.h, cDVE.v)
                    cDVE.tick(v.tensor_tensor(accd[:, :], accd[:, :],
                                              pbuf[0][:, 512:1024], ALU.add))
                else:
                    cDVE.tick(v.tensor_tensor(accd[:, :], accd[:, :],
                                              pbuf[g_ % 2][:, 0:512], ALU.add))
                    v.wait_ge(cDVE.h, cDVE.v)
                    cDVE.tick(v.tensor_tensor(accd[:, :], accd[:, :],
                                              pbuf[g_ % 2][:, 512:1024], ALU.add))
                v.wait_ge(cDVE.h, cDVE.v)

            def tail_a(u):
                if u < 0:
                    cDVE.tick(v.nop())
                    return
                v.wait_ge(cPE.h, M[("3dn", u)])
                cDVE.tick(v.tensor_scalar(row_t[0:1, 0:512], pms[0:1, 0:512],
                                          1e-30, None, ALU.add))
                v.wait_ge(cDVE.h, cDVE.v)

            def tail_b(u):
                if u < 0:
                    cDVE.tick(v.nop())
                    cDVE.tick(v.nop())
                    return
                cDVE.tick(v.reciprocal(row_q[0:1, 0:512], row_t[0:1, 0:512]))
                v.wait_ge(cDVE.h, cDVE.v)
                with lp():
                    cDVE.tick(v.tensor_copy(row_r[0:1, 0:512], row_q[0:1, 0:512]))
                v.wait_ge(cDVE.h, cDVE.v)

            def tail_c(u):
                if u < 0:
                    cDVE.tick(v.nop())
                    cDVE.tick(v.nop())
                    return
                v.wait_ge(cPE.h, M[("3rb", u)])
                cDVE.tick(v.tensor_copy(rbsb, pms))
                v.wait_ge(cDVE.h, cDVE.v)
                if u >= 2:
                    ds_av_w(v, u - 2)
                cDVE.tick(v.tensor_tensor(avob[u % 2][:, :], pav[u % 2], rbsb, ALU.mult))
                v.wait_ge(cDVE.h, cDVE.v)

            for u in range(64):
                tail_a(u - 1)
                acc_pair(u, 0)
                tail_b(u - 1)
                for g_ in range(1, 5):
                    acc_pair(u, g_)
                tail_c(u - 1)
                for g_ in range(5, 10):
                    acc_pair(u, g_)
                if u <= 62:
                    v.wait_ge(cPE.h, M[("3qb", u + 1)])
                    if u >= 1:
                        v.wait_ge(cPE.h, M[("3sc", u - 1, NG - 1)])
                    cDVE.tick(v.tensor_copy(qnb[(u + 1) % 2], psp))
                else:
                    cDVE.tick(v.nop())
                v.wait_ge(cDVE.h, cDVE.v)
                for g_ in range(10, NG):
                    acc_pair(u, g_)
            tail_a(63)
            tail_b(63)
            tail_c(63)
        assert KSTOP < 30 or (cDVE.v == dve0 + 4486)

        @blk3.sync
        def _(sy):
            sy.wait_ge(cCC.h, M["ag1"])
            sy.wait_ge(dS.h, M["d_roped"])
            sy.wait_ge(cPE.h, M["2_pe"])

            def avob_out(u):
                sy.wait_ge(cDVE.h, d_avob(u))
                sy.dma_start(out=ag2i[128 * (u % 4):128 * (u % 4 + 1),
                                      512 * (u // 4):512 * (u // 4 + 1)],
                             in_=avob[u % 2][:, :]).then_inc(dSv[u % 2].h, 16)
                dSv[u % 2].v += 16
                assert dSv[u % 2].v == dsv0[u % 2] + 16 * (u // 2 + 1)

            for u in range(64):
                b, hh = u // 4, u % 4
                rr, co = b // 2, (b % 2) * 512
                if hh == 0:
                    if b >= 1:
                        sy.wait_ge(cPE.h, M[("3qb", 4 * (b - 1) + 3)])
                    for m in range(12):
                        sy.dma_start(
                            out=qcb[:, 512 * m:512 * (m + 1)],
                            in_=ag1o[AG1R * rr + QC_OFF + 128 * m:
                                     AG1R * rr + QC_OFF + 128 * (m + 1),
                                     co:co + 512]).then_inc(dSq.h, 16)
                        dSq.v += 16
                if u >= 2:
                    sy.wait_ge(cACT.h, a_sqr(u - 2))
                    sy.wait_ge(cPE.h, M[("3sc", u - 2, NG - 1)])
                sy.dma_start(out=qra[u % 2][0:64, :],
                             in_=qrl[HPC * RD * rr + RD * hh:HPC * RD * rr + RD * (hh + 1),
                                     co:co + 512]).then_inc(dSr[u % 2].h, 16)
                dSr[u % 2].v += 16
                assert dSr[u % 2].v == dsr0[u % 2] + 16 * (u // 2 + 1)
                if u >= 2:
                    avob_out(u - 2)
            avob_out(62)
            avob_out(63)
        M["3_out"] = (dSq.v, dSv[0].v, dSv[1].v)

        @blk4.gpsimd
        def _(g):
            g.wait_ge(dSq.h, M["3_out"][0])
            g.wait_ge(dSv[0].h, M["3_out"][1])
            g.wait_ge(dSv[1].h, M["3_out"][2])
            ins = g.collective_compute("AllGather", ALU.bypass, replica_groups=RG,
                                       ins=[ag2i.opt()], outs=[ag2o.opt()])
            cCC.tick(ins)
        M["ag2"] = cCC.v

        if DEBUG:
            @blk4.sync
            def _(sy):
                sy.wait_ge(cCC.h, M["ag2"])
                for hh in range(HPC):
                    sy.dma_start(out=dbg["av"][128 * hh:128 * (hh + 1), :],
                                 in_=ag2i[128 * hh:128 * (hh + 1), :]).then_inc(dS.h, 16)
                    dS.v += 16

        for s_ in range(2):
            pe0, act0, dve0, ds0, dp0 = cPE.v, cACT.v, cDVE.v, dS.v, dP.v
            dpa0 = [dPs[0].v, dPs[1].v]
            dsx0 = [dSx[0].v, dSx[1].v]
            DVE1 = dve0

            @blk4.sync
            def _(sy, s_=s_, ds0=ds0, dve0=dve0, dSx=dSx, dsx0=dsx0):
                sy.wait_ge(cCC.h, M["ag2"])
                if s_ == 1:
                    sy.wait_ge(cPE.h, M[("4wo_last", 0)])
                    sy.wait_ge(cDVE.h, M[("4ln_n", 0, 3)])
                    sy.wait_ge(dSv[0].h, dSv[0].v)
                    sy.wait_ge(dSv[1].h, dSv[1].v)
                pid_sp = pid_holder["sp"]
                v3 = ag2o.rearrange("(a p) n -> p a n", p=128)
                sy.dma_start(out=avsl[:, :].rearrange("p (t n) -> p t n", n=512),
                             in_=v3[:, :, bass.ts(pid_sp * 2 + s_, 512)]).then_inc(dS.h, 16)
                dS.v += 16
                for m in range(32):
                    if m >= 2:
                        sy.wait_ge(cDVE.h, dve0 + 2 * (m - 2) + 1)
                    sy.dma_start(out=xres[m % 2][:, :],
                                 in_=xt[128 * m:128 * (m + 1), 512 * s_:512 * s_ + 512]
                                 ).then_inc(dSx[m % 2].h, 16)
                    dSx[m % 2].v += 16
            assert KSTOP < 40 or (dS.v == ds0 + 16)

            @blk4.tensor
            def _(t, s_=s_, act0=act0, dve0=dve0, ds0=ds0, dp0=dp0, dpa0=dpa0):
                t.wait_ge(dS.h, ds0 + 16)
                t.wait_ge(cDVE.h, dve0)
                t.wait_ge(cACT.h, act0)
                for m in range(32):
                    t.wait_ge(dPs[m % 2].h, dpa0[m % 2] + 16 * (m // 2 + 1))
                    if m >= 2:
                        t.wait_ge(cDVE.h, dve0 + 2 * (m - 2) + 2)
                    pt = pg0[:, 0:512] if m % 2 == 0 else pg0[:, 512:1024]
                    for c_ in range(32):
                        ins = t.matmul(pt, wob[m % 2][:, 128 * c_:128 * (c_ + 1)],
                                       avsl[:, 512 * c_:512 * (c_ + 1)],
                                       start=(c_ == 0), stop=(c_ == 31))
                    cPE.tick(ins)
                    M[("4wo", s_, m)] = cPE.v
                    if m >= 1:
                        mm_ = m - 1
                        t.wait_ge(cDVE.h, dve0 + 2 * mm_ + 2)
                        t.wait_ge(cACT.h, act0 + mm_ + 1)
                        sq_ = sqzb if mm_ % 2 == 0 else sqzb2
                        t.matmul(pav[0][0:1, 0:512], c_ones_bf, zbf[:, 512 * mm_:512 * (mm_ + 1)],
                                 start=(mm_ == 0), stop=False)
                        ins = t.matmul(pav[1][0:1, 0:512], c_ones_bf, sq_,
                                       start=(mm_ == 0), stop=False)
                        cPE.tick(ins)
                        M[("4sq", s_, mm_)] = cPE.v
                M[("4wo_last", s_)] = cPE.v
                mm_ = 31
                t.wait_ge(cDVE.h, dve0 + 2 * mm_ + 2)
                t.wait_ge(cACT.h, act0 + mm_ + 1)
                sq_ = sqzb if mm_ % 2 == 0 else sqzb2
                t.matmul(pav[0][0:1, 0:512], c_ones_bf, zbf[:, 512 * mm_:512 * (mm_ + 1)],
                         start=False, stop=True)
                ins = t.matmul(pav[1][0:1, 0:512], c_ones_bf, sqzb2, start=False, stop=True)
                cPE.tick(ins)
                M[("4sq", s_, 31)] = cPE.v
                M[("4st", s_)] = cPE.v
                t.wait_ge(cDVE.h, dve0 + 75)
                t.matmul(pms, c_ones1r, row_r[0:1, 0:512], start=True, stop=True)
                t.matmul(psp, c_ones1r, row_r[0:1, 512:1024], start=True, stop=True)
                ins = t.matmul(pg1[:, 512:1024], c_ones1r, row_r2[0:1, 0:512], start=True, stop=True)
                cPE.tick(ins)
                M[("4bc", s_)] = cPE.v
                t.wait_ge(dP.h, dp0 + 16)
                t.wait_ge(cDVE.h, dve0 + 78)
                for c_ in range(32):
                    t.matmul(pg1[0:64, 0:512], wrtb[:, 64 * c_:64 * (c_ + 1)],
                             zh[:, 512 * c_:512 * (c_ + 1)],
                             start=(c_ == 0), stop=(c_ == 31))
                    ins = t.matmul(pav[0][0:64, 0:1], wrtb[:, 64 * c_:64 * (c_ + 1)], c_ones_f,
                                   start=(c_ == 0), stop=(c_ == 31))
                cPE.tick(ins)
                M[("4rt", s_)] = cPE.v
                for m in range(8):
                    t.wait_ge(dPs[m % 2].h, dpa0[m % 2] + 16 * (16 + m // 2 + 1))
                    if m >= 2:
                        t.wait_ge(cDVE.h, dve0 + 84 + 3 * (m - 2))
                    if m >= 1:
                        t.wait_ge(cDVE.h, dve0 + 82 + 3 * (m - 1))
                    pt = pg0[:, 0:512] if m % 2 == 0 else pg0[:, 512:1024]
                    for c_ in range(32):
                        t.matmul(pt, atb[m % 2][:, 128 * c_:128 * (c_ + 1)],
                                 zbf[:, 512 * c_:512 * (c_ + 1)],
                                 start=(c_ == 0), stop=(c_ == 31))
                        ins = t.matmul(pav[1][:, m:m + 1],
                                       atb[m % 2][:, 128 * c_:128 * (c_ + 1)], c_ones_bf,
                                       start=(c_ == 0), stop=(c_ == 31))
                    cPE.tick(ins)
                    M[("4h", s_, m)] = cPE.v

            @blk4.gpsimd
            def _(g, s_=s_, dpa0=dpa0):
                for m in range(32):
                    if m >= 2:
                        g.wait_ge(cPE.h, M[("4wo", s_, m - 2)])
                    elif s_ == 1:
                        g.wait_ge(cPE.h, M[("4B", 0, 31)])
                    g.dma_start(out=wob[m % 2][:, :],
                                in_=wo[:, 128 * m:128 * (m + 1)].rearrange("(t p) m -> p t m", p=128)
                                ).then_inc(dPs[m % 2].h, 16)
                    dPs[m % 2].v += 16
                    assert dPs[m % 2].v == dpa0[m % 2] + 16 * (m // 2 + 1)
                g.dma_start(out=wrtb[:, :],
                            in_=wrt.rearrange("(t p) m -> p t m", p=128)).then_inc(dP.h, 16)
                dP.v += 16
                for m in range(8):
                    if m >= 2:
                        g.wait_ge(cPE.h, M[("4h", s_, m - 2)])
                    else:
                        g.wait_ge(cPE.h, M[("4wo", s_, 31)])
                    g.dma_start(out=atb[m % 2][:, :],
                                in_=at[:, 128 * m:128 * (m + 1)].rearrange("(t p) m -> p t m", p=128)
                                ).then_inc(dPs[m % 2].h, 16)
                    dPs[m % 2].v += 16
                    assert dPs[m % 2].v == dpa0[m % 2] + 16 * (16 + m // 2 + 1)
            @blk4.scalar
            def _(s, s_=s_, dve0=dve0):
                for m in range(32):
                    s.wait_ge(cDVE.h, dve0 + 2 * m + 2)
                    if m >= 2:
                        s.wait_ge(cPE.h, M[("4sq", s_, m - 2)])
                    sq_ = sqzb if m % 2 == 0 else sqzb2
                    cACT.tick(s.activation(sq_, zbf[:, 512 * m:512 * (m + 1)], AF.Square))
                s.wait_ge(cDVE.h, dve0 + 71)
                cACT.tick(s.activation(row_t[0:1, 0:512], row_k[0:1, 0:512], AF.Sqrt))
            M[("4act1", s_)] = cACT.v

            @blk4.vector
            def _(v, s_=s_, act0=act0, dve0=dve0, dsx0=dsx0,
                  dsv_tot=(dSv[0].v, dSv[1].v)):
                if s_ == 1:
                    v.wait_ge(dSv[0].h, dsv_tot[0])
                    v.wait_ge(dSv[1].h, dsv_tot[1])
                for m in range(32):
                    v.wait_ge(cPE.h, M[("4wo", s_, m)])
                    v.wait_ge(cDVE.h, cDVE.v)
                    v.wait_ge(dSx[m % 2].h, dsx0[m % 2] + 16 * (m // 2 + 1))
                    pt = pg0[:, 0:512] if m % 2 == 0 else pg0[:, 512:1024]
                    cDVE.tick(v.tensor_tensor(zh[:, 512 * m:512 * (m + 1)], pt,
                                              xres[m % 2][:, :], ALU.add))
                    v.wait_ge(cDVE.h, cDVE.v)
                    cDVE.tick(v.tensor_copy(zbf[:, 512 * m:512 * (m + 1)],
                                            zh[:, 512 * m:512 * (m + 1)]))
                v.wait_ge(cPE.h, M[("4st", s_)])
                cDVE.tick(v.tensor_scalar(row_q[0:1, 512:1024], pav[0][0:1, 0:512],
                                          -1.0 / D, None, ALU.mult))
                v.wait_ge(cDVE.h, cDVE.v)
                with lp():
                    cDVE.tick(v.tensor_copy(row_r[0:1, 512:1024], row_q[0:1, 512:1024]))
                cDVE.tick(v.tensor_scalar(row_t[0:1, 0:512], pav[0][0:1, 0:512],
                                          1.0 / D, None, ALU.mult))
                cDVE.tick(v.tensor_scalar(row_k[0:1, 0:512], pav[1][0:1, 0:512],
                                          1.0 / D, None, ALU.mult))
                v.wait_ge(cDVE.h, cDVE.v)
                cDVE.tick(v.tensor_mul(row_t[0:1, 512:1024], row_t[0:1, 0:512],
                                       row_t[0:1, 0:512]))
                v.wait_ge(cDVE.h, cDVE.v)
                cDVE.tick(v.tensor_sub(row_k[0:1, 0:512], row_k[0:1, 0:512],
                                       row_t[0:1, 512:1024]))
                v.wait_ge(cDVE.h, cDVE.v)
                cDVE.tick(v.tensor_scalar(row_k[0:1, 0:512], row_k[0:1, 0:512],
                                          EPS_LN, None, ALU.add))
                v.wait_ge(cACT.h, M[("4act1", s_)])
                cDVE.tick(v.reciprocal(row_q[0:1, 0:512], row_t[0:1, 0:512]))
                v.wait_ge(cDVE.h, cDVE.v)
                with lp():
                    cDVE.tick(v.tensor_copy(row_r[0:1, 0:512], row_q[0:1, 0:512]))
                cDVE.tick(v.tensor_scalar(row_k[0:1, 512:1024], row_t[0:1, 0:512],
                                          2.0, None, ALU.mult))
                v.wait_ge(cDVE.h, cDVE.v)
                with lp():
                    cDVE.tick(v.tensor_copy(row_r2[0:1, 0:512], row_k[0:1, 512:1024]))
                v.wait_ge(cPE.h, M[("4bc", s_)])
                cDVE.tick(v.tensor_copy(rstdb[:, :], pms))
                cDVE.tick(v.tensor_copy(negmub[:, :], psp))
                cDVE.tick(v.tensor_copy(scl2b[:, :], pg1[:, 512:1024]))
                v.wait_ge(cPE.h, M[("4rt", s_)])
                cDVE.tick(v.tensor_copy(cwsb[0:64, 0:1], pav[0][0:64, 0:1]))
                v.wait_ge(cDVE.h, cDVE.v)
                cDVE.tick(v.scalar_tensor_tensor(hscr[0:64, :], negmub[0:64, :],
                                                 cwsb[0:64, 0:1], pg1[0:64, 0:512],
                                                 ALU.mult, ALU.add))
                v.wait_ge(cDVE.h, cDVE.v)
                cDVE.tick(v.tensor_mul(rrsb[0:64, :], hscr[0:64, :], rstdb[0:64, :]))
                for m in range(8):
                    v.wait_ge(cPE.h, M[("4h", s_, m)])
                    v.wait_ge(cDVE.h, cDVE.v)
                    cDVE.tick(v.tensor_copy(casb[:, m:m + 1], pav[1][:, m:m + 1]))
                    v.wait_ge(cDVE.h, cDVE.v)
                    pt = pg0[:, 0:512] if m % 2 == 0 else pg0[:, 512:1024]
                    cDVE.tick(v.scalar_tensor_tensor(hscr[:, :], negmub[:, :],
                                                     casb[:, m:m + 1], pt,
                                                     ALU.mult, ALU.add))
                    v.wait_ge(cDVE.h, cDVE.v)
                    cDVE.tick(v.tensor_mul(hbf[:, 512 * m:512 * (m + 1)], hscr[:, :],
                                           rstdb[:, :]))
            assert KSTOP < 40 or (cDVE.v == dve0 + 105)
            M[("4dve1", s_)] = cDVE.v

            @blk4.tensor
            def _(t, s_=s_):
                t.wait_ge(cDVE.h, M[("4dve1", s_)] - 24)
                for j in range(4):
                    ins = t.transpose(pms[:, 64 * j:64 * (j + 1)],
                                      rrsb[0:64, 128 * j:128 * (j + 1)], c_ident[0:64, 0:64])
                    cPE.tick(ins)
                M[("4tr", s_)] = cPE.v

            @blk4.vector
            def _(v, s_=s_):
                v.wait_ge(cPE.h, M[("4tr", s_)])
                v.wait_ge(cDVE.h, cDVE.v)
                for j in range(4):
                    cDVE.tick(v.tensor_copy(rsrow[:, 64 * j:64 * (j + 1)], pms[:, 64 * j:64 * (j + 1)]))
                for j in range(4):
                    sl = slice(64 * j, 64 * j + 64)
                    v.wait_ge(cDVE.h, cDVE.v)
                    cDVE.tick(v.max(out=mx16[:, 0:8], in_=rsrow[:, sl]))
                    v.wait_ge(cDVE.h, cDVE.v)
                    cDVE.tick(v.match_replace(out=zap[:, sl], in_to_replace=mx16[:, 0:8],
                                              in_values=rsrow[:, sl], imm_value=-3e38))
                    v.wait_ge(cDVE.h, cDVE.v)
                    cDVE.tick(v.max(out=mx16[:, 8:16], in_=zap[:, sl]))
                    v.wait_ge(cDVE.h, cDVE.v)
                    cDVE.tick(v.match_replace(out=zap[:, sl], in_to_replace=mx16[:, 8:16],
                                              in_values=zap[:, sl], imm_value=-3e38))
                    v.wait_ge(cDVE.h, cDVE.v)
                    cDVE.tick(v.tensor_scalar(colr[:, j:j + 1], mx16[:, 0:1], -1.0, None, ALU.mult))
                    v.wait_ge(cDVE.h, cDVE.v)
                    M[("4tk", s_, j)] = cDVE.v
            M[("4tk_all", s_)] = cDVE.v

            @blk4.scalar
            def _(s, s_=s_):
                for j in range(4):
                    sl = slice(64 * j, 64 * j + 64)
                    s.wait_ge(cDVE.h, M[("4tk", s_, j)])
                    cACT.tick(s.activation(gvals[:, sl], rsrow[:, sl], AF.Exp,
                                           bias=colr[:, j:j + 1]))
                    M[("4ex", s_, j)] = cACT.v

            @blk4.vector
            def _(v, s_=s_):
                for j in range(4):
                    sl = slice(64 * j, 64 * j + 64)
                    v.wait_ge(cACT.h, M[("4ex", s_, j)])
                    v.wait_ge(cDVE.h, cDVE.v)
                    cDVE.tick(v.tensor_scalar(rsrow[:, sl], zap[:, sl], -1e37, None, ALU.is_le))
                    v.wait_ge(cDVE.h, cDVE.v)
                    cDVE.tick(v.tensor_mul(gvals[:, sl], gvals[:, sl], rsrow[:, sl]))
                    v.wait_ge(cDVE.h, cDVE.v)
                    cDVE.tick(v.reduce_sum(colr[:, 4:5], gvals[:, sl], axis=AX.X))
                    v.wait_ge(cDVE.h, cDVE.v)
                    cDVE.tick(v.reciprocal(colr[:, 5:6], colr[:, 4:5]))
                    v.wait_ge(cDVE.h, cDVE.v)
                    cDVE.tick(v.tensor_scalar(gvals[:, sl], gvals[:, sl], colr[:, 5:6],
                                              None, ALU.mult))
                    v.wait_ge(cDVE.h, cDVE.v)
            M[("4g", s_)] = cDVE.v

            @blk4.tensor
            def _(t, s_=s_):
                t.wait_ge(cDVE.h, M[("4g", s_)])
                for j in range(4):
                    ins = t.transpose(psp[0:64, 128 * j:128 * (j + 1)],
                                      gvals[:, 64 * j:64 * (j + 1)], c_ident[:, :])
                    cPE.tick(ins)
                M[("4gt", s_)] = cPE.v

            @blk4.vector
            def _(v, s_=s_):
                v.wait_ge(cPE.h, M[("4gt", s_)])
                v.wait_ge(cDVE.h, cDVE.v)
                for j in range(4):
                    ins = v.tensor_mul(gtb[0:64, 128 * j:128 * (j + 1)],
                                       psp[0:64, 128 * j:128 * (j + 1)],
                                       scl2b[0:64, 128 * j:128 * (j + 1)])
                    cDVE.tick(ins)
            M[("4gtb", s_)] = cDVE.v

            @blk4.tensor
            def _(t, s_=s_):
                t.wait_ge(cDVE.h, M[("4gtb", s_)])
                for j8 in range(8):
                    if j8 >= 2:
                        t.wait_ge(cDVE.h, M[("4gtb", s_)] + (j8 - 2) + 1)
                    pt = pg1[:, 0:512] if j8 % 2 == 0 else pg1[:, 512:1024]
                    ins = t.matmul(pt, c_emat[0:64, 128 * j8:128 * (j8 + 1)],
                                   gtb[0:64, 0:512], start=True, stop=True)
                    cPE.tick(ins)
                    M[("4ge", s_, j8)] = cPE.v

            @blk4.vector
            def _(v, s_=s_):
                for j8 in range(8):
                    v.wait_ge(cPE.h, M[("4ge", s_, j8)])
                    v.wait_ge(cDVE.h, cDVE.v)
                    pt = pg1[:, 0:512] if j8 % 2 == 0 else pg1[:, 512:1024]
                    ins = v.tensor_tensor(hgbf[:, 512 * j8:512 * (j8 + 1)],
                                          hbf[:, 512 * j8:512 * (j8 + 1)], pt, ALU.mult)
                    cDVE.tick(ins)
                    M[("4hg", s_, j8)] = cDVE.v
                    assert M[("4hg", s_, j8)] == M[("4gtb", s_)] + j8 + 1
            M[("4hg_all", s_)] = cDVE.v

            dve_B0 = cDVE.v

            @blk4.tensor
            def _(t, s_=s_, dpa0=dpa0, dve0=dve0, dve_B0=dve_B0):
                t.wait_ge(cDVE.h, M[("4hg_all", s_)])
                for m in range(32):
                    t.wait_ge(dPs[m % 2].h, dpa0[m % 2] + 16 * (20 + m // 2 + 1))
                    if m >= 2:
                        t.wait_ge(cDVE.h, dve_B0 + (m - 2) + 1)
                    pt = pg0[:, 0:512] if m % 2 == 0 else pg0[:, 512:1024]
                    for kr in range(8):
                        ins = t.matmul(pt, btb[m % 2][:, 128 * kr:128 * (kr + 1)],
                                       hgbf[:, 512 * kr:512 * (kr + 1)],
                                       start=(kr == 0), stop=(kr == 7))
                    cPE.tick(ins)
                    M[("4B", s_, m)] = cPE.v

            @blk4.gpsimd
            def _(g, s_=s_, dpa0=dpa0):
                for m in range(32):
                    if m >= 2:
                        g.wait_ge(cPE.h, M[("4B", s_, m - 2)])
                    else:
                        g.wait_ge(cPE.h, M[("4h", s_, 7)])
                    g.dma_start(out=btb[m % 2][:, :],
                                in_=bt[:, 128 * m:128 * (m + 1)].rearrange("(t p) m -> p t m", p=128)
                                ).then_inc(dPs[m % 2].h, 16)
                    dPs[m % 2].v += 16
                    assert dPs[m % 2].v == dpa0[m % 2] + 16 * (20 + m // 2 + 1)

            @blk4.vector
            def _(v, s_=s_, dve_B0=dve_B0):
                for m in range(32):
                    v.wait_ge(cPE.h, M[("4B", s_, m)])
                    v.wait_ge(cDVE.h, cDVE.v)
                    pt = pg0[:, 0:512] if m % 2 == 0 else pg0[:, 512:1024]
                    ins = v.tensor_tensor(zh[:, 512 * m:512 * (m + 1)], pt,
                                          zh[:, 512 * m:512 * (m + 1)], ALU.add)
                    cDVE.tick(ins)
                    M[("4v", s_, m)] = cDVE.v
                    assert cDVE.v == dve_B0 + m + 1
            M[("4v_all", s_)] = cDVE.v

            if DEBUG and s_ == 0:
                @blk4.sync
                def _(sy):
                    sy.wait_ge(cDVE.h, M[("4v_all", 0)])
                    sy.dma_start(out=dbg["z"], in_=zh).then_inc(dS.h, 16)
                    sy.dma_start(out=dbg["gg"], in_=gvals).then_inc(dS.h, 16)
                    sy.dma_start(out=dbg["hg"], in_=hgbf).then_inc(dS.h, 16)
                    dS.v += 48

            for q in range(4):
                dve_q0 = cDVE.v

                @blk4.tensor
                def _(t, s_=s_, q=q, dve_q0=dve_q0):
                    t.wait_ge(cDVE.h, M[("4v_all", s_)])
                    if q >= 1:
                        t.wait_ge(cDVE.h, M[("4ln_e", s_, q - 1, 7)])
                    if q >= 2:
                        t.wait_ge(cDVE.h, M[("4ln_n", s_, q - 2)])
                    for c_ in range(32):
                        grp = c_ // 4
                        pt = pms if grp % 2 == 0 else psp
                        if c_ % 4 == 0 and c_ >= 8:
                            t.wait_ge(cDVE.h, dve_q0 + (grp - 2) + 1)
                        ins = t.transpose(pt[:, 128 * (c_ % 4):128 * (c_ % 4 + 1)],
                                          zh[:, 512 * c_ + 128 * q:512 * c_ + 128 * q + 128],
                                          c_ident[:, :])
                        cPE.tick(ins)
                        M[("4ln_t", s_, q, c_)] = cPE.v

                @blk4.vector
                def _(v, s_=s_, q=q, dve_q0=dve_q0):
                    for grp in range(8):
                        v.wait_ge(cPE.h, M[("4ln_t", s_, q, 4 * grp + 3)])
                        v.wait_ge(cDVE.h, cDVE.v)
                        pt = pms if grp % 2 == 0 else psp
                        ins = v.tensor_copy(vTq[q % 2][:, 512 * grp:512 * (grp + 1)], pt[:, 0:512])
                        cDVE.tick(ins)
                        M[("4ln_e", s_, q, grp)] = cDVE.v
                        assert cDVE.v == dve_q0 + grp + 1
                    v.wait_ge(cDVE.h, cDVE.v)
                    for jj in range(8):
                        ins = v.bn_stats(bnst[:, 6 * jj:6 * jj + 6],
                                         vTq[q % 2][:, 512 * jj:512 * (jj + 1)])
                    cDVE.tick(ins)
                    v.wait_ge(cDVE.h, cDVE.v)
                    cDVE.tick(v.bn_aggr(colr[:, 4:6], bnst[:, 0:48]))
                    v.wait_ge(cDVE.h, cDVE.v)
                    cDVE.tick(v.tensor_scalar(colr[:, 6:7], colr[:, 5:6], EPS_LN, None, ALU.add))
                    v.wait_ge(cDVE.h, cDVE.v)
                    cDVE.tick(v.tensor_scalar(colr[:, 4:5], colr[:, 4:5], -1.0, None, ALU.mult))
                M[("4bn", s_, q)] = cDVE.v

                @blk4.scalar
                def _(s, s_=s_, q=q):
                    s.wait_ge(cDVE.h, M[("4bn", s_, q)])
                    cACT.tick(s.activation(colr[:, 7:8], colr[:, 6:7], AF.Sqrt))
                M[("4sd2", s_, q)] = cACT.v

                @blk4.vector
                def _(v, s_=s_, q=q, ds0=ds0):
                    v.wait_ge(cACT.h, M[("4sd2", s_, q)])
                    cDVE.tick(v.reciprocal(colr[:, 6:7], colr[:, 7:8]))
                    v.wait_ge(cDVE.h, cDVE.v)
                    if q >= 2:
                        v.wait_ge(M[("4out", s_, q - 2)][0].h, M[("4out", s_, q - 2)][1])
                    ins = v.tensor_scalar(vTo[q % 2][:, :], vTq[q % 2][:, :],
                                          colr[:, 4:5], colr[:, 6:7], ALU.add, ALU.mult)
                    cDVE.tick(ins)
                    M[("4ln_n", s_, q)] = cDVE.v

                @blk4.sync
                def _(sy, s_=s_, q=q):
                    sy.wait_ge(cDVE.h, M[("4ln_n", s_, q)])
                    sy.dma_start(out=out[512 * s_ + 128 * q:512 * s_ + 128 * (q + 1), :],
                                 in_=vTo[q % 2][:, :]).then_inc(dSv[q % 2].h, 16)
                    dSv[q % 2].v += 16
                    M[("4out", s_, q)] = (dSv[q % 2], dSv[q % 2].v)

        @blk.sync
        def _(sy):
            for c in [dS, dP, dSq] + dPs + dSk + dSr + dSv + dSx:
                sy.wait_ge(c.h, c.v)

        return nc
    return nc


def kernel(**inputs):
    per_core = _host_layout(inputs)
    nc = build_nc()
    res = run_bass_kernel_spmd(nc, per_core, core_ids=list(range(NCORES)))
    outs = [res.results[c]["out"] for c in range(NCORES)]
    return np.concatenate(outs, axis=0).reshape(1, S, D)


# version 14
# speedup vs baseline: 1.2581x; 1.0224x over previous
import os
import numpy as np
from contextlib import ExitStack

import concourse.bass as bass
import concourse.mybir as mybir
from concourse.bass_utils import run_bass_kernel_spmd

F32 = mybir.dt.float32
F32R = mybir.dt.float32r
BF16 = mybir.dt.bfloat16
AF = mybir.ActivationFunctionType
ALU = mybir.AluOpType
AX = mybir.AxisListType
ET = mybir.EngineType

S, D, NH = 8192, 4096, 32
QR, KVR, RD, ND, VD = 1536, 512, 64, 128, 128
QK = ND + RD
NCORES, HPC, SQC = 8, 4, 1024
EPS_RMS, EPS_LN = 1e-6, 1e-5
INV_SQRT_QK = float(QK ** (-0.5))

QC_OFF, QROPE_OFF = 0, QR
KC_OFF = QR + NH * RD
KR_OFF = KC_OFF + KVR
AG1R = KR_OFF + RD

DEBUG = False
_KS = os.environ.get("KSTOP", "4")
KSTOP = {"0": 10, "1a": 11, "1b": 12, "1": 13, "1c": 13, "2": 20, "3": 30, "4": 40}[_KS]


class _SkipBlk:
    def __getattr__(self, name):
        return lambda f: None


def _host_layout(inputs):
    x = np.ascontiguousarray(inputs["L0_attn_hidden_states"][0])
    xT = np.ascontiguousarray(x.T)
    w_qa = np.ascontiguousarray(inputs["L0_attn_w_qa"])
    g_qa = np.ascontiguousarray(inputs["L0_attn_g_qa"])
    w_qb = np.ascontiguousarray(inputs["L0_attn_w_qb"])
    w_kva = np.ascontiguousarray(inputs["L0_attn_w_kva"])
    g_kva = np.ascontiguousarray(inputs["L0_attn_g_kva"])
    w_kvb = np.ascontiguousarray(inputs["L0_attn_w_kvb"])
    w_o = np.ascontiguousarray(inputs["L0_attn_w_o"])
    A = np.ascontiguousarray(inputs["L0_ffn_A"])
    B = np.ascontiguousarray(inputs["L0_ffn_B"])
    Wr = np.ascontiguousarray(inputs["L0_ffn_Wr"])

    wq3 = w_qb.reshape(QR, NH, QK)
    wqb_rope = np.ascontiguousarray(wq3[:, :, ND:].reshape(QR, NH * RD))
    wk3 = w_kvb.reshape(KVR, NH, ND + VD)

    A_t = np.ascontiguousarray(A.transpose(2, 0, 1).reshape(D, 1024))
    B_t = np.ascontiguousarray(B.transpose(0, 2, 1).reshape(1024, D))
    WrT = np.ascontiguousarray(Wr.T)
    gqa_t = np.ascontiguousarray(g_qa.reshape(QR // 128, 128).T)
    gkva_t = np.ascontiguousarray(g_kva.reshape(KVR // 128, 128).T)

    inv_freq = 1.0 / (10000.0 ** (np.arange(0, RD, 2, dtype=np.float64) / RD))
    t = np.arange(S, dtype=np.float64)
    freqs = np.outer(t, inv_freq)
    emb = np.concatenate([freqs, freqs], axis=-1)
    cosT = np.ascontiguousarray(np.cos(emb).T.astype(np.float32))
    sinT = np.sin(emb).T.astype(np.float32)
    sinTs = sinT.copy()
    sinTs[: RD // 2] *= -1.0
    sinTs = np.ascontiguousarray(sinTs)

    emat = np.zeros((64, 1024), dtype=np.float32)
    for e_ in range(64):
        emat[e_, 16 * e_:16 * e_ + 16] = 1.0
    ident = np.eye(128, dtype=np.float32)

    per_core = []
    for c in range(NCORES):
        sl = slice(SQC * c, SQC * (c + 1))
        heads = list(range(HPC * c, HPC * (c + 1)))
        per_core.append(dict(
            xt=np.ascontiguousarray(xT[:, sl]),
            wqa=w_qa,
            wqbr=wqb_rope,
            wqbn=np.ascontiguousarray(np.concatenate([wq3[:, h, :ND] for h in heads], 1)),
            wkva=w_kva,
            wkn=np.ascontiguousarray(np.concatenate([wk3[:, h, :ND] for h in heads], 1)),
            wkv=np.ascontiguousarray(np.concatenate([wk3[:, h, ND:] for h in heads], 1)),
            wo=w_o, at=A_t, bt=B_t, wrt=WrT,
            gqa=gqa_t, gkva=gkva_t,
            coss=np.ascontiguousarray(cosT[:, sl]),
            sins=np.ascontiguousarray(sinTs[:, sl]),
            emat=emat, ident=ident,
        ))
    return per_core


class Ctr:
    def __init__(self, handle):
        self.h = handle
        self.v = 0

    def tick(self, instr, n=1):
        self.v += n
        instr.then_inc(self.h, n)
        return self.v


def build_nc():
    nc = bass.Bass("TRN2", target_bir_lowering=False)

    def din(name, shape):
        return nc.dram_tensor(name, shape, F32, kind="ExternalInput").ap()

    xt = din("xt", [D, SQC])
    wqa = din("wqa", [D, QR])
    wqbr = din("wqbr", [QR, NH * RD])
    wqbn = din("wqbn", [QR, HPC * ND])
    wkva = din("wkva", [D, KVR + RD])
    wkn = din("wkn", [KVR, HPC * ND])
    wkv = din("wkv", [KVR, HPC * VD])
    wo = din("wo", [NH * VD, D])
    at = din("at", [D, 1024])
    bt = din("bt", [1024, D])
    wrt = din("wrt", [D, 64])
    gqa = din("gqa", [128, QR // 128])
    gkva = din("gkva", [128, KVR // 128])
    coss = din("coss", [RD, SQC])
    sins = din("sins", [RD, SQC])
    emat = din("emat", [64, 1024])
    ident = din("ident", [128, 128])
    out = nc.dram_tensor("out", [SQC, D], F32, kind="ExternalOutput").ap()

    AGA, AGB = AG1R - KC_OFF, KC_OFF
    ag1i = nc.dram_tensor("ag1_in", [AG1R, SQC], BF16, kind="Internal").ap()
    ag1oA = nc.dram_tensor("ag1_outA", [NCORES * AGA, SQC], BF16, kind="Internal", addr_space="Shared").ap()
    ag1oB = nc.dram_tensor("ag1_outB", [NCORES * AGB, SQC], BF16, kind="Internal", addr_space="Shared").ap()
    ag2i = nc.dram_tensor("ag2_in", [HPC * VD, S], BF16, kind="Internal").ap()
    qrl = nc.dram_tensor("qrl", [NCORES * HPC * RD, SQC], BF16, kind="Internal").ap()
    ag2o = nc.dram_tensor("ag2_out", [NH * VD, S], BF16, kind="Internal", addr_space="Shared").ap()

    dbg = {}
    if DEBUG:
        def dout(name, shape, dt=F32):
            return nc.dram_tensor(name, shape, dt, kind="ExternalOutput").ap()
        dbg["qc"] = dout("dbg_qc", [12 * 128, SQC])
        dbg["kc"] = dout("dbg_kc", [4 * 128, SQC])
        dbg["kr"] = dout("dbg_kr", [64, SQC])
        dbg["qrope"] = dout("dbg_qrope", [NH * RD, SQC])
        dbg["kn"] = dout("dbg_kn", [128, HPC * S], BF16)
        dbg["v4"] = dout("dbg_v4", [128, 64 * 512], BF16)
        dbg["av"] = dout("dbg_av", [HPC * VD, S])
        dbg["z"] = dout("dbg_z", [128, 32 * 512])
        dbg["gg"] = dout("dbg_gg", [128, 4 * 64])
        dbg["hg"] = dout("dbg_hg", [128, 8 * 512])

    with ExitStack() as ctx:
        e = ctx.enter_context

        ARENA = 196608
        arena = nc.alloc_sbuf_tensor("arena", [128, ARENA], mybir.dt.uint8)
        arena_ap = arena.ap()

        def sb(name, shape, dtype, off):
            assert off % 32 == 0, name
            nb = shape[-1] * mybir.dt.size(dtype)
            assert off + nb <= ARENA, (name, off + nb)
            return arena_ap[0:shape[0], off:off + nb].bitcast(dtype)

        OA, OB, OC = 0, 65536, 131072
        xtb = sb("xtb", [128, 32 * 1024], BF16, OA)
        kn4 = sb("kn4", [128, HPC * S], BF16, OA)
        zh = sb("zh", [128, 32 * 512], F32, OA)
        qcpre = sb("qcpre", [128, 12 * 1024], F32, OB)
        kctb = sb("kctb", [128, 4 * 1024], BF16, OB)
        krb = sb("krb", [64, 1024], BF16, OB + 8192)
        qrb = [sb(f"qrb{i}", [64, 1024], BF16, OB + 10240 + 2048 * i) for i in range(2)]
        ropescr = sb("ropescr", [64, 1024], F32, OB + 14336)
        ropescr2 = sb("ropescr2", [64, 1024], F32, OB + 18432)
        p1wb = [sb(f"p1wb{i}", [128, 32 * 128], BF16, OB + 49152 + 8192 * i) for i in range(2)]
        v4 = sb("v4", [128, 64 * 512], BF16, OB)
        avsl = sb("avsl", [128, 32 * 512], BF16, OB)
        zbf = sb("zbf", [128, 32 * 512], BF16, OB + 32768)
        hbf = sb("hbf", [128, 8 * 512], BF16, OB)
        hgbf = sb("hgbf", [128, 8 * 512], BF16, OB + 8192)
        atb = [sb(f"atb{i}", [128, 32 * 128], BF16, OB + 16384 + 8192 * i) for i in range(2)]
        btb = [sb(f"btb{i}", [128, 8 * 128], BF16, OB + 16384 + 2048 * i) for i in range(2)]
        vTq = [sb(f"vTq{i}", [128, 4096], F32, OB + 16384 * i) for i in range(2)]
        vTo = [sb(f"vTo{i}", [128, 4096], F32, OB + 32768 + 16384 * i) for i in range(2)]
        row_q = sb("row_q", [1, 1024], F32, OC)
        row_k = sb("row_k", [1, 1024], F32, OC + 4096)
        row_t = sb("row_t", [1, 1024], F32, OC + 8192)
        row_r = None
        row_r2 = None
        q2row = sb("q2row", [1, 512], F32, OC + 18432)
        CC = OC + 20480
        rbsb = sb("rbsb", [128, 512], F32, OC + 2048)
        avob = [sb(f"avob{i}", [128, 512], BF16, OC + 4096 + 1024 * i) for i in range(2)]
        qnb = [sb("qnb", [128, 512], BF16, OC + 6144),
               sb("qnb2", [128, 512], BF16, OC + 17408)]
        qra = [sb(f"qra{i}", [65, 512], BF16, OC + 7168 + 7168 * i) for i in range(1)] + \
              [sb("qra1", [65, 512], BF16, OC + 14336)]
        accd = sb("accd", [128, 512], F32, OC + 10240)
        sscr = sb("sscr", [128, 512], BF16, OC + 16384)
        sscr2 = sb("sscr2", [64, 512], BF16, OC + 15360)
        hscr = sb("hscr", [128, 512], F32, OC + 10240)
        p1sq = sb("p1sq", [128, 512], BF16, OC)
        p1sq2 = sb("p1sq2", [128, 512], BF16, OC + 2048)
        qsb = sb("qsb", [128, 1024], F32, OC + 4096)
        kvpre = sb("kvpre", [128, 5 * 1024], F32, CC)
        qctb = sb("qctb", [128, 12 * 1024], BF16, CC + 20480)
        roped = sb("roped", [65, S], BF16, CC)
        qcb = sb("qcb", [128, 12 * 512], BF16, CC + 16384)
        wqbnb = sb("wqbnb", [128, 12 * 512], BF16, CC + 28672)
        pbuf = [sb(f"pbuf{i}", [128, 1024], BF16, CC + 40960 + 2048 * i) for i in range(2)]
        kcw = [sb(f"kcw{i}", [128, 4 * 512], BF16, CC + 16384 + 4096 * i) for i in range(2)]
        wknb = sb("wknb", [128, 4 * 512], BF16, CC + 24576)
        wkvb_ = sb("wkvb_", [128, 4 * 512], BF16, CC + 28672)
        wob = [sb(f"wob{i}", [128, 32 * 128], BF16, CC + 8192 * i) for i in range(2)]
        xres = [sb(f"xres{i}", [128, 512], F32, CC + 16384 + 2048 * i) for i in range(2)]
        wrtb = sb("wrtb", [128, 32 * 64], F32, CC + 20480)
        rrsb = sb("rrsb", [64, 512], F32, CC + 28672)
        rsrow = sb("rsrow", [128, 4 * 64], F32, CC + 30720)
        zap = sb("zap", [128, 4 * 64], F32, CC + 31744)
        gvals = sb("gvals", [128, 4 * 64], F32, CC + 32768)
        mx16 = sb("mx16", [128, 16], F32, CC + 33792)
        colr = sb("colr", [128, 8], F32, CC + 33856)
        gtb = sb("gtb", [64, 512], BF16, CC + 34048)
        casb = sb("casb", [128, 8], F32, CC + 35072)
        cwsb = sb("cwsb", [64, 1], F32, CC + 35104)
        rstdb = sb("rstdb", [128, 512], F32, CC + 35840)
        scl2b = sb("scl2b", [128, 512], F32, CC + 37888)
        negmub = sb("negmub", [128, 512], F32, CC + 39936)
        sqzb = sb("sqzb", [128, 512], BF16, CC + 41984)
        sqzb2 = sb("sqzb2", [128, 512], BF16, CC + 43008)
        bnst = sb("bnst", [128, 64], F32, CC + 44032)

        row_r = e(nc.sbuf_tensor("row_r", [1, 1024], F32R)).ap()
        row_r2 = e(nc.sbuf_tensor("row_r2", [1, 512], F32R)).ap()
        c_ident = e(nc.sbuf_tensor("c_ident", [128, 128], F32)).ap()
        c_emat = e(nc.sbuf_tensor("c_emat", [64, 1024], BF16)).ap()
        c_gqa = e(nc.sbuf_tensor("c_gqa", [128, 12], F32)).ap()
        c_gkva = e(nc.sbuf_tensor("c_gkva", [128, 4], F32)).ap()
        c_ones_bf = e(nc.sbuf_tensor("c_ones_bf", [128, 1], BF16)).ap()
        c_ones_r = e(nc.sbuf_tensor("c_ones_r", [128, 1], F32R)).ap()
        c_ones1r = e(nc.sbuf_tensor("c_ones1r", [1, 128], F32R)).ap()
        c_ones_f = e(nc.sbuf_tensor("c_ones_f", [128, 1], F32)).ap()
        c_cos = sb("c_cos", [64, 1024], F32, OB + 22528)
        c_sin = sb("c_sin", [64, 1024], F32, OB + 26624)
        c_knmax = e(nc.sbuf_tensor("c_knmax", [1, 8], F32)).ap()
        c_mrow = e(nc.sbuf_tensor("c_mrow", [1, 16], F32)).ap()

        pg0 = e(nc.psum_tensor("pg0", [128, 1024], F32)).ap()
        pg1 = e(nc.psum_tensor("pg1", [128, 1024], F32)).ap()
        pav = [e(nc.psum_tensor(f"pav{i}", [128, 512], F32)).ap() for i in range(2)]
        pms = e(nc.psum_tensor("pms", [128, 512], F32)).ap()
        psp = e(nc.psum_tensor("psp", [128, 512], F32)).ap()
        pg = [pg0, pg1]

        cPE = Ctr(e(nc.semaphore("cPE")))
        cACT = Ctr(e(nc.semaphore("cACT")))
        cDVE = Ctr(e(nc.semaphore("cDVE")))
        cCC = Ctr(e(nc.semaphore("cCC")))
        dS = Ctr(e(nc.semaphore("dS")))
        dP = Ctr(e(nc.semaphore("dP")))
        dPs = [Ctr(e(nc.semaphore("dPa"))), Ctr(e(nc.semaphore("dPb")))]
        dSk = [Ctr(e(nc.semaphore("dSka"))), Ctr(e(nc.semaphore("dSkb")))]
        dSq = Ctr(e(nc.semaphore("dSq")))
        dSr = [Ctr(e(nc.semaphore("dSra"))), Ctr(e(nc.semaphore("dSrb")))]
        dSv = [Ctr(e(nc.semaphore("dSva"))), Ctr(e(nc.semaphore("dSvb")))]
        dSx = [Ctr(e(nc.semaphore("dSxa"))), Ctr(e(nc.semaphore("dSxb")))]

        blk = e(nc.Block())
        blk1a = blk if KSTOP >= 11 else _SkipBlk()
        blk1b = blk if KSTOP >= 12 else _SkipBlk()
        blk1c = blk if KSTOP >= 13 else _SkipBlk()
        blk2 = blk if KSTOP >= 20 else _SkipBlk()
        blk3 = blk if KSTOP >= 30 else _SkipBlk()
        blk4 = blk if KSTOP >= 40 else _SkipBlk()
        RG = [list(range(NCORES))]
        M = {}

        def lp():
            return nc.allow_low_precision(reason="f32r rounding for broadcast matmuls is intentional")

        @blk.sync
        def _(sy):
            for ap_, src in [(c_ident, ident), (c_gqa, gqa), (c_gkva, gkva)]:
                sy.dma_start(out=ap_[:, :], in_=src).then_inc(dS.h, 16)
                dS.v += 16
        M["d_const"] = dS.v

        @blk.gpsimd
        def _(g):
            g.dma_start(out=c_emat[:, :], in_=emat).then_inc(dP.h, 16)
            dP.v += 16
            for j in range(32):
                g.dma_start(out=xtb[:, 1024 * j:1024 * (j + 1)],
                            in_=xt[128 * j:128 * (j + 1), :]).then_inc(dP.h, 16)
                dP.v += 16
            for m in range(2):
                g.dma_start(out=p1wb[m][:, :],
                            in_=wqa[:, 128 * m:128 * (m + 1)].rearrange("(t p) m -> p t m", p=128)
                            ).then_inc(dP.h, 16)
                dP.v += 16
        M["d_x"] = dP.v

        @blk.vector
        def _(v):
            v.memset(c_ones_bf[:, :], 1.0)
            cDVE.tick(v.memset(c_ones_f[:, :], 1.0))
            v.wait_ge(cDVE.h, cDVE.v)
            v.memset(row_t[0:1, 0:128], 1.0)
            with lp():
                cDVE.tick(v.tensor_copy(c_ones_r[:, :], c_ones_f[:, :]))
            v.wait_ge(cDVE.h, cDVE.v)
            with lp():
                cDVE.tick(v.tensor_copy(c_ones1r[0:1, :], row_t[0:1, 0:128]))
        M["ones"] = cDVE.v

        pe0, act0, dve0 = cPE.v, cACT.v, cDVE.v
        dpa0 = [dPs[0].v, dPs[1].v]

        @blk1a.tensor
        def _(t):
            t.wait_ge(dP.h, M["d_x"])
            t.wait_ge(dS.h, M["d_const"])
            t.wait_ge(cDVE.h, M["ones"])
            for m in range(12):
                if m >= 2:
                    t.wait_ge(dPs[m % 2].h, dpa0[m % 2] + 16 * ((m - 2) // 2 + 1))
                for sq in range(2):
                    if m >= 2:
                        t.wait_ge(cDVE.h, dve0 + 2 * (m - 2) + sq + 1)
                        t.wait_ge(cACT.h, act0 + 2 * (m - 2) + sq + 1)
                    pt = pg[m % 2][:, 512 * sq:512 * sq + 512]
                    for c_ in range(32):
                        ins = t.matmul(pt, p1wb[m % 2][:, 128 * c_:128 * (c_ + 1)],
                                       xtb[:, 1024 * c_ + 512 * sq:1024 * c_ + 512 * sq + 512],
                                       start=(c_ == 0), stop=(c_ == 31))
                    cPE.tick(ins)
                    M[("1a", m, sq)] = cPE.v
                    if m >= 1:
                        j = m - 1
                        t.wait_ge(cACT.h, act0 + 2 * j + sq + 1)
                        dst = pms[0:1, 0:512] if sq == 0 else psp[0:1, 0:512]
                        ins = t.matmul(dst, c_ones_bf, p1sq if sq == 0 else p1sq2,
                                       start=(j == 0), stop=False)
                        cPE.tick(ins)
                        M[("1a_ss", j, sq)] = cPE.v
            for sq in range(2):
                j = 11
                t.wait_ge(cACT.h, act0 + 2 * j + sq + 1)
                dst = pms[0:1, 0:512] if sq == 0 else psp[0:1, 0:512]
                ins = t.matmul(dst, c_ones_bf, p1sq if sq == 0 else p1sq2,
                               start=False, stop=True)
                cPE.tick(ins)
                M[("1a_ss", j, sq)] = cPE.v
        M["1a_pe"] = cPE.v

        @blk1a.gpsimd
        def _(g):
            for m in range(2, 12):
                g.wait_ge(cPE.h, M[("1a", m - 2, 1)])
                g.dma_start(out=p1wb[m % 2][:, :],
                            in_=wqa[:, 128 * m:128 * (m + 1)].rearrange("(t p) m -> p t m", p=128)
                            ).then_inc(dPs[m % 2].h, 16)
                dPs[m % 2].v += 16
                assert dPs[m % 2].v == dpa0[m % 2] + 16 * ((m - 2) // 2 + 1)

        @blk1a.scalar
        def _(s):
            for m in range(12):
                for sq in range(2):
                    s.wait_ge(cPE.h, M[("1a", m, sq)])
                    s.wait_ge(cDVE.h, dve0 + 2 * m + sq + 1)
                    if m >= 1:
                        s.wait_ge(cPE.h, M[("1a_ss", m - 1, sq)])
                    pt = pg[m % 2][:, 512 * sq:512 * sq + 512]
                    ins = s.activation(p1sq if sq == 0 else p1sq2, pt, AF.Square)
                    cACT.tick(ins)
        assert KSTOP < 11 or (cACT.v == act0 + 24)

        @blk1a.vector
        def _(v):
            for m in range(12):
                for sq in range(2):
                    v.wait_ge(cPE.h, M[("1a", m, sq)])
                    ins = v.tensor_copy(
                        qcpre[:, 1024 * m + 512 * sq:1024 * m + 512 * sq + 512],
                        pg[m % 2][:, 512 * sq:512 * sq + 512])
                    cDVE.tick(ins)
        assert KSTOP < 11 or (cDVE.v == dve0 + 24)

        @blk1a.vector
        def _(v):
            v.wait_ge(cPE.h, M["1a_pe"])
            v.tensor_copy(row_q[0:1, 0:512], pms[0:1, 0:512])
            cDVE.tick(v.tensor_copy(row_q[0:1, 512:1024], psp[0:1, 0:512]))
        M["1a_s2"] = cDVE.v

        @blk1a.scalar
        def _(s):
            s.wait_ge(cDVE.h, M["1a_s2"])
            cACT.tick(s.activation(row_t[0:1, :], row_q[0:1, :], AF.Sqrt, scale=1.0 / QR))
        M["1a_sqrt"] = cACT.v

        @blk1a.vector
        def _(v):
            v.wait_ge(cACT.h, M["1a_sqrt"])
            cDVE.tick(v.tensor_scalar(row_t[0:1, :], row_t[0:1, :], EPS_RMS, None, ALU.add))
            v.wait_ge(cDVE.h, cDVE.v)
            cDVE.tick(v.reciprocal(row_q[0:1, :], row_t[0:1, :]))
            v.wait_ge(cDVE.h, cDVE.v)
            cDVE.tick(v.tensor_scalar(row_t[0:1, :], row_q[0:1, :], INV_SQRT_QK, None, ALU.mult))
            v.wait_ge(cDVE.h, cDVE.v)
            with lp():
                cDVE.tick(v.tensor_copy(row_r[0:1, :], row_t[0:1, :]))
        M["1a_scale"] = cDVE.v

        @blk1a.tensor
        def _(t):
            t.wait_ge(cDVE.h, M["1a_scale"])
            t.matmul(pg0[:, 0:512], c_ones1r, row_r[0:1, 0:512], start=True, stop=True)
            cPE.tick(t.matmul(pg0[:, 512:1024], c_ones1r, row_r[0:1, 512:1024], start=True, stop=True))
        M["1a_bc"] = cPE.v

        @blk1a.vector
        def _(v):
            v.wait_ge(cPE.h, M["1a_bc"])
            cDVE.tick(v.tensor_copy(qsb[:, :], pg0[:, :]))
            v.wait_ge(cDVE.h, cDVE.v)
            for m in range(12):
                for sq in range(2):
                    ins = v.scalar_tensor_tensor(
                        qctb[:, 1024 * m + 512 * sq:1024 * m + 512 * sq + 512],
                        qcpre[:, 1024 * m + 512 * sq:1024 * m + 512 * sq + 512],
                        c_gqa[:, m:m + 1],
                        qsb[:, 512 * sq:512 * sq + 512],
                        ALU.mult, ALU.mult)
            cDVE.tick(ins)
        M["1a_qctb"] = cDVE.v

        @blk1a.sync
        def _(sy):
            sy.wait_ge(cDVE.h, M["1a_qctb"])
            for m in range(12):
                sy.dma_start(out=ag1i[QC_OFF + 128 * m:QC_OFF + 128 * (m + 1), :],
                             in_=qctb[:, 1024 * m:1024 * (m + 1)]).then_inc(dS.h, 16)
                dS.v += 16
            if DEBUG:
                for m in range(12):
                    sy.dma_start(out=dbg["qc"][128 * m:128 * (m + 1), :],
                                 in_=qctb[:, 1024 * m:1024 * (m + 1)]).then_inc(dS.h, 16)
                    dS.v += 16

        @blk1b.sync
        def _(sy):
            sy.wait_ge(cDVE.h, M["1a_qctb"])
            sy.dma_start(out=c_cos[:, :], in_=coss).then_inc(dS.h, 16)
            sy.dma_start(out=c_sin[:, :], in_=sins).then_inc(dS.h, 16)
            dS.v += 32
        M["d_cossin"] = dS.v

        pe0, act0, dve0 = cPE.v, cACT.v, cDVE.v
        dpa0 = [dPs[0].v, dPs[1].v]

        @blk1b.gpsimd
        def _(g):
            for m in range(5):
                w = 128 if m < 4 else 64
                if m >= 2:
                    g.wait_ge(cPE.h, pe0 + (2 if m - 2 == 0 else 4 * (m - 2) + 1))
                else:
                    g.wait_ge(cPE.h, M["1a_pe"])
                g.dma_start(out=p1wb[m % 2][:, 0:32 * w],
                            in_=wkva[:, 128 * m:128 * m + w].rearrange("(t p) m -> p t m", p=128)
                            ).then_inc(dPs[m % 2].h, 16)
                dPs[m % 2].v += 16
                assert dPs[m % 2].v == dpa0[m % 2] + 16 * (m // 2 + 1)

        @blk1b.tensor
        def _(t):
            t.wait_ge(cDVE.h, M["1a_qctb"])
            for m in range(5):
                w = 128 if m < 4 else 64
                t.wait_ge(dPs[m % 2].h, dpa0[m % 2] + 16 * (m // 2 + 1))
                for sq in range(2):
                    if m >= 2:
                        t.wait_ge(cDVE.h, dve0 + 2 * (m - 2) + sq + 1)
                        if m - 2 < 4:
                            t.wait_ge(cACT.h, act0 + 2 * (m - 2) + sq + 1)
                    pt = pg[m % 2][0:w, 512 * sq:512 * sq + 512]
                    for c_ in range(32):
                        ins = t.matmul(pt, p1wb[m % 2][:, w * c_:w * (c_ + 1)],
                                       xtb[:, 1024 * c_ + 512 * sq:1024 * c_ + 512 * sq + 512],
                                       start=(c_ == 0), stop=(c_ == 31))
                    cPE.tick(ins)
                    M[("1b", m, sq)] = cPE.v
                    if 1 <= m <= 4:
                        j = m - 1
                        t.wait_ge(cACT.h, act0 + 2 * j + sq + 1)
                        dst = pms[0:1, 0:512] if sq == 0 else psp[0:1, 0:512]
                        ins = t.matmul(dst, c_ones_bf, p1sq if sq == 0 else p1sq2,
                                       start=(j == 0), stop=(j == 3))
                        cPE.tick(ins)
                        M[("1b_ss", j, sq)] = cPE.v
                assert M[("1b", m, 1)] == pe0 + (2 if m == 0 else 4 * m + 1), m
        M["1b_pe"] = cPE.v

        @blk1b.scalar
        def _(s):
            for m in range(4):
                for sq in range(2):
                    s.wait_ge(cPE.h, M[("1b", m, sq)])
                    s.wait_ge(cDVE.h, dve0 + 2 * m + sq + 1)
                    if m >= 1:
                        s.wait_ge(cPE.h, M[("1b_ss", m - 1, sq)])
                    pt = pg[m % 2][:, 512 * sq:512 * sq + 512]
                    ins = s.activation(p1sq if sq == 0 else p1sq2, pt, AF.Square)
                    cACT.tick(ins)
        assert KSTOP < 12 or (cACT.v == act0 + 8)

        @blk1b.vector
        def _(v):
            for m in range(5):
                w = 128 if m < 4 else 64
                for sq in range(2):
                    v.wait_ge(cPE.h, M[("1b", m, sq)])
                    ins = v.tensor_copy(
                        kvpre[0:w, 1024 * m + 512 * sq:1024 * m + 512 * sq + 512],
                        pg[m % 2][0:w, 512 * sq:512 * sq + 512])
                    cDVE.tick(ins)
        assert KSTOP < 12 or (cDVE.v == dve0 + 10)
        M["1b_kvpre"] = cDVE.v

        @blk1b.vector
        def _(v):
            v.wait_ge(cPE.h, M["1b_pe"])
            v.tensor_copy(row_k[0:1, 0:512], pms[0:1, 0:512])
            cDVE.tick(v.tensor_copy(row_k[0:1, 512:1024], psp[0:1, 0:512]))
        M["1b_s2"] = cDVE.v

        @blk1b.scalar
        def _(s):
            s.wait_ge(cDVE.h, M["1b_s2"])
            cACT.tick(s.activation(row_t[0:1, :], row_k[0:1, :], AF.Sqrt, scale=1.0 / KVR))
        M["1b_sqrt"] = cACT.v

        @blk1b.vector
        def _(v):
            v.wait_ge(cACT.h, M["1b_sqrt"])
            cDVE.tick(v.tensor_scalar(row_t[0:1, :], row_t[0:1, :], EPS_RMS, None, ALU.add))
            v.wait_ge(cDVE.h, cDVE.v)
            cDVE.tick(v.reciprocal(row_k[0:1, :], row_t[0:1, :]))
            v.wait_ge(cDVE.h, cDVE.v)
            with lp():
                cDVE.tick(v.tensor_copy(row_r[0:1, :], row_k[0:1, :]))
        M["1b_scale"] = cDVE.v

        @blk1b.tensor
        def _(t):
            t.wait_ge(cDVE.h, M["1b_scale"])
            t.matmul(pg0[:, 0:512], c_ones1r, row_r[0:1, 0:512], start=True, stop=True)
            cPE.tick(t.matmul(pg0[:, 512:1024], c_ones1r, row_r[0:1, 512:1024], start=True, stop=True))
        M["1b_bc"] = cPE.v

        @blk1b.vector
        def _(v):
            v.wait_ge(cPE.h, M["1b_bc"])
            v.wait_ge(cDVE.h, M["1a_qctb"])
            cDVE.tick(v.tensor_copy(qsb[:, :], pg0[:, :]))
            v.wait_ge(cDVE.h, cDVE.v)
            for m in range(4):
                for sq in range(2):
                    ins = v.scalar_tensor_tensor(
                        kctb[:, 1024 * m + 512 * sq:1024 * m + 512 * sq + 512],
                        kvpre[:, 1024 * m + 512 * sq:1024 * m + 512 * sq + 512],
                        c_gkva[:, m:m + 1],
                        qsb[:, 512 * sq:512 * sq + 512],
                        ALU.mult, ALU.mult)
            cDVE.tick(ins)
            v.wait_ge(cDVE.h, cDVE.v)
            v.wait_ge(dS.h, M["d_cossin"])
            kr_ = kvpre[0:64, 4096:5120]
            v.tensor_copy(ropescr[0:32, :], kr_[32:64, :])
            cDVE.tick(v.tensor_copy(ropescr[32:64, :], kr_[0:32, :]))
            v.wait_ge(cDVE.h, cDVE.v)
            v.tensor_mul(ropescr[0:64, :], ropescr[0:64, :], c_sin[:, :])
            cDVE.tick(v.tensor_mul(ropescr2[0:64, :], kr_, c_cos[:, :]))
            v.wait_ge(cDVE.h, cDVE.v)
            cDVE.tick(v.tensor_add(krb[0:64, :], ropescr[0:64, :], ropescr2[0:64, :]))
        M["1b_done"] = cDVE.v

        @blk1b.sync
        def _(sy):
            sy.wait_ge(cDVE.h, M["1b_done"])
            for m in range(4):
                sy.dma_start(out=ag1i[KC_OFF + 128 * m:KC_OFF + 128 * (m + 1), :],
                             in_=kctb[:, 1024 * m:1024 * (m + 1)]).then_inc(dS.h, 16)
                dS.v += 16
            sy.dma_start(out=ag1i[KR_OFF:KR_OFF + 64, :], in_=krb[0:64, :]).then_inc(dS.h, 16)
            dS.v += 16
            if DEBUG:
                for m in range(4):
                    sy.dma_start(out=dbg["kc"][128 * m:128 * (m + 1), :],
                                 in_=kctb[:, 1024 * m:1024 * (m + 1)]).then_inc(dS.h, 16)
                    dS.v += 16
                sy.dma_start(out=dbg["kr"][:, :], in_=krb[0:64, :]).then_inc(dS.h, 16)
                dS.v += 16
        M["1b_out"] = dS.v

        pe0, dve0 = cPE.v, cDVE.v
        dpa0 = [dPs[0].v, dPs[1].v]
        dsr0 = [dSr[0].v, dSr[1].v]

        @blk1c.gpsimd
        def _(g):
            for h in range(NH):
                if h >= 2:
                    g.wait_ge(cPE.h, pe0 + 2 * (h - 1))
                else:
                    g.wait_ge(cPE.h, M["1b_pe"])
                g.dma_start(out=p1wb[h % 2][:, 0:12 * 64],
                            in_=wqbr[:, 64 * h:64 * (h + 1)].rearrange("(t p) m -> p t m", p=128)
                            ).then_inc(dPs[h % 2].h, 16)
                dPs[h % 2].v += 16

        @blk1c.tensor
        def _(t):
            t.wait_ge(cDVE.h, M["1b_done"])
            for h in range(NH):
                t.wait_ge(dPs[h % 2].h, dpa0[h % 2] + 16 * (h // 2 + 1))
                if h >= 2:
                    t.wait_ge(cDVE.h, dve0 + 4 * (h - 2) + 1)
                for sq in range(2):
                    pt = pg[h % 2][0:64, 512 * sq:512 * sq + 512]
                    for c_ in range(12):
                        ins = t.matmul(pt, p1wb[h % 2][:, 64 * c_:64 * (c_ + 1)],
                                       qctb[:, 1024 * c_ + 512 * sq:1024 * c_ + 512 * sq + 512],
                                       start=(c_ == 0), stop=(c_ == 11))
                    cPE.tick(ins)
                M[("1c", h)] = cPE.v
                assert M[("1c", h)] == pe0 + 2 * (h + 1)
        M["1c_pe"] = cPE.v

        @blk1c.vector
        def _(v):
            for h in range(NH):
                v.wait_ge(cPE.h, M[("1c", h)])
                v.wait_ge(cDVE.h, cDVE.v)
                if h >= 2:
                    v.wait_ge(dSr[h % 2].h, dsr0[h % 2] + 16 * ((h - 2) // 2 + 1))
                qr_ = ropescr[0:64, :]
                sh_ = ropescr2[0:64, :]
                v.tensor_copy(qr_[0:64, 0:512], pg[h % 2][0:64, 0:512])
                cDVE.tick(v.tensor_copy(qr_[0:64, 512:1024], pg[h % 2][0:64, 512:1024]))
                v.wait_ge(cDVE.h, cDVE.v)
                v.tensor_copy(sh_[0:32, :], qr_[32:64, :])
                cDVE.tick(v.tensor_copy(sh_[32:64, :], qr_[0:32, :]))
                v.wait_ge(cDVE.h, cDVE.v)
                v.tensor_mul(sh_, sh_, c_sin[:, :])
                cDVE.tick(v.tensor_mul(qr_, qr_, c_cos[:, :]))
                v.wait_ge(cDVE.h, cDVE.v)
                cDVE.tick(v.tensor_add(qrb[h % 2][0:64, :], qr_, sh_))
                M[("1c_r", h)] = cDVE.v
        assert KSTOP < 13 or (cDVE.v == dve0 + 4 * NH)

        @blk1c.sync
        def _(sy):
            for h in range(NH):
                sy.wait_ge(cDVE.h, M[("1c_r", h)])
                sy.dma_start(out=ag1i[QROPE_OFF + 64 * h:QROPE_OFF + 64 * (h + 1), :],
                             in_=qrb[h % 2][0:64, :]).then_inc(dSr[h % 2].h, 16)
                dSr[h % 2].v += 16
                if DEBUG:
                    sy.dma_start(out=dbg["qrope"][64 * h:64 * (h + 1), :],
                                 in_=qrb[h % 2][0:64, :]).then_inc(dSr[h % 2].h, 16)
                    dSr[h % 2].v += 16
        M["ag1_ready"] = (dS.v, dSr[0].v, dSr[1].v)

        @blk2.gpsimd
        def _(g):
            g.wait_ge(dS.h, M["1b_out"])
            ins = g.collective_compute("AllGather", ALU.bypass, replica_groups=RG,
                                       ins=[ag1i[KC_OFF:AG1R, :].opt()], outs=[ag1oA.opt()])
            cCC.tick(ins)
            M["agA"] = cCC.v
            g.wait_ge(dSr[0].h, M["ag1_ready"][1])
            g.wait_ge(dSr[1].h, M["ag1_ready"][2])
            ins = g.collective_compute("AllGather", ALU.bypass, replica_groups=RG,
                                       ins=[ag1i[0:KC_OFF, :].opt()], outs=[ag1oB.opt()])
            cCC.tick(ins)
            M["agB"] = cCC.v
            g.dma_start(out=wknb[:, :],
                        in_=wkn.rearrange("(t p) m -> p t m", p=128)).then_inc(dP.h, 16)
            g.dma_start(out=wkvb_[:, :],
                        in_=wkv.rearrange("(t p) m -> p t m", p=128)).then_inc(dP.h, 16)
            dP.v += 32
        M["d_wk"] = dP.v

        pid_holder = {}
        pe0, act0, dve0, ds0 = cPE.v, cACT.v, cDVE.v, dS.v

        @blk2.vector
        def _(v):
            cDVE.tick(v.memset(roped[64:65, :], -1.0))
        M["negrow"] = cDVE.v

        @blk2.sync
        def _(sy):
            sy.wait_ge(cCC.h, M["agA"])
            pid_holder["sp"] = nc.partition_id([ET.SP])
            for tt in range(16):
                rr, co = tt // 2, (tt % 2) * 512
                if tt >= 2:
                    sy.wait_ge(cPE.h, pe0 + tt - 1)
                for r_ in range(4):
                    sy.dma_start(
                        out=kcw[tt % 2][:, 512 * r_:512 * (r_ + 1)],
                        in_=ag1oA[AGA * rr + 128 * r_:AGA * rr + 128 * (r_ + 1),
                                  co:co + 512]).then_inc(dSk[tt % 2].h, 16)
                    dSk[tt % 2].v += 16
                M[("kcw", tt)] = (dSk[tt % 2], dSk[tt % 2].v)
            for rr in range(NCORES):
                sy.dma_start(out=roped[0:64, SQC * rr:SQC * (rr + 1)],
                             in_=ag1oA[AGA * rr + 512:AGA * rr + 576, :]
                             ).then_inc(dS.h, 16)
                dS.v += 16
            sy.wait_ge(cCC.h, M["agB"])
            v3 = ag1oB.rearrange("(r x) n -> r x n", x=AGB)
            sy.dma_start(out=qrl.rearrange("(r x) n -> r x n", x=HPC * RD),
                         in_=v3[:, bass.ts(pid_holder["sp"] + (QROPE_OFF // (HPC * RD)), HPC * RD), :]
                         ).then_inc(dS.h, 16)
            dS.v += 16
        M["d_roped"] = dS.v

        @blk2.tensor
        def _(t):
            t.wait_ge(dP.h, M["d_wk"])
            t.wait_ge(cDVE.h, dve0)
            for tt in range(16):
                t.wait_ge(M[("kcw", tt)][0].h, M[("kcw", tt)][1])
                if tt >= 1:
                    t.wait_ge(cDVE.h, dve0 + 1 + tt)
                kpt = [pg0[:, 0:512], pg0[:, 512:1024], pg1[:, 0:512], pg1[:, 512:1024]]
                vpt = [pav[0], pav[1], pms, psp]
                for m in range(4):
                    for r_ in range(4):
                        t.matmul(kpt[m],
                                 wknb[:, 512 * r_ + 128 * m:512 * r_ + 128 * (m + 1)],
                                 kcw[tt % 2][:, 512 * r_:512 * (r_ + 1)],
                                 start=(r_ == 0), stop=(r_ == 3))
                for sk in range(4):
                    for r_ in range(4):
                        ins = t.matmul(vpt[sk],
                                       kcw[tt % 2][:, 512 * r_ + 128 * sk:512 * r_ + 128 * (sk + 1)],
                                       wkvb_[:, 512 * r_:512 * (r_ + 1)],
                                       start=(r_ == 0), stop=(r_ == 3))
                cPE.tick(ins)
                M[("2", tt)] = cPE.v
        assert KSTOP < 20 or (cPE.v == pe0 + 16)
        M["2_pe"] = cPE.v

        @blk2.vector
        def _(v):
            for tt in range(16):
                v.wait_ge(cPE.h, M[("2", tt)])
                v.wait_ge(cDVE.h, cDVE.v)
                kpt = [pg0[:, 0:512], pg0[:, 512:1024], pg1[:, 0:512], pg1[:, 512:1024]]
                vpt = [pav[0], pav[1], pms, psp]
                for m in range(4):
                    v.tensor_copy(kn4[:, S * m + 512 * tt:S * m + 512 * (tt + 1)], kpt[m])
                for sk in range(4):
                    ins = v.tensor_copy(v4[:, 512 * (4 * tt + sk):512 * (4 * tt + sk + 1)], vpt[sk])
                cDVE.tick(ins)
        assert KSTOP < 20 or (cDVE.v == dve0 + 1 + 16)
        M["2_dve"] = cDVE.v

        pe0, act0, dve0 = cPE.v, cACT.v, cDVE.v

        @blk2.scalar
        def _(s):
            s.wait_ge(cDVE.h, M["2_dve"])
            s.wait_ge(dS.h, M["d_roped"])
            for h in range(HPC):
                for tt in range(16):
                    u = 16 * h + tt
                    if u >= 1:
                        s.wait_ge(cPE.h, pe0 + u)
                    s.activation(sscr, kn4[:, S * h + 512 * tt:S * h + 512 * (tt + 1)], AF.Square)
                    ins = s.activation(sscr2[0:64, :], roped[0:64, 512 * tt:512 * (tt + 1)], AF.Square)
                    cACT.tick(ins)
        assert KSTOP < 20 or (cACT.v == act0 + 64)

        @blk2.tensor
        def _(t):
            for h in range(HPC):
                for tt in range(16):
                    u = 16 * h + tt
                    t.wait_ge(cACT.h, act0 + u + 1)
                    if u >= 1:
                        t.wait_ge(cDVE.h, dve0 + 17 * (u // 16) + (u % 16))
                    t.matmul(pms[0:1, 0:512], c_ones_bf, sscr, start=True, stop=False)
                    ins = t.matmul(pms[0:1, 0:512], c_ones_bf[0:64, :], sscr2[0:64, :],
                                   start=False, stop=True)
                    cPE.tick(ins)
        assert KSTOP < 20 or (cPE.v == pe0 + 64)

        @blk2.vector
        def _(v):
            for h in range(HPC):
                for tt in range(16):
                    u = 16 * h + tt
                    v.wait_ge(cPE.h, pe0 + u + 1)
                    cDVE.tick(v.reduce_max(c_mrow[0:1, tt:tt + 1], pms[0:1, 0:512], axis=AX.X))
                v.wait_ge(cDVE.h, cDVE.v)
                cDVE.tick(v.reduce_max(c_knmax[0:1, h:h + 1], c_mrow[0:1, 0:16], axis=AX.X))
                v.wait_ge(cDVE.h, cDVE.v)
        assert KSTOP < 20 or (cDVE.v == dve0 + 68)
        M["2_knm"] = cDVE.v

        @blk2.scalar
        def _(s):
            s.wait_ge(cDVE.h, M["2_knm"])
            cACT.tick(s.activation(c_knmax[0:1, 0:HPC], c_knmax[0:1, 0:HPC], AF.Ln))
        M["knln"] = cACT.v

        @blk2.vector
        def _(v):
            v.wait_ge(cACT.h, M["knln"])
            v.wait_ge(cDVE.h, cDVE.v)
            cDVE.tick(v.tensor_scalar(c_knmax[0:1, 0:HPC], c_knmax[0:1, 0:HPC],
                                      0.5, None, ALU.mult))
        M["knm2"] = cDVE.v

        @blk2.gpsimd
        def _(g):
            g.wait_ge(cPE.h, M["2_pe"])
            g.dma_start(out=wqbnb[:, :],
                        in_=wqbn.rearrange("(t p) m -> p t m", p=128)).then_inc(dP.h, 16)
            dP.v += 16
        M["d_wqbn"] = dP.v

        if DEBUG:
            @blk2.sync
            def _(sy):
                sy.wait_ge(cDVE.h, M["2_dve"])
                sy.dma_start(out=dbg["kn"], in_=kn4).then_inc(dS.h, 16)
                sy.dma_start(out=dbg["v4"], in_=v4).then_inc(dS.h, 16)
                dS.v += 32

        NG = 32
        pe0, act0, dve0 = cPE.v, cACT.v, cDVE.v
        dsq0 = dSq.v
        dsr0 = [dSr[0].v, dSr[1].v]
        dsv0 = [dSv[0].v, dSv[1].v]
        ACT_U, DVE_U = 36, 70
        import builtins; builtins.DBG3 = (pe0, act0, dve0, dsq0)

        def act_m(u, i):
            return act0 + ACT_U * u + i

        def dve_m(u, i):
            return dve0 + DVE_U * u + i

        def ds_qcb(b):
            return dsq0 + 16 * 12 * (b + 1)

        def ds_qra_w(eng, u):
            eng.wait_ge(dSr[u % 2].h, dsr0[u % 2] + 16 * (u // 2 + 1))

        def ds_av_w(eng, u):
            eng.wait_ge(dSv[u % 2].h, dsv0[u % 2] + 16 * (u // 2 + 1))

        def a_exp(u, g):
            base = act0 + 4 + 36 * u
            off = 1 + g if (g < 16 or u == 63) else 5 + g
            return base + off

        def a_sqn(u):
            return act0 + 1 if u == 0 else act0 + 4 + 36 * (u - 1) + 17

        def a_sqr(u):
            return act0 + 2 if u == 0 else act0 + 4 + 36 * (u - 1) + 18

        def a_ln(u):
            return act0 + 3 if u == 0 else act0 + 4 + 36 * (u - 1) + 19

        def a_bexp(u):
            return act0 + 4 if u == 0 else act0 + 4 + 36 * (u - 1) + 20

        def d_base(u):
            return dve0 + 1 + 70 * u

        def d_accd(u, g):
            off = [3, 7, 9, 11, 13][g] if g < 5 else (2 * g + 7 if g < 10 else 2 * g + 8)
            return d_base(u) + off

        def d_qne(u):
            return dve0 + 1 if u == 0 else d_base(u - 1) + 26

        def d_cast(u):
            return d_base(u + 1) + 5 if u < 63 else dve0 + 4484

        def d_rbsb(u):
            return d_base(u + 1) + 14 if u < 63 else dve0 + 4485

        def d_avob(u):
            return d_base(u + 1) + 15 if u < 63 else dve0 + 4486

        @blk3.tensor
        def _(t):
            t.wait_ge(cDVE.h, M["knm2"])
            t.wait_ge(dP.h, M["d_wqbn"])
            t.wait_ge(cDVE.h, M["negrow"])

            def emit_qb(u):
                if u % 4 == 0:
                    t.wait_ge(dSq.h, ds_qcb(u // 4))
                if u >= 1:
                    t.wait_ge(cDVE.h, d_qne(u - 1))
                    t.wait_ge(cACT.h, a_sqn(u - 1))
                hh = u % 4
                for c_ in range(12):
                    ins = t.matmul(psp, wqbnb[:, 512 * c_ + 128 * hh:512 * c_ + 128 * (hh + 1)],
                                   qcb[:, 512 * c_:512 * (c_ + 1)],
                                   start=(c_ == 0), stop=(c_ == 11))
                cPE.tick(ins)
                M[("3qb", u)] = cPE.v

            def emit_bm(u):
                t.wait_ge(cACT.h, a_sqr(u))
                if u >= 2:
                    t.wait_ge(cDVE.h, d_rbsb(u - 2))
                t.matmul(pms[0:1, 0:512], c_ones_bf, sscr, start=True, stop=False)
                ins = t.matmul(pms[0:1, 0:512], c_ones_bf[0:64, :], sscr2[0:64, :],
                               start=False, stop=True)
                cPE.tick(ins)
                M[("3bm", u)] = cPE.v

            emit_qb(0)
            emit_bm(0)
            for u in range(64):
                hh = u % 4
                def emit_av(gg):
                    t.wait_ge(cACT.h, a_exp(u, gg))
                    if gg == 0 and u >= 2:
                        t.wait_ge(cDVE.h, d_avob(u - 2))
                    for cc in (2 * gg, 2 * gg + 1):
                        ins = t.matmul(pav[u % 2],
                                       v4[:, 512 * cc + 128 * hh:512 * cc + 128 * (hh + 1)],
                                       pbuf[gg % 2][:, (cc % 2) * 512:(cc % 2) * 512 + 512],
                                       start=(cc == 0), stop=(cc == 63))
                    cPE.tick(ins)
                    M[("3av", u, gg)] = cPE.v

                for g_ in range(NG):
                    par = g_ % 2
                    if g_ == 0:
                        t.wait_ge(cDVE.h, d_qne(u))
                        t.wait_ge(cACT.h, a_bexp(u) if u == 0 else a_exp(u - 1, 30))
                    elif g_ == 1:
                        if u >= 1:
                            t.wait_ge(cACT.h, a_exp(u - 1, 31))
                    else:
                        t.wait_ge(cACT.h, a_exp(u, g_ - 2))
                    for cc in (2 * g_, 2 * g_ + 1):
                        col = (cc % 2) * 512
                        pt = pg[par][:, col:col + 512]
                        t.matmul(pt, kn4[:, S * hh + 128 * cc:S * hh + 128 * (cc + 1)],
                                 qnb[u % 2], start=True, stop=False)
                        ins = t.matmul(pt, roped[0:65, 128 * cc:128 * (cc + 1)],
                                       qra[u % 2][0:65, :], start=False, stop=True)
                    cPE.tick(ins)
                    M[("3sc", u, g_)] = cPE.v
                    if g_ == 2 and u >= 1:
                        t.wait_ge(cDVE.h, d_base(u - 1) + 70)
                        cPE.tick(t.matmul(pms[0:1, 0:512], c_ones_f, accd[:, :],
                                          start=True, stop=True))
                        M[("3dn", u - 1)] = cPE.v
                    if g_ >= 2:
                        emit_av(g_ - 2)
                    if g_ == 6 and u >= 1:
                        t.wait_ge(cDVE.h, d_cast(u - 1))
                        cPE.tick(t.matmul(pms, c_ones1r, row_r[0:1, 0:512],
                                          start=True, stop=True))
                        M[("3rb", u - 1)] = cPE.v
                    if g_ == 8 and u <= 62:
                        emit_qb(u + 1)
                    if g_ == 16 and u <= 62:
                        emit_bm(u + 1)
                emit_av(NG - 2)
                emit_av(NG - 1)
            t.wait_ge(cDVE.h, d_base(63) + 70)
            cPE.tick(t.matmul(pms[0:1, 0:512], c_ones_f, accd[:, :], start=True, stop=True))
            M[("3dn", 63)] = cPE.v
            t.wait_ge(cDVE.h, d_cast(63))
            cPE.tick(t.matmul(pms, c_ones1r, row_r[0:1, 0:512], start=True, stop=True))
            M[("3rb", 63)] = cPE.v
        M["3_pe"] = cPE.v

        @blk3.scalar
        def _(s):
            def emit_head(u):
                s.wait_ge(cPE.h, M[("3qb", u)])
                s.wait_ge(cDVE.h, d_qne(u))
                cACT.tick(s.activation(sscr, psp, AF.Square))
                ds_qra_w(s, u)
                cACT.tick(s.activation(sscr2[0:64, :], qra[u % 2][0:64, :], AF.Square))
                s.wait_ge(cPE.h, M[("3bm", u)])
                cACT.tick(s.activation(q2row[0:1, :], pms[0:1, 0:512], AF.Ln))
                s.wait_ge(cACT.h, cACT.v)
                if u >= 2:
                    s.wait_ge(cPE.h, M[("3sc", u - 2, NG - 1)])
                cACT.tick(s.activation(qra[u % 2][64:65, :], q2row[0:1, :], AF.Exp,
                                       scale=0.5, bias=c_knmax[0:1, (u % 4):(u % 4) + 1]))

            emit_head(0)
            for u in range(64):
                for g_ in range(NG):
                    s.wait_ge(cPE.h, M[("3sc", u, g_)])
                    if g_ >= 2:
                        s.wait_ge(cPE.h, M[("3av", u, g_ - 2)])
                        s.wait_ge(cDVE.h, d_accd(u, g_ - 2))
                    elif u >= 1:
                        s.wait_ge(cDVE.h, d_accd(u - 1, 30 + g_))
                    cACT.tick(s.activation(pbuf[g_ % 2][:, :], pg[g_ % 2][:, :], AF.Exp))
                    if g_ == 15 and u <= 62:
                        emit_head(u + 1)
        assert KSTOP < 30 or (cACT.v == act0 + 4 + 36 * 63 + 32)

        @blk3.vector
        def _(v):
            v.wait_ge(cPE.h, M[("3qb", 0)])
            v.wait_ge(cDVE.h, cDVE.v)
            cDVE.tick(v.tensor_copy(qnb[0], psp))

            def acc_pair(u, g_):
                v.wait_ge(cACT.h, a_exp(u, g_))
                if g_ == 0:
                    cDVE.tick(v.tensor_copy(accd[:, :], pbuf[0][:, 0:512]))
                    v.wait_ge(cDVE.h, cDVE.v)
                    cDVE.tick(v.tensor_tensor(accd[:, :], accd[:, :],
                                              pbuf[0][:, 512:1024], ALU.add))
                else:
                    cDVE.tick(v.tensor_tensor(accd[:, :], accd[:, :],
                                              pbuf[g_ % 2][:, 0:512], ALU.add))
                    v.wait_ge(cDVE.h, cDVE.v)
                    cDVE.tick(v.tensor_tensor(accd[:, :], accd[:, :],
                                              pbuf[g_ % 2][:, 512:1024], ALU.add))
                v.wait_ge(cDVE.h, cDVE.v)

            def tail_a(u):
                if u < 0:
                    cDVE.tick(v.nop())
                    return
                v.wait_ge(cPE.h, M[("3dn", u)])
                cDVE.tick(v.tensor_scalar(row_t[0:1, 0:512], pms[0:1, 0:512],
                                          1e-30, None, ALU.add))
                v.wait_ge(cDVE.h, cDVE.v)

            def tail_b(u):
                if u < 0:
                    cDVE.tick(v.nop())
                    cDVE.tick(v.nop())
                    return
                cDVE.tick(v.reciprocal(row_q[0:1, 0:512], row_t[0:1, 0:512]))
                v.wait_ge(cDVE.h, cDVE.v)
                with lp():
                    cDVE.tick(v.tensor_copy(row_r[0:1, 0:512], row_q[0:1, 0:512]))
                v.wait_ge(cDVE.h, cDVE.v)

            def tail_c(u):
                if u < 0:
                    cDVE.tick(v.nop())
                    cDVE.tick(v.nop())
                    return
                v.wait_ge(cPE.h, M[("3rb", u)])
                cDVE.tick(v.tensor_copy(rbsb, pms))
                v.wait_ge(cDVE.h, cDVE.v)
                if u >= 2:
                    ds_av_w(v, u - 2)
                cDVE.tick(v.tensor_tensor(avob[u % 2][:, :], pav[u % 2], rbsb, ALU.mult))
                v.wait_ge(cDVE.h, cDVE.v)

            for u in range(64):
                tail_a(u - 1)
                acc_pair(u, 0)
                tail_b(u - 1)
                for g_ in range(1, 5):
                    acc_pair(u, g_)
                tail_c(u - 1)
                for g_ in range(5, 10):
                    acc_pair(u, g_)
                if u <= 62:
                    v.wait_ge(cPE.h, M[("3qb", u + 1)])
                    if u >= 1:
                        v.wait_ge(cPE.h, M[("3sc", u - 1, NG - 1)])
                    cDVE.tick(v.tensor_copy(qnb[(u + 1) % 2], psp))
                else:
                    cDVE.tick(v.nop())
                v.wait_ge(cDVE.h, cDVE.v)
                for g_ in range(10, NG):
                    acc_pair(u, g_)
            tail_a(63)
            tail_b(63)
            tail_c(63)
        assert KSTOP < 30 or (cDVE.v == dve0 + 4486)

        @blk3.sync
        def _(sy):
            sy.wait_ge(cCC.h, M["agB"])
            sy.wait_ge(dS.h, M["d_roped"])
            sy.wait_ge(cPE.h, M["2_pe"])

            def avob_out(u):
                sy.wait_ge(cDVE.h, d_avob(u))
                sy.dma_start(out=ag2i[128 * (u % 4):128 * (u % 4 + 1),
                                      512 * (u // 4):512 * (u // 4 + 1)],
                             in_=avob[u % 2][:, :]).then_inc(dSv[u % 2].h, 16)
                dSv[u % 2].v += 16
                assert dSv[u % 2].v == dsv0[u % 2] + 16 * (u // 2 + 1)

            for u in range(64):
                b, hh = u // 4, u % 4
                rr, co = b // 2, (b % 2) * 512
                if hh == 0:
                    if b >= 1:
                        sy.wait_ge(cPE.h, M[("3qb", 4 * (b - 1) + 3)])
                    for m in range(12):
                        sy.dma_start(
                            out=qcb[:, 512 * m:512 * (m + 1)],
                            in_=ag1oB[AGB * rr + QC_OFF + 128 * m:
                                      AGB * rr + QC_OFF + 128 * (m + 1),
                                      co:co + 512]).then_inc(dSq.h, 16)
                        dSq.v += 16
                if u >= 2:
                    sy.wait_ge(cACT.h, a_sqr(u - 2))
                    sy.wait_ge(cPE.h, M[("3sc", u - 2, NG - 1)])
                sy.dma_start(out=qra[u % 2][0:64, :],
                             in_=qrl[HPC * RD * rr + RD * hh:HPC * RD * rr + RD * (hh + 1),
                                     co:co + 512]).then_inc(dSr[u % 2].h, 16)
                dSr[u % 2].v += 16
                assert dSr[u % 2].v == dsr0[u % 2] + 16 * (u // 2 + 1)
                if u >= 2:
                    avob_out(u - 2)
            avob_out(62)
            avob_out(63)
        M["3_out"] = (dSq.v, dSv[0].v, dSv[1].v)

        @blk4.gpsimd
        def _(g):
            g.wait_ge(dSq.h, M["3_out"][0])
            g.wait_ge(dSv[0].h, M["3_out"][1])
            g.wait_ge(dSv[1].h, M["3_out"][2])
            ins = g.collective_compute("AllGather", ALU.bypass, replica_groups=RG,
                                       ins=[ag2i.opt()], outs=[ag2o.opt()])
            cCC.tick(ins)
        M["ag2"] = cCC.v

        if DEBUG:
            @blk4.sync
            def _(sy):
                sy.wait_ge(cCC.h, M["ag2"])
                for hh in range(HPC):
                    sy.dma_start(out=dbg["av"][128 * hh:128 * (hh + 1), :],
                                 in_=ag2i[128 * hh:128 * (hh + 1), :]).then_inc(dS.h, 16)
                    dS.v += 16

        for s_ in range(2):
            pe0, act0, dve0, ds0, dp0 = cPE.v, cACT.v, cDVE.v, dS.v, dP.v
            dpa0 = [dPs[0].v, dPs[1].v]
            dsx0 = [dSx[0].v, dSx[1].v]
            DVE1 = dve0

            @blk4.sync
            def _(sy, s_=s_, ds0=ds0, dve0=dve0, dSx=dSx, dsx0=dsx0):
                sy.wait_ge(cCC.h, M["ag2"])
                if s_ == 1:
                    sy.wait_ge(cPE.h, M[("4wo_last", 0)])
                    sy.wait_ge(cDVE.h, M[("4ln_n", 0, 3)])
                    sy.wait_ge(dSv[0].h, dSv[0].v)
                    sy.wait_ge(dSv[1].h, dSv[1].v)
                pid_sp = pid_holder["sp"]
                v3 = ag2o.rearrange("(a p) n -> p a n", p=128)
                sy.dma_start(out=avsl[:, :].rearrange("p (t n) -> p t n", n=512),
                             in_=v3[:, :, bass.ts(pid_sp * 2 + s_, 512)]).then_inc(dS.h, 16)
                dS.v += 16
                for m in range(32):
                    if m >= 2:
                        sy.wait_ge(cDVE.h, dve0 + 2 * (m - 2) + 1)
                    sy.dma_start(out=xres[m % 2][:, :],
                                 in_=xt[128 * m:128 * (m + 1), 512 * s_:512 * s_ + 512]
                                 ).then_inc(dSx[m % 2].h, 16)
                    dSx[m % 2].v += 16
            assert KSTOP < 40 or (dS.v == ds0 + 16)

            @blk4.tensor
            def _(t, s_=s_, act0=act0, dve0=dve0, ds0=ds0, dp0=dp0, dpa0=dpa0):
                t.wait_ge(dS.h, ds0 + 16)
                t.wait_ge(cDVE.h, dve0)
                t.wait_ge(cACT.h, act0)
                for m in range(32):
                    t.wait_ge(dPs[m % 2].h, dpa0[m % 2] + 16 * (m // 2 + 1))
                    if m >= 2:
                        t.wait_ge(cDVE.h, dve0 + 2 * (m - 2) + 2)
                    pt = pg0[:, 0:512] if m % 2 == 0 else pg0[:, 512:1024]
                    for c_ in range(32):
                        ins = t.matmul(pt, wob[m % 2][:, 128 * c_:128 * (c_ + 1)],
                                       avsl[:, 512 * c_:512 * (c_ + 1)],
                                       start=(c_ == 0), stop=(c_ == 31))
                    cPE.tick(ins)
                    M[("4wo", s_, m)] = cPE.v
                    if m >= 1:
                        mm_ = m - 1
                        t.wait_ge(cDVE.h, dve0 + 2 * mm_ + 2)
                        t.wait_ge(cACT.h, act0 + mm_ + 1)
                        sq_ = sqzb if mm_ % 2 == 0 else sqzb2
                        t.matmul(pav[0][0:1, 0:512], c_ones_bf, zbf[:, 512 * mm_:512 * (mm_ + 1)],
                                 start=(mm_ == 0), stop=False)
                        ins = t.matmul(pav[1][0:1, 0:512], c_ones_bf, sq_,
                                       start=(mm_ == 0), stop=False)
                        cPE.tick(ins)
                        M[("4sq", s_, mm_)] = cPE.v
                M[("4wo_last", s_)] = cPE.v
                mm_ = 31
                t.wait_ge(cDVE.h, dve0 + 2 * mm_ + 2)
                t.wait_ge(cACT.h, act0 + mm_ + 1)
                sq_ = sqzb if mm_ % 2 == 0 else sqzb2
                t.matmul(pav[0][0:1, 0:512], c_ones_bf, zbf[:, 512 * mm_:512 * (mm_ + 1)],
                         start=False, stop=True)
                ins = t.matmul(pav[1][0:1, 0:512], c_ones_bf, sqzb2, start=False, stop=True)
                cPE.tick(ins)
                M[("4sq", s_, 31)] = cPE.v
                M[("4st", s_)] = cPE.v
                t.wait_ge(cDVE.h, dve0 + 75)
                t.matmul(pms, c_ones1r, row_r[0:1, 0:512], start=True, stop=True)
                t.matmul(psp, c_ones1r, row_r[0:1, 512:1024], start=True, stop=True)
                ins = t.matmul(pg1[:, 512:1024], c_ones1r, row_r2[0:1, 0:512], start=True, stop=True)
                cPE.tick(ins)
                M[("4bc", s_)] = cPE.v
                t.wait_ge(dP.h, dp0 + 16)
                t.wait_ge(cDVE.h, dve0 + 78)
                for c_ in range(32):
                    t.matmul(pg1[0:64, 0:512], wrtb[:, 64 * c_:64 * (c_ + 1)],
                             zh[:, 512 * c_:512 * (c_ + 1)],
                             start=(c_ == 0), stop=(c_ == 31))
                    ins = t.matmul(pav[0][0:64, 0:1], wrtb[:, 64 * c_:64 * (c_ + 1)], c_ones_f,
                                   start=(c_ == 0), stop=(c_ == 31))
                cPE.tick(ins)
                M[("4rt", s_)] = cPE.v
                for m in range(8):
                    t.wait_ge(dPs[m % 2].h, dpa0[m % 2] + 16 * (16 + m // 2 + 1))
                    if m >= 2:
                        t.wait_ge(cDVE.h, dve0 + 84 + 3 * (m - 2))
                    if m >= 1:
                        t.wait_ge(cDVE.h, dve0 + 82 + 3 * (m - 1))
                    pt = pg0[:, 0:512] if m % 2 == 0 else pg0[:, 512:1024]
                    for c_ in range(32):
                        t.matmul(pt, atb[m % 2][:, 128 * c_:128 * (c_ + 1)],
                                 zbf[:, 512 * c_:512 * (c_ + 1)],
                                 start=(c_ == 0), stop=(c_ == 31))
                        ins = t.matmul(pav[1][:, m:m + 1],
                                       atb[m % 2][:, 128 * c_:128 * (c_ + 1)], c_ones_bf,
                                       start=(c_ == 0), stop=(c_ == 31))
                    cPE.tick(ins)
                    M[("4h", s_, m)] = cPE.v

            @blk4.gpsimd
            def _(g, s_=s_, dpa0=dpa0):
                for m in range(32):
                    if m >= 2:
                        g.wait_ge(cPE.h, M[("4wo", s_, m - 2)])
                    elif s_ == 1:
                        g.wait_ge(cPE.h, M[("4B", 0, 31)])
                    g.dma_start(out=wob[m % 2][:, :],
                                in_=wo[:, 128 * m:128 * (m + 1)].rearrange("(t p) m -> p t m", p=128)
                                ).then_inc(dPs[m % 2].h, 16)
                    dPs[m % 2].v += 16
                    assert dPs[m % 2].v == dpa0[m % 2] + 16 * (m // 2 + 1)
                g.dma_start(out=wrtb[:, :],
                            in_=wrt.rearrange("(t p) m -> p t m", p=128)).then_inc(dP.h, 16)
                dP.v += 16
                for m in range(8):
                    if m >= 2:
                        g.wait_ge(cPE.h, M[("4h", s_, m - 2)])
                    else:
                        g.wait_ge(cPE.h, M[("4wo", s_, 31)])
                    g.dma_start(out=atb[m % 2][:, :],
                                in_=at[:, 128 * m:128 * (m + 1)].rearrange("(t p) m -> p t m", p=128)
                                ).then_inc(dPs[m % 2].h, 16)
                    dPs[m % 2].v += 16
                    assert dPs[m % 2].v == dpa0[m % 2] + 16 * (16 + m // 2 + 1)
            @blk4.scalar
            def _(s, s_=s_, dve0=dve0):
                for m in range(32):
                    s.wait_ge(cDVE.h, dve0 + 2 * m + 2)
                    if m >= 2:
                        s.wait_ge(cPE.h, M[("4sq", s_, m - 2)])
                    sq_ = sqzb if m % 2 == 0 else sqzb2
                    cACT.tick(s.activation(sq_, zbf[:, 512 * m:512 * (m + 1)], AF.Square))
                s.wait_ge(cDVE.h, dve0 + 71)
                cACT.tick(s.activation(row_t[0:1, 0:512], row_k[0:1, 0:512], AF.Sqrt))
            M[("4act1", s_)] = cACT.v

            @blk4.vector
            def _(v, s_=s_, act0=act0, dve0=dve0, dsx0=dsx0,
                  dsv_tot=(dSv[0].v, dSv[1].v)):
                if s_ == 1:
                    v.wait_ge(dSv[0].h, dsv_tot[0])
                    v.wait_ge(dSv[1].h, dsv_tot[1])
                for m in range(32):
                    v.wait_ge(cPE.h, M[("4wo", s_, m)])
                    v.wait_ge(cDVE.h, cDVE.v)
                    v.wait_ge(dSx[m % 2].h, dsx0[m % 2] + 16 * (m // 2 + 1))
                    pt = pg0[:, 0:512] if m % 2 == 0 else pg0[:, 512:1024]
                    cDVE.tick(v.tensor_tensor(zh[:, 512 * m:512 * (m + 1)], pt,
                                              xres[m % 2][:, :], ALU.add))
                    v.wait_ge(cDVE.h, cDVE.v)
                    cDVE.tick(v.tensor_copy(zbf[:, 512 * m:512 * (m + 1)],
                                            zh[:, 512 * m:512 * (m + 1)]))
                v.wait_ge(cPE.h, M[("4st", s_)])
                cDVE.tick(v.tensor_scalar(row_q[0:1, 512:1024], pav[0][0:1, 0:512],
                                          -1.0 / D, None, ALU.mult))
                v.wait_ge(cDVE.h, cDVE.v)
                with lp():
                    cDVE.tick(v.tensor_copy(row_r[0:1, 512:1024], row_q[0:1, 512:1024]))
                cDVE.tick(v.tensor_scalar(row_t[0:1, 0:512], pav[0][0:1, 0:512],
                                          1.0 / D, None, ALU.mult))
                cDVE.tick(v.tensor_scalar(row_k[0:1, 0:512], pav[1][0:1, 0:512],
                                          1.0 / D, None, ALU.mult))
                v.wait_ge(cDVE.h, cDVE.v)
                cDVE.tick(v.tensor_mul(row_t[0:1, 512:1024], row_t[0:1, 0:512],
                                       row_t[0:1, 0:512]))
                v.wait_ge(cDVE.h, cDVE.v)
                cDVE.tick(v.tensor_sub(row_k[0:1, 0:512], row_k[0:1, 0:512],
                                       row_t[0:1, 512:1024]))
                v.wait_ge(cDVE.h, cDVE.v)
                cDVE.tick(v.tensor_scalar(row_k[0:1, 0:512], row_k[0:1, 0:512],
                                          EPS_LN, None, ALU.add))
                v.wait_ge(cACT.h, M[("4act1", s_)])
                cDVE.tick(v.reciprocal(row_q[0:1, 0:512], row_t[0:1, 0:512]))
                v.wait_ge(cDVE.h, cDVE.v)
                with lp():
                    cDVE.tick(v.tensor_copy(row_r[0:1, 0:512], row_q[0:1, 0:512]))
                cDVE.tick(v.tensor_scalar(row_k[0:1, 512:1024], row_t[0:1, 0:512],
                                          2.0, None, ALU.mult))
                v.wait_ge(cDVE.h, cDVE.v)
                with lp():
                    cDVE.tick(v.tensor_copy(row_r2[0:1, 0:512], row_k[0:1, 512:1024]))
                v.wait_ge(cPE.h, M[("4bc", s_)])
                cDVE.tick(v.tensor_copy(rstdb[:, :], pms))
                cDVE.tick(v.tensor_copy(negmub[:, :], psp))
                cDVE.tick(v.tensor_copy(scl2b[:, :], pg1[:, 512:1024]))
                v.wait_ge(cPE.h, M[("4rt", s_)])
                cDVE.tick(v.tensor_copy(cwsb[0:64, 0:1], pav[0][0:64, 0:1]))
                v.wait_ge(cDVE.h, cDVE.v)
                cDVE.tick(v.scalar_tensor_tensor(hscr[0:64, :], negmub[0:64, :],
                                                 cwsb[0:64, 0:1], pg1[0:64, 0:512],
                                                 ALU.mult, ALU.add))
                v.wait_ge(cDVE.h, cDVE.v)
                cDVE.tick(v.tensor_mul(rrsb[0:64, :], hscr[0:64, :], rstdb[0:64, :]))
                for m in range(8):
                    v.wait_ge(cPE.h, M[("4h", s_, m)])
                    v.wait_ge(cDVE.h, cDVE.v)
                    cDVE.tick(v.tensor_copy(casb[:, m:m + 1], pav[1][:, m:m + 1]))
                    v.wait_ge(cDVE.h, cDVE.v)
                    pt = pg0[:, 0:512] if m % 2 == 0 else pg0[:, 512:1024]
                    cDVE.tick(v.scalar_tensor_tensor(hscr[:, :], negmub[:, :],
                                                     casb[:, m:m + 1], pt,
                                                     ALU.mult, ALU.add))
                    v.wait_ge(cDVE.h, cDVE.v)
                    cDVE.tick(v.tensor_mul(hbf[:, 512 * m:512 * (m + 1)], hscr[:, :],
                                           rstdb[:, :]))
            assert KSTOP < 40 or (cDVE.v == dve0 + 105)
            M[("4dve1", s_)] = cDVE.v

            @blk4.tensor
            def _(t, s_=s_):
                t.wait_ge(cDVE.h, M[("4dve1", s_)] - 24)
                for j in range(4):
                    ins = t.transpose(pms[:, 64 * j:64 * (j + 1)],
                                      rrsb[0:64, 128 * j:128 * (j + 1)], c_ident[0:64, 0:64])
                    cPE.tick(ins)
                M[("4tr", s_)] = cPE.v

            @blk4.vector
            def _(v, s_=s_):
                v.wait_ge(cPE.h, M[("4tr", s_)])
                v.wait_ge(cDVE.h, cDVE.v)
                for j in range(4):
                    cDVE.tick(v.tensor_copy(rsrow[:, 64 * j:64 * (j + 1)], pms[:, 64 * j:64 * (j + 1)]))
                for j in range(4):
                    sl = slice(64 * j, 64 * j + 64)
                    v.wait_ge(cDVE.h, cDVE.v)
                    cDVE.tick(v.max(out=mx16[:, 0:8], in_=rsrow[:, sl]))
                    v.wait_ge(cDVE.h, cDVE.v)
                    cDVE.tick(v.match_replace(out=zap[:, sl], in_to_replace=mx16[:, 0:8],
                                              in_values=rsrow[:, sl], imm_value=-3e38))
                    v.wait_ge(cDVE.h, cDVE.v)
                    cDVE.tick(v.max(out=mx16[:, 8:16], in_=zap[:, sl]))
                    v.wait_ge(cDVE.h, cDVE.v)
                    cDVE.tick(v.match_replace(out=zap[:, sl], in_to_replace=mx16[:, 8:16],
                                              in_values=zap[:, sl], imm_value=-3e38))
                    v.wait_ge(cDVE.h, cDVE.v)
                    cDVE.tick(v.tensor_scalar(colr[:, j:j + 1], mx16[:, 0:1], -1.0, None, ALU.mult))
                    v.wait_ge(cDVE.h, cDVE.v)
                    M[("4tk", s_, j)] = cDVE.v
            M[("4tk_all", s_)] = cDVE.v

            @blk4.scalar
            def _(s, s_=s_):
                for j in range(4):
                    sl = slice(64 * j, 64 * j + 64)
                    s.wait_ge(cDVE.h, M[("4tk", s_, j)])
                    cACT.tick(s.activation(gvals[:, sl], rsrow[:, sl], AF.Exp,
                                           bias=colr[:, j:j + 1]))
                    M[("4ex", s_, j)] = cACT.v

            @blk4.vector
            def _(v, s_=s_):
                for j in range(4):
                    sl = slice(64 * j, 64 * j + 64)
                    v.wait_ge(cACT.h, M[("4ex", s_, j)])
                    v.wait_ge(cDVE.h, cDVE.v)
                    cDVE.tick(v.tensor_scalar(rsrow[:, sl], zap[:, sl], -1e37, None, ALU.is_le))
                    v.wait_ge(cDVE.h, cDVE.v)
                    cDVE.tick(v.tensor_mul(gvals[:, sl], gvals[:, sl], rsrow[:, sl]))
                    v.wait_ge(cDVE.h, cDVE.v)
                    cDVE.tick(v.reduce_sum(colr[:, 4:5], gvals[:, sl], axis=AX.X))
                    v.wait_ge(cDVE.h, cDVE.v)
                    cDVE.tick(v.reciprocal(colr[:, 5:6], colr[:, 4:5]))
                    v.wait_ge(cDVE.h, cDVE.v)
                    cDVE.tick(v.tensor_scalar(gvals[:, sl], gvals[:, sl], colr[:, 5:6],
                                              None, ALU.mult))
                    v.wait_ge(cDVE.h, cDVE.v)
            M[("4g", s_)] = cDVE.v

            @blk4.tensor
            def _(t, s_=s_):
                t.wait_ge(cDVE.h, M[("4g", s_)])
                for j in range(4):
                    ins = t.transpose(psp[0:64, 128 * j:128 * (j + 1)],
                                      gvals[:, 64 * j:64 * (j + 1)], c_ident[:, :])
                    cPE.tick(ins)
                M[("4gt", s_)] = cPE.v

            @blk4.vector
            def _(v, s_=s_):
                v.wait_ge(cPE.h, M[("4gt", s_)])
                v.wait_ge(cDVE.h, cDVE.v)
                for j in range(4):
                    ins = v.tensor_mul(gtb[0:64, 128 * j:128 * (j + 1)],
                                       psp[0:64, 128 * j:128 * (j + 1)],
                                       scl2b[0:64, 128 * j:128 * (j + 1)])
                    cDVE.tick(ins)
            M[("4gtb", s_)] = cDVE.v

            @blk4.tensor
            def _(t, s_=s_):
                t.wait_ge(cDVE.h, M[("4gtb", s_)])
                for j8 in range(8):
                    if j8 >= 2:
                        t.wait_ge(cDVE.h, M[("4gtb", s_)] + (j8 - 2) + 1)
                    pt = pg1[:, 0:512] if j8 % 2 == 0 else pg1[:, 512:1024]
                    ins = t.matmul(pt, c_emat[0:64, 128 * j8:128 * (j8 + 1)],
                                   gtb[0:64, 0:512], start=True, stop=True)
                    cPE.tick(ins)
                    M[("4ge", s_, j8)] = cPE.v

            @blk4.vector
            def _(v, s_=s_):
                for j8 in range(8):
                    v.wait_ge(cPE.h, M[("4ge", s_, j8)])
                    v.wait_ge(cDVE.h, cDVE.v)
                    pt = pg1[:, 0:512] if j8 % 2 == 0 else pg1[:, 512:1024]
                    ins = v.tensor_tensor(hgbf[:, 512 * j8:512 * (j8 + 1)],
                                          hbf[:, 512 * j8:512 * (j8 + 1)], pt, ALU.mult)
                    cDVE.tick(ins)
                    M[("4hg", s_, j8)] = cDVE.v
                    assert M[("4hg", s_, j8)] == M[("4gtb", s_)] + j8 + 1
            M[("4hg_all", s_)] = cDVE.v

            dve_B0 = cDVE.v

            @blk4.tensor
            def _(t, s_=s_, dpa0=dpa0, dve0=dve0, dve_B0=dve_B0):
                t.wait_ge(cDVE.h, M[("4hg_all", s_)])
                for m in range(32):
                    t.wait_ge(dPs[m % 2].h, dpa0[m % 2] + 16 * (20 + m // 2 + 1))
                    if m >= 2:
                        t.wait_ge(cDVE.h, dve_B0 + (m - 2) + 1)
                    pt = pg0[:, 0:512] if m % 2 == 0 else pg0[:, 512:1024]
                    for kr in range(8):
                        ins = t.matmul(pt, btb[m % 2][:, 128 * kr:128 * (kr + 1)],
                                       hgbf[:, 512 * kr:512 * (kr + 1)],
                                       start=(kr == 0), stop=(kr == 7))
                    cPE.tick(ins)
                    M[("4B", s_, m)] = cPE.v

            @blk4.gpsimd
            def _(g, s_=s_, dpa0=dpa0):
                for m in range(32):
                    if m >= 2:
                        g.wait_ge(cPE.h, M[("4B", s_, m - 2)])
                    else:
                        g.wait_ge(cPE.h, M[("4h", s_, 7)])
                    g.dma_start(out=btb[m % 2][:, :],
                                in_=bt[:, 128 * m:128 * (m + 1)].rearrange("(t p) m -> p t m", p=128)
                                ).then_inc(dPs[m % 2].h, 16)
                    dPs[m % 2].v += 16
                    assert dPs[m % 2].v == dpa0[m % 2] + 16 * (20 + m // 2 + 1)

            @blk4.vector
            def _(v, s_=s_, dve_B0=dve_B0):
                for m in range(32):
                    v.wait_ge(cPE.h, M[("4B", s_, m)])
                    v.wait_ge(cDVE.h, cDVE.v)
                    pt = pg0[:, 0:512] if m % 2 == 0 else pg0[:, 512:1024]
                    ins = v.tensor_tensor(zh[:, 512 * m:512 * (m + 1)], pt,
                                          zh[:, 512 * m:512 * (m + 1)], ALU.add)
                    cDVE.tick(ins)
                    M[("4v", s_, m)] = cDVE.v
                    assert cDVE.v == dve_B0 + m + 1
            M[("4v_all", s_)] = cDVE.v

            if DEBUG and s_ == 0:
                @blk4.sync
                def _(sy):
                    sy.wait_ge(cDVE.h, M[("4v_all", 0)])
                    sy.dma_start(out=dbg["z"], in_=zh).then_inc(dS.h, 16)
                    sy.dma_start(out=dbg["gg"], in_=gvals).then_inc(dS.h, 16)
                    sy.dma_start(out=dbg["hg"], in_=hgbf).then_inc(dS.h, 16)
                    dS.v += 48

            for q in range(4):
                dve_q0 = cDVE.v

                @blk4.tensor
                def _(t, s_=s_, q=q, dve_q0=dve_q0):
                    t.wait_ge(cDVE.h, M[("4v_all", s_)])
                    if q >= 1:
                        t.wait_ge(cDVE.h, M[("4ln_e", s_, q - 1, 7)])
                    if q >= 2:
                        t.wait_ge(cDVE.h, M[("4ln_n", s_, q - 2)])
                    for c_ in range(32):
                        grp = c_ // 4
                        pt = pms if grp % 2 == 0 else psp
                        if c_ % 4 == 0 and c_ >= 8:
                            t.wait_ge(cDVE.h, dve_q0 + (grp - 2) + 1)
                        ins = t.transpose(pt[:, 128 * (c_ % 4):128 * (c_ % 4 + 1)],
                                          zh[:, 512 * c_ + 128 * q:512 * c_ + 128 * q + 128],
                                          c_ident[:, :])
                        cPE.tick(ins)
                        M[("4ln_t", s_, q, c_)] = cPE.v

                @blk4.vector
                def _(v, s_=s_, q=q, dve_q0=dve_q0):
                    for grp in range(8):
                        v.wait_ge(cPE.h, M[("4ln_t", s_, q, 4 * grp + 3)])
                        v.wait_ge(cDVE.h, cDVE.v)
                        pt = pms if grp % 2 == 0 else psp
                        ins = v.tensor_copy(vTq[q % 2][:, 512 * grp:512 * (grp + 1)], pt[:, 0:512])
                        cDVE.tick(ins)
                        M[("4ln_e", s_, q, grp)] = cDVE.v
                        assert cDVE.v == dve_q0 + grp + 1
                    v.wait_ge(cDVE.h, cDVE.v)
                    for jj in range(8):
                        ins = v.bn_stats(bnst[:, 6 * jj:6 * jj + 6],
                                         vTq[q % 2][:, 512 * jj:512 * (jj + 1)])
                    cDVE.tick(ins)
                    v.wait_ge(cDVE.h, cDVE.v)
                    cDVE.tick(v.bn_aggr(colr[:, 4:6], bnst[:, 0:48]))
                    v.wait_ge(cDVE.h, cDVE.v)
                    cDVE.tick(v.tensor_scalar(colr[:, 6:7], colr[:, 5:6], EPS_LN, None, ALU.add))
                    v.wait_ge(cDVE.h, cDVE.v)
                    cDVE.tick(v.tensor_scalar(colr[:, 4:5], colr[:, 4:5], -1.0, None, ALU.mult))
                M[("4bn", s_, q)] = cDVE.v

                @blk4.scalar
                def _(s, s_=s_, q=q):
                    s.wait_ge(cDVE.h, M[("4bn", s_, q)])
                    cACT.tick(s.activation(colr[:, 7:8], colr[:, 6:7], AF.Sqrt))
                M[("4sd2", s_, q)] = cACT.v

                @blk4.vector
                def _(v, s_=s_, q=q, ds0=ds0):
                    v.wait_ge(cACT.h, M[("4sd2", s_, q)])
                    cDVE.tick(v.reciprocal(colr[:, 6:7], colr[:, 7:8]))
                    v.wait_ge(cDVE.h, cDVE.v)
                    if q >= 2:
                        v.wait_ge(M[("4out", s_, q - 2)][0].h, M[("4out", s_, q - 2)][1])
                    ins = v.tensor_scalar(vTo[q % 2][:, :], vTq[q % 2][:, :],
                                          colr[:, 4:5], colr[:, 6:7], ALU.add, ALU.mult)
                    cDVE.tick(ins)
                    M[("4ln_n", s_, q)] = cDVE.v

                @blk4.sync
                def _(sy, s_=s_, q=q):
                    sy.wait_ge(cDVE.h, M[("4ln_n", s_, q)])
                    sy.dma_start(out=out[512 * s_ + 128 * q:512 * s_ + 128 * (q + 1), :],
                                 in_=vTo[q % 2][:, :]).then_inc(dSv[q % 2].h, 16)
                    dSv[q % 2].v += 16
                    M[("4out", s_, q)] = (dSv[q % 2], dSv[q % 2].v)

        @blk.sync
        def _(sy):
            for c in [dS, dP, dSq] + dPs + dSk + dSr + dSv + dSx:
                sy.wait_ge(c.h, c.v)

        return nc
    return nc


def kernel(**inputs):
    per_core = _host_layout(inputs)
    nc = build_nc()
    res = run_bass_kernel_spmd(nc, per_core, core_ids=list(range(NCORES)))
    outs = [res.results[c]["out"] for c in range(NCORES)]
    return np.concatenate(outs, axis=0).reshape(1, S, D)


# version 15 (final)
# speedup vs baseline: 1.2644x; 1.0050x over previous
import os
import numpy as np
from contextlib import ExitStack

import concourse.bass as bass
import concourse.mybir as mybir
from concourse.bass_utils import run_bass_kernel_spmd

F32 = mybir.dt.float32
F32R = mybir.dt.float32r
BF16 = mybir.dt.bfloat16
AF = mybir.ActivationFunctionType
ALU = mybir.AluOpType
AX = mybir.AxisListType
ET = mybir.EngineType

S, D, NH = 8192, 4096, 32
QR, KVR, RD, ND, VD = 1536, 512, 64, 128, 128
QK = ND + RD
NCORES, HPC, SQC = 8, 4, 1024
EPS_RMS, EPS_LN = 1e-6, 1e-5
INV_SQRT_QK = float(QK ** (-0.5))

QC_OFF, QROPE_OFF = 0, QR
KC_OFF = QR + NH * RD
KR_OFF = KC_OFF + KVR
AG1R = KR_OFF + RD

DEBUG = False
_KS = os.environ.get("KSTOP", "4")
KSTOP = {"0": 10, "1a": 11, "1b": 12, "1": 13, "1c": 13, "2": 20, "3": 30, "4": 40}[_KS]


class _SkipBlk:
    def __getattr__(self, name):
        return lambda f: None


def _host_layout(inputs):
    x = np.ascontiguousarray(inputs["L0_attn_hidden_states"][0])
    xT = np.ascontiguousarray(x.T)
    w_qa = np.ascontiguousarray(inputs["L0_attn_w_qa"])
    g_qa = np.ascontiguousarray(inputs["L0_attn_g_qa"])
    w_qb = np.ascontiguousarray(inputs["L0_attn_w_qb"])
    w_kva = np.ascontiguousarray(inputs["L0_attn_w_kva"])
    g_kva = np.ascontiguousarray(inputs["L0_attn_g_kva"])
    w_kvb = np.ascontiguousarray(inputs["L0_attn_w_kvb"])
    w_o = np.ascontiguousarray(inputs["L0_attn_w_o"])
    A = np.ascontiguousarray(inputs["L0_ffn_A"])
    B = np.ascontiguousarray(inputs["L0_ffn_B"])
    Wr = np.ascontiguousarray(inputs["L0_ffn_Wr"])

    wq3 = w_qb.reshape(QR, NH, QK)
    wqb_rope = np.ascontiguousarray(wq3[:, :, ND:].reshape(QR, NH * RD))
    wk3 = w_kvb.reshape(KVR, NH, ND + VD)

    A_t = np.ascontiguousarray(A.transpose(2, 0, 1).reshape(D, 1024))
    B_t = np.ascontiguousarray(B.transpose(0, 2, 1).reshape(1024, D))
    WrT = np.ascontiguousarray(Wr.T)
    gqa_t = np.ascontiguousarray(g_qa.reshape(QR // 128, 128).T)
    gkva_t = np.ascontiguousarray(g_kva.reshape(KVR // 128, 128).T)

    inv_freq = 1.0 / (10000.0 ** (np.arange(0, RD, 2, dtype=np.float64) / RD))
    t = np.arange(S, dtype=np.float64)
    freqs = np.outer(t, inv_freq)
    emb = np.concatenate([freqs, freqs], axis=-1)
    cosT = np.ascontiguousarray(np.cos(emb).T.astype(np.float32))
    sinT = np.sin(emb).T.astype(np.float32)
    sinTs = sinT.copy()
    sinTs[: RD // 2] *= -1.0
    sinTs = np.ascontiguousarray(sinTs)

    emat = np.zeros((64, 1024), dtype=np.float32)
    for e_ in range(64):
        emat[e_, 16 * e_:16 * e_ + 16] = 1.0
    ident = np.eye(128, dtype=np.float32)

    per_core = []
    for c in range(NCORES):
        sl = slice(SQC * c, SQC * (c + 1))
        heads = list(range(HPC * c, HPC * (c + 1)))
        per_core.append(dict(
            xt=np.ascontiguousarray(xT[:, sl]),
            wqa=w_qa,
            wqbr=wqb_rope,
            wqbn=np.ascontiguousarray(np.concatenate([wq3[:, h, :ND] for h in heads], 1)),
            wkva=w_kva,
            wkn=np.ascontiguousarray(np.concatenate([wk3[:, h, :ND] for h in heads], 1)),
            wkv=np.ascontiguousarray(np.concatenate([wk3[:, h, ND:] for h in heads], 1)),
            wo=w_o, at=A_t, bt=B_t, wrt=WrT,
            gqa=gqa_t, gkva=gkva_t,
            coss=np.ascontiguousarray(cosT[:, sl]),
            sins=np.ascontiguousarray(sinTs[:, sl]),
            emat=emat, ident=ident,
        ))
    return per_core


class Ctr:
    def __init__(self, handle):
        self.h = handle
        self.v = 0

    def tick(self, instr, n=1):
        self.v += n
        instr.then_inc(self.h, n)
        return self.v


def build_nc():
    nc = bass.Bass("TRN2", target_bir_lowering=False)

    def din(name, shape):
        return nc.dram_tensor(name, shape, F32, kind="ExternalInput").ap()

    xt = din("xt", [D, SQC])
    wqa = din("wqa", [D, QR])
    wqbr = din("wqbr", [QR, NH * RD])
    wqbn = din("wqbn", [QR, HPC * ND])
    wkva = din("wkva", [D, KVR + RD])
    wkn = din("wkn", [KVR, HPC * ND])
    wkv = din("wkv", [KVR, HPC * VD])
    wo = din("wo", [NH * VD, D])
    at = din("at", [D, 1024])
    bt = din("bt", [1024, D])
    wrt = din("wrt", [D, 64])
    gqa = din("gqa", [128, QR // 128])
    gkva = din("gkva", [128, KVR // 128])
    coss = din("coss", [RD, SQC])
    sins = din("sins", [RD, SQC])
    emat = din("emat", [64, 1024])
    ident = din("ident", [128, 128])
    out = nc.dram_tensor("out", [SQC, D], F32, kind="ExternalOutput").ap()

    AGA, AGB = AG1R - KC_OFF, KC_OFF
    ag1i = nc.dram_tensor("ag1_in", [AG1R, SQC], BF16, kind="Internal").ap()
    ag1oA = nc.dram_tensor("ag1_outA", [NCORES * AGA, SQC], BF16, kind="Internal", addr_space="Shared").ap()
    ag1oB = nc.dram_tensor("ag1_outB", [NCORES * AGB, SQC], BF16, kind="Internal", addr_space="Shared").ap()
    ag2i = nc.dram_tensor("ag2_in", [HPC * VD, S], BF16, kind="Internal").ap()
    qrl = nc.dram_tensor("qrl", [NCORES * HPC * RD, SQC], BF16, kind="Internal").ap()
    wo_bf = nc.dram_tensor("wo_bf", [4096, 4096], BF16, kind="Internal").ap()
    ag2o = nc.dram_tensor("ag2_out", [NH * VD, S], BF16, kind="Internal", addr_space="Shared").ap()

    dbg = {}
    if DEBUG:
        def dout(name, shape, dt=F32):
            return nc.dram_tensor(name, shape, dt, kind="ExternalOutput").ap()
        dbg["qc"] = dout("dbg_qc", [12 * 128, SQC])
        dbg["kc"] = dout("dbg_kc", [4 * 128, SQC])
        dbg["kr"] = dout("dbg_kr", [64, SQC])
        dbg["qrope"] = dout("dbg_qrope", [NH * RD, SQC])
        dbg["kn"] = dout("dbg_kn", [128, HPC * S], BF16)
        dbg["v4"] = dout("dbg_v4", [128, 64 * 512], BF16)
        dbg["av"] = dout("dbg_av", [HPC * VD, S])
        dbg["z"] = dout("dbg_z", [128, 32 * 512])
        dbg["gg"] = dout("dbg_gg", [128, 4 * 64])
        dbg["hg"] = dout("dbg_hg", [128, 8 * 512])

    with ExitStack() as ctx:
        e = ctx.enter_context

        ARENA = 196608
        arena = nc.alloc_sbuf_tensor("arena", [128, ARENA], mybir.dt.uint8)
        arena_ap = arena.ap()

        def sb(name, shape, dtype, off):
            assert off % 32 == 0, name
            nb = shape[-1] * mybir.dt.size(dtype)
            assert off + nb <= ARENA, (name, off + nb)
            return arena_ap[0:shape[0], off:off + nb].bitcast(dtype)

        OA, OB, OC = 0, 65536, 131072
        xtb = sb("xtb", [128, 32 * 1024], BF16, OA)
        kn4 = sb("kn4", [128, HPC * S], BF16, OA)
        zh = sb("zh", [128, 32 * 512], F32, OA)
        qcpre = sb("qcpre", [128, 12 * 1024], F32, OB)
        kctb = sb("kctb", [128, 4 * 1024], BF16, OB)
        krb = sb("krb", [64, 1024], BF16, OB + 8192)
        qrb = [sb(f"qrb{i}", [64, 1024], BF16, OB + 10240 + 2048 * i) for i in range(2)]
        ropescr = sb("ropescr", [64, 1024], F32, OB + 14336)
        ropescr2 = sb("ropescr2", [64, 1024], F32, OB + 18432)
        p1wb = [sb(f"p1wb{i}", [128, 32 * 128], BF16, OB + 49152 + 8192 * i) for i in range(2)]
        v4 = sb("v4", [128, 64 * 512], BF16, OB)
        avsl = sb("avsl", [128, 32 * 512], BF16, OB)
        zbf = sb("zbf", [128, 32 * 512], BF16, OB + 32768)
        hbf = sb("hbf", [128, 8 * 512], BF16, OB)
        hgbf = sb("hgbf", [128, 8 * 512], BF16, OB + 8192)
        atb = [sb(f"atb{i}", [128, 32 * 128], BF16, OB + 16384 + 8192 * i) for i in range(2)]
        btb = [sb(f"btb{i}", [128, 8 * 128], BF16, OB + 16384 + 2048 * i) for i in range(2)]
        vTq = [sb(f"vTq{i}", [128, 4096], F32, OB + 16384 * i) for i in range(2)]
        vTo = [sb(f"vTo{i}", [128, 4096], F32, OB + 32768 + 16384 * i) for i in range(2)]
        row_q = sb("row_q", [1, 1024], F32, OC)
        row_k = sb("row_k", [1, 1024], F32, OC + 4096)
        row_t = sb("row_t", [1, 1024], F32, OC + 8192)
        row_r = None
        row_r2 = None
        q2row = sb("q2row", [1, 512], F32, OC + 18432)
        CC = OC + 20480
        rbsb = sb("rbsb", [128, 512], F32, OC + 2048)
        avob = [sb(f"avob{i}", [128, 512], BF16, OC + 4096 + 1024 * i) for i in range(2)]
        qnb = [sb("qnb", [128, 512], BF16, OC + 6144),
               sb("qnb2", [128, 512], BF16, OC + 17408)]
        qra = [sb(f"qra{i}", [65, 512], BF16, OC + 7168 + 7168 * i) for i in range(1)] + \
              [sb("qra1", [65, 512], BF16, OC + 14336)]
        accd = sb("accd", [128, 512], F32, OC + 10240)
        sscr = sb("sscr", [128, 512], BF16, OC + 16384)
        sscr2 = sb("sscr2", [64, 512], BF16, OC + 15360)
        hscr = sb("hscr", [128, 512], F32, OC + 10240)
        p1sq = sb("p1sq", [128, 512], BF16, OC)
        p1sq2 = sb("p1sq2", [128, 512], BF16, OC + 2048)
        qsb = sb("qsb", [128, 1024], F32, OC + 4096)
        kvpre = sb("kvpre", [128, 5 * 1024], F32, CC)
        qctb = sb("qctb", [128, 12 * 1024], BF16, CC + 20480)
        roped = sb("roped", [65, S], BF16, CC)
        qcb = sb("qcb", [128, 12 * 512], BF16, CC + 16384)
        wqbnb = sb("wqbnb", [128, 12 * 512], BF16, CC + 28672)
        pbuf = [sb(f"pbuf{i}", [128, 1024], BF16, CC + 40960 + 2048 * i) for i in range(2)]
        kcw = [sb(f"kcw{i}", [128, 4 * 512], BF16, CC + 16384 + 4096 * i) for i in range(2)]
        wknb = sb("wknb", [128, 4 * 512], BF16, CC + 24576)
        wkvb_ = sb("wkvb_", [128, 4 * 512], BF16, CC + 28672)
        wob = [sb(f"wob{i}", [128, 32 * 128], BF16, CC + 8192 * i) for i in range(2)]
        xres = [sb(f"xres{i}", [128, 512], F32, CC + 16384 + 2048 * i) for i in range(2)]
        wrtb = sb("wrtb", [128, 32 * 64], F32, CC + 20480)
        rrsb = sb("rrsb", [64, 512], F32, CC + 28672)
        rsrow = sb("rsrow", [128, 4 * 64], F32, CC + 30720)
        zap = sb("zap", [128, 4 * 64], F32, CC + 31744)
        gvals = sb("gvals", [128, 4 * 64], F32, CC + 32768)
        mx16 = sb("mx16", [128, 16], F32, CC + 33792)
        colr = sb("colr", [128, 8], F32, CC + 33856)
        gtb = sb("gtb", [64, 512], BF16, CC + 34048)
        casb = sb("casb", [128, 8], F32, CC + 35072)
        cwsb = sb("cwsb", [64, 1], F32, CC + 35104)
        rstdb = sb("rstdb", [128, 512], F32, CC + 35840)
        scl2b = sb("scl2b", [128, 512], F32, CC + 37888)
        negmub = sb("negmub", [128, 512], F32, CC + 39936)
        sqzb = sb("sqzb", [128, 512], BF16, CC + 41984)
        sqzb2 = sb("sqzb2", [128, 512], BF16, CC + 43008)
        bnst = sb("bnst", [128, 64], F32, CC + 44032)

        row_r = e(nc.sbuf_tensor("row_r", [1, 1024], F32R)).ap()
        row_r2 = e(nc.sbuf_tensor("row_r2", [1, 512], F32R)).ap()
        c_ident = e(nc.sbuf_tensor("c_ident", [128, 128], F32)).ap()
        c_emat = e(nc.sbuf_tensor("c_emat", [64, 1024], BF16)).ap()
        c_gqa = e(nc.sbuf_tensor("c_gqa", [128, 12], F32)).ap()
        c_gkva = e(nc.sbuf_tensor("c_gkva", [128, 4], F32)).ap()
        c_ones_bf = e(nc.sbuf_tensor("c_ones_bf", [128, 1], BF16)).ap()
        c_ones_r = e(nc.sbuf_tensor("c_ones_r", [128, 1], F32R)).ap()
        c_ones1r = e(nc.sbuf_tensor("c_ones1r", [1, 128], F32R)).ap()
        c_ones_f = e(nc.sbuf_tensor("c_ones_f", [128, 1], F32)).ap()
        c_cos = sb("c_cos", [64, 1024], F32, OB + 22528)
        c_sin = sb("c_sin", [64, 1024], F32, OB + 26624)
        c_knmax = e(nc.sbuf_tensor("c_knmax", [1, 8], F32)).ap()
        c_mrow = e(nc.sbuf_tensor("c_mrow", [1, 16], F32)).ap()

        pg0 = e(nc.psum_tensor("pg0", [128, 1024], F32)).ap()
        pg1 = e(nc.psum_tensor("pg1", [128, 1024], F32)).ap()
        pav = [e(nc.psum_tensor(f"pav{i}", [128, 512], F32)).ap() for i in range(2)]
        pms = e(nc.psum_tensor("pms", [128, 512], F32)).ap()
        psp = e(nc.psum_tensor("psp", [128, 512], F32)).ap()
        pg = [pg0, pg1]

        cPE = Ctr(e(nc.semaphore("cPE")))
        cACT = Ctr(e(nc.semaphore("cACT")))
        cDVE = Ctr(e(nc.semaphore("cDVE")))
        cCC = Ctr(e(nc.semaphore("cCC")))
        dS = Ctr(e(nc.semaphore("dS")))
        dP = Ctr(e(nc.semaphore("dP")))
        dPs = [Ctr(e(nc.semaphore("dPa"))), Ctr(e(nc.semaphore("dPb")))]
        dSk = [Ctr(e(nc.semaphore("dSka"))), Ctr(e(nc.semaphore("dSkb")))]
        dSq = Ctr(e(nc.semaphore("dSq")))
        dSr = [Ctr(e(nc.semaphore("dSra"))), Ctr(e(nc.semaphore("dSrb")))]
        dSv = [Ctr(e(nc.semaphore("dSva"))), Ctr(e(nc.semaphore("dSvb")))]
        dSx = [Ctr(e(nc.semaphore("dSxa"))), Ctr(e(nc.semaphore("dSxb")))]

        blk = e(nc.Block())
        blk1a = blk if KSTOP >= 11 else _SkipBlk()
        blk1b = blk if KSTOP >= 12 else _SkipBlk()
        blk1c = blk if KSTOP >= 13 else _SkipBlk()
        blk2 = blk if KSTOP >= 20 else _SkipBlk()
        blk3 = blk if KSTOP >= 30 else _SkipBlk()
        blk4 = blk if KSTOP >= 40 else _SkipBlk()
        RG = [list(range(NCORES))]
        M = {}

        def lp():
            return nc.allow_low_precision(reason="f32r rounding for broadcast matmuls is intentional")

        @blk.sync
        def _(sy):
            for ap_, src in [(c_ident, ident), (c_gqa, gqa), (c_gkva, gkva)]:
                sy.dma_start(out=ap_[:, :], in_=src).then_inc(dS.h, 16)
                dS.v += 16
        M["d_const"] = dS.v

        @blk.gpsimd
        def _(g):
            g.dma_start(out=c_emat[:, :], in_=emat).then_inc(dP.h, 16)
            dP.v += 16
            for j in range(32):
                g.dma_start(out=xtb[:, 1024 * j:1024 * (j + 1)],
                            in_=xt[128 * j:128 * (j + 1), :]).then_inc(dP.h, 16)
                dP.v += 16
            for m in range(2):
                g.dma_start(out=p1wb[m][:, :],
                            in_=wqa[:, 128 * m:128 * (m + 1)].rearrange("(t p) m -> p t m", p=128)
                            ).then_inc(dP.h, 16)
                dP.v += 16
        M["d_x"] = dP.v

        @blk.vector
        def _(v):
            v.memset(c_ones_bf[:, :], 1.0)
            cDVE.tick(v.memset(c_ones_f[:, :], 1.0))
            v.wait_ge(cDVE.h, cDVE.v)
            v.memset(row_t[0:1, 0:128], 1.0)
            with lp():
                cDVE.tick(v.tensor_copy(c_ones_r[:, :], c_ones_f[:, :]))
            v.wait_ge(cDVE.h, cDVE.v)
            with lp():
                cDVE.tick(v.tensor_copy(c_ones1r[0:1, :], row_t[0:1, 0:128]))
        M["ones"] = cDVE.v

        pe0, act0, dve0 = cPE.v, cACT.v, cDVE.v
        dpa0 = [dPs[0].v, dPs[1].v]

        @blk1a.tensor
        def _(t):
            t.wait_ge(dP.h, M["d_x"])
            t.wait_ge(dS.h, M["d_const"])
            t.wait_ge(cDVE.h, M["ones"])
            for m in range(12):
                if m >= 2:
                    t.wait_ge(dPs[m % 2].h, dpa0[m % 2] + 16 * ((m - 2) // 2 + 1))
                for sq in range(2):
                    if m >= 2:
                        t.wait_ge(cDVE.h, dve0 + 2 * (m - 2) + sq + 1)
                        t.wait_ge(cACT.h, act0 + 2 * (m - 2) + sq + 1)
                    pt = pg[m % 2][:, 512 * sq:512 * sq + 512]
                    for c_ in range(32):
                        ins = t.matmul(pt, p1wb[m % 2][:, 128 * c_:128 * (c_ + 1)],
                                       xtb[:, 1024 * c_ + 512 * sq:1024 * c_ + 512 * sq + 512],
                                       start=(c_ == 0), stop=(c_ == 31))
                    cPE.tick(ins)
                    M[("1a", m, sq)] = cPE.v
                    if m >= 1:
                        j = m - 1
                        t.wait_ge(cACT.h, act0 + 2 * j + sq + 1)
                        dst = pms[0:1, 0:512] if sq == 0 else psp[0:1, 0:512]
                        ins = t.matmul(dst, c_ones_bf, p1sq if sq == 0 else p1sq2,
                                       start=(j == 0), stop=False)
                        cPE.tick(ins)
                        M[("1a_ss", j, sq)] = cPE.v
            for sq in range(2):
                j = 11
                t.wait_ge(cACT.h, act0 + 2 * j + sq + 1)
                dst = pms[0:1, 0:512] if sq == 0 else psp[0:1, 0:512]
                ins = t.matmul(dst, c_ones_bf, p1sq if sq == 0 else p1sq2,
                               start=False, stop=True)
                cPE.tick(ins)
                M[("1a_ss", j, sq)] = cPE.v
        M["1a_pe"] = cPE.v

        @blk1a.gpsimd
        def _(g):
            for m in range(2, 12):
                g.wait_ge(cPE.h, M[("1a", m - 2, 1)])
                g.dma_start(out=p1wb[m % 2][:, :],
                            in_=wqa[:, 128 * m:128 * (m + 1)].rearrange("(t p) m -> p t m", p=128)
                            ).then_inc(dPs[m % 2].h, 16)
                dPs[m % 2].v += 16
                assert dPs[m % 2].v == dpa0[m % 2] + 16 * ((m - 2) // 2 + 1)

        @blk1a.scalar
        def _(s):
            for m in range(12):
                for sq in range(2):
                    s.wait_ge(cPE.h, M[("1a", m, sq)])
                    s.wait_ge(cDVE.h, dve0 + 2 * m + sq + 1)
                    if m >= 1:
                        s.wait_ge(cPE.h, M[("1a_ss", m - 1, sq)])
                    pt = pg[m % 2][:, 512 * sq:512 * sq + 512]
                    ins = s.activation(p1sq if sq == 0 else p1sq2, pt, AF.Square)
                    cACT.tick(ins)
        assert KSTOP < 11 or (cACT.v == act0 + 24)

        @blk1a.vector
        def _(v):
            for m in range(12):
                for sq in range(2):
                    v.wait_ge(cPE.h, M[("1a", m, sq)])
                    ins = v.tensor_copy(
                        qcpre[:, 1024 * m + 512 * sq:1024 * m + 512 * sq + 512],
                        pg[m % 2][:, 512 * sq:512 * sq + 512])
                    cDVE.tick(ins)
        assert KSTOP < 11 or (cDVE.v == dve0 + 24)

        @blk1a.vector
        def _(v):
            v.wait_ge(cPE.h, M["1a_pe"])
            v.tensor_copy(row_q[0:1, 0:512], pms[0:1, 0:512])
            cDVE.tick(v.tensor_copy(row_q[0:1, 512:1024], psp[0:1, 0:512]))
        M["1a_s2"] = cDVE.v

        @blk1a.scalar
        def _(s):
            s.wait_ge(cDVE.h, M["1a_s2"])
            cACT.tick(s.activation(row_t[0:1, :], row_q[0:1, :], AF.Sqrt, scale=1.0 / QR))
        M["1a_sqrt"] = cACT.v

        @blk1a.vector
        def _(v):
            v.wait_ge(cACT.h, M["1a_sqrt"])
            cDVE.tick(v.tensor_scalar(row_t[0:1, :], row_t[0:1, :], EPS_RMS, None, ALU.add))
            v.wait_ge(cDVE.h, cDVE.v)
            cDVE.tick(v.reciprocal(row_q[0:1, :], row_t[0:1, :]))
            v.wait_ge(cDVE.h, cDVE.v)
            cDVE.tick(v.tensor_scalar(row_t[0:1, :], row_q[0:1, :], INV_SQRT_QK, None, ALU.mult))
            v.wait_ge(cDVE.h, cDVE.v)
            with lp():
                cDVE.tick(v.tensor_copy(row_r[0:1, :], row_t[0:1, :]))
        M["1a_scale"] = cDVE.v

        @blk1a.tensor
        def _(t):
            t.wait_ge(cDVE.h, M["1a_scale"])
            t.matmul(pg0[:, 0:512], c_ones1r, row_r[0:1, 0:512], start=True, stop=True)
            cPE.tick(t.matmul(pg0[:, 512:1024], c_ones1r, row_r[0:1, 512:1024], start=True, stop=True))
        M["1a_bc"] = cPE.v

        @blk1a.vector
        def _(v):
            v.wait_ge(cPE.h, M["1a_bc"])
            cDVE.tick(v.tensor_copy(qsb[:, :], pg0[:, :]))
            v.wait_ge(cDVE.h, cDVE.v)
            for m in range(12):
                for sq in range(2):
                    ins = v.scalar_tensor_tensor(
                        qctb[:, 1024 * m + 512 * sq:1024 * m + 512 * sq + 512],
                        qcpre[:, 1024 * m + 512 * sq:1024 * m + 512 * sq + 512],
                        c_gqa[:, m:m + 1],
                        qsb[:, 512 * sq:512 * sq + 512],
                        ALU.mult, ALU.mult)
            cDVE.tick(ins)
        M["1a_qctb"] = cDVE.v

        @blk1a.sync
        def _(sy):
            sy.wait_ge(cDVE.h, M["1a_qctb"])
            for m in range(12):
                sy.dma_start(out=ag1i[QC_OFF + 128 * m:QC_OFF + 128 * (m + 1), :],
                             in_=qctb[:, 1024 * m:1024 * (m + 1)]).then_inc(dS.h, 16)
                dS.v += 16
            if DEBUG:
                for m in range(12):
                    sy.dma_start(out=dbg["qc"][128 * m:128 * (m + 1), :],
                                 in_=qctb[:, 1024 * m:1024 * (m + 1)]).then_inc(dS.h, 16)
                    dS.v += 16

        @blk1b.sync
        def _(sy):
            sy.wait_ge(cDVE.h, M["1a_qctb"])
            sy.dma_start(out=c_cos[:, :], in_=coss).then_inc(dS.h, 16)
            sy.dma_start(out=c_sin[:, :], in_=sins).then_inc(dS.h, 16)
            dS.v += 32
        M["d_cossin"] = dS.v

        pe0, act0, dve0 = cPE.v, cACT.v, cDVE.v
        dpa0 = [dPs[0].v, dPs[1].v]

        @blk1b.gpsimd
        def _(g):
            for m in range(5):
                w = 128 if m < 4 else 64
                if m >= 2:
                    g.wait_ge(cPE.h, pe0 + (2 if m - 2 == 0 else 4 * (m - 2) + 1))
                else:
                    g.wait_ge(cPE.h, M["1a_pe"])
                g.dma_start(out=p1wb[m % 2][:, 0:32 * w],
                            in_=wkva[:, 128 * m:128 * m + w].rearrange("(t p) m -> p t m", p=128)
                            ).then_inc(dPs[m % 2].h, 16)
                dPs[m % 2].v += 16
                assert dPs[m % 2].v == dpa0[m % 2] + 16 * (m // 2 + 1)

        @blk1b.tensor
        def _(t):
            t.wait_ge(cDVE.h, M["1a_qctb"])
            for m in range(5):
                w = 128 if m < 4 else 64
                t.wait_ge(dPs[m % 2].h, dpa0[m % 2] + 16 * (m // 2 + 1))
                for sq in range(2):
                    if m >= 2:
                        t.wait_ge(cDVE.h, dve0 + 2 * (m - 2) + sq + 1)
                        if m - 2 < 4:
                            t.wait_ge(cACT.h, act0 + 2 * (m - 2) + sq + 1)
                    pt = pg[m % 2][0:w, 512 * sq:512 * sq + 512]
                    for c_ in range(32):
                        ins = t.matmul(pt, p1wb[m % 2][:, w * c_:w * (c_ + 1)],
                                       xtb[:, 1024 * c_ + 512 * sq:1024 * c_ + 512 * sq + 512],
                                       start=(c_ == 0), stop=(c_ == 31))
                    cPE.tick(ins)
                    M[("1b", m, sq)] = cPE.v
                    if 1 <= m <= 4:
                        j = m - 1
                        t.wait_ge(cACT.h, act0 + 2 * j + sq + 1)
                        dst = pms[0:1, 0:512] if sq == 0 else psp[0:1, 0:512]
                        ins = t.matmul(dst, c_ones_bf, p1sq if sq == 0 else p1sq2,
                                       start=(j == 0), stop=(j == 3))
                        cPE.tick(ins)
                        M[("1b_ss", j, sq)] = cPE.v
                assert M[("1b", m, 1)] == pe0 + (2 if m == 0 else 4 * m + 1), m
        M["1b_pe"] = cPE.v

        @blk1b.scalar
        def _(s):
            for m in range(4):
                for sq in range(2):
                    s.wait_ge(cPE.h, M[("1b", m, sq)])
                    s.wait_ge(cDVE.h, dve0 + 2 * m + sq + 1)
                    if m >= 1:
                        s.wait_ge(cPE.h, M[("1b_ss", m - 1, sq)])
                    pt = pg[m % 2][:, 512 * sq:512 * sq + 512]
                    ins = s.activation(p1sq if sq == 0 else p1sq2, pt, AF.Square)
                    cACT.tick(ins)
        assert KSTOP < 12 or (cACT.v == act0 + 8)

        @blk1b.vector
        def _(v):
            for m in range(5):
                w = 128 if m < 4 else 64
                for sq in range(2):
                    v.wait_ge(cPE.h, M[("1b", m, sq)])
                    ins = v.tensor_copy(
                        kvpre[0:w, 1024 * m + 512 * sq:1024 * m + 512 * sq + 512],
                        pg[m % 2][0:w, 512 * sq:512 * sq + 512])
                    cDVE.tick(ins)
        assert KSTOP < 12 or (cDVE.v == dve0 + 10)
        M["1b_kvpre"] = cDVE.v

        @blk1b.vector
        def _(v):
            v.wait_ge(cPE.h, M["1b_pe"])
            v.tensor_copy(row_k[0:1, 0:512], pms[0:1, 0:512])
            cDVE.tick(v.tensor_copy(row_k[0:1, 512:1024], psp[0:1, 0:512]))
        M["1b_s2"] = cDVE.v

        @blk1b.scalar
        def _(s):
            s.wait_ge(cDVE.h, M["1b_s2"])
            cACT.tick(s.activation(row_t[0:1, :], row_k[0:1, :], AF.Sqrt, scale=1.0 / KVR))
        M["1b_sqrt"] = cACT.v

        @blk1b.vector
        def _(v):
            v.wait_ge(cACT.h, M["1b_sqrt"])
            cDVE.tick(v.tensor_scalar(row_t[0:1, :], row_t[0:1, :], EPS_RMS, None, ALU.add))
            v.wait_ge(cDVE.h, cDVE.v)
            cDVE.tick(v.reciprocal(row_k[0:1, :], row_t[0:1, :]))
            v.wait_ge(cDVE.h, cDVE.v)
            with lp():
                cDVE.tick(v.tensor_copy(row_r[0:1, :], row_k[0:1, :]))
        M["1b_scale"] = cDVE.v

        @blk1b.tensor
        def _(t):
            t.wait_ge(cDVE.h, M["1b_scale"])
            t.matmul(pg0[:, 0:512], c_ones1r, row_r[0:1, 0:512], start=True, stop=True)
            cPE.tick(t.matmul(pg0[:, 512:1024], c_ones1r, row_r[0:1, 512:1024], start=True, stop=True))
        M["1b_bc"] = cPE.v

        @blk1b.vector
        def _(v):
            v.wait_ge(cPE.h, M["1b_bc"])
            v.wait_ge(cDVE.h, M["1a_qctb"])
            cDVE.tick(v.tensor_copy(qsb[:, :], pg0[:, :]))
            v.wait_ge(cDVE.h, cDVE.v)
            for m in range(4):
                for sq in range(2):
                    ins = v.scalar_tensor_tensor(
                        kctb[:, 1024 * m + 512 * sq:1024 * m + 512 * sq + 512],
                        kvpre[:, 1024 * m + 512 * sq:1024 * m + 512 * sq + 512],
                        c_gkva[:, m:m + 1],
                        qsb[:, 512 * sq:512 * sq + 512],
                        ALU.mult, ALU.mult)
            cDVE.tick(ins)
            v.wait_ge(cDVE.h, cDVE.v)
            v.wait_ge(dS.h, M["d_cossin"])
            kr_ = kvpre[0:64, 4096:5120]
            v.tensor_copy(ropescr[0:32, :], kr_[32:64, :])
            cDVE.tick(v.tensor_copy(ropescr[32:64, :], kr_[0:32, :]))
            v.wait_ge(cDVE.h, cDVE.v)
            v.tensor_mul(ropescr[0:64, :], ropescr[0:64, :], c_sin[:, :])
            cDVE.tick(v.tensor_mul(ropescr2[0:64, :], kr_, c_cos[:, :]))
            v.wait_ge(cDVE.h, cDVE.v)
            cDVE.tick(v.tensor_add(krb[0:64, :], ropescr[0:64, :], ropescr2[0:64, :]))
        M["1b_done"] = cDVE.v

        @blk1b.sync
        def _(sy):
            sy.wait_ge(cDVE.h, M["1b_done"])
            for m in range(4):
                sy.dma_start(out=ag1i[KC_OFF + 128 * m:KC_OFF + 128 * (m + 1), :],
                             in_=kctb[:, 1024 * m:1024 * (m + 1)]).then_inc(dS.h, 16)
                dS.v += 16
            sy.dma_start(out=ag1i[KR_OFF:KR_OFF + 64, :], in_=krb[0:64, :]).then_inc(dS.h, 16)
            dS.v += 16
            if DEBUG:
                for m in range(4):
                    sy.dma_start(out=dbg["kc"][128 * m:128 * (m + 1), :],
                                 in_=kctb[:, 1024 * m:1024 * (m + 1)]).then_inc(dS.h, 16)
                    dS.v += 16
                sy.dma_start(out=dbg["kr"][:, :], in_=krb[0:64, :]).then_inc(dS.h, 16)
                dS.v += 16
        M["1b_out"] = dS.v

        pe0, dve0 = cPE.v, cDVE.v
        dpa0 = [dPs[0].v, dPs[1].v]
        dsr0 = [dSr[0].v, dSr[1].v]

        @blk1c.gpsimd
        def _(g):
            for h in range(NH):
                if h >= 2:
                    g.wait_ge(cPE.h, pe0 + 2 * (h - 1))
                else:
                    g.wait_ge(cPE.h, M["1b_pe"])
                g.dma_start(out=p1wb[h % 2][:, 0:12 * 64],
                            in_=wqbr[:, 64 * h:64 * (h + 1)].rearrange("(t p) m -> p t m", p=128)
                            ).then_inc(dPs[h % 2].h, 16)
                dPs[h % 2].v += 16

        @blk1c.tensor
        def _(t):
            t.wait_ge(cDVE.h, M["1b_done"])
            for h in range(NH):
                t.wait_ge(dPs[h % 2].h, dpa0[h % 2] + 16 * (h // 2 + 1))
                if h >= 2:
                    t.wait_ge(cDVE.h, dve0 + 4 * (h - 2) + 1)
                for sq in range(2):
                    pt = pg[h % 2][0:64, 512 * sq:512 * sq + 512]
                    for c_ in range(12):
                        ins = t.matmul(pt, p1wb[h % 2][:, 64 * c_:64 * (c_ + 1)],
                                       qctb[:, 1024 * c_ + 512 * sq:1024 * c_ + 512 * sq + 512],
                                       start=(c_ == 0), stop=(c_ == 11))
                    cPE.tick(ins)
                M[("1c", h)] = cPE.v
                assert M[("1c", h)] == pe0 + 2 * (h + 1)
        M["1c_pe"] = cPE.v

        @blk1c.vector
        def _(v):
            for h in range(NH):
                v.wait_ge(cPE.h, M[("1c", h)])
                v.wait_ge(cDVE.h, cDVE.v)
                if h >= 2:
                    v.wait_ge(dSr[h % 2].h, dsr0[h % 2] + 16 * ((h - 2) // 2 + 1))
                qr_ = ropescr[0:64, :]
                sh_ = ropescr2[0:64, :]
                v.tensor_copy(qr_[0:64, 0:512], pg[h % 2][0:64, 0:512])
                cDVE.tick(v.tensor_copy(qr_[0:64, 512:1024], pg[h % 2][0:64, 512:1024]))
                v.wait_ge(cDVE.h, cDVE.v)
                v.tensor_copy(sh_[0:32, :], qr_[32:64, :])
                cDVE.tick(v.tensor_copy(sh_[32:64, :], qr_[0:32, :]))
                v.wait_ge(cDVE.h, cDVE.v)
                v.tensor_mul(sh_, sh_, c_sin[:, :])
                cDVE.tick(v.tensor_mul(qr_, qr_, c_cos[:, :]))
                v.wait_ge(cDVE.h, cDVE.v)
                cDVE.tick(v.tensor_add(qrb[h % 2][0:64, :], qr_, sh_))
                M[("1c_r", h)] = cDVE.v
        assert KSTOP < 13 or (cDVE.v == dve0 + 4 * NH)

        @blk1c.sync
        def _(sy):
            for h in range(NH):
                sy.wait_ge(cDVE.h, M[("1c_r", h)])
                sy.dma_start(out=ag1i[QROPE_OFF + 64 * h:QROPE_OFF + 64 * (h + 1), :],
                             in_=qrb[h % 2][0:64, :]).then_inc(dSr[h % 2].h, 16)
                dSr[h % 2].v += 16
                if DEBUG:
                    sy.dma_start(out=dbg["qrope"][64 * h:64 * (h + 1), :],
                                 in_=qrb[h % 2][0:64, :]).then_inc(dSr[h % 2].h, 16)
                    dSr[h % 2].v += 16
        M["ag1_ready"] = (dS.v, dSr[0].v, dSr[1].v)

        @blk2.gpsimd
        def _(g):
            g.wait_ge(dS.h, M["1b_out"])
            ins = g.collective_compute("AllGather", ALU.bypass, replica_groups=RG,
                                       ins=[ag1i[KC_OFF:AG1R, :].opt()], outs=[ag1oA.opt()])
            cCC.tick(ins)
            M["agA"] = cCC.v
            g.wait_ge(dSr[0].h, M["ag1_ready"][1])
            g.wait_ge(dSr[1].h, M["ag1_ready"][2])
            ins = g.collective_compute("AllGather", ALU.bypass, replica_groups=RG,
                                       ins=[ag1i[0:KC_OFF, :].opt()], outs=[ag1oB.opt()])
            cCC.tick(ins)
            M["agB"] = cCC.v
            g.dma_start(out=wknb[:, :],
                        in_=wkn.rearrange("(t p) m -> p t m", p=128)).then_inc(dP.h, 16)
            g.dma_start(out=wkvb_[:, :],
                        in_=wkv.rearrange("(t p) m -> p t m", p=128)).then_inc(dP.h, 16)
            dP.v += 32
        M["d_wk"] = dP.v

        @blk2.gpsimd
        def _(g):
            for c in range(4):
                g.dma_start(out=wo_bf[:, 1024 * c:1024 * (c + 1)],
                            in_=wo[:, 1024 * c:1024 * (c + 1)]).then_inc(dP.h, 16)
                dP.v += 16
        M["d_wobf"] = dP.v

        pid_holder = {}
        pe0, act0, dve0, ds0 = cPE.v, cACT.v, cDVE.v, dS.v

        @blk2.vector
        def _(v):
            cDVE.tick(v.memset(roped[64:65, :], -1.0))
        M["negrow"] = cDVE.v

        @blk2.sync
        def _(sy):
            sy.wait_ge(cCC.h, M["agA"])
            pid_holder["sp"] = nc.partition_id([ET.SP])
            for tt in range(16):
                rr, co = tt // 2, (tt % 2) * 512
                if tt >= 2:
                    sy.wait_ge(cPE.h, pe0 + tt - 1)
                for r_ in range(4):
                    sy.dma_start(
                        out=kcw[tt % 2][:, 512 * r_:512 * (r_ + 1)],
                        in_=ag1oA[AGA * rr + 128 * r_:AGA * rr + 128 * (r_ + 1),
                                  co:co + 512]).then_inc(dSk[tt % 2].h, 16)
                    dSk[tt % 2].v += 16
                M[("kcw", tt)] = (dSk[tt % 2], dSk[tt % 2].v)
            for rr in range(NCORES):
                sy.dma_start(out=roped[0:64, SQC * rr:SQC * (rr + 1)],
                             in_=ag1oA[AGA * rr + 512:AGA * rr + 576, :]
                             ).then_inc(dS.h, 16)
                dS.v += 16
            sy.wait_ge(cCC.h, M["agB"])
            v3 = ag1oB.rearrange("(r x) n -> r x n", x=AGB)
            sy.dma_start(out=qrl.rearrange("(r x) n -> r x n", x=HPC * RD),
                         in_=v3[:, bass.ts(pid_holder["sp"] + (QROPE_OFF // (HPC * RD)), HPC * RD), :]
                         ).then_inc(dS.h, 16)
            dS.v += 16
        M["d_roped"] = dS.v

        @blk2.tensor
        def _(t):
            t.wait_ge(dP.h, M["d_wk"])
            t.wait_ge(cDVE.h, dve0)
            for tt in range(16):
                t.wait_ge(M[("kcw", tt)][0].h, M[("kcw", tt)][1])
                if tt >= 1:
                    t.wait_ge(cDVE.h, dve0 + 1 + tt)
                kpt = [pg0[:, 0:512], pg0[:, 512:1024], pg1[:, 0:512], pg1[:, 512:1024]]
                vpt = [pav[0], pav[1], pms, psp]
                for m in range(4):
                    for r_ in range(4):
                        t.matmul(kpt[m],
                                 wknb[:, 512 * r_ + 128 * m:512 * r_ + 128 * (m + 1)],
                                 kcw[tt % 2][:, 512 * r_:512 * (r_ + 1)],
                                 start=(r_ == 0), stop=(r_ == 3))
                for sk in range(4):
                    for r_ in range(4):
                        ins = t.matmul(vpt[sk],
                                       kcw[tt % 2][:, 512 * r_ + 128 * sk:512 * r_ + 128 * (sk + 1)],
                                       wkvb_[:, 512 * r_:512 * (r_ + 1)],
                                       start=(r_ == 0), stop=(r_ == 3))
                cPE.tick(ins)
                M[("2", tt)] = cPE.v
        assert KSTOP < 20 or (cPE.v == pe0 + 16)
        M["2_pe"] = cPE.v

        @blk2.vector
        def _(v):
            for tt in range(16):
                v.wait_ge(cPE.h, M[("2", tt)])
                v.wait_ge(cDVE.h, cDVE.v)
                kpt = [pg0[:, 0:512], pg0[:, 512:1024], pg1[:, 0:512], pg1[:, 512:1024]]
                vpt = [pav[0], pav[1], pms, psp]
                for m in range(4):
                    v.tensor_copy(kn4[:, S * m + 512 * tt:S * m + 512 * (tt + 1)], kpt[m])
                for sk in range(4):
                    ins = v.tensor_copy(v4[:, 512 * (4 * tt + sk):512 * (4 * tt + sk + 1)], vpt[sk])
                cDVE.tick(ins)
        assert KSTOP < 20 or (cDVE.v == dve0 + 1 + 16)
        M["2_dve"] = cDVE.v

        pe0, act0, dve0 = cPE.v, cACT.v, cDVE.v

        @blk2.scalar
        def _(s):
            s.wait_ge(cDVE.h, M["2_dve"])
            s.wait_ge(dS.h, M["d_roped"])
            for h in range(HPC):
                for tt in range(16):
                    u = 16 * h + tt
                    if u >= 1:
                        s.wait_ge(cPE.h, pe0 + u)
                    s.activation(sscr, kn4[:, S * h + 512 * tt:S * h + 512 * (tt + 1)], AF.Square)
                    ins = s.activation(sscr2[0:64, :], roped[0:64, 512 * tt:512 * (tt + 1)], AF.Square)
                    cACT.tick(ins)
        assert KSTOP < 20 or (cACT.v == act0 + 64)

        @blk2.tensor
        def _(t):
            for h in range(HPC):
                for tt in range(16):
                    u = 16 * h + tt
                    t.wait_ge(cACT.h, act0 + u + 1)
                    if u >= 1:
                        t.wait_ge(cDVE.h, dve0 + 17 * (u // 16) + (u % 16))
                    t.matmul(pms[0:1, 0:512], c_ones_bf, sscr, start=True, stop=False)
                    ins = t.matmul(pms[0:1, 0:512], c_ones_bf[0:64, :], sscr2[0:64, :],
                                   start=False, stop=True)
                    cPE.tick(ins)
        assert KSTOP < 20 or (cPE.v == pe0 + 64)

        @blk2.vector
        def _(v):
            for h in range(HPC):
                for tt in range(16):
                    u = 16 * h + tt
                    v.wait_ge(cPE.h, pe0 + u + 1)
                    cDVE.tick(v.reduce_max(c_mrow[0:1, tt:tt + 1], pms[0:1, 0:512], axis=AX.X))
                v.wait_ge(cDVE.h, cDVE.v)
                cDVE.tick(v.reduce_max(c_knmax[0:1, h:h + 1], c_mrow[0:1, 0:16], axis=AX.X))
                v.wait_ge(cDVE.h, cDVE.v)
        assert KSTOP < 20 or (cDVE.v == dve0 + 68)
        M["2_knm"] = cDVE.v

        @blk2.scalar
        def _(s):
            s.wait_ge(cDVE.h, M["2_knm"])
            cACT.tick(s.activation(c_knmax[0:1, 0:HPC], c_knmax[0:1, 0:HPC], AF.Ln))
        M["knln"] = cACT.v

        @blk2.vector
        def _(v):
            v.wait_ge(cACT.h, M["knln"])
            v.wait_ge(cDVE.h, cDVE.v)
            cDVE.tick(v.tensor_scalar(c_knmax[0:1, 0:HPC], c_knmax[0:1, 0:HPC],
                                      0.5, None, ALU.mult))
        M["knm2"] = cDVE.v

        @blk2.gpsimd
        def _(g):
            g.wait_ge(cPE.h, M["2_pe"])
            g.dma_start(out=wqbnb[:, :],
                        in_=wqbn.rearrange("(t p) m -> p t m", p=128)).then_inc(dP.h, 16)
            dP.v += 16
        M["d_wqbn"] = dP.v

        if DEBUG:
            @blk2.sync
            def _(sy):
                sy.wait_ge(cDVE.h, M["2_dve"])
                sy.dma_start(out=dbg["kn"], in_=kn4).then_inc(dS.h, 16)
                sy.dma_start(out=dbg["v4"], in_=v4).then_inc(dS.h, 16)
                dS.v += 32

        NG = 32
        pe0, act0, dve0 = cPE.v, cACT.v, cDVE.v
        dsq0 = dSq.v
        dsr0 = [dSr[0].v, dSr[1].v]
        dsv0 = [dSv[0].v, dSv[1].v]
        ACT_U, DVE_U = 36, 70
        import builtins; builtins.DBG3 = (pe0, act0, dve0, dsq0)

        def act_m(u, i):
            return act0 + ACT_U * u + i

        def dve_m(u, i):
            return dve0 + DVE_U * u + i

        def ds_qcb(b):
            return dsq0 + 16 * 12 * (b + 1)

        def ds_qra_w(eng, u):
            eng.wait_ge(dSr[u % 2].h, dsr0[u % 2] + 16 * (u // 2 + 1))

        def ds_av_w(eng, u):
            eng.wait_ge(dSv[u % 2].h, dsv0[u % 2] + 16 * (u // 2 + 1))

        def a_exp(u, g):
            base = act0 + 4 + 36 * u
            off = 1 + g if (g < 16 or u == 63) else 5 + g
            return base + off

        def a_sqn(u):
            return act0 + 1 if u == 0 else act0 + 4 + 36 * (u - 1) + 17

        def a_sqr(u):
            return act0 + 2 if u == 0 else act0 + 4 + 36 * (u - 1) + 18

        def a_ln(u):
            return act0 + 3 if u == 0 else act0 + 4 + 36 * (u - 1) + 19

        def a_bexp(u):
            return act0 + 4 if u == 0 else act0 + 4 + 36 * (u - 1) + 20

        def d_base(u):
            return dve0 + 1 + 70 * u

        def d_accd(u, g):
            off = [3, 7, 9, 11, 13][g] if g < 5 else (2 * g + 7 if g < 10 else 2 * g + 8)
            return d_base(u) + off

        def d_qne(u):
            return dve0 + 1 if u == 0 else d_base(u - 1) + 26

        def d_cast(u):
            return d_base(u + 1) + 5 if u < 63 else dve0 + 4484

        def d_rbsb(u):
            return d_base(u + 1) + 14 if u < 63 else dve0 + 4485

        def d_avob(u):
            return d_base(u + 1) + 15 if u < 63 else dve0 + 4486

        @blk3.tensor
        def _(t):
            t.wait_ge(cDVE.h, M["knm2"])
            t.wait_ge(dP.h, M["d_wqbn"])
            t.wait_ge(cDVE.h, M["negrow"])

            def emit_qb(u):
                if u % 4 == 0:
                    t.wait_ge(dSq.h, ds_qcb(u // 4))
                if u >= 1:
                    t.wait_ge(cDVE.h, d_qne(u - 1))
                    t.wait_ge(cACT.h, a_sqn(u - 1))
                hh = u % 4
                for c_ in range(12):
                    ins = t.matmul(psp, wqbnb[:, 512 * c_ + 128 * hh:512 * c_ + 128 * (hh + 1)],
                                   qcb[:, 512 * c_:512 * (c_ + 1)],
                                   start=(c_ == 0), stop=(c_ == 11))
                cPE.tick(ins)
                M[("3qb", u)] = cPE.v

            def emit_bm(u):
                t.wait_ge(cACT.h, a_sqr(u))
                if u >= 2:
                    t.wait_ge(cDVE.h, d_rbsb(u - 2))
                t.matmul(pms[0:1, 0:512], c_ones_bf, sscr, start=True, stop=False)
                ins = t.matmul(pms[0:1, 0:512], c_ones_bf[0:64, :], sscr2[0:64, :],
                               start=False, stop=True)
                cPE.tick(ins)
                M[("3bm", u)] = cPE.v

            emit_qb(0)
            emit_bm(0)
            for u in range(64):
                hh = u % 4
                def emit_av(gg):
                    t.wait_ge(cACT.h, a_exp(u, gg))
                    if gg == 0 and u >= 2:
                        t.wait_ge(cDVE.h, d_avob(u - 2))
                    for cc in (2 * gg, 2 * gg + 1):
                        ins = t.matmul(pav[u % 2],
                                       v4[:, 512 * cc + 128 * hh:512 * cc + 128 * (hh + 1)],
                                       pbuf[gg % 2][:, (cc % 2) * 512:(cc % 2) * 512 + 512],
                                       start=(cc == 0), stop=(cc == 63))
                    cPE.tick(ins)
                    M[("3av", u, gg)] = cPE.v

                for g_ in range(NG):
                    par = g_ % 2
                    if g_ == 0:
                        t.wait_ge(cDVE.h, d_qne(u))
                        t.wait_ge(cACT.h, a_bexp(u) if u == 0 else a_exp(u - 1, 30))
                    elif g_ == 1:
                        if u >= 1:
                            t.wait_ge(cACT.h, a_exp(u - 1, 31))
                    else:
                        t.wait_ge(cACT.h, a_exp(u, g_ - 2))
                    for cc in (2 * g_, 2 * g_ + 1):
                        col = (cc % 2) * 512
                        pt = pg[par][:, col:col + 512]
                        t.matmul(pt, kn4[:, S * hh + 128 * cc:S * hh + 128 * (cc + 1)],
                                 qnb[u % 2], start=True, stop=False)
                        ins = t.matmul(pt, roped[0:65, 128 * cc:128 * (cc + 1)],
                                       qra[u % 2][0:65, :], start=False, stop=True)
                    cPE.tick(ins)
                    M[("3sc", u, g_)] = cPE.v
                    if g_ == 2 and u >= 1:
                        t.wait_ge(cDVE.h, d_base(u - 1) + 70)
                        cPE.tick(t.matmul(pms[0:1, 0:512], c_ones_f, accd[:, :],
                                          start=True, stop=True))
                        M[("3dn", u - 1)] = cPE.v
                    if g_ >= 2:
                        emit_av(g_ - 2)
                    if g_ == 6 and u >= 1:
                        t.wait_ge(cDVE.h, d_cast(u - 1))
                        cPE.tick(t.matmul(pms, c_ones1r, row_r[0:1, 0:512],
                                          start=True, stop=True))
                        M[("3rb", u - 1)] = cPE.v
                    if g_ == 8 and u <= 62:
                        emit_qb(u + 1)
                    if g_ == 16 and u <= 62:
                        emit_bm(u + 1)
                emit_av(NG - 2)
                emit_av(NG - 1)
            t.wait_ge(cDVE.h, d_base(63) + 70)
            cPE.tick(t.matmul(pms[0:1, 0:512], c_ones_f, accd[:, :], start=True, stop=True))
            M[("3dn", 63)] = cPE.v
            t.wait_ge(cDVE.h, d_cast(63))
            cPE.tick(t.matmul(pms, c_ones1r, row_r[0:1, 0:512], start=True, stop=True))
            M[("3rb", 63)] = cPE.v
        M["3_pe"] = cPE.v

        @blk3.scalar
        def _(s):
            def emit_head(u):
                s.wait_ge(cPE.h, M[("3qb", u)])
                s.wait_ge(cDVE.h, d_qne(u))
                cACT.tick(s.activation(sscr, psp, AF.Square))
                ds_qra_w(s, u)
                cACT.tick(s.activation(sscr2[0:64, :], qra[u % 2][0:64, :], AF.Square))
                s.wait_ge(cPE.h, M[("3bm", u)])
                cACT.tick(s.activation(q2row[0:1, :], pms[0:1, 0:512], AF.Ln))
                s.wait_ge(cACT.h, cACT.v)
                if u >= 2:
                    s.wait_ge(cPE.h, M[("3sc", u - 2, NG - 1)])
                cACT.tick(s.activation(qra[u % 2][64:65, :], q2row[0:1, :], AF.Exp,
                                       scale=0.5, bias=c_knmax[0:1, (u % 4):(u % 4) + 1]))

            emit_head(0)
            for u in range(64):
                for g_ in range(NG):
                    s.wait_ge(cPE.h, M[("3sc", u, g_)])
                    if g_ >= 2:
                        s.wait_ge(cPE.h, M[("3av", u, g_ - 2)])
                        s.wait_ge(cDVE.h, d_accd(u, g_ - 2))
                    elif u >= 1:
                        s.wait_ge(cDVE.h, d_accd(u - 1, 30 + g_))
                    cACT.tick(s.activation(pbuf[g_ % 2][:, :], pg[g_ % 2][:, :], AF.Exp))
                    if g_ == 15 and u <= 62:
                        emit_head(u + 1)
        assert KSTOP < 30 or (cACT.v == act0 + 4 + 36 * 63 + 32)

        @blk3.vector
        def _(v):
            v.wait_ge(cPE.h, M[("3qb", 0)])
            v.wait_ge(cDVE.h, cDVE.v)
            cDVE.tick(v.tensor_copy(qnb[0], psp))

            def acc_pair(u, g_):
                v.wait_ge(cACT.h, a_exp(u, g_))
                if g_ == 0:
                    cDVE.tick(v.tensor_copy(accd[:, :], pbuf[0][:, 0:512]))
                    v.wait_ge(cDVE.h, cDVE.v)
                    cDVE.tick(v.tensor_tensor(accd[:, :], accd[:, :],
                                              pbuf[0][:, 512:1024], ALU.add))
                else:
                    cDVE.tick(v.tensor_tensor(accd[:, :], accd[:, :],
                                              pbuf[g_ % 2][:, 0:512], ALU.add))
                    v.wait_ge(cDVE.h, cDVE.v)
                    cDVE.tick(v.tensor_tensor(accd[:, :], accd[:, :],
                                              pbuf[g_ % 2][:, 512:1024], ALU.add))
                v.wait_ge(cDVE.h, cDVE.v)

            def tail_a(u):
                if u < 0:
                    cDVE.tick(v.nop())
                    return
                v.wait_ge(cPE.h, M[("3dn", u)])
                cDVE.tick(v.tensor_scalar(row_t[0:1, 0:512], pms[0:1, 0:512],
                                          1e-30, None, ALU.add))
                v.wait_ge(cDVE.h, cDVE.v)

            def tail_b(u):
                if u < 0:
                    cDVE.tick(v.nop())
                    cDVE.tick(v.nop())
                    return
                cDVE.tick(v.reciprocal(row_q[0:1, 0:512], row_t[0:1, 0:512]))
                v.wait_ge(cDVE.h, cDVE.v)
                with lp():
                    cDVE.tick(v.tensor_copy(row_r[0:1, 0:512], row_q[0:1, 0:512]))
                v.wait_ge(cDVE.h, cDVE.v)

            def tail_c(u):
                if u < 0:
                    cDVE.tick(v.nop())
                    cDVE.tick(v.nop())
                    return
                v.wait_ge(cPE.h, M[("3rb", u)])
                cDVE.tick(v.tensor_copy(rbsb, pms))
                v.wait_ge(cDVE.h, cDVE.v)
                if u >= 2:
                    ds_av_w(v, u - 2)
                cDVE.tick(v.tensor_tensor(avob[u % 2][:, :], pav[u % 2], rbsb, ALU.mult))
                v.wait_ge(cDVE.h, cDVE.v)

            for u in range(64):
                tail_a(u - 1)
                acc_pair(u, 0)
                tail_b(u - 1)
                for g_ in range(1, 5):
                    acc_pair(u, g_)
                tail_c(u - 1)
                for g_ in range(5, 10):
                    acc_pair(u, g_)
                if u <= 62:
                    v.wait_ge(cPE.h, M[("3qb", u + 1)])
                    if u >= 1:
                        v.wait_ge(cPE.h, M[("3sc", u - 1, NG - 1)])
                    cDVE.tick(v.tensor_copy(qnb[(u + 1) % 2], psp))
                else:
                    cDVE.tick(v.nop())
                v.wait_ge(cDVE.h, cDVE.v)
                for g_ in range(10, NG):
                    acc_pair(u, g_)
            tail_a(63)
            tail_b(63)
            tail_c(63)
        assert KSTOP < 30 or (cDVE.v == dve0 + 4486)

        @blk3.sync
        def _(sy):
            sy.wait_ge(cCC.h, M["agB"])
            sy.wait_ge(dS.h, M["d_roped"])
            sy.wait_ge(cPE.h, M["2_pe"])

            def avob_out(u):
                sy.wait_ge(cDVE.h, d_avob(u))
                sy.dma_start(out=ag2i[128 * (u % 4):128 * (u % 4 + 1),
                                      512 * (u // 4):512 * (u // 4 + 1)],
                             in_=avob[u % 2][:, :]).then_inc(dSv[u % 2].h, 16)
                dSv[u % 2].v += 16
                assert dSv[u % 2].v == dsv0[u % 2] + 16 * (u // 2 + 1)

            for u in range(64):
                b, hh = u // 4, u % 4
                rr, co = b // 2, (b % 2) * 512
                if hh == 0:
                    if b >= 1:
                        sy.wait_ge(cPE.h, M[("3qb", 4 * (b - 1) + 3)])
                    for m in range(12):
                        sy.dma_start(
                            out=qcb[:, 512 * m:512 * (m + 1)],
                            in_=ag1oB[AGB * rr + QC_OFF + 128 * m:
                                      AGB * rr + QC_OFF + 128 * (m + 1),
                                      co:co + 512]).then_inc(dSq.h, 16)
                        dSq.v += 16
                if u >= 2:
                    sy.wait_ge(cACT.h, a_sqr(u - 2))
                    sy.wait_ge(cPE.h, M[("3sc", u - 2, NG - 1)])
                sy.dma_start(out=qra[u % 2][0:64, :],
                             in_=qrl[HPC * RD * rr + RD * hh:HPC * RD * rr + RD * (hh + 1),
                                     co:co + 512]).then_inc(dSr[u % 2].h, 16)
                dSr[u % 2].v += 16
                assert dSr[u % 2].v == dsr0[u % 2] + 16 * (u // 2 + 1)
                if u >= 2:
                    avob_out(u - 2)
            avob_out(62)
            avob_out(63)
        M["3_out"] = (dSq.v, dSv[0].v, dSv[1].v)

        @blk4.gpsimd
        def _(g):
            g.wait_ge(dSq.h, M["3_out"][0])
            g.wait_ge(dSv[0].h, M["3_out"][1])
            g.wait_ge(dSv[1].h, M["3_out"][2])
            ins = g.collective_compute("AllGather", ALU.bypass, replica_groups=RG,
                                       ins=[ag2i.opt()], outs=[ag2o.opt()])
            cCC.tick(ins)
        M["ag2"] = cCC.v

        if DEBUG:
            @blk4.sync
            def _(sy):
                sy.wait_ge(cCC.h, M["ag2"])
                for hh in range(HPC):
                    sy.dma_start(out=dbg["av"][128 * hh:128 * (hh + 1), :],
                                 in_=ag2i[128 * hh:128 * (hh + 1), :]).then_inc(dS.h, 16)
                    dS.v += 16

        for s_ in range(2):
            pe0, act0, dve0, ds0, dp0 = cPE.v, cACT.v, cDVE.v, dS.v, dP.v
            dpa0 = [dPs[0].v, dPs[1].v]
            dsx0 = [dSx[0].v, dSx[1].v]
            DVE1 = dve0

            @blk4.sync
            def _(sy, s_=s_, ds0=ds0, dve0=dve0, dSx=dSx, dsx0=dsx0):
                sy.wait_ge(cCC.h, M["ag2"])
                if s_ == 1:
                    sy.wait_ge(cPE.h, M[("4wo_last", 0)])
                    sy.wait_ge(cDVE.h, M[("4ln_n", 0, 3)])
                    sy.wait_ge(dSv[0].h, dSv[0].v)
                    sy.wait_ge(dSv[1].h, dSv[1].v)
                pid_sp = pid_holder["sp"]
                v3 = ag2o.rearrange("(a p) n -> p a n", p=128)
                sy.dma_start(out=avsl[:, :].rearrange("p (t n) -> p t n", n=512),
                             in_=v3[:, :, bass.ts(pid_sp * 2 + s_, 512)]).then_inc(dS.h, 16)
                dS.v += 16
                for m in range(32):
                    if m >= 2:
                        sy.wait_ge(cDVE.h, dve0 + 2 * (m - 2) + 1)
                    sy.dma_start(out=xres[m % 2][:, :],
                                 in_=xt[128 * m:128 * (m + 1), 512 * s_:512 * s_ + 512]
                                 ).then_inc(dSx[m % 2].h, 16)
                    dSx[m % 2].v += 16
            assert KSTOP < 40 or (dS.v == ds0 + 16)

            @blk4.tensor
            def _(t, s_=s_, act0=act0, dve0=dve0, ds0=ds0, dp0=dp0, dpa0=dpa0):
                t.wait_ge(dS.h, ds0 + 16)
                t.wait_ge(cDVE.h, dve0)
                t.wait_ge(cACT.h, act0)
                for m in range(32):
                    t.wait_ge(dPs[m % 2].h, dpa0[m % 2] + 32 * (m // 2 + 1))
                    if m >= 2:
                        t.wait_ge(cDVE.h, dve0 + 2 * (m - 2) + 2)
                    pt = pg0[:, 0:512] if m % 2 == 0 else pg0[:, 512:1024]
                    for c_ in range(32):
                        ins = t.matmul(pt, wob[m % 2][:, 128 * c_:128 * (c_ + 1)],
                                       avsl[:, 512 * c_:512 * (c_ + 1)],
                                       start=(c_ == 0), stop=(c_ == 31))
                    cPE.tick(ins)
                    M[("4wo", s_, m)] = cPE.v
                    if m >= 1:
                        mm_ = m - 1
                        t.wait_ge(cDVE.h, dve0 + 2 * mm_ + 2)
                        t.wait_ge(cACT.h, act0 + mm_ + 1)
                        sq_ = sqzb if mm_ % 2 == 0 else sqzb2
                        t.matmul(pav[0][0:1, 0:512], c_ones_bf, zbf[:, 512 * mm_:512 * (mm_ + 1)],
                                 start=(mm_ == 0), stop=False)
                        ins = t.matmul(pav[1][0:1, 0:512], c_ones_bf, sq_,
                                       start=(mm_ == 0), stop=False)
                        cPE.tick(ins)
                        M[("4sq", s_, mm_)] = cPE.v
                M[("4wo_last", s_)] = cPE.v
                mm_ = 31
                t.wait_ge(cDVE.h, dve0 + 2 * mm_ + 2)
                t.wait_ge(cACT.h, act0 + mm_ + 1)
                sq_ = sqzb if mm_ % 2 == 0 else sqzb2
                t.matmul(pav[0][0:1, 0:512], c_ones_bf, zbf[:, 512 * mm_:512 * (mm_ + 1)],
                         start=False, stop=True)
                ins = t.matmul(pav[1][0:1, 0:512], c_ones_bf, sqzb2, start=False, stop=True)
                cPE.tick(ins)
                M[("4sq", s_, 31)] = cPE.v
                M[("4st", s_)] = cPE.v
                t.wait_ge(cDVE.h, dve0 + 75)
                t.matmul(pms, c_ones1r, row_r[0:1, 0:512], start=True, stop=True)
                t.matmul(psp, c_ones1r, row_r[0:1, 512:1024], start=True, stop=True)
                ins = t.matmul(pg1[:, 512:1024], c_ones1r, row_r2[0:1, 0:512], start=True, stop=True)
                cPE.tick(ins)
                M[("4bc", s_)] = cPE.v
                t.wait_ge(dP.h, dp0 + 16)
                t.wait_ge(cDVE.h, dve0 + 78)
                for c_ in range(32):
                    t.matmul(pg1[0:64, 0:512], wrtb[:, 64 * c_:64 * (c_ + 1)],
                             zh[:, 512 * c_:512 * (c_ + 1)],
                             start=(c_ == 0), stop=(c_ == 31))
                    ins = t.matmul(pav[0][0:64, 0:1], wrtb[:, 64 * c_:64 * (c_ + 1)], c_ones_f,
                                   start=(c_ == 0), stop=(c_ == 31))
                cPE.tick(ins)
                M[("4rt", s_)] = cPE.v
                for m in range(8):
                    t.wait_ge(dPs[m % 2].h, dpa0[m % 2] + 16 * (32 + m // 2 + 1))
                    if m >= 2:
                        t.wait_ge(cDVE.h, dve0 + 84 + 3 * (m - 2))
                    if m >= 1:
                        t.wait_ge(cDVE.h, dve0 + 82 + 3 * (m - 1))
                    pt = pg0[:, 0:512] if m % 2 == 0 else pg0[:, 512:1024]
                    for c_ in range(32):
                        t.matmul(pt, atb[m % 2][:, 128 * c_:128 * (c_ + 1)],
                                 zbf[:, 512 * c_:512 * (c_ + 1)],
                                 start=(c_ == 0), stop=(c_ == 31))
                        ins = t.matmul(pav[1][:, m:m + 1],
                                       atb[m % 2][:, 128 * c_:128 * (c_ + 1)], c_ones_bf,
                                       start=(c_ == 0), stop=(c_ == 31))
                    cPE.tick(ins)
                    M[("4h", s_, m)] = cPE.v

            @blk4.gpsimd
            def _(g, s_=s_, dpa0=dpa0):
                if s_ == 0:
                    g.wait_ge(dP.h, M["d_wobf"])
                for m in range(32):
                    if m >= 2:
                        g.wait_ge(cPE.h, M[("4wo", s_, m - 2)])
                    elif s_ == 1:
                        g.wait_ge(cPE.h, M[("4B", 0, 31)])
                    g.dma_start(out=wob[m % 2][:, 0:16 * 128],
                                in_=wo_bf[0:2048, 128 * m:128 * (m + 1)].rearrange(
                                    "(t p) m -> p t m", p=128)).then_inc(dPs[m % 2].h, 16)
                    dPs[m % 2].v += 16
                g.dma_start(out=wrtb[:, :],
                            in_=wrt.rearrange("(t p) m -> p t m", p=128)).then_inc(dP.h, 16)
                dP.v += 16
                for m in range(8):
                    if m >= 2:
                        g.wait_ge(cPE.h, M[("4h", s_, m - 2)])
                    else:
                        g.wait_ge(cPE.h, M[("4wo", s_, 31)])
                    g.dma_start(out=atb[m % 2][:, :],
                                in_=at[:, 128 * m:128 * (m + 1)].rearrange("(t p) m -> p t m", p=128)
                                ).then_inc(dPs[m % 2].h, 16)
                    dPs[m % 2].v += 16

            @blk4.scalar
            def _(s, s_=s_, dve0=dve0):
                def wob_half(m):
                    if m >= 2:
                        s.wait_ge(cPE.h, M[("4wo", s_, m - 2)])
                    elif s_ == 1:
                        s.wait_ge(cPE.h, M[("4B", 0, 31)])
                    elif m == 0:
                        s.wait_ge(dP.h, M["d_wobf"])
                    s.dma_start(out=wob[m % 2][:, 16 * 128:32 * 128],
                                in_=wo_bf[2048:4096, 128 * m:128 * (m + 1)].rearrange(
                                    "(t p) m -> p t m", p=128)).then_inc(dPs[m % 2].h, 16)
                    dPs[m % 2].v += 16

                wob_half(0)
                wob_half(1)
                for m in range(32):
                    s.wait_ge(cDVE.h, dve0 + 2 * m + 2)
                    if m >= 2:
                        s.wait_ge(cPE.h, M[("4sq", s_, m - 2)])
                    sq_ = sqzb if m % 2 == 0 else sqzb2
                    cACT.tick(s.activation(sq_, zbf[:, 512 * m:512 * (m + 1)], AF.Square))
                    if m + 2 < 32:
                        wob_half(m + 2)
                s.wait_ge(cDVE.h, dve0 + 71)
                cACT.tick(s.activation(row_t[0:1, 0:512], row_k[0:1, 0:512], AF.Sqrt))
            M[("4act1", s_)] = cACT.v

            @blk4.vector
            def _(v, s_=s_, act0=act0, dve0=dve0, dsx0=dsx0,
                  dsv_tot=(dSv[0].v, dSv[1].v)):
                if s_ == 1:
                    v.wait_ge(dSv[0].h, dsv_tot[0])
                    v.wait_ge(dSv[1].h, dsv_tot[1])
                for m in range(32):
                    v.wait_ge(cPE.h, M[("4wo", s_, m)])
                    v.wait_ge(cDVE.h, cDVE.v)
                    v.wait_ge(dSx[m % 2].h, dsx0[m % 2] + 16 * (m // 2 + 1))
                    pt = pg0[:, 0:512] if m % 2 == 0 else pg0[:, 512:1024]
                    cDVE.tick(v.tensor_tensor(zh[:, 512 * m:512 * (m + 1)], pt,
                                              xres[m % 2][:, :], ALU.add))
                    v.wait_ge(cDVE.h, cDVE.v)
                    cDVE.tick(v.tensor_copy(zbf[:, 512 * m:512 * (m + 1)],
                                            zh[:, 512 * m:512 * (m + 1)]))
                v.wait_ge(cPE.h, M[("4st", s_)])
                cDVE.tick(v.tensor_scalar(row_q[0:1, 512:1024], pav[0][0:1, 0:512],
                                          -1.0 / D, None, ALU.mult))
                v.wait_ge(cDVE.h, cDVE.v)
                with lp():
                    cDVE.tick(v.tensor_copy(row_r[0:1, 512:1024], row_q[0:1, 512:1024]))
                cDVE.tick(v.tensor_scalar(row_t[0:1, 0:512], pav[0][0:1, 0:512],
                                          1.0 / D, None, ALU.mult))
                cDVE.tick(v.tensor_scalar(row_k[0:1, 0:512], pav[1][0:1, 0:512],
                                          1.0 / D, None, ALU.mult))
                v.wait_ge(cDVE.h, cDVE.v)
                cDVE.tick(v.tensor_mul(row_t[0:1, 512:1024], row_t[0:1, 0:512],
                                       row_t[0:1, 0:512]))
                v.wait_ge(cDVE.h, cDVE.v)
                cDVE.tick(v.tensor_sub(row_k[0:1, 0:512], row_k[0:1, 0:512],
                                       row_t[0:1, 512:1024]))
                v.wait_ge(cDVE.h, cDVE.v)
                cDVE.tick(v.tensor_scalar(row_k[0:1, 0:512], row_k[0:1, 0:512],
                                          EPS_LN, None, ALU.add))
                v.wait_ge(cACT.h, M[("4act1", s_)])
                cDVE.tick(v.reciprocal(row_q[0:1, 0:512], row_t[0:1, 0:512]))
                v.wait_ge(cDVE.h, cDVE.v)
                with lp():
                    cDVE.tick(v.tensor_copy(row_r[0:1, 0:512], row_q[0:1, 0:512]))
                cDVE.tick(v.tensor_scalar(row_k[0:1, 512:1024], row_t[0:1, 0:512],
                                          2.0, None, ALU.mult))
                v.wait_ge(cDVE.h, cDVE.v)
                with lp():
                    cDVE.tick(v.tensor_copy(row_r2[0:1, 0:512], row_k[0:1, 512:1024]))
                v.wait_ge(cPE.h, M[("4bc", s_)])
                cDVE.tick(v.tensor_copy(rstdb[:, :], pms))
                cDVE.tick(v.tensor_copy(negmub[:, :], psp))
                cDVE.tick(v.tensor_copy(scl2b[:, :], pg1[:, 512:1024]))
                v.wait_ge(cPE.h, M[("4rt", s_)])
                cDVE.tick(v.tensor_copy(cwsb[0:64, 0:1], pav[0][0:64, 0:1]))
                v.wait_ge(cDVE.h, cDVE.v)
                cDVE.tick(v.scalar_tensor_tensor(hscr[0:64, :], negmub[0:64, :],
                                                 cwsb[0:64, 0:1], pg1[0:64, 0:512],
                                                 ALU.mult, ALU.add))
                v.wait_ge(cDVE.h, cDVE.v)
                cDVE.tick(v.tensor_mul(rrsb[0:64, :], hscr[0:64, :], rstdb[0:64, :]))
                for m in range(8):
                    v.wait_ge(cPE.h, M[("4h", s_, m)])
                    v.wait_ge(cDVE.h, cDVE.v)
                    cDVE.tick(v.tensor_copy(casb[:, m:m + 1], pav[1][:, m:m + 1]))
                    v.wait_ge(cDVE.h, cDVE.v)
                    pt = pg0[:, 0:512] if m % 2 == 0 else pg0[:, 512:1024]
                    cDVE.tick(v.scalar_tensor_tensor(hscr[:, :], negmub[:, :],
                                                     casb[:, m:m + 1], pt,
                                                     ALU.mult, ALU.add))
                    v.wait_ge(cDVE.h, cDVE.v)
                    cDVE.tick(v.tensor_mul(hbf[:, 512 * m:512 * (m + 1)], hscr[:, :],
                                           rstdb[:, :]))
            assert KSTOP < 40 or (cDVE.v == dve0 + 105)
            M[("4dve1", s_)] = cDVE.v

            @blk4.tensor
            def _(t, s_=s_):
                t.wait_ge(cDVE.h, M[("4dve1", s_)] - 24)
                for j in range(4):
                    ins = t.transpose(pms[:, 64 * j:64 * (j + 1)],
                                      rrsb[0:64, 128 * j:128 * (j + 1)], c_ident[0:64, 0:64])
                    cPE.tick(ins)
                M[("4tr", s_)] = cPE.v

            @blk4.vector
            def _(v, s_=s_):
                v.wait_ge(cPE.h, M[("4tr", s_)])
                v.wait_ge(cDVE.h, cDVE.v)
                for j in range(4):
                    cDVE.tick(v.tensor_copy(rsrow[:, 64 * j:64 * (j + 1)], pms[:, 64 * j:64 * (j + 1)]))
                for j in range(4):
                    sl = slice(64 * j, 64 * j + 64)
                    v.wait_ge(cDVE.h, cDVE.v)
                    cDVE.tick(v.max(out=mx16[:, 0:8], in_=rsrow[:, sl]))
                    v.wait_ge(cDVE.h, cDVE.v)
                    cDVE.tick(v.match_replace(out=zap[:, sl], in_to_replace=mx16[:, 0:8],
                                              in_values=rsrow[:, sl], imm_value=-3e38))
                    v.wait_ge(cDVE.h, cDVE.v)
                    cDVE.tick(v.max(out=mx16[:, 8:16], in_=zap[:, sl]))
                    v.wait_ge(cDVE.h, cDVE.v)
                    cDVE.tick(v.match_replace(out=zap[:, sl], in_to_replace=mx16[:, 8:16],
                                              in_values=zap[:, sl], imm_value=-3e38))
                    v.wait_ge(cDVE.h, cDVE.v)
                    cDVE.tick(v.tensor_scalar(colr[:, j:j + 1], mx16[:, 0:1], -1.0, None, ALU.mult))
                    v.wait_ge(cDVE.h, cDVE.v)
                    M[("4tk", s_, j)] = cDVE.v
            M[("4tk_all", s_)] = cDVE.v

            @blk4.scalar
            def _(s, s_=s_):
                for j in range(4):
                    sl = slice(64 * j, 64 * j + 64)
                    s.wait_ge(cDVE.h, M[("4tk", s_, j)])
                    cACT.tick(s.activation(gvals[:, sl], rsrow[:, sl], AF.Exp,
                                           bias=colr[:, j:j + 1]))
                    M[("4ex", s_, j)] = cACT.v

            @blk4.vector
            def _(v, s_=s_):
                for j in range(4):
                    sl = slice(64 * j, 64 * j + 64)
                    v.wait_ge(cACT.h, M[("4ex", s_, j)])
                    v.wait_ge(cDVE.h, cDVE.v)
                    cDVE.tick(v.tensor_scalar(rsrow[:, sl], zap[:, sl], -1e37, None, ALU.is_le))
                    v.wait_ge(cDVE.h, cDVE.v)
                    cDVE.tick(v.tensor_mul(gvals[:, sl], gvals[:, sl], rsrow[:, sl]))
                    v.wait_ge(cDVE.h, cDVE.v)
                    cDVE.tick(v.reduce_sum(colr[:, 4:5], gvals[:, sl], axis=AX.X))
                    v.wait_ge(cDVE.h, cDVE.v)
                    cDVE.tick(v.reciprocal(colr[:, 5:6], colr[:, 4:5]))
                    v.wait_ge(cDVE.h, cDVE.v)
                    cDVE.tick(v.tensor_scalar(gvals[:, sl], gvals[:, sl], colr[:, 5:6],
                                              None, ALU.mult))
                    v.wait_ge(cDVE.h, cDVE.v)
            M[("4g", s_)] = cDVE.v

            @blk4.tensor
            def _(t, s_=s_):
                t.wait_ge(cDVE.h, M[("4g", s_)])
                for j in range(4):
                    ins = t.transpose(psp[0:64, 128 * j:128 * (j + 1)],
                                      gvals[:, 64 * j:64 * (j + 1)], c_ident[:, :])
                    cPE.tick(ins)
                M[("4gt", s_)] = cPE.v

            @blk4.vector
            def _(v, s_=s_):
                v.wait_ge(cPE.h, M[("4gt", s_)])
                v.wait_ge(cDVE.h, cDVE.v)
                for j in range(4):
                    ins = v.tensor_mul(gtb[0:64, 128 * j:128 * (j + 1)],
                                       psp[0:64, 128 * j:128 * (j + 1)],
                                       scl2b[0:64, 128 * j:128 * (j + 1)])
                    cDVE.tick(ins)
            M[("4gtb", s_)] = cDVE.v

            @blk4.tensor
            def _(t, s_=s_):
                t.wait_ge(cDVE.h, M[("4gtb", s_)])
                for j8 in range(8):
                    if j8 >= 2:
                        t.wait_ge(cDVE.h, M[("4gtb", s_)] + (j8 - 2) + 1)
                    pt = pg1[:, 0:512] if j8 % 2 == 0 else pg1[:, 512:1024]
                    ins = t.matmul(pt, c_emat[0:64, 128 * j8:128 * (j8 + 1)],
                                   gtb[0:64, 0:512], start=True, stop=True)
                    cPE.tick(ins)
                    M[("4ge", s_, j8)] = cPE.v

            @blk4.vector
            def _(v, s_=s_):
                for j8 in range(8):
                    v.wait_ge(cPE.h, M[("4ge", s_, j8)])
                    v.wait_ge(cDVE.h, cDVE.v)
                    pt = pg1[:, 0:512] if j8 % 2 == 0 else pg1[:, 512:1024]
                    ins = v.tensor_tensor(hgbf[:, 512 * j8:512 * (j8 + 1)],
                                          hbf[:, 512 * j8:512 * (j8 + 1)], pt, ALU.mult)
                    cDVE.tick(ins)
                    M[("4hg", s_, j8)] = cDVE.v
                    assert M[("4hg", s_, j8)] == M[("4gtb", s_)] + j8 + 1
            M[("4hg_all", s_)] = cDVE.v

            dve_B0 = cDVE.v

            @blk4.tensor
            def _(t, s_=s_, dpa0=dpa0, dve0=dve0, dve_B0=dve_B0):
                t.wait_ge(cDVE.h, M[("4hg_all", s_)])
                for m in range(32):
                    t.wait_ge(dPs[m % 2].h, dpa0[m % 2] + 16 * (36 + m // 2 + 1))
                    if m >= 2:
                        t.wait_ge(cDVE.h, dve_B0 + (m - 2) + 1)
                    pt = pg0[:, 0:512] if m % 2 == 0 else pg0[:, 512:1024]
                    for kr in range(8):
                        ins = t.matmul(pt, btb[m % 2][:, 128 * kr:128 * (kr + 1)],
                                       hgbf[:, 512 * kr:512 * (kr + 1)],
                                       start=(kr == 0), stop=(kr == 7))
                    cPE.tick(ins)
                    M[("4B", s_, m)] = cPE.v

            @blk4.gpsimd
            def _(g, s_=s_, dpa0=dpa0):
                for m in range(32):
                    if m >= 2:
                        g.wait_ge(cPE.h, M[("4B", s_, m - 2)])
                    else:
                        g.wait_ge(cPE.h, M[("4h", s_, 7)])
                    g.dma_start(out=btb[m % 2][:, :],
                                in_=bt[:, 128 * m:128 * (m + 1)].rearrange("(t p) m -> p t m", p=128)
                                ).then_inc(dPs[m % 2].h, 16)
                    dPs[m % 2].v += 16
                    assert dPs[m % 2].v == dpa0[m % 2] + 16 * (36 + m // 2 + 1)

            @blk4.vector
            def _(v, s_=s_, dve_B0=dve_B0):
                for m in range(32):
                    v.wait_ge(cPE.h, M[("4B", s_, m)])
                    v.wait_ge(cDVE.h, cDVE.v)
                    pt = pg0[:, 0:512] if m % 2 == 0 else pg0[:, 512:1024]
                    ins = v.tensor_tensor(zh[:, 512 * m:512 * (m + 1)], pt,
                                          zh[:, 512 * m:512 * (m + 1)], ALU.add)
                    cDVE.tick(ins)
                    M[("4v", s_, m)] = cDVE.v
                    assert cDVE.v == dve_B0 + m + 1
            M[("4v_all", s_)] = cDVE.v

            if DEBUG and s_ == 0:
                @blk4.sync
                def _(sy):
                    sy.wait_ge(cDVE.h, M[("4v_all", 0)])
                    sy.dma_start(out=dbg["z"], in_=zh).then_inc(dS.h, 16)
                    sy.dma_start(out=dbg["gg"], in_=gvals).then_inc(dS.h, 16)
                    sy.dma_start(out=dbg["hg"], in_=hgbf).then_inc(dS.h, 16)
                    dS.v += 48

            for q in range(4):
                dve_q0 = cDVE.v

                @blk4.tensor
                def _(t, s_=s_, q=q, dve_q0=dve_q0):
                    t.wait_ge(cDVE.h, M[("4v_all", s_)])
                    if q >= 1:
                        t.wait_ge(cDVE.h, M[("4ln_e", s_, q - 1, 7)])
                    if q >= 2:
                        t.wait_ge(cDVE.h, M[("4ln_n", s_, q - 2)])
                    for c_ in range(32):
                        grp = c_ // 4
                        pt = pms if grp % 2 == 0 else psp
                        if c_ % 4 == 0 and c_ >= 8:
                            t.wait_ge(cDVE.h, dve_q0 + (grp - 2) + 1)
                        ins = t.transpose(pt[:, 128 * (c_ % 4):128 * (c_ % 4 + 1)],
                                          zh[:, 512 * c_ + 128 * q:512 * c_ + 128 * q + 128],
                                          c_ident[:, :])
                        cPE.tick(ins)
                        M[("4ln_t", s_, q, c_)] = cPE.v

                @blk4.vector
                def _(v, s_=s_, q=q, dve_q0=dve_q0):
                    for grp in range(8):
                        v.wait_ge(cPE.h, M[("4ln_t", s_, q, 4 * grp + 3)])
                        v.wait_ge(cDVE.h, cDVE.v)
                        pt = pms if grp % 2 == 0 else psp
                        ins = v.tensor_copy(vTq[q % 2][:, 512 * grp:512 * (grp + 1)], pt[:, 0:512])
                        cDVE.tick(ins)
                        M[("4ln_e", s_, q, grp)] = cDVE.v
                        assert cDVE.v == dve_q0 + grp + 1
                    v.wait_ge(cDVE.h, cDVE.v)
                    for jj in range(8):
                        ins = v.bn_stats(bnst[:, 6 * jj:6 * jj + 6],
                                         vTq[q % 2][:, 512 * jj:512 * (jj + 1)])
                    cDVE.tick(ins)
                    v.wait_ge(cDVE.h, cDVE.v)
                    cDVE.tick(v.bn_aggr(colr[:, 4:6], bnst[:, 0:48]))
                    v.wait_ge(cDVE.h, cDVE.v)
                    cDVE.tick(v.tensor_scalar(colr[:, 6:7], colr[:, 5:6], EPS_LN, None, ALU.add))
                    v.wait_ge(cDVE.h, cDVE.v)
                    cDVE.tick(v.tensor_scalar(colr[:, 4:5], colr[:, 4:5], -1.0, None, ALU.mult))
                M[("4bn", s_, q)] = cDVE.v

                @blk4.scalar
                def _(s, s_=s_, q=q):
                    s.wait_ge(cDVE.h, M[("4bn", s_, q)])
                    cACT.tick(s.activation(colr[:, 7:8], colr[:, 6:7], AF.Sqrt))
                M[("4sd2", s_, q)] = cACT.v

                @blk4.vector
                def _(v, s_=s_, q=q, ds0=ds0):
                    v.wait_ge(cACT.h, M[("4sd2", s_, q)])
                    cDVE.tick(v.reciprocal(colr[:, 6:7], colr[:, 7:8]))
                    v.wait_ge(cDVE.h, cDVE.v)
                    if q >= 2:
                        v.wait_ge(M[("4out", s_, q - 2)][0].h, M[("4out", s_, q - 2)][1])
                    ins = v.tensor_scalar(vTo[q % 2][:, :], vTq[q % 2][:, :],
                                          colr[:, 4:5], colr[:, 6:7], ALU.add, ALU.mult)
                    cDVE.tick(ins)
                    M[("4ln_n", s_, q)] = cDVE.v

                @blk4.sync
                def _(sy, s_=s_, q=q):
                    sy.wait_ge(cDVE.h, M[("4ln_n", s_, q)])
                    sy.dma_start(out=out[512 * s_ + 128 * q:512 * s_ + 128 * (q + 1), :],
                                 in_=vTo[q % 2][:, :]).then_inc(dSv[q % 2].h, 16)
                    dSv[q % 2].v += 16
                    M[("4out", s_, q)] = (dSv[q % 2], dSv[q % 2].v)

        @blk.sync
        def _(sy):
            for c in [dS, dP, dSq] + dPs + dSk + dSr + dSv + dSx:
                sy.wait_ge(c.h, c.v)

        return nc
    return nc


def kernel(**inputs):
    per_core = _host_layout(inputs)
    nc = build_nc()
    res = run_bass_kernel_spmd(nc, per_core, core_ids=list(range(NCORES)))
    outs = [res.results[c]["out"] for c in range(NCORES)]
    return np.concatenate(outs, axis=0).reshape(1, S, D)
